# version 06
# baseline (speedup 1.0000x reference)
import os as _os
import numpy as np
import ml_dtypes
import concourse.bass as bass
import concourse.mybir as mybir
from concourse.bass_utils import run_bass_kernel_spmd
from concourse.bass_core_rust import IndirectOffsetOnAxis

F32 = mybir.dt.float32
BF16 = mybir.dt.bfloat16
FP16 = mybir.dt.float16
I32 = mybir.dt.int32
AF = mybir.ActivationFunctionType
ALU = mybir.AluOpType
AX = mybir.AxisListType

NB = 2
NSEG = 4
D = 2048
S = 2048
DFF = 8192
NSLOT = 256
SCALE = float(1.0 / np.sqrt(128.0))
EPS = 1e-5
NBIS = 33
WOFF = 16.0


class Sem:
    def __init__(self, es, nc, name):
        self.h = es.enter_context(nc.semaphore(name))
        self.n = 0

    def inc(self, k=1):
        self.n += k
        return self.n


def build_nc():
    from contextlib import ExitStack
    nc = bass.Bass("TRN2", target_bir_lowering=False, detect_race_conditions=False)
    es = ExitStack()

    x_h = nc.dram_tensor("x", [NB, S, D], F32, kind="ExternalInput").ap()
    wr_h = nc.dram_tensor("Wr", [1, D], F32, kind="ExternalInput").ap()
    br_h = nc.dram_tensor("br", [1, 1], F32, kind="ExternalInput").ap()
    w1_h = nc.dram_tensor("W1", [D, DFF], F32, kind="ExternalInput").ap()
    b1_h = nc.dram_tensor("b1", [128, 64], F32, kind="ExternalInput").ap()
    w2_h = nc.dram_tensor("W2", [DFF, D], F32, kind="ExternalInput").ap()
    b2_h = nc.dram_tensor("b2", [128, 16], F32, kind="ExternalInput").ap()
    idt_h = nc.dram_tensor("IDT", [128, 128], BF16, kind="ExternalInput").ap()
    idtf_h = nc.dram_tensor("IDTF", [128, 128], F32, kind="ExternalInput").ap()
    tri_h = nc.dram_tensor("TRI", [128, 128], BF16, kind="ExternalInput").ap()
    tri16_h = nc.dram_tensor("TRI16", [16, 16], BF16, kind="ExternalInput").ap()
    onesc_h = nc.dram_tensor("ONESC", [128, 128], BF16, kind="ExternalInput").ap()
    ones1f_h = nc.dram_tensor("ONES1F", [1, 128], F32, kind="ExternalInput").ap()
    onescf_h = nc.dram_tensor("ONESCF", [128, 1], F32, kind="ExternalInput").ap()
    io256_h = nc.dram_tensor("IO256", [128, 256], F32, kind="ExternalInput").ap()
    sg_h = nc.dram_tensor("SG", [128, 16], FP16, kind="ExternalInput").ap()
    tg_h = nc.dram_tensor("TG", [128, 16], FP16, kind="ExternalInput").ap()
    out_h = nc.dram_tensor("out", [NB, S, D], F32, kind="ExternalOutput").ap()
    dbg_h = nc.dram_tensor("dbg", [128, 6176], F32, kind="ExternalOutput").ap()
    x1m_h = nc.dram_tensor("x1m", [NB, S, D], BF16, kind="ExternalOutput" if _os.environ.get("KDBG") else "Internal").ap()

    def xview(h, b, s, par, tl):
        return h[b].rearrange("(s tl p two) d -> s two tl p d", s=4, tl=2, p=128, two=2)[s, par, tl]

    sb = lambda name, shape, dt: es.enter_context(nc.sbuf_tensor(name, shape, dt))
    IDT = sb("c_idt", [128, 128], BF16)
    IDTF = sb("c_idtf", [128, 128], F32)
    TRI = sb("c_tri", [128, 128], BF16)
    TRI16 = sb("c_tri16", [16, 16], BF16)
    ONESC = sb("c_onesc", [128, 128], BF16)
    ONES1F = sb("c_ones1f", [1, 128], F32)
    ONESCF = sb("c_onescf", [128, 1], F32)
    IO256 = sb("c_io256", [128, 256], F32)
    SG = sb("c_sg", [128, 16], FP16)
    TG = sb("c_tg", [128, 16], FP16)
    WrT = sb("c_wrt", [1, D], BF16)
    WrB = sb("c_wrb", [128, D], BF16)
    brS = sb("c_brs", [1, 1], F32)
    brB = sb("c_brb", [128, 1], F32)
    b1T = sb("c_b1t", [128, 64], F32)
    b2T = sb("c_b2t", [128, 16], F32)

    lnstats = sb("lnstats", [128, 4 * 24], F32)
    lnmv = sb("lnmv", [128, 4 * 8], F32)
    lo_b = sb("lo", [128, 2], F32)
    hi_b = sb("hi", [128, 2], F32)
    mid_b = sb("mid", [128, 2], F32)
    tmp16 = sb("tmp16", [128, 16], BF16)
    prt_b = sb("prt", [128, 2], BF16)
    cge_b = sb("cge", [128, 2], F32)
    dlt_b = sb("dlt", [128, 2], F32)
    dlt2_b = sb("dlt2", [128, 2], F32)
    L2g = sb("L2g", [128, 16], F32)
    vspc = sb("vspc", [128, 2], F32)
    mask_b = sb("mask", [128, 2 * 16], F32)
    maskbf = sb("maskbf", [128, 16], BF16)
    ct16 = sb("ct16", [16, 1], BF16)
    cb16 = sb("cb16", [1, 16], BF16)
    pos_b = sb("pos", [128, 16], F32)
    lbr_b = sb("lbr", [128, 16], F32)
    wexp = sb("wexp", [128, 16], F32)
    wm32 = sb("wm32", [128, 16], F32)
    wprt = sb("wprt", [128, 1], F32)
    bndc = sb("bndc", [128, 2], F32)
    bnds = sb("bnds", [1, 2], F32)
    bndB = sb("bndB", [128, 2], F32)
    VALS = sb("VALS", [128, 16 * 3], FP16)
    zt_s = sb("zt", [1, 2], F32)
    rzB = sb("rzB", [128, 1], F32)
    slotv = sb("slotv", [128, 4 * 4], F32)
    sigI = sb("sigI", [128, 4], I32)
    tI = sb("tI", [128, 4], I32)

    pbank = [es.enter_context(nc.psum_tensor(f"pb{i}", [128, 512], F32)) for i in range(6)]
    pbank_bf = [es.enter_context(nc.psum_tensor(f"pbb{i}", [128, 512], BF16)) for i in range(2)]

    sT = Sem(es, nc, "sT")
    sV = Sem(es, nc, "sV")
    sS = Sem(es, nc, "sS")
    sC = Sem(es, nc, "sC")
    sWrt = Sem(es, nc, "sWrt")
    sX = [Sem(es, nc, f"sX{i}") for i in range(2)]
    sXB = [Sem(es, nc, f"sXB{i}") for i in range(2)]
    sE2 = [Sem(es, nc, f"sE2{i}") for i in range(2)]
    sM4 = [Sem(es, nc, f"sM4{i}") for i in range(4)]
    sW1 = [Sem(es, nc, f"sW1{i}") for i in range(2)]
    sW2 = [Sem(es, nc, f"sW2{i}") for i in range(2)]
    sGG = Sem(es, nc, "sGG")

    OPS = {"T": [], "V": [], "S": [], "G": [], "Y": []}
    T = lambda fn: OPS["T"].append(fn)
    V = lambda fn: OPS["V"].append(fn)
    SC = lambda fn: OPS["S"].append(fn)
    G = lambda fn: OPS["G"].append(fn)
    Y = lambda fn: OPS["Y"].append(fn)

    def setup_loads(e):
        for dst, src in ((IDT, idt_h), (IDTF, idtf_h), (TRI, tri_h), (TRI16, tri16_h), (ONESC, onesc_h),
                         (ONES1F, ones1f_h), (ONESCF, onescf_h), (IO256, io256_h),
                         (SG, sg_h), (TG, tg_h), (brS, br_h)):
            e.dma_start(out=dst[:, :], in_=src[:, :]).then_inc(sC.h, 16)
        e.dma_start(out=b1T[:, :], in_=b1_h[:, :]).then_inc(sC.h, 16)
        e.dma_start(out=b2T[:, :], in_=b2_h[:, :]).then_inc(sC.h, 16)
    Y(setup_loads)
    LD_CONST = sC.inc(16 * 13)

    def setup_wrt(e):
        e.dma_start(out=WrT[:, :], in_=wr_h[:, :]).then_inc(sWrt.h, 16)
    G(setup_wrt)
    LD_WRT = sWrt.inc(16)

    def setup_bcast_t(e):
        e.wait_ge(sC.h, LD_CONST)
        e.wait_ge(sWrt.h, LD_WRT)
        for i in range(4):
            e.matmul(pbank[i][:, :], ONESC[0:1, :], WrT[:, 512 * i:512 * (i + 1)], start=True, stop=True)
        e.matmul(pbank[4][:, 0:1], ONES1F[:, :], brS[:, :], start=True, stop=True).then_inc(sT.h, 1)
    T(setup_bcast_t)
    T_BCAST = sT.inc()

    def setup_bcast_v(e):
        e.memset(vspc[:, :], 0.0)
        e.wait_ge(sT.h, T_BCAST)
        for i in range(4):
            e.tensor_copy(WrB[:, 512 * i:512 * (i + 1)], pbank[i][:, :])
        e.tensor_copy(brB[:, :], pbank[4][:, 0:1]).then_inc(sV.h, 1)
    V(setup_bcast_v)
    V_BCAST = sV.inc()

    esA = ExitStack()
    xz = esA.enter_context(nc.sbuf_tensor("xz", [128, 2 * 4 * D], F32))
    xbuf = esA.enter_context(nc.sbuf_tensor("xbw", [128, 2 * 4 * D], BF16))
    x1st = esA.enter_context(nc.sbuf_tensor("x1stage", [128, 4 * D], BF16))
    sttscr = esA.enter_context(nc.sbuf_tensor("sttscr", [128, D], BF16))
    Lgrid = esA.enter_context(nc.sbuf_tensor("Lgrid", [128, 2 * 16], F32))
    outear = esA.enter_context(nc.sbuf_tensor("outear", [128, 2 * D], F32))
    PT = esA.enter_context(nc.sbuf_tensor("PT", [128, 16 * 256], FP16))
    gT = esA.enter_context(nc.sbuf_tensor("gT", [128, 2 * 256], BF16))
    Ebuf = esA.enter_context(nc.sbuf_tensor("Ebuf", [128, 2 * 512], BF16))
    Dbuf = esA.enter_context(nc.sbuf_tensor("Dbuf", [128, 2 * 2], F32))
    Rbuf = esA.enter_context(nc.sbuf_tensor("Rbuf", [128, 2 * 2], F32))

    def xz_t(sbuf_i, par, tl):
        off = (sbuf_i * 4 + par * 2 + tl) * D
        return xz[:, off:off + D]

    def xb_t(sbuf_i, par, tl):
        off = (sbuf_i * 4 + par * 2 + tl) * D
        return xbuf[:, off:off + D]

    pair_list = [(b, s) for b in range(NB) for s in range(NSEG)]
    xload_done = {}
    xcast_done = {}
    seg_consumed = {}

    uT_tp, uT_sc, uT_pv = {}, {}, {}
    uS_E = {}
    uS_gT = {}
    uV_gT, uV_R, uV_z = {}, {}, {}
    ls_apply, lv_logit, lm_mirror, ly_early = {}, {}, {}, {}
    lv_stats, ls_inv, lv_bias = {}, {}, {}

    pend_vz = [None]
    attn_units = []
    for b, s in pair_list:
        for par in range(2):
            for hh in range(8):
                attn_units.append((b, s, par, hh))
    last_unit_of_pair = {}
    for ui, u in enumerate(attn_units):
        last_unit_of_pair[(u[0], u[1], u[2])] = ui

    ln_units = []
    for gi, (b, s) in enumerate(pair_list):
        for par in range(2):
            for tl in range(2):
                ln_units.append((gi, b, s, par, tl))
    last_li_of_b = {b: max(i for i, u in enumerate(ln_units) if u[1] == b) for b in range(NB)}

    def emit_load(gi):
        b, s = pair_list[gi]
        sbuf_i = gi % 2

        def loadseg(e, b=b, s=s, sbuf_i=sbuf_i, gi=gi):
            if gi == 2 and _os.environ.get("KDBG"):
                e.wait_ge(sC.h, DBGC[0])
            if gi >= 2:
                tS, tT = seg_consumed[pair_list[gi - 2]]
                e.wait_ge(sS.h, tS)
                e.wait_ge(sT.h, tT)
            for par in range(2):
                for tl in range(2):
                    e.dma_start(out=xz_t(sbuf_i, par, tl), in_=xview(x_h, b, s, par, tl)).then_inc(sX[gi % 2].h, 16)
        Y(loadseg)
        xload_done[(b, s)] = (gi % 2, sX[gi % 2].inc(16 * 4))

    def emit_cast(gi):
        b, s = pair_list[gi]
        sbuf_i = gi % 2

        def castseg(e, b=b, s=s, sbuf_i=sbuf_i, gi=gi):
            if gi == 2 and _os.environ.get("KDBG"):
                e.wait_ge(sC.h, DBGC[0])
            xi, xv = xload_done[(b, s)]
            e.wait_ge(sX[xi].h, xv)
            if gi >= 2:
                tS, tT = seg_consumed[pair_list[gi - 2]]
                e.wait_ge(sS.h, tS)
                e.wait_ge(sT.h, tT)
            for par in range(2):
                for tl in range(2):
                    e.dma_start(out=xb_t(sbuf_i, par, tl), in_=xz_t(sbuf_i, par, tl)).then_inc(sXB[gi % 2].h, 16)
        G(castseg)
        xcast_done[(b, s)] = (gi % 2, sXB[gi % 2].inc(16 * 4))

    def emit_attn_unit(ui):
        b, s, par, hh = attn_units[ui]
        gi = pair_list.index((b, s))
        sbuf_i = gi % 2
        slot = ui % 2
        col = (2 * hh + par) * 128

        def t_tp(e, b=b, s=s, par=par, ui=ui, sbuf_i=sbuf_i, col=col):
            if ui == 0:
                e.wait_ge(sV.h, V_BCAST)
            ci, cv = xcast_done[(b, s)]
            e.wait_ge(sXB[ci].h, cv)
            if _os.environ.get("KSERIAL") and ui >= 1:
                e.wait_ge(sV.h, uV_z[ui - 1])
            if _os.environ.get("KSERIAL") and ui % 16 == 0 and ui >= 16:
                li_prev = max(i for i, lu in enumerate(ln_units) if lu[0] == (ui // 16) - 1)
                e.wait_ge(sV.h, lv_logit[li_prev])
            if ui >= 1:
                e.wait_ge(sV.h, uV_gT[ui - 1])
            e.transpose(pbank_bf[0][:, 0:128], xb_t(sbuf_i, par, 0)[:, col:col + 128], IDT[:, :])
            e.transpose(pbank_bf[1][:, 0:128], xb_t(sbuf_i, par, 1)[:, col:col + 128], IDT[:, :]).then_inc(sT.h, 1)
        T(t_tp)
        uT_tp[ui] = sT.inc()

        def v_gT(e, ui=ui, slot=slot):
            e.wait_ge(sT.h, uT_tp[ui])
            if ui >= 2:
                e.wait_ge(sT.h, uT_sc[ui - 2])
            go = slot * 256
            e.tensor_copy(gT[:, go:go + 128], pbank_bf[0][:, 0:128])
            e.tensor_copy(gT[:, go + 128:go + 256], pbank_bf[1][:, 0:128]).then_inc(sV.h, 1)
        V(v_gT)
        uV_gT[ui] = sV.inc()

        if pend_vz[0] is not None:
            fn, pu = pend_vz[0]
            V(fn)
            uV_z[pu] = sV.inc()
            pend_vz[0] = None

        def t_sc(e, ui=ui, slot=slot):
            e.wait_ge(sV.h, uV_gT[ui])
            if ui >= 1:
                e.wait_ge(sS.h, uS_E[ui - 1] - 1)
            go = slot * 256
            e.matmul(pbank[2][:, 0:256], gT[:, go:go + 128], gT[:, go:go + 256], start=True, stop=True)
            if ui >= 1:
                e.wait_ge(sS.h, uS_E[ui - 1])
            e.matmul(pbank[3][:, 0:256], gT[:, go + 128:go + 256], gT[:, go:go + 256], start=True, stop=True).then_inc(sT.h, 1)
        T(t_sc)
        uT_sc[ui] = sT.inc()

        def s_E(e, ui=ui, slot=slot):
            e.wait_ge(sT.h, uT_sc[ui])
            if ui >= 2:
                e.wait_ge(sT.h, uT_pv[ui - 2])
                e.wait_ge(sV.h, uV_R[ui - 2])
            eo = slot * 512
            do = slot * 2
            e.activation(out=Ebuf[:, eo:eo + 256], in_=pbank[2][:, 0:256], func=AF.Exp,
                         scale=SCALE, accum_out=Dbuf[:, do:do + 1]).then_inc(sS.h, 1)
            e.activation(out=Ebuf[:, eo + 256:eo + 512], in_=pbank[3][:, 0:256], func=AF.Exp,
                         scale=SCALE, accum_out=Dbuf[:, do + 1:do + 2]).then_inc(sS.h, 1)
        SC(s_E)
        sS.inc()
        uS_E[ui] = sS.inc()

        def v_R(e, ui=ui, slot=slot):
            e.wait_ge(sS.h, uS_E[ui])
            do = slot * 2
            e.reciprocal(Rbuf[:, do:do + 2], Dbuf[:, do:do + 2]).then_inc(sV.h, 1)
        V(v_R)
        uV_R[ui] = sV.inc()

        def t_pv(e, ui=ui, slot=slot, sbuf_i=sbuf_i, par=par, col=col):
            e.wait_ge(sS.h, uS_E[ui])
            if ui >= 1:
                e.wait_ge(sV.h, uV_z[ui - 1])
            eo = slot * 512
            last = None
            for j in range(2):
                for k in range(2):
                    last = e.matmul(pbank[4 + j][:, 0:128],
                                    Ebuf[:, eo + k * 256 + j * 128:eo + k * 256 + (j + 1) * 128],
                                    xb_t(sbuf_i, par, k)[:, col:col + 128],
                                    start=(k == 0), stop=(k == 1))
            last.then_inc(sT.h, 1)
        T(t_pv)
        uT_pv[ui] = sT.inc()

        def v_z(e, ui=ui, slot=slot, sbuf_i=sbuf_i, par=par, col=col, flushed=False):
            e.wait_ge(sT.h, uT_pv[ui])
            e.wait_ge(sV.h, uV_R[ui])
            if flushed:
                e.drain()
            do = slot * 2
            last = None
            for j in range(2):
                zt = xz_t(sbuf_i, par, j)
                last = e.scalar_tensor_tensor(zt[:, col:col + 128], pbank[4 + j][:, 0:128],
                                              Rbuf[:, do + j:do + j + 1], zt[:, col:col + 128],
                                              ALU.mult, ALU.add)
            last.then_inc(sV.h, 1)
        pend_vz[0] = (v_z, ui)

    def emit_ln_unit(li):
        gi, b, s, par, tl = ln_units[li]
        sbuf_i = gi % 2
        k = 4 * s + 2 * par + tl
        ring = k % 4
        ering = li % 2

        def v_stats(e, li=li, b=b, s=s, par=par, tl=tl, ring=ring, sbuf_i=sbuf_i):
            e.wait_ge(sV.h, uV_z[last_unit_of_pair[(b, s, par)]])
            if li >= 4:
                e.wait_ge(sS.h, ls_apply[li - 4])
            zt = xz_t(sbuf_i, par, tl)
            for i in range(4):
                e.bn_stats(lnstats[:, ring * 24 + 6 * i:ring * 24 + 6 * (i + 1)], zt[:, 512 * i:512 * (i + 1)])
            o = ring * 8
            e.drain()
            e.bn_aggr(lnmv[:, o:o + 2], lnstats[:, ring * 24:ring * 24 + 24])
            e.drain()
            e.tensor_scalar(lnmv[:, o + 7:o + 8], lnmv[:, o + 1:o + 2], EPS, None, ALU.add)
            e.drain()
            e.reciprocal(lnmv[:, o + 7:o + 8], lnmv[:, o + 7:o + 8]).then_inc(sV.h, 1)
        V(v_stats)
        lv_stats[li] = sV.inc()

        def s_inv(e, li=li, ring=ring):
            e.wait_ge(sV.h, lv_stats[li])
            o = ring * 8
            e.activation(out=lnmv[:, o + 2:o + 3], in_=lnmv[:, o + 7:o + 8], func=AF.Ln)
            e.activation(out=lnmv[:, o + 2:o + 3], in_=lnmv[:, o + 2:o + 3], func=AF.Exp, scale=0.5).then_inc(sS.h, 1)
        SC(s_inv)
        ls_inv[li] = sS.inc()

        def v_bias(e, li=li, ring=ring):
            e.wait_ge(sS.h, ls_inv[li])
            o = ring * 8
            e.tensor_tensor(lnmv[:, o + 4:o + 5], lnmv[:, o + 1:o + 2], lnmv[:, o + 7:o + 8], ALU.mult)
            e.scalar_tensor_tensor(lnmv[:, o + 3:o + 4], lnmv[:, o + 0:o + 1], -1.0,
                                   lnmv[:, o + 2:o + 3], ALU.mult, ALU.mult)
            e.tensor_scalar(lnmv[:, o + 4:o + 5], lnmv[:, o + 4:o + 5], EPS / 4.0, None, ALU.add)
            e.tensor_copy(lnmv[:, o + 5:o + 6], lnmv[:, o + 2:o + 3])
            e.reciprocal(lnmv[:, o + 4:o + 5], lnmv[:, o + 4:o + 5]).then_inc(sV.h, 1)
        V(v_bias)
        lv_bias[li] = sV.inc()

        def s_s2(e, li=li, ring=ring):
            e.wait_ge(sV.h, lv_bias[li])
            o = ring * 8
            e.activation(out=lnmv[:, o + 4:o + 5], in_=lnmv[:, o + 4:o + 5], func=AF.Ln)
            e.activation(out=lnmv[:, o + 4:o + 5], in_=lnmv[:, o + 4:o + 5], func=AF.Exp,
                         scale=0.5).then_inc(sS.h, 1)
        SC(s_s2)
        ls_s2 = sS.inc()

        def v_scale2(e, li=li, ring=ring, ls_s2=ls_s2):
            e.wait_ge(sS.h, ls_s2)
            o = ring * 8
            e.tensor_tensor(lnmv[:, o + 5:o + 6], lnmv[:, o + 2:o + 3], lnmv[:, o + 4:o + 5], ALU.mult)
            e.tensor_tensor(lnmv[:, o + 6:o + 7], lnmv[:, o + 3:o + 4], lnmv[:, o + 4:o + 5], ALU.mult).then_inc(sV.h, 1)
        V(v_scale2)
        lv_sc2 = sV.inc()

        def s_apply(e, li=li, ring=ring, ering=ering, b=b, s=s, par=par, tl=tl, sbuf_i=sbuf_i, k=k, lv_sc2=lv_sc2):
            e.wait_ge(sV.h, lv_sc2)
            o = ring * 8
            zt = xz_t(sbuf_i, par, tl)
            if li == 4 and _os.environ.get("KDBG"):
                e.wait_ge(sC.h, DBGC[0])
            if li >= 4:
                e.wait_ge(sM4[li % 4].h, lm_mirror[li - 4])
                e.wait_ge(sV.h, lv_logit[li - 4])
            if li >= 2:
                e.wait_ge(sE2[li % 2].h, ly_early[li - 2])
            e.activation(out=x1st[:, ring * D:(ring + 1) * D], in_=zt[:, :], func=AF.Identity,
                         bias=lnmv[:, o + 3:o + 4], scale=lnmv[:, o + 2:o + 3])
            e.activation(out=outear[:, ering * D:(ering + 1) * D], in_=zt[:, :], func=AF.Identity,
                         bias=lnmv[:, o + 6:o + 7], scale=lnmv[:, o + 5:o + 6]).then_inc(sS.h, 1)
            e.wait_ge(sS.h, ls_apply[li])
            e.dma_start(out=x1m_h[b][128 * k:128 * (k + 1), :],
                        in_=x1st[:, ring * D:(ring + 1) * D]).then_inc(sM4[li % 4].h, 16)
        ls_apply[li] = sS.inc()
        SC(s_apply)
        lm_mirror[li] = sM4[li % 4].inc(16)

        def v_logit(e, li=li, ring=ring, b=b, k=k):
            e.wait_ge(sS.h, ls_apply[li])
            e.scalar_tensor_tensor(sttscr[:, :], x1st[:, ring * D:(ring + 1) * D], 1.0, WrB[:, :],
                                   ALU.mult, ALU.mult,
                                   accum_out=Lgrid[:, b * 16 + k:b * 16 + k + 1]).then_inc(sV.h, 1)
        V(v_logit)
        lv_logit[li] = sV.inc()

        def y_early(e, li=li, ering=ering, b=b, s=s, par=par, tl=tl):
            e.wait_ge(sS.h, ls_apply[li])
            e.dma_start(out=xview(out_h, b, s, par, tl), in_=outear[:, ering * D:(ering + 1) * D]).then_inc(sE2[li % 2].h, 16)
        Y(y_early)
        ly_early[li] = sE2[li % 2].inc(16)

        if par == 1 and tl == 1:
            seg_consumed[(b, s)] = (ls_apply[li], uT_pv[last_unit_of_pair[(b, s, 1)]])

    router_state = {}

    def build_router_ops(b, prev_ext_t):
        st = {}
        router_state[b] = st
        logits_ready = lv_logit[last_li_of_b[b]]

        def v_bnd1(e, b=b, logits_ready=logits_ready, prev_ext_t=prev_ext_t):
            e.wait_ge(sV.h, logits_ready)
            if prev_ext_t is not None:
                e.wait_ge(sT.h, prev_ext_t)
            e.reduce_max(bndc[:, 0:1], Lgrid[:, b * 16:(b + 1) * 16], axis=AX.X)
            e.tensor_scalar(lbr_b[:, :], Lgrid[:, b * 16:(b + 1) * 16], -1.0, None, ALU.mult)
            e.reduce_max(bndc[:, 1:2], lbr_b[:, :], axis=AX.X).then_inc(sV.h, 1)
        st["vbnd1"] = sV.inc()
        yield ("V", v_bnd1)

        def t_bnd(e, b=b, st=st):
            e.wait_ge(sV.h, st["vbnd1"])
            e.transpose(pbank[0][0:1, 0:128], bndc[:, 0:1], IDTF[:, :])
            e.transpose(pbank[1][0:1, 0:128], bndc[:, 1:2], IDTF[:, :]).then_inc(sT.h, 1)
        st["tbnd"] = sT.inc()
        yield ("T", t_bnd)

        def v_bnd2(e, b=b, st=st):
            e.wait_ge(sT.h, st["tbnd"])
            e.reduce_max(bnds[:, 0:1], pbank[0][0:1, 0:128], axis=AX.X)
            e.reduce_max(bnds[:, 1:2], pbank[1][0:1, 0:128], axis=AX.X).then_inc(sV.h, 1)
        st["vbnd2"] = sV.inc()
        yield ("V", v_bnd2)

        def t_bnd2(e, b=b, st=st):
            e.wait_ge(sV.h, st["vbnd2"])
            e.matmul(pbank[0][:, 0:2], ONES1F[:, :], bnds[:, :], start=True, stop=True).then_inc(sT.h, 1)
        st["tbnd2"] = sT.inc()
        yield ("T", t_bnd2)

        def v_binit(e, b=b, st=st):
            e.wait_ge(sT.h, st["tbnd2"])
            e.tensor_copy(bndB[:, :], pbank[0][:, 0:2])
            e.drain()
            e.tensor_scalar(hi_b[:, b:b + 1], bndB[:, 0:1], 2.0, None, ALU.mult)
            e.tensor_scalar(lo_b[:, b:b + 1], bndB[:, 1:2], -2.0, -2.0, ALU.mult, ALU.add)
            e.tensor_scalar(L2g[:, :], Lgrid[:, b * 16:(b + 1) * 16], 2.0, None, ALU.mult)
            e.tensor_tensor(mid_b[:, b:b + 1], lo_b[:, b:b + 1], hi_b[:, b:b + 1], ALU.add)
            e.drain()
            with nc.allow_low_precision("bisection counts <=16 are exact in bf16"):
                e.tensor_scalar(tmp16[:, :], L2g[:, :], mid_b[:, b:b + 1], None, ALU.is_gt,
                                ALU.add, accum_out=prt_b[:, b:b + 1]).then_inc(sV.h, 1)
        st[("v", 0)] = sV.inc()
        yield ("V", v_binit)

        for it in range(NBIS):
            def t_it(e, b=b, it=it, st=st):
                e.wait_ge(sV.h, st[("v", it)])
                e.matmul(pbank[it % 2][:, 0:1], ONESC[:, :], prt_b[:, b:b + 1], start=True, stop=True).then_inc(sT.h, 1)
            st[("t", it)] = sT.inc()
            yield ("T", t_it)

            def v_it(e, b=b, it=it, st=st):
                e.wait_ge(sT.h, st[("t", it)])
                ps = pbank[it % 2]
                e.tensor_scalar(cge_b[:, b:b + 1], ps[:, 0:1], float(NSLOT) - 0.5, None, ALU.is_ge)
                e.tensor_tensor(dlt_b[:, b:b + 1], mid_b[:, b:b + 1], lo_b[:, b:b + 1], ALU.subtract)
                e.tensor_tensor(dlt2_b[:, b:b + 1], hi_b[:, b:b + 1], mid_b[:, b:b + 1], ALU.subtract)
                e.scalar_tensor_tensor(lo_b[:, b:b + 1], dlt_b[:, b:b + 1], cge_b[:, b:b + 1],
                                       lo_b[:, b:b + 1], ALU.mult, ALU.add)
                e.scalar_tensor_tensor(hi_b[:, b:b + 1], dlt2_b[:, b:b + 1], cge_b[:, b:b + 1],
                                       mid_b[:, b:b + 1], ALU.mult, ALU.add)
                last = None
                if it < NBIS - 1:
                    e.drain()
                    e.tensor_tensor(mid_b[:, b:b + 1], lo_b[:, b:b + 1], hi_b[:, b:b + 1], ALU.add)
                    e.drain()
                    with nc.allow_low_precision("bisection counts <=16 are exact in bf16"):
                        last = e.tensor_scalar(tmp16[:, :], L2g[:, :], mid_b[:, b:b + 1], None, ALU.is_gt,
                                               ALU.add, accum_out=prt_b[:, b:b + 1])
                else:
                    e.drain()
                    last = e.tensor_scalar(mask_b[:, b * 16:(b + 1) * 16], L2g[:, :],
                                           lo_b[:, b:b + 1], None, ALU.is_gt)
                last.then_inc(sV.h, 1)
            st[("v", it + 1)] = sV.inc()
            yield ("V", v_it)

        def v_fin(e, b=b, st=st):
            e.wait_ge(sV.h, st[("v", NBIS)])
            e.tensor_scalar(lbr_b[:, :], Lgrid[:, b * 16:(b + 1) * 16], bndB[:, 0:1], None,
                            ALU.subtract)
            e.tensor_copy(maskbf[:, :], mask_b[:, b * 16:(b + 1) * 16]).then_inc(sV.h, 1)
        st["vfin"] = sV.inc()
        yield ("V", v_fin)

        def s_wexp(e, b=b, st=st):
            e.wait_ge(sV.h, st["vfin"])
            e.activation(out=wexp[:, :], in_=lbr_b[:, :], func=AF.Exp).then_inc(sS.h, 1)
        st["swexp"] = sS.inc()
        yield ("S", s_wexp)

        def t_ct(e, b=b, st=st):
            e.wait_ge(sV.h, st["vfin"])
            e.matmul(pbank[0][0:16, 0:1], maskbf[:, :], ONESC[:, 0:1], start=True, stop=True).then_inc(sT.h, 1)
        st["tct"] = sT.inc()
        yield ("T", t_ct)

        def v_ct(e, b=b, st=st):
            e.wait_ge(sT.h, st["tct"])
            e.wait_ge(sS.h, st["swexp"])
            e.tensor_tensor(wm32[:, :], wexp[:, :], mask_b[:, b * 16:(b + 1) * 16], ALU.mult)
            e.tensor_copy(ct16[:, :], pbank[0][0:16, 0:1])
            e.reduce_sum(wprt[:, :], wm32[:, :], axis=AX.X).then_inc(sV.h, 1)
        st["vct"] = sV.inc()
        yield ("V", v_ct)

        def t_cb(e, b=b, st=st):
            e.wait_ge(sV.h, st["vct"])
            e.matmul(pbank[1][0:1, 0:16], ct16[:, :], TRI16[:, :], start=True, stop=True)
            e.matmul(pbank[0][0:1, 0:1], wprt[:, :], ONESCF[:, :], start=True, stop=True).then_inc(sT.h, 1)
        st["tcb"] = sT.inc()
        yield ("T", t_cb)

        def v_cb(e, b=b, st=st):
            e.wait_ge(sT.h, st["tcb"])
            e.tensor_copy(zt_s[:, b:b + 1], pbank[0][0:1, 0:1])
            e.tensor_copy(cb16[:, :], pbank[1][0:1, 0:16])
            e.reciprocal(zt_s[:, b:b + 1], zt_s[:, b:b + 1]).then_inc(sV.h, 1)
        st["vcb"] = sV.inc()
        yield ("V", v_cb)

        def t_pos(e, b=b, st=st):
            e.wait_ge(sV.h, st["vcb"])
            e.matmul(pbank[0][:, 0:16], ONESC[0:1, :], cb16[:, :], start=True, stop=True)
            e.matmul(pbank[1][:, 0:16], TRI[:, :], maskbf[:, :], start=True, stop=True).then_inc(sT.h, 1)
        st["tpos"] = sT.inc()
        yield ("T", t_pos)

        def v_pos(e, b=b, st=st):
            e.wait_ge(sT.h, st["tpos"])
            e.tensor_copy(pos_b[:, :], pbank[0][:, 0:16])
            e.tensor_copy(vspc[:, 0:1], vspc[:, 1:2])
            e.tensor_tensor(pos_b[:, :], pos_b[:, :], pbank[1][:, 0:16], ALU.add)
            e.drain()
            for kk in range(16):
                e.tensor_scalar(PT[:, 256 * kk:256 * (kk + 1)], IO256[:, :], pos_b[:, kk:kk + 1],
                                mask_b[:, b * 16 + kk:b * 16 + kk + 1], ALU.is_equal, ALU.mult)
            e.tensor_copy(VALS[:, 0:48:3], SG[:, :])
            e.tensor_copy(VALS[:, 1:48:3], TG[:, :])
            last = None
            for kk in range(16):
                last = e.tensor_scalar(VALS[:, 3 * kk + 2:3 * kk + 3], wm32[:, kk:kk + 1], 1.0, None, ALU.mult)
            last.then_inc(sV.h, 1)
        st["vpos"] = sV.inc()
        yield ("V", v_pos)

        def t_rz(e, b=b, st=st):
            e.wait_ge(sV.h, st["vpos"])
            e.matmul(pbank[0][:, 0:1], ONES1F[:, :], zt_s[:, b:b + 1], start=True, stop=True).then_inc(sT.h, 1)
        st["trz"] = sT.inc()
        yield ("T", t_rz)

        def v_rz(e, b=b, st=st):
            e.wait_ge(sT.h, st["trz"])
            e.tensor_copy(rzB[:, :], pbank[0][:, 0:1]).then_inc(sV.h, 1)
        st["vrz"] = sV.inc()
        yield ("V", v_rz)

        for half in range(2):
            def t_ext(e, b=b, half=half, st=st):
                e.wait_ge(sV.h, st["vrz"] if half == 0 else st[("vext", 0)])
                last = None
                for kk in range(16):
                    last = e.matmul(pbank[half][:, 0:3],
                                    PT[:, 256 * kk + 128 * half:256 * kk + 128 * (half + 1)],
                                    VALS[:, 3 * kk:3 * (kk + 1)], start=(kk == 0), stop=(kk == 15))
                last.then_inc(sT.h, 1)
            st[("text", half)] = sT.inc()
            yield ("T", t_ext)

            def v_ext(e, b=b, half=half, st=st):
                e.wait_ge(sT.h, st[("text", half)])
                stg = b * 2 + half
                o = stg * 4
                e.tensor_copy(slotv[:, o:o + 3], pbank[half][:, 0:3])
                e.tensor_copy(vspc[:, 0:1], vspc[:, 1:2])
                e.tensor_tensor(slotv[:, o + 3:o + 4], slotv[:, o + 2:o + 3], rzB[:, :], ALU.mult)
                e.tensor_copy(sigI[:, stg:stg + 1], slotv[:, o:o + 1])
                e.tensor_copy(tI[:, stg:stg + 1], slotv[:, o + 1:o + 2]).then_inc(sV.h, 1)
            st[("vext", half)] = sV.inc()
            yield ("V", v_ext)

        router_state[b] = st

    emit_load(0)
    emit_load(1)
    emit_cast(0)
    emit_cast(1)

    n_units_per_pair = 16
    r0gen = [None]

    def splice(n):
        if r0gen[0] is None:
            return
        for _ in range(n):
            try:
                t_, f_ = next(r0gen[0])
            except StopIteration:
                r0gen[0] = None
                return
            OPS[t_].append(f_)

    DBGC = [0]
    for gi, (b, s) in enumerate(pair_list):
        for u in range(n_units_per_pair):
            emit_attn_unit(gi * n_units_per_pair + u)
            if b == 1:
                splice(2)
        if pend_vz[0] is not None:
            fn, pu = pend_vz[0]
            import functools
            V(functools.partial(fn, flushed=True))
            uV_z[pu] = sV.inc()
            pend_vz[0] = None
        for li, lu in enumerate(ln_units):
            if lu[0] == gi:
                emit_ln_unit(li)
        if gi == 0 and _os.environ.get("KDBG"):
            def g_dbg(e, li_p0=3):
                e.wait_ge(sS.h, ls_apply[li_p0])
                e.dma_start(out=dbg_h[:, 0:2048], in_=xz[:, 0:2048]).then_inc(sC.h, 16)
                e.dma_start(out=dbg_h[:, 2048:4096], in_=xbuf[:, 0:2048]).then_inc(sC.h, 16)
                e.dma_start(out=dbg_h[:, 4096:6144], in_=x1st[:, 0:2048]).then_inc(sC.h, 16)
                e.dma_start(out=dbg_h[0:128, 6144:6176], in_=lnmv[:, :]).then_inc(sC.h, 16)
            G(g_dbg)
            DBGC[0] = sC.inc(16 * 4)
        if gi + 2 < len(pair_list):
            emit_load(gi + 2)
            emit_cast(gi + 2)
        if b == 0 and s == NSEG - 1:
            r0gen[0] = build_router_ops(0, None)

    splice(9999)

    for t_, f_ in build_router_ops(1, router_state[0][("text", 1)]):
        OPS[t_].append(f_)


    esA.close()
    esB = ExitStack()
    g1T = esB.enter_context(nc.sbuf_tensor("g1T", [128, 64 * 512], BF16))
    w1cb = esB.enter_context(nc.sbuf_tensor("w1cb", [128, 2 * 4096], BF16))
    h2sb = esB.enter_context(nc.sbuf_tensor("h2sb", [128, 2 * 512], BF16))
    w2cb = esB.enter_context(nc.sbuf_tensor("w2cb", [128, 2 * 8192], BF16))
    ybuf = esB.enter_context(nc.sbuf_tensor("ybuf", [128, 4 * D], F32))
    fxb = esB.enter_context(nc.sbuf_tensor("fxb", [128, 4 * D], BF16))
    fxT = esB.enter_context(nc.sbuf_tensor("fxT", [128, 16 * 512], BF16))
    outro = esB.enter_context(nc.sbuf_tensor("outro", [128, 2 * D], F32))

    gath_done = {}
    for b in range(NB):
        for half in range(2):
            stg = b * 2 + half

            def g_gather(e, b=b, half=half, stg=stg):
                if stg == 0:
                    tS, tT = seg_consumed[(1, NSEG - 1)]
                    e.wait_ge(sS.h, tS)
                    e.wait_ge(sT.h, tT)
                    e.wait_ge(sV.h, router_state[1][("vext", 1)])
                    for mm in range(4):
                        e.wait_ge(sM4[mm].h, sM4[mm].n)
                else:
                    e.wait_ge(sGG.h, gath_done[stg - 1])
                e.wait_ge(sV.h, router_state[b][("vext", half)])
                e.indirect_dma_start(fxb[:, stg * D:(stg + 1) * D], None,
                                     x1m_h.rearrange("b s d -> (b s) d"),
                                     IndirectOffsetOnAxis(sigI[:, stg:stg + 1], 0),
                                     element_offset=b * S * D).then_inc(sGG.h, 16)
            G(g_gather)
            gath_done[stg] = sGG.inc(16)

    fxT_done = {}
    tp_evac = {}
    for stg in range(4):
        for t in range(16):
            idx = stg * 16 + t

            def t_fxT(e, stg=stg, t=t, idx=idx):
                e.wait_ge(sGG.h, gath_done[stg])
                if idx == 0:
                    e.wait_ge(sV.h, uV_gT[len(attn_units) - 1])
                if idx >= 2:
                    e.wait_ge(sV.h, tp_evac[idx - 2])
                e.transpose(pbank_bf[idx % 2][:, 0:128], fxb[:, stg * D + 128 * t:stg * D + 128 * (t + 1)],
                            IDT[:, :]).then_inc(sT.h, 1)
            T(t_fxT)
            fxT_done[idx] = sT.inc()

            def v_fxT(e, stg=stg, t=t, idx=idx):
                e.wait_ge(sT.h, fxT_done[idx])
                e.tensor_copy(fxT[:, 512 * t + 128 * stg:512 * t + 128 * (stg + 1)],
                              pbank_bf[idx % 2][:, 0:128]).then_inc(sV.h, 1)
            V(v_fxT)
            tp_evac[idx] = sV.inc()
    FXT_ALL = sV.n

    w1_done = {}
    h1_done = {}
    g1_done = {}
    for fb in range(32):
        def g_w1(e, fb=fb):
            if fb >= 2:
                e.wait_ge(sT.h, h1_done[2 * (fb - 2) + 1])
            e.dma_start(out=w1cb[:, (fb % 2) * 4096:(fb % 2 + 1) * 4096],
                        in_=w1_h[:, 256 * fb:256 * (fb + 1)].rearrange("(t p) f -> p t f", p=128)).then_inc(sW1[fb % 2].h, 16)
        G(g_w1)
        w1_done[fb] = sW1[fb % 2].inc(16)

    for f in range(64):
        fb = f // 2
        fo = f % 2

        def t_h1(e, f=f, fb=fb, fo=fo):
            if fo == 0:
                e.wait_ge(sW1[fb % 2].h, w1_done[fb])
            if f == 0:
                e.wait_ge(sV.h, FXT_ALL)
            if f >= 4:
                e.wait_ge(sS.h, g1_done[f - 4])
            bk = f % 4
            base = (fb % 2) * 4096 + fo * 128
            last = None
            for t in range(16):
                last = e.matmul(pbank[bk][:, 0:512],
                                w1cb[:, base + 256 * t:base + 256 * t + 128],
                                fxT[:, 512 * t:512 * (t + 1)], start=(t == 0), stop=(t == 15))
            last.then_inc(sT.h, 1)
        T(t_h1)
        h1_done[f] = sT.inc()

        def s_g1(e, f=f):
            e.wait_ge(sT.h, h1_done[f])
            bk = f % 4
            e.activation(out=g1T[:, 512 * f:512 * (f + 1)], in_=pbank[bk][:, 0:512], func=AF.Gelu,
                         bias=b1T[:, f:f + 1]).then_inc(sS.h, 1)
        SC(s_g1)
        g1_done[f] = sS.inc()

    w2_done = {}
    h2_done = {}
    h2s_done = {}
    h2t_done = {}
    yev_done = {}
    for c in range(16):
        def g_w2(e, c=c):
            if c == 0:
                tS, tT = seg_consumed[(1, NSEG - 1)]
                e.wait_ge(sS.h, tS)
                e.wait_ge(sT.h, tT)
            if c >= 2:
                e.wait_ge(sT.h, h2_done[c - 2])
            e.dma_start(out=w2cb[:, (c % 2) * 8192:(c % 2 + 1) * 8192],
                        in_=w2_h[:, 128 * c:128 * (c + 1)].rearrange("(t p) f -> p t f", p=128)).then_inc(sW2[c % 2].h, 16)
        G(g_w2)
        w2_done[c] = sW2[c % 2].inc(16)

    for c in range(16):
        def t_h2(e, c=c):
            e.wait_ge(sW2[c % 2].h, w2_done[c])
            if c == 0:
                e.wait_ge(sS.h, g1_done[63])
            if c >= 2:
                e.wait_ge(sS.h, h2s_done[c - 2])
            bk = 4 + c % 2
            last = None
            for f in range(64):
                last = e.matmul(pbank[bk][:, 0:512],
                                w2cb[:, (c % 2) * 8192 + 128 * f:(c % 2) * 8192 + 128 * (f + 1)],
                                g1T[:, 512 * f:512 * (f + 1)], start=(f == 0), stop=(f == 63))
            last.then_inc(sT.h, 1)
        T(t_h2)
        h2_done[c] = sT.inc()

        def s_h2(e, c=c):
            e.wait_ge(sT.h, h2_done[c])
            if c >= 2:
                e.wait_ge(sT.h, h2t_done[(c - 2, 3)])
            bk = 4 + c % 2
            e.activation(out=h2sb[:, (c % 2) * 512:(c % 2 + 1) * 512], in_=pbank[bk][:, 0:512],
                         func=AF.Identity, bias=b2T[:, c:c + 1]).then_inc(sS.h, 1)
        SC(s_h2)
        h2s_done[c] = sS.inc()

        for stg in range(4):
            idx = c * 4 + stg

            def t_h2t(e, c=c, stg=stg, idx=idx):
                e.wait_ge(sS.h, h2s_done[c])
                if idx >= 2:
                    e.wait_ge(sV.h, yev_done[idx - 2])
                e.transpose(pbank_bf[idx % 2][:, 0:128],
                            h2sb[:, (c % 2) * 512 + 128 * stg:(c % 2) * 512 + 128 * (stg + 1)],
                            IDT[:, :]).then_inc(sT.h, 1)
            T(t_h2t)
            h2t_done[(c, stg)] = sT.inc()

            def v_yev(e, c=c, stg=stg, idx=idx):
                e.wait_ge(sT.h, h2t_done[(c, stg)])
                e.tensor_scalar(ybuf[:, stg * D + 128 * c:stg * D + 128 * (c + 1)],
                                pbank_bf[idx % 2][:, 0:128],
                                slotv[:, stg * 4 + 3:stg * 4 + 4], None, ALU.mult).then_inc(sV.h, 1)
            V(v_yev)
            yev_done[idx] = sV.inc()

    fin_scat = {}
    fin_conv = {}
    for stg in range(4):
        b = stg // 2

        def s_conv(e, stg=stg):
            e.wait_ge(sGG.h, gath_done[stg])
            if stg >= 2:
                e.wait_ge(sGG.h, fin_scat[stg - 2])
            e.activation(out=outro[:, (stg % 2) * D:(stg % 2 + 1) * D],
                         in_=fxb[:, stg * D:(stg + 1) * D], func=AF.Identity,
                         scale=2.0).then_inc(sS.h, 1)
        SC(s_conv)
        fin_conv[stg] = sS.inc()

        def v_fin2(e, stg=stg):
            e.wait_ge(sV.h, yev_done[15 * 4 + stg])
            e.wait_ge(sS.h, fin_conv[stg])
            yt = ybuf[:, stg * D:(stg + 1) * D]
            e.tensor_tensor(yt[:, :], yt[:, :], outro[:, (stg % 2) * D:(stg % 2 + 1) * D], ALU.add)
            o = stg * 8
            e.drain()
            for i in range(4):
                e.bn_stats(lnstats[:, stg * 24 + 6 * i:stg * 24 + 6 * (i + 1)], yt[:, 512 * i:512 * (i + 1)])
            e.drain()
            e.bn_aggr(lnmv[:, o:o + 2], lnstats[:, stg * 24:stg * 24 + 24])
            e.drain()
            e.tensor_scalar(lnmv[:, o + 1:o + 2], lnmv[:, o + 1:o + 2], EPS, None, ALU.add)
            e.drain()
            e.reciprocal(lnmv[:, o + 1:o + 2], lnmv[:, o + 1:o + 2]).then_inc(sV.h, 1)
        V(v_fin2)
        fin_stats = sV.inc()

        def s_fin2(e, stg=stg, fin_stats=fin_stats):
            e.wait_ge(sV.h, fin_stats)
            o = stg * 8
            e.activation(out=lnmv[:, o + 1:o + 2], in_=lnmv[:, o + 1:o + 2], func=AF.Ln)
            e.activation(out=lnmv[:, o + 1:o + 2], in_=lnmv[:, o + 1:o + 2], func=AF.Exp, scale=0.5).then_inc(sS.h, 1)
        SC(s_fin2)
        fin_inv = sS.inc()

        def v_fin3(e, stg=stg, fin_inv=fin_inv):
            e.wait_ge(sS.h, fin_inv)
            o = stg * 8
            e.scalar_tensor_tensor(lnmv[:, o + 3:o + 4], lnmv[:, o + 0:o + 1], -1.0,
                                   lnmv[:, o + 1:o + 2], ALU.mult, ALU.mult).then_inc(sV.h, 1)
        V(v_fin3)
        fin_b = sV.inc()

        def s_fin3(e, stg=stg, fin_b=fin_b):
            e.wait_ge(sV.h, fin_b)
            o = stg * 8
            e.activation(out=outro[:, (stg % 2) * D:(stg % 2 + 1) * D], in_=ybuf[:, stg * D:(stg + 1) * D],
                         func=AF.Identity, bias=lnmv[:, o + 3:o + 4],
                         scale=lnmv[:, o + 1:o + 2]).then_inc(sS.h, 1)
        SC(s_fin3)
        fin_app = sS.inc()

        def g_scat(e, stg=stg, b=b, fin_app=fin_app, prev_gg=sGG.n):
            e.wait_ge(sS.h, fin_app)
            e.wait_ge(sGG.h, prev_gg)
            if stg == 0:
                for ee in range(2):
                    e.wait_ge(sE2[ee].h, sE2[ee].n)
            e.indirect_dma_start(out_h.rearrange("b s d -> (b s) d"),
                                 IndirectOffsetOnAxis(tI[:, stg:stg + 1], 0),
                                 outro[:, (stg % 2) * D:(stg % 2 + 1) * D], None,
                                 element_offset=b * S * D).then_inc(sGG.h, 16)
        G(g_scat)
        fin_scat[stg] = sGG.inc(16)

    FINAL_GG = sGG.n
    FINAL_E = [s.n for s in sE2]

    def g_end(e):
        e.wait_ge(sGG.h, FINAL_GG)
        for i in range(2):
            e.wait_ge(sE2[i].h, FINAL_E[i])
    G(g_end)

    def y_end(e):
        e.wait_ge(sGG.h, FINAL_GG)
        for i in range(2):
            e.wait_ge(sE2[i].h, FINAL_E[i])
    Y(y_end)

    import os
    _tr = float(os.environ.get("KTRUNC", "1.0"))
    if _tr < 1.0:
        for _k in OPS:
            OPS[_k] = OPS[_k][:max(1, int(len(OPS[_k]) * _tr))]
    with nc.Block() as block:
        @block.tensor
        def _(eng):
            for fn in OPS["T"]:
                fn(eng)

        @block.vector
        def _(eng):
            for fn in OPS["V"]:
                fn(eng)

        @block.scalar
        def _(eng):
            for fn in OPS["S"]:
                fn(eng)

        @block.gpsimd
        def _(eng):
            for fn in OPS["G"]:
                fn(eng)

        @block.sync
        def _(eng):
            for fn in OPS["Y"]:
                fn(eng)

    esB.close()
    es.close()
    return nc


def _consts():
    c = {}
    c["IDT"] = np.eye(128, dtype=np.float32).astype(ml_dtypes.bfloat16)
    c["IDTF"] = np.eye(128, dtype=np.float32)
    c["TRI"] = np.triu(np.ones((128, 128), np.float32), 1).astype(ml_dtypes.bfloat16)
    c["TRI16"] = np.triu(np.ones((16, 16), np.float32), 1).astype(ml_dtypes.bfloat16)
    c["ONESC"] = np.ones((128, 128), np.float32).astype(ml_dtypes.bfloat16)
    c["ONES1F"] = np.ones((1, 128), np.float32)
    c["ONESCF"] = np.ones((128, 1), np.float32)
    c["IO256"] = np.broadcast_to(np.arange(256, dtype=np.float32)[None, :], (128, 256)).copy()
    q = np.arange(128)[:, None]
    k = np.arange(16)[None, :]
    c["SG"] = (128 * k + q).astype(np.float16)
    s_ = k // 4
    par_ = (k % 4) // 2
    tl_ = k % 2
    c["TG"] = (512 * s_ + 256 * tl_ + 2 * q + par_).astype(np.float16)
    return c


_NC_CACHE = {}


def kernel(L0_attn_x, L0_ffn_Wr, L0_ffn_br, L0_ffn_W1, L0_ffn_b1, L0_ffn_W2, L0_ffn_b2):
    x = np.ascontiguousarray(np.asarray(L0_attn_x, dtype=np.float32))
    Wr = np.ascontiguousarray(np.asarray(L0_ffn_Wr, dtype=np.float32).reshape(1, D))
    br = np.ascontiguousarray(np.asarray(L0_ffn_br, dtype=np.float32).reshape(1, 1))
    W1 = np.ascontiguousarray(np.asarray(L0_ffn_W1, dtype=np.float32))
    b1 = np.ascontiguousarray(np.asarray(L0_ffn_b1, dtype=np.float32).reshape(64, 128).T)
    W2 = np.ascontiguousarray(np.asarray(L0_ffn_W2, dtype=np.float32))
    b2 = np.ascontiguousarray(np.asarray(L0_ffn_b2, dtype=np.float32).reshape(16, 128).T)
    consts = _consts()

    if "nc" not in _NC_CACHE:
        _NC_CACHE["nc"] = build_nc()
    nc = _NC_CACHE["nc"]

    in_maps = []
    for c in range(8):
        m = {"x": np.ascontiguousarray(x[2 * c:2 * c + 2]),
             "Wr": Wr, "br": br, "W1": W1, "b1": b1, "W2": W2, "b2": b2}
        m.update(consts)
        in_maps.append(m)
    res = run_bass_kernel_spmd(nc, in_maps, core_ids=list(range(8)))
    out = np.concatenate([res.results[i]["out"] for i in range(8)], axis=0)
    return out.astype(np.float32)


# version 07
# speedup vs baseline: 1.0483x; 1.0483x over previous
import os as _os
import numpy as np
import ml_dtypes
import concourse.bass as bass
import concourse.mybir as mybir
from concourse.bass_utils import run_bass_kernel_spmd
from concourse.bass_core_rust import IndirectOffsetOnAxis

F32 = mybir.dt.float32
BF16 = mybir.dt.bfloat16
FP16 = mybir.dt.float16
I32 = mybir.dt.int32
AF = mybir.ActivationFunctionType
ALU = mybir.AluOpType
AX = mybir.AxisListType

NB = 2
NSEG = 4
D = 2048
S = 2048
DFF = 8192
NSLOT = 256
SCALE = float(1.0 / np.sqrt(128.0))
EPS = 1e-5
NBIS = 33
WOFF = 16.0


class Sem:
    def __init__(self, es, nc, name):
        self.h = es.enter_context(nc.semaphore(name))
        self.n = 0

    def inc(self, k=1):
        self.n += k
        return self.n


def build_nc():
    from contextlib import ExitStack
    nc = bass.Bass("TRN2", target_bir_lowering=False, detect_race_conditions=False)
    es = ExitStack()

    x_h = nc.dram_tensor("x", [NB, S, D], F32, kind="ExternalInput").ap()
    wr_h = nc.dram_tensor("Wr", [1, D], F32, kind="ExternalInput").ap()
    br_h = nc.dram_tensor("br", [1, 1], F32, kind="ExternalInput").ap()
    w1_h = nc.dram_tensor("W1", [D, DFF], F32, kind="ExternalInput").ap()
    b1_h = nc.dram_tensor("b1", [128, 64], F32, kind="ExternalInput").ap()
    w2_h = nc.dram_tensor("W2", [DFF, D], F32, kind="ExternalInput").ap()
    b2_h = nc.dram_tensor("b2", [128, 16], F32, kind="ExternalInput").ap()
    idt_h = nc.dram_tensor("IDT", [128, 128], BF16, kind="ExternalInput").ap()
    idtf_h = nc.dram_tensor("IDTF", [128, 128], F32, kind="ExternalInput").ap()
    tri_h = nc.dram_tensor("TRI", [128, 128], BF16, kind="ExternalInput").ap()
    tri16_h = nc.dram_tensor("TRI16", [16, 16], BF16, kind="ExternalInput").ap()
    onesc_h = nc.dram_tensor("ONESC", [128, 128], BF16, kind="ExternalInput").ap()
    ones1f_h = nc.dram_tensor("ONES1F", [1, 128], F32, kind="ExternalInput").ap()
    onescf_h = nc.dram_tensor("ONESCF", [128, 1], F32, kind="ExternalInput").ap()
    io256_h = nc.dram_tensor("IO256", [128, 256], F32, kind="ExternalInput").ap()
    sg_h = nc.dram_tensor("SG", [128, 16], FP16, kind="ExternalInput").ap()
    tg_h = nc.dram_tensor("TG", [128, 16], FP16, kind="ExternalInput").ap()
    out_h = nc.dram_tensor("out", [NB, S, D], F32, kind="ExternalOutput").ap()
    dbg_h = nc.dram_tensor("dbg", [128, 6176], F32, kind="ExternalOutput").ap()
    x1m_h = nc.dram_tensor("x1m", [NB, S, D], BF16, kind="ExternalOutput" if _os.environ.get("KDBG") else "Internal").ap()

    def xview(h, b, s, par, tl):
        return h[b].rearrange("(s tl p two) d -> s two tl p d", s=4, tl=2, p=128, two=2)[s, par, tl]

    sb = lambda name, shape, dt: es.enter_context(nc.sbuf_tensor(name, shape, dt))
    IDT = sb("c_idt", [128, 128], BF16)
    IDTF = sb("c_idtf", [128, 128], F32)
    TRI = sb("c_tri", [128, 128], BF16)
    TRI16 = sb("c_tri16", [16, 16], BF16)
    ONESC = sb("c_onesc", [128, 128], BF16)
    ONES1F = sb("c_ones1f", [1, 128], F32)
    ONESCF = sb("c_onescf", [128, 1], F32)
    IO256 = sb("c_io256", [128, 256], F32)
    SG = sb("c_sg", [128, 16], FP16)
    TG = sb("c_tg", [128, 16], FP16)
    WrT = sb("c_wrt", [1, D], BF16)
    WrB = sb("c_wrb", [128, D], BF16)
    brS = sb("c_brs", [1, 1], F32)
    brB = sb("c_brb", [128, 1], F32)
    b1T = sb("c_b1t", [128, 64], F32)
    b2T = sb("c_b2t", [128, 16], F32)

    lnstats = sb("lnstats", [128, 4 * 24], F32)
    lnmv = sb("lnmv", [128, 4 * 8], F32)
    lo_b = sb("lo", [128, 2], F32)
    hi_b = sb("hi", [128, 2], F32)
    mid_b = sb("mid", [128, 2], F32)
    tmp16 = sb("tmp16", [128, 16], BF16)
    prt_b = sb("prt", [128, 2], BF16)
    cge_b = sb("cge", [128, 2], F32)
    dlt_b = sb("dlt", [128, 2], F32)
    dlt2_b = sb("dlt2", [128, 2], F32)
    L2g = sb("L2g", [128, 16], F32)
    vspc = sb("vspc", [128, 2], F32)
    mask_b = sb("mask", [128, 2 * 16], F32)
    maskbf = sb("maskbf", [128, 16], BF16)
    ct16 = sb("ct16", [16, 1], BF16)
    cb16 = sb("cb16", [1, 16], BF16)
    pos_b = sb("pos", [128, 16], F32)
    lbr_b = sb("lbr", [128, 16], F32)
    wexp = sb("wexp", [128, 16], F32)
    wm32 = sb("wm32", [128, 16], F32)
    wprt = sb("wprt", [128, 1], F32)
    bndc = sb("bndc", [128, 2], F32)
    bnds = sb("bnds", [1, 2], F32)
    bndB = sb("bndB", [128, 2], F32)
    VALS = sb("VALS", [128, 16 * 3], FP16)
    zt_s = sb("zt", [1, 2], F32)
    rzB = sb("rzB", [128, 1], F32)
    slotv = sb("slotv", [128, 4 * 4], F32)
    sigI = sb("sigI", [128, 4], I32)
    tI = sb("tI", [128, 4], I32)

    pbank = [es.enter_context(nc.psum_tensor(f"pb{i}", [128, 512], F32)) for i in range(6)]
    pbank_bf = [es.enter_context(nc.psum_tensor(f"pbb{i}", [128, 512], BF16)) for i in range(2)]

    sT = Sem(es, nc, "sT")
    sV = Sem(es, nc, "sV")
    sS = Sem(es, nc, "sS")
    sC = Sem(es, nc, "sC")
    sWrt = Sem(es, nc, "sWrt")
    sX = [Sem(es, nc, f"sX{i}") for i in range(2)]
    sXB = [Sem(es, nc, f"sXB{i}") for i in range(2)]
    sE2 = [Sem(es, nc, f"sE2{i}") for i in range(2)]
    sM4 = [Sem(es, nc, f"sM4{i}") for i in range(4)]
    sW1 = [Sem(es, nc, f"sW1{i}") for i in range(2)]
    sW2 = [Sem(es, nc, f"sW2{i}") for i in range(2)]
    sGG = Sem(es, nc, "sGG")

    OPS = {"T": [], "V": [], "S": [], "G": [], "Y": []}
    T = lambda fn: OPS["T"].append(fn)
    V = lambda fn: OPS["V"].append(fn)
    SC = lambda fn: OPS["S"].append(fn)
    G = lambda fn: OPS["G"].append(fn)
    Y = lambda fn: OPS["Y"].append(fn)

    def setup_loads(e):
        for dst, src in ((IDT, idt_h), (IDTF, idtf_h), (TRI, tri_h), (TRI16, tri16_h), (ONESC, onesc_h),
                         (ONES1F, ones1f_h), (ONESCF, onescf_h), (IO256, io256_h),
                         (SG, sg_h), (TG, tg_h), (brS, br_h)):
            e.dma_start(out=dst[:, :], in_=src[:, :]).then_inc(sC.h, 16)
        e.dma_start(out=b1T[:, :], in_=b1_h[:, :]).then_inc(sC.h, 16)
        e.dma_start(out=b2T[:, :], in_=b2_h[:, :]).then_inc(sC.h, 16)
    Y(setup_loads)
    LD_CONST = sC.inc(16 * 13)

    def setup_wrt(e):
        e.dma_start(out=WrT[:, :], in_=wr_h[:, :]).then_inc(sWrt.h, 16)
    G(setup_wrt)
    LD_WRT = sWrt.inc(16)

    def setup_bcast_t(e):
        e.wait_ge(sC.h, LD_CONST)
        e.wait_ge(sWrt.h, LD_WRT)
        for i in range(4):
            e.matmul(pbank[i][:, :], ONESC[0:1, :], WrT[:, 512 * i:512 * (i + 1)], start=True, stop=True)
        e.matmul(pbank[4][:, 0:1], ONES1F[:, :], brS[:, :], start=True, stop=True).then_inc(sT.h, 1)
    T(setup_bcast_t)
    T_BCAST = sT.inc()

    def setup_bcast_v(e):
        e.memset(vspc[:, :], 0.0)
        e.wait_ge(sT.h, T_BCAST)
        for i in range(4):
            e.tensor_copy(WrB[:, 512 * i:512 * (i + 1)], pbank[i][:, :])
        e.tensor_copy(brB[:, :], pbank[4][:, 0:1]).then_inc(sV.h, 1)
    V(setup_bcast_v)
    V_BCAST = sV.inc()

    esA = ExitStack()
    xz = esA.enter_context(nc.sbuf_tensor("xz", [128, 2 * 4 * D], F32))
    xbuf = esA.enter_context(nc.sbuf_tensor("xbw", [128, 2 * 4 * D], BF16))
    x1st = esA.enter_context(nc.sbuf_tensor("x1stage", [128, 4 * D], BF16))
    sttscr = esA.enter_context(nc.sbuf_tensor("sttscr", [128, D], BF16))
    Lgrid = esA.enter_context(nc.sbuf_tensor("Lgrid", [128, 2 * 16], F32))
    outear = esA.enter_context(nc.sbuf_tensor("outear", [128, 2 * D], F32))
    PT = esA.enter_context(nc.sbuf_tensor("PT", [128, 16 * 256], FP16))
    gT = esA.enter_context(nc.sbuf_tensor("gT", [128, 2 * 256], BF16))
    Ebuf = esA.enter_context(nc.sbuf_tensor("Ebuf", [128, 2 * 512], BF16))
    Dbuf = esA.enter_context(nc.sbuf_tensor("Dbuf", [128, 2 * 2], F32))
    Rbuf = esA.enter_context(nc.sbuf_tensor("Rbuf", [128, 2 * 2], F32))

    def xz_t(sbuf_i, par, tl):
        off = (sbuf_i * 4 + par * 2 + tl) * D
        return xz[:, off:off + D]

    def xb_t(sbuf_i, par, tl):
        off = (sbuf_i * 4 + par * 2 + tl) * D
        return xbuf[:, off:off + D]

    pair_list = [(b, s) for b in range(NB) for s in range(NSEG)]
    xload_done = {}
    xcast_done = {}
    seg_consumed = {}

    uT_tp, uT_sc, uT_pv = {}, {}, {}
    uT_sc1 = {}
    uS_E = {}
    uV_gT, uV_R, uV_z = {}, {}, {}
    ls_apply, lv_logit, lm_mirror, ly_early = {}, {}, {}, {}
    lv_stats, ls_inv, lv_bias = {}, {}, {}

    pend_vz = [None]
    pend_tpv = [None]
    attn_units = []
    for b, s in pair_list:
        for par in range(2):
            for hh in range(8):
                attn_units.append((b, s, par, hh))
    last_unit_of_pair = {}
    for ui, u in enumerate(attn_units):
        last_unit_of_pair[(u[0], u[1], u[2])] = ui

    ln_units = []
    for gi, (b, s) in enumerate(pair_list):
        for par in range(2):
            for tl in range(2):
                ln_units.append((gi, b, s, par, tl))
    last_li_of_b = {b: max(i for i, u in enumerate(ln_units) if u[1] == b) for b in range(NB)}

    def emit_load(gi):
        b, s = pair_list[gi]
        sbuf_i = gi % 2

        def loadseg(e, b=b, s=s, sbuf_i=sbuf_i, gi=gi):
            if gi == 2 and _os.environ.get("KDBG"):
                e.wait_ge(sC.h, DBGC[0])
            if gi >= 2:
                tS, tT = seg_consumed[pair_list[gi - 2]]
                e.wait_ge(sS.h, tS)
                e.wait_ge(sT.h, tT)
            for par in range(2):
                for tl in range(2):
                    e.dma_start(out=xz_t(sbuf_i, par, tl), in_=xview(x_h, b, s, par, tl)).then_inc(sX[gi % 2].h, 16)
        Y(loadseg)
        xload_done[(b, s)] = (gi % 2, sX[gi % 2].inc(16 * 4))

    def emit_cast(gi):
        b, s = pair_list[gi]
        sbuf_i = gi % 2

        def castseg(e, b=b, s=s, sbuf_i=sbuf_i, gi=gi):
            if gi == 2 and _os.environ.get("KDBG"):
                e.wait_ge(sC.h, DBGC[0])
            xi, xv = xload_done[(b, s)]
            e.wait_ge(sX[xi].h, xv)
            if gi >= 2:
                tS, tT = seg_consumed[pair_list[gi - 2]]
                e.wait_ge(sS.h, tS)
                e.wait_ge(sT.h, tT)
            for par in range(2):
                for tl in range(2):
                    e.dma_start(out=xb_t(sbuf_i, par, tl), in_=xz_t(sbuf_i, par, tl)).then_inc(sXB[gi % 2].h, 16)
        G(castseg)
        xcast_done[(b, s)] = (gi % 2, sXB[gi % 2].inc(16 * 4))

    def emit_attn_unit(ui):
        b, s, par, hh = attn_units[ui]
        gi = pair_list.index((b, s))
        sbuf_i = gi % 2
        slot = ui % 2
        col = (2 * hh + par) * 128

        def t_tp(e, b=b, s=s, par=par, ui=ui, sbuf_i=sbuf_i, col=col):
            if ui == 0:
                e.wait_ge(sV.h, V_BCAST)
            ci, cv = xcast_done[(b, s)]
            e.wait_ge(sXB[ci].h, cv)
            if _os.environ.get("KSERIAL") and ui >= 1:
                e.wait_ge(sV.h, uV_z[ui - 1])
            if _os.environ.get("KSERIAL") and ui % 16 == 0 and ui >= 16:
                li_prev = max(i for i, lu in enumerate(ln_units) if lu[0] == (ui // 16) - 1)
                e.wait_ge(sV.h, lv_logit[li_prev])
            if ui >= 1:
                e.wait_ge(sV.h, uV_gT[ui - 1])
            e.transpose(pbank_bf[0][:, 0:128], xb_t(sbuf_i, par, 0)[:, col:col + 128], IDT[:, :])
            e.transpose(pbank_bf[1][:, 0:128], xb_t(sbuf_i, par, 1)[:, col:col + 128], IDT[:, :]).then_inc(sT.h, 1)
        T(t_tp)
        uT_tp[ui] = sT.inc()

        if pend_tpv[0] is not None:
            fnp, pu = pend_tpv[0]
            T(fnp)
            uT_pv[pu] = sT.inc()
            pend_tpv[0] = None

        def v_gT(e, ui=ui, slot=slot):
            e.wait_ge(sT.h, uT_tp[ui])
            if ui >= 2:
                e.wait_ge(sT.h, uT_sc[ui - 2])
            go = slot * 256
            e.tensor_copy(gT[:, go:go + 128], pbank_bf[0][:, 0:128])
            e.tensor_copy(gT[:, go + 128:go + 256], pbank_bf[1][:, 0:128]).then_inc(sV.h, 1)
        V(v_gT)
        uV_gT[ui] = sV.inc()

        if pend_vz[0] is not None:
            fn, pu = pend_vz[0]
            V(fn)
            uV_z[pu] = sV.inc()
            pend_vz[0] = None

        def t_sc(e, ui=ui, slot=slot):
            e.wait_ge(sV.h, uV_gT[ui])
            if ui >= 1:
                e.wait_ge(sS.h, uS_E[ui - 1] - 1)
            go = slot * 256
            e.matmul(pbank[2][:, 0:256], gT[:, go:go + 128], gT[:, go:go + 256], start=True, stop=True).then_inc(sT.h, 1)
            if ui >= 1:
                e.wait_ge(sS.h, uS_E[ui - 1])
            e.matmul(pbank[3][:, 0:256], gT[:, go + 128:go + 256], gT[:, go:go + 256], start=True, stop=True).then_inc(sT.h, 1)
        T(t_sc)
        uT_sc1[ui] = sT.inc()
        uT_sc[ui] = sT.inc()

        def s_E(e, ui=ui, slot=slot):
            e.wait_ge(sT.h, uT_sc1[ui])
            if ui >= 2:
                e.wait_ge(sV.h, uV_R[ui - 2])
            eo = slot * 512
            do = slot * 2
            e.activation(out=Ebuf[:, eo:eo + 256], in_=pbank[2][:, 0:256], func=AF.Exp,
                         scale=SCALE, accum_out=Dbuf[:, do:do + 1]).then_inc(sS.h, 1)
            e.wait_ge(sT.h, uT_sc[ui])
            e.activation(out=Ebuf[:, eo + 256:eo + 512], in_=pbank[3][:, 0:256], func=AF.Exp,
                         scale=SCALE, accum_out=Dbuf[:, do + 1:do + 2]).then_inc(sS.h, 1)
        SC(s_E)
        sS.inc()
        uS_E[ui] = sS.inc()

        def v_R(e, ui=ui, slot=slot):
            e.wait_ge(sS.h, uS_E[ui])
            do = slot * 2
            e.reciprocal(Rbuf[:, do:do + 2], Dbuf[:, do:do + 2]).then_inc(sV.h, 1)
        V(v_R)
        uV_R[ui] = sV.inc()

        def t_pv(e, ui=ui, slot=slot, sbuf_i=sbuf_i, par=par, col=col):
            e.wait_ge(sS.h, uS_E[ui])
            if ui >= 1:
                e.wait_ge(sV.h, uV_z[ui - 1])
            eo = slot * 512
            last = None
            for j in range(2):
                for k in range(2):
                    last = e.matmul(pbank[4 + j][:, 0:128],
                                    Ebuf[:, eo + k * 256 + j * 128:eo + k * 256 + (j + 1) * 128],
                                    xb_t(sbuf_i, par, k)[:, col:col + 128],
                                    start=(k == 0), stop=(k == 1))
            last.then_inc(sT.h, 1)
        pend_tpv[0] = (t_pv, ui)

        def v_z(e, ui=ui, slot=slot, sbuf_i=sbuf_i, par=par, col=col, flushed=False):
            e.wait_ge(sT.h, uT_pv[ui])
            e.wait_ge(sV.h, uV_R[ui])
            if flushed:
                e.drain()
            do = slot * 2
            last = None
            for j in range(2):
                zt = xz_t(sbuf_i, par, j)
                last = e.scalar_tensor_tensor(zt[:, col:col + 128], pbank[4 + j][:, 0:128],
                                              Rbuf[:, do + j:do + j + 1], zt[:, col:col + 128],
                                              ALU.mult, ALU.add)
            last.then_inc(sV.h, 1)
        pend_vz[0] = (v_z, ui)

    def emit_ln_unit(li):
        gi, b, s, par, tl = ln_units[li]
        sbuf_i = gi % 2
        k = 4 * s + 2 * par + tl
        ring = k % 4
        ering = li % 2

        def v_stats(e, li=li, b=b, s=s, par=par, tl=tl, ring=ring, sbuf_i=sbuf_i):
            e.wait_ge(sV.h, uV_z[last_unit_of_pair[(b, s, par)]])
            if li >= 4:
                e.wait_ge(sS.h, ls_apply[li - 4])
            zt = xz_t(sbuf_i, par, tl)
            for i in range(4):
                e.bn_stats(lnstats[:, ring * 24 + 6 * i:ring * 24 + 6 * (i + 1)], zt[:, 512 * i:512 * (i + 1)])
            o = ring * 8
            e.drain()
            e.bn_aggr(lnmv[:, o:o + 2], lnstats[:, ring * 24:ring * 24 + 24])
            e.drain()
            e.tensor_scalar(lnmv[:, o + 7:o + 8], lnmv[:, o + 1:o + 2], EPS, None, ALU.add)
            e.drain()
            e.reciprocal(lnmv[:, o + 7:o + 8], lnmv[:, o + 7:o + 8]).then_inc(sV.h, 1)
        V(v_stats)
        lv_stats[li] = sV.inc()

        def s_inv(e, li=li, ring=ring):
            e.wait_ge(sV.h, lv_stats[li])
            o = ring * 8
            e.activation(out=lnmv[:, o + 2:o + 3], in_=lnmv[:, o + 7:o + 8], func=AF.Ln)
            e.activation(out=lnmv[:, o + 2:o + 3], in_=lnmv[:, o + 2:o + 3], func=AF.Exp, scale=0.5).then_inc(sS.h, 1)
        SC(s_inv)
        ls_inv[li] = sS.inc()

        def v_bias(e, li=li, ring=ring):
            e.wait_ge(sS.h, ls_inv[li])
            o = ring * 8
            e.tensor_tensor(lnmv[:, o + 4:o + 5], lnmv[:, o + 1:o + 2], lnmv[:, o + 7:o + 8], ALU.mult)
            e.scalar_tensor_tensor(lnmv[:, o + 3:o + 4], lnmv[:, o + 0:o + 1], -1.0,
                                   lnmv[:, o + 2:o + 3], ALU.mult, ALU.mult)
            e.tensor_scalar(lnmv[:, o + 4:o + 5], lnmv[:, o + 4:o + 5], EPS / 4.0, None, ALU.add)
            e.tensor_copy(lnmv[:, o + 5:o + 6], lnmv[:, o + 2:o + 3])
            e.reciprocal(lnmv[:, o + 4:o + 5], lnmv[:, o + 4:o + 5]).then_inc(sV.h, 1)
        V(v_bias)
        lv_bias[li] = sV.inc()

        def s_s2(e, li=li, ring=ring):
            e.wait_ge(sV.h, lv_bias[li])
            o = ring * 8
            e.activation(out=lnmv[:, o + 4:o + 5], in_=lnmv[:, o + 4:o + 5], func=AF.Ln)
            e.activation(out=lnmv[:, o + 4:o + 5], in_=lnmv[:, o + 4:o + 5], func=AF.Exp,
                         scale=0.5).then_inc(sS.h, 1)
        SC(s_s2)
        ls_s2 = sS.inc()

        def v_scale2(e, li=li, ring=ring, ls_s2=ls_s2):
            e.wait_ge(sS.h, ls_s2)
            o = ring * 8
            e.tensor_tensor(lnmv[:, o + 5:o + 6], lnmv[:, o + 2:o + 3], lnmv[:, o + 4:o + 5], ALU.mult)
            e.tensor_tensor(lnmv[:, o + 6:o + 7], lnmv[:, o + 3:o + 4], lnmv[:, o + 4:o + 5], ALU.mult).then_inc(sV.h, 1)
        V(v_scale2)
        lv_sc2 = sV.inc()

        def s_apply(e, li=li, ring=ring, ering=ering, b=b, s=s, par=par, tl=tl, sbuf_i=sbuf_i, k=k, lv_sc2=lv_sc2):
            e.wait_ge(sV.h, lv_sc2)
            o = ring * 8
            zt = xz_t(sbuf_i, par, tl)
            if li == 4 and _os.environ.get("KDBG"):
                e.wait_ge(sC.h, DBGC[0])
            if li >= 4:
                e.wait_ge(sM4[li % 4].h, lm_mirror[li - 4])
                e.wait_ge(sV.h, lv_logit[li - 4])
            if li >= 2:
                e.wait_ge(sE2[li % 2].h, ly_early[li - 2])
            e.activation(out=x1st[:, ring * D:(ring + 1) * D], in_=zt[:, :], func=AF.Identity,
                         bias=lnmv[:, o + 3:o + 4], scale=lnmv[:, o + 2:o + 3])
            e.activation(out=outear[:, ering * D:(ering + 1) * D], in_=zt[:, :], func=AF.Identity,
                         bias=lnmv[:, o + 6:o + 7], scale=lnmv[:, o + 5:o + 6]).then_inc(sS.h, 1)
            e.wait_ge(sS.h, ls_apply[li])
            e.dma_start(out=x1m_h[b][128 * k:128 * (k + 1), :],
                        in_=x1st[:, ring * D:(ring + 1) * D]).then_inc(sM4[li % 4].h, 16)
        ls_apply[li] = sS.inc()
        SC(s_apply)
        lm_mirror[li] = sM4[li % 4].inc(16)

        def v_logit(e, li=li, ring=ring, b=b, k=k):
            e.wait_ge(sS.h, ls_apply[li])
            e.scalar_tensor_tensor(sttscr[:, :], x1st[:, ring * D:(ring + 1) * D], 1.0, WrB[:, :],
                                   ALU.mult, ALU.mult,
                                   accum_out=Lgrid[:, b * 16 + k:b * 16 + k + 1]).then_inc(sV.h, 1)
        V(v_logit)
        lv_logit[li] = sV.inc()

        def y_early(e, li=li, ering=ering, b=b, s=s, par=par, tl=tl):
            e.wait_ge(sS.h, ls_apply[li])
            e.dma_start(out=xview(out_h, b, s, par, tl), in_=outear[:, ering * D:(ering + 1) * D]).then_inc(sE2[li % 2].h, 16)
        Y(y_early)
        ly_early[li] = sE2[li % 2].inc(16)

        if par == 1 and tl == 1:
            seg_consumed[(b, s)] = (ls_apply[li], uT_pv[last_unit_of_pair[(b, s, 1)]])

    router_state = {}

    def build_router_ops(b, prev_ext_t):
        st = {}
        router_state[b] = st
        logits_ready = lv_logit[last_li_of_b[b]]

        def v_bnd1(e, b=b, logits_ready=logits_ready, prev_ext_t=prev_ext_t):
            e.wait_ge(sV.h, logits_ready)
            if prev_ext_t is not None:
                e.wait_ge(sT.h, prev_ext_t)
            e.reduce_max(bndc[:, 0:1], Lgrid[:, b * 16:(b + 1) * 16], axis=AX.X)
            e.tensor_scalar(lbr_b[:, :], Lgrid[:, b * 16:(b + 1) * 16], -1.0, None, ALU.mult)
            e.reduce_max(bndc[:, 1:2], lbr_b[:, :], axis=AX.X).then_inc(sV.h, 1)
        st["vbnd1"] = sV.inc()
        yield ("V", v_bnd1)

        def t_bnd(e, b=b, st=st):
            e.wait_ge(sV.h, st["vbnd1"])
            e.transpose(pbank[0][0:1, 0:128], bndc[:, 0:1], IDTF[:, :])
            e.transpose(pbank[1][0:1, 0:128], bndc[:, 1:2], IDTF[:, :]).then_inc(sT.h, 1)
        st["tbnd"] = sT.inc()
        yield ("T", t_bnd)

        def v_bnd2(e, b=b, st=st):
            e.wait_ge(sT.h, st["tbnd"])
            e.reduce_max(bnds[:, 0:1], pbank[0][0:1, 0:128], axis=AX.X)
            e.reduce_max(bnds[:, 1:2], pbank[1][0:1, 0:128], axis=AX.X).then_inc(sV.h, 1)
        st["vbnd2"] = sV.inc()
        yield ("V", v_bnd2)

        def t_bnd2(e, b=b, st=st):
            e.wait_ge(sV.h, st["vbnd2"])
            e.matmul(pbank[0][:, 0:2], ONES1F[:, :], bnds[:, :], start=True, stop=True).then_inc(sT.h, 1)
        st["tbnd2"] = sT.inc()
        yield ("T", t_bnd2)

        def v_binit(e, b=b, st=st):
            e.wait_ge(sT.h, st["tbnd2"])
            e.tensor_copy(bndB[:, :], pbank[0][:, 0:2])
            e.drain()
            e.tensor_scalar(hi_b[:, b:b + 1], bndB[:, 0:1], 2.0, None, ALU.mult)
            e.tensor_scalar(lo_b[:, b:b + 1], bndB[:, 1:2], -2.0, -2.0, ALU.mult, ALU.add)
            e.tensor_scalar(L2g[:, :], Lgrid[:, b * 16:(b + 1) * 16], 2.0, None, ALU.mult)
            e.tensor_tensor(mid_b[:, b:b + 1], lo_b[:, b:b + 1], hi_b[:, b:b + 1], ALU.add)
            e.drain()
            with nc.allow_low_precision("bisection counts <=16 are exact in bf16"):
                e.tensor_scalar(tmp16[:, :], L2g[:, :], mid_b[:, b:b + 1], None, ALU.is_gt,
                                ALU.add, accum_out=prt_b[:, b:b + 1]).then_inc(sV.h, 1)
        st[("v", 0)] = sV.inc()
        yield ("V", v_binit)

        for it in range(NBIS):
            def t_it(e, b=b, it=it, st=st):
                e.wait_ge(sV.h, st[("v", it)])
                e.matmul(pbank[it % 2][:, 0:1], ONESC[:, :], prt_b[:, b:b + 1], start=True, stop=True).then_inc(sT.h, 1)
            st[("t", it)] = sT.inc()
            yield ("T", t_it)

            def v_it(e, b=b, it=it, st=st):
                e.wait_ge(sT.h, st[("t", it)])
                ps = pbank[it % 2]
                e.tensor_scalar(cge_b[:, b:b + 1], ps[:, 0:1], float(NSLOT) - 0.5, None, ALU.is_ge)
                e.tensor_tensor(dlt_b[:, b:b + 1], mid_b[:, b:b + 1], lo_b[:, b:b + 1], ALU.subtract)
                e.tensor_tensor(dlt2_b[:, b:b + 1], hi_b[:, b:b + 1], mid_b[:, b:b + 1], ALU.subtract)
                e.scalar_tensor_tensor(lo_b[:, b:b + 1], dlt_b[:, b:b + 1], cge_b[:, b:b + 1],
                                       lo_b[:, b:b + 1], ALU.mult, ALU.add)
                e.scalar_tensor_tensor(hi_b[:, b:b + 1], dlt2_b[:, b:b + 1], cge_b[:, b:b + 1],
                                       mid_b[:, b:b + 1], ALU.mult, ALU.add)
                last = None
                if it < NBIS - 1:
                    e.drain()
                    e.tensor_tensor(mid_b[:, b:b + 1], lo_b[:, b:b + 1], hi_b[:, b:b + 1], ALU.add)
                    e.drain()
                    with nc.allow_low_precision("bisection counts <=16 are exact in bf16"):
                        last = e.tensor_scalar(tmp16[:, :], L2g[:, :], mid_b[:, b:b + 1], None, ALU.is_gt,
                                               ALU.add, accum_out=prt_b[:, b:b + 1])
                else:
                    e.drain()
                    last = e.tensor_scalar(mask_b[:, b * 16:(b + 1) * 16], L2g[:, :],
                                           lo_b[:, b:b + 1], None, ALU.is_gt)
                last.then_inc(sV.h, 1)
            st[("v", it + 1)] = sV.inc()
            yield ("V", v_it)

        def v_fin(e, b=b, st=st):
            e.wait_ge(sV.h, st[("v", NBIS)])
            e.tensor_scalar(lbr_b[:, :], Lgrid[:, b * 16:(b + 1) * 16], bndB[:, 0:1], None,
                            ALU.subtract)
            e.tensor_copy(maskbf[:, :], mask_b[:, b * 16:(b + 1) * 16]).then_inc(sV.h, 1)
        st["vfin"] = sV.inc()
        yield ("V", v_fin)

        def s_wexp(e, b=b, st=st):
            e.wait_ge(sV.h, st["vfin"])
            e.activation(out=wexp[:, :], in_=lbr_b[:, :], func=AF.Exp).then_inc(sS.h, 1)
        st["swexp"] = sS.inc()
        yield ("S", s_wexp)

        def t_ct(e, b=b, st=st):
            e.wait_ge(sV.h, st["vfin"])
            e.matmul(pbank[0][0:16, 0:1], maskbf[:, :], ONESC[:, 0:1], start=True, stop=True).then_inc(sT.h, 1)
        st["tct"] = sT.inc()
        yield ("T", t_ct)

        def v_ct(e, b=b, st=st):
            e.wait_ge(sT.h, st["tct"])
            e.wait_ge(sS.h, st["swexp"])
            e.tensor_tensor(wm32[:, :], wexp[:, :], mask_b[:, b * 16:(b + 1) * 16], ALU.mult)
            e.tensor_copy(ct16[:, :], pbank[0][0:16, 0:1])
            e.reduce_sum(wprt[:, :], wm32[:, :], axis=AX.X).then_inc(sV.h, 1)
        st["vct"] = sV.inc()
        yield ("V", v_ct)

        def t_cb(e, b=b, st=st):
            e.wait_ge(sV.h, st["vct"])
            e.matmul(pbank[1][0:1, 0:16], ct16[:, :], TRI16[:, :], start=True, stop=True)
            e.matmul(pbank[0][0:1, 0:1], wprt[:, :], ONESCF[:, :], start=True, stop=True).then_inc(sT.h, 1)
        st["tcb"] = sT.inc()
        yield ("T", t_cb)

        def v_cb(e, b=b, st=st):
            e.wait_ge(sT.h, st["tcb"])
            e.tensor_copy(zt_s[:, b:b + 1], pbank[0][0:1, 0:1])
            e.tensor_copy(cb16[:, :], pbank[1][0:1, 0:16])
            e.reciprocal(zt_s[:, b:b + 1], zt_s[:, b:b + 1]).then_inc(sV.h, 1)
        st["vcb"] = sV.inc()
        yield ("V", v_cb)

        def t_pos(e, b=b, st=st):
            e.wait_ge(sV.h, st["vcb"])
            e.matmul(pbank[0][:, 0:16], ONESC[0:1, :], cb16[:, :], start=True, stop=True)
            e.matmul(pbank[1][:, 0:16], TRI[:, :], maskbf[:, :], start=True, stop=True).then_inc(sT.h, 1)
        st["tpos"] = sT.inc()
        yield ("T", t_pos)

        def v_pos(e, b=b, st=st):
            e.wait_ge(sT.h, st["tpos"])
            e.tensor_copy(pos_b[:, :], pbank[0][:, 0:16])
            e.tensor_copy(vspc[:, 0:1], vspc[:, 1:2])
            e.tensor_tensor(pos_b[:, :], pos_b[:, :], pbank[1][:, 0:16], ALU.add)
            e.drain()
            for kk in range(16):
                e.tensor_scalar(PT[:, 256 * kk:256 * (kk + 1)], IO256[:, :], pos_b[:, kk:kk + 1],
                                mask_b[:, b * 16 + kk:b * 16 + kk + 1], ALU.is_equal, ALU.mult)
            e.tensor_copy(VALS[:, 0:48:3], SG[:, :])
            e.tensor_copy(VALS[:, 1:48:3], TG[:, :])
            last = None
            for kk in range(16):
                last = e.tensor_scalar(VALS[:, 3 * kk + 2:3 * kk + 3], wm32[:, kk:kk + 1], 1.0, None, ALU.mult)
            last.then_inc(sV.h, 1)
        st["vpos"] = sV.inc()
        yield ("V", v_pos)

        def t_rz(e, b=b, st=st):
            e.wait_ge(sV.h, st["vpos"])
            e.matmul(pbank[0][:, 0:1], ONES1F[:, :], zt_s[:, b:b + 1], start=True, stop=True).then_inc(sT.h, 1)
        st["trz"] = sT.inc()
        yield ("T", t_rz)

        def v_rz(e, b=b, st=st):
            e.wait_ge(sT.h, st["trz"])
            e.tensor_copy(rzB[:, :], pbank[0][:, 0:1]).then_inc(sV.h, 1)
        st["vrz"] = sV.inc()
        yield ("V", v_rz)

        for half in range(2):
            def t_ext(e, b=b, half=half, st=st):
                e.wait_ge(sV.h, st["vrz"] if half == 0 else st[("vext", 0)])
                last = None
                for kk in range(16):
                    last = e.matmul(pbank[half][:, 0:3],
                                    PT[:, 256 * kk + 128 * half:256 * kk + 128 * (half + 1)],
                                    VALS[:, 3 * kk:3 * (kk + 1)], start=(kk == 0), stop=(kk == 15))
                last.then_inc(sT.h, 1)
            st[("text", half)] = sT.inc()
            yield ("T", t_ext)

            def v_ext(e, b=b, half=half, st=st):
                e.wait_ge(sT.h, st[("text", half)])
                stg = b * 2 + half
                o = stg * 4
                e.tensor_copy(slotv[:, o:o + 3], pbank[half][:, 0:3])
                e.tensor_copy(vspc[:, 0:1], vspc[:, 1:2])
                e.tensor_tensor(slotv[:, o + 3:o + 4], slotv[:, o + 2:o + 3], rzB[:, :], ALU.mult)
                e.tensor_copy(sigI[:, stg:stg + 1], slotv[:, o:o + 1])
                e.tensor_copy(tI[:, stg:stg + 1], slotv[:, o + 1:o + 2]).then_inc(sV.h, 1)
            st[("vext", half)] = sV.inc()
            yield ("V", v_ext)

        router_state[b] = st

    emit_load(0)
    emit_load(1)
    emit_cast(0)
    emit_cast(1)

    n_units_per_pair = 16
    r0gen = [None]

    def splice(n):
        if r0gen[0] is None:
            return
        for _ in range(n):
            try:
                t_, f_ = next(r0gen[0])
            except StopIteration:
                r0gen[0] = None
                return
            OPS[t_].append(f_)

    DBGC = [0]
    for gi, (b, s) in enumerate(pair_list):
        for u in range(n_units_per_pair):
            emit_attn_unit(gi * n_units_per_pair + u)
            if b == 1:
                splice(2)
        if pend_tpv[0] is not None:
            fnp, pu = pend_tpv[0]
            T(fnp)
            uT_pv[pu] = sT.inc()
            pend_tpv[0] = None
        if pend_vz[0] is not None:
            fn, pu = pend_vz[0]
            import functools
            V(functools.partial(fn, flushed=True))
            uV_z[pu] = sV.inc()
            pend_vz[0] = None
        for li, lu in enumerate(ln_units):
            if lu[0] == gi:
                emit_ln_unit(li)
        if gi == 0 and _os.environ.get("KDBG"):
            def g_dbg(e, li_p0=3):
                e.wait_ge(sS.h, ls_apply[li_p0])
                e.dma_start(out=dbg_h[:, 0:2048], in_=xz[:, 0:2048]).then_inc(sC.h, 16)
                e.dma_start(out=dbg_h[:, 2048:4096], in_=xbuf[:, 0:2048]).then_inc(sC.h, 16)
                e.dma_start(out=dbg_h[:, 4096:6144], in_=x1st[:, 0:2048]).then_inc(sC.h, 16)
                e.dma_start(out=dbg_h[0:128, 6144:6176], in_=lnmv[:, :]).then_inc(sC.h, 16)
            G(g_dbg)
            DBGC[0] = sC.inc(16 * 4)
        if gi + 2 < len(pair_list):
            emit_load(gi + 2)
            emit_cast(gi + 2)
        if b == 0 and s == NSEG - 1:
            r0gen[0] = build_router_ops(0, None)

    splice(9999)

    for t_, f_ in build_router_ops(1, router_state[0][("text", 1)]):
        OPS[t_].append(f_)


    esA.close()
    esB = ExitStack()
    g1T = esB.enter_context(nc.sbuf_tensor("g1T", [128, 64 * 512], BF16))
    w1cb = esB.enter_context(nc.sbuf_tensor("w1cb", [128, 2 * 4096], BF16))
    h2sb = esB.enter_context(nc.sbuf_tensor("h2sb", [128, 2 * 512], BF16))
    w2cb = esB.enter_context(nc.sbuf_tensor("w2cb", [128, 2 * 8192], BF16))
    ybuf = esB.enter_context(nc.sbuf_tensor("ybuf", [128, 4 * D], F32))
    fxb = esB.enter_context(nc.sbuf_tensor("fxb", [128, 4 * D], BF16))
    fxT = esB.enter_context(nc.sbuf_tensor("fxT", [128, 16 * 512], BF16))
    outro = esB.enter_context(nc.sbuf_tensor("outro", [128, 2 * D], F32))

    gath_done = {}
    for b in range(NB):
        for half in range(2):
            stg = b * 2 + half

            def g_gather(e, b=b, half=half, stg=stg):
                if stg == 0:
                    tS, tT = seg_consumed[(1, NSEG - 1)]
                    e.wait_ge(sS.h, tS)
                    e.wait_ge(sT.h, tT)
                    e.wait_ge(sV.h, router_state[1][("vext", 1)])
                    for mm in range(4):
                        e.wait_ge(sM4[mm].h, sM4[mm].n)
                else:
                    e.wait_ge(sGG.h, gath_done[stg - 1])
                e.wait_ge(sV.h, router_state[b][("vext", half)])
                e.indirect_dma_start(fxb[:, stg * D:(stg + 1) * D], None,
                                     x1m_h.rearrange("b s d -> (b s) d"),
                                     IndirectOffsetOnAxis(sigI[:, stg:stg + 1], 0),
                                     element_offset=b * S * D).then_inc(sGG.h, 16)
            G(g_gather)
            gath_done[stg] = sGG.inc(16)

    fxT_done = {}
    tp_evac = {}
    for stg in range(4):
        for t in range(16):
            idx = stg * 16 + t

            def t_fxT(e, stg=stg, t=t, idx=idx):
                e.wait_ge(sGG.h, gath_done[stg])
                if idx == 0:
                    e.wait_ge(sV.h, uV_gT[len(attn_units) - 1])
                if idx >= 2:
                    e.wait_ge(sV.h, tp_evac[idx - 2])
                e.transpose(pbank_bf[idx % 2][:, 0:128], fxb[:, stg * D + 128 * t:stg * D + 128 * (t + 1)],
                            IDT[:, :]).then_inc(sT.h, 1)
            T(t_fxT)
            fxT_done[idx] = sT.inc()

            def v_fxT(e, stg=stg, t=t, idx=idx):
                e.wait_ge(sT.h, fxT_done[idx])
                e.tensor_copy(fxT[:, 512 * t + 128 * stg:512 * t + 128 * (stg + 1)],
                              pbank_bf[idx % 2][:, 0:128]).then_inc(sV.h, 1)
            V(v_fxT)
            tp_evac[idx] = sV.inc()
    FXT_ALL = sV.n

    w1_done = {}
    h1_done = {}
    g1_done = {}
    for fb in range(32):
        def g_w1(e, fb=fb):
            if fb >= 2:
                e.wait_ge(sT.h, h1_done[2 * (fb - 2) + 1])
            e.dma_start(out=w1cb[:, (fb % 2) * 4096:(fb % 2 + 1) * 4096],
                        in_=w1_h[:, 256 * fb:256 * (fb + 1)].rearrange("(t p) f -> p t f", p=128)).then_inc(sW1[fb % 2].h, 16)
        G(g_w1)
        w1_done[fb] = sW1[fb % 2].inc(16)

    for f in range(64):
        fb = f // 2
        fo = f % 2

        def t_h1(e, f=f, fb=fb, fo=fo):
            if fo == 0:
                e.wait_ge(sW1[fb % 2].h, w1_done[fb])
            if f == 0:
                e.wait_ge(sV.h, FXT_ALL)
            if f >= 4:
                e.wait_ge(sS.h, g1_done[f - 4])
            bk = f % 4
            base = (fb % 2) * 4096 + fo * 128
            last = None
            for t in range(16):
                last = e.matmul(pbank[bk][:, 0:512],
                                w1cb[:, base + 256 * t:base + 256 * t + 128],
                                fxT[:, 512 * t:512 * (t + 1)], start=(t == 0), stop=(t == 15))
            last.then_inc(sT.h, 1)
        T(t_h1)
        h1_done[f] = sT.inc()

        def s_g1(e, f=f):
            e.wait_ge(sT.h, h1_done[f])
            bk = f % 4
            e.activation(out=g1T[:, 512 * f:512 * (f + 1)], in_=pbank[bk][:, 0:512], func=AF.Gelu,
                         bias=b1T[:, f:f + 1]).then_inc(sS.h, 1)
        SC(s_g1)
        g1_done[f] = sS.inc()

    w2_done = {}
    h2_done = {}
    h2s_done = {}
    h2t_done = {}
    yev_done = {}
    for c in range(16):
        def g_w2(e, c=c):
            if c == 0:
                tS, tT = seg_consumed[(1, NSEG - 1)]
                e.wait_ge(sS.h, tS)
                e.wait_ge(sT.h, tT)
            if c >= 2:
                e.wait_ge(sT.h, h2_done[c - 2])
            e.dma_start(out=w2cb[:, (c % 2) * 8192:(c % 2 + 1) * 8192],
                        in_=w2_h[:, 128 * c:128 * (c + 1)].rearrange("(t p) f -> p t f", p=128)).then_inc(sW2[c % 2].h, 16)
        G(g_w2)
        w2_done[c] = sW2[c % 2].inc(16)

    for c in range(16):
        def t_h2(e, c=c):
            e.wait_ge(sW2[c % 2].h, w2_done[c])
            if c == 0:
                e.wait_ge(sS.h, g1_done[63])
            if c >= 2:
                e.wait_ge(sS.h, h2s_done[c - 2])
            bk = 4 + c % 2
            last = None
            for f in range(64):
                last = e.matmul(pbank[bk][:, 0:512],
                                w2cb[:, (c % 2) * 8192 + 128 * f:(c % 2) * 8192 + 128 * (f + 1)],
                                g1T[:, 512 * f:512 * (f + 1)], start=(f == 0), stop=(f == 63))
            last.then_inc(sT.h, 1)
        T(t_h2)
        h2_done[c] = sT.inc()

        def s_h2(e, c=c):
            e.wait_ge(sT.h, h2_done[c])
            if c >= 2:
                e.wait_ge(sT.h, h2t_done[(c - 2, 3)])
            bk = 4 + c % 2
            e.activation(out=h2sb[:, (c % 2) * 512:(c % 2 + 1) * 512], in_=pbank[bk][:, 0:512],
                         func=AF.Identity, bias=b2T[:, c:c + 1]).then_inc(sS.h, 1)
        SC(s_h2)
        h2s_done[c] = sS.inc()

        for stg in range(4):
            idx = c * 4 + stg

            def t_h2t(e, c=c, stg=stg, idx=idx):
                e.wait_ge(sS.h, h2s_done[c])
                if idx >= 2:
                    e.wait_ge(sV.h, yev_done[idx - 2])
                e.transpose(pbank_bf[idx % 2][:, 0:128],
                            h2sb[:, (c % 2) * 512 + 128 * stg:(c % 2) * 512 + 128 * (stg + 1)],
                            IDT[:, :]).then_inc(sT.h, 1)
            T(t_h2t)
            h2t_done[(c, stg)] = sT.inc()

            def v_yev(e, c=c, stg=stg, idx=idx):
                e.wait_ge(sT.h, h2t_done[(c, stg)])
                e.tensor_scalar(ybuf[:, stg * D + 128 * c:stg * D + 128 * (c + 1)],
                                pbank_bf[idx % 2][:, 0:128],
                                slotv[:, stg * 4 + 3:stg * 4 + 4], None, ALU.mult).then_inc(sV.h, 1)
            V(v_yev)
            yev_done[idx] = sV.inc()

    fin_scat = {}
    fin_conv = {}
    for stg in range(4):
        b = stg // 2

        def s_conv(e, stg=stg):
            e.wait_ge(sGG.h, gath_done[stg])
            if stg >= 2:
                e.wait_ge(sGG.h, fin_scat[stg - 2])
            e.activation(out=outro[:, (stg % 2) * D:(stg % 2 + 1) * D],
                         in_=fxb[:, stg * D:(stg + 1) * D], func=AF.Identity,
                         scale=2.0).then_inc(sS.h, 1)
        SC(s_conv)
        fin_conv[stg] = sS.inc()

        def v_fin2(e, stg=stg):
            e.wait_ge(sV.h, yev_done[15 * 4 + stg])
            e.wait_ge(sS.h, fin_conv[stg])
            yt = ybuf[:, stg * D:(stg + 1) * D]
            e.tensor_tensor(yt[:, :], yt[:, :], outro[:, (stg % 2) * D:(stg % 2 + 1) * D], ALU.add)
            o = stg * 8
            e.drain()
            for i in range(4):
                e.bn_stats(lnstats[:, stg * 24 + 6 * i:stg * 24 + 6 * (i + 1)], yt[:, 512 * i:512 * (i + 1)])
            e.drain()
            e.bn_aggr(lnmv[:, o:o + 2], lnstats[:, stg * 24:stg * 24 + 24])
            e.drain()
            e.tensor_scalar(lnmv[:, o + 1:o + 2], lnmv[:, o + 1:o + 2], EPS, None, ALU.add)
            e.drain()
            e.reciprocal(lnmv[:, o + 1:o + 2], lnmv[:, o + 1:o + 2]).then_inc(sV.h, 1)
        V(v_fin2)
        fin_stats = sV.inc()

        def s_fin2(e, stg=stg, fin_stats=fin_stats):
            e.wait_ge(sV.h, fin_stats)
            o = stg * 8
            e.activation(out=lnmv[:, o + 1:o + 2], in_=lnmv[:, o + 1:o + 2], func=AF.Ln)
            e.activation(out=lnmv[:, o + 1:o + 2], in_=lnmv[:, o + 1:o + 2], func=AF.Exp, scale=0.5).then_inc(sS.h, 1)
        SC(s_fin2)
        fin_inv = sS.inc()

        def v_fin3(e, stg=stg, fin_inv=fin_inv):
            e.wait_ge(sS.h, fin_inv)
            o = stg * 8
            e.scalar_tensor_tensor(lnmv[:, o + 3:o + 4], lnmv[:, o + 0:o + 1], -1.0,
                                   lnmv[:, o + 1:o + 2], ALU.mult, ALU.mult).then_inc(sV.h, 1)
        V(v_fin3)
        fin_b = sV.inc()

        def s_fin3(e, stg=stg, fin_b=fin_b):
            e.wait_ge(sV.h, fin_b)
            o = stg * 8
            e.activation(out=outro[:, (stg % 2) * D:(stg % 2 + 1) * D], in_=ybuf[:, stg * D:(stg + 1) * D],
                         func=AF.Identity, bias=lnmv[:, o + 3:o + 4],
                         scale=lnmv[:, o + 1:o + 2]).then_inc(sS.h, 1)
        SC(s_fin3)
        fin_app = sS.inc()

        def g_scat(e, stg=stg, b=b, fin_app=fin_app, prev_gg=sGG.n):
            e.wait_ge(sS.h, fin_app)
            e.wait_ge(sGG.h, prev_gg)
            if stg == 0:
                for ee in range(2):
                    e.wait_ge(sE2[ee].h, sE2[ee].n)
            e.indirect_dma_start(out_h.rearrange("b s d -> (b s) d"),
                                 IndirectOffsetOnAxis(tI[:, stg:stg + 1], 0),
                                 outro[:, (stg % 2) * D:(stg % 2 + 1) * D], None,
                                 element_offset=b * S * D).then_inc(sGG.h, 16)
        G(g_scat)
        fin_scat[stg] = sGG.inc(16)

    FINAL_GG = sGG.n
    FINAL_E = [s.n for s in sE2]

    def g_end(e):
        e.wait_ge(sGG.h, FINAL_GG)
        for i in range(2):
            e.wait_ge(sE2[i].h, FINAL_E[i])
    G(g_end)

    def y_end(e):
        e.wait_ge(sGG.h, FINAL_GG)
        for i in range(2):
            e.wait_ge(sE2[i].h, FINAL_E[i])
    Y(y_end)

    import os
    _tr = float(os.environ.get("KTRUNC", "1.0"))
    if _tr < 1.0:
        for _k in OPS:
            OPS[_k] = OPS[_k][:max(1, int(len(OPS[_k]) * _tr))]
    with nc.Block() as block:
        @block.tensor
        def _(eng):
            for fn in OPS["T"]:
                fn(eng)

        @block.vector
        def _(eng):
            for fn in OPS["V"]:
                fn(eng)

        @block.scalar
        def _(eng):
            for fn in OPS["S"]:
                fn(eng)

        @block.gpsimd
        def _(eng):
            for fn in OPS["G"]:
                fn(eng)

        @block.sync
        def _(eng):
            for fn in OPS["Y"]:
                fn(eng)

    esB.close()
    es.close()
    return nc


def _consts():
    c = {}
    c["IDT"] = np.eye(128, dtype=np.float32).astype(ml_dtypes.bfloat16)
    c["IDTF"] = np.eye(128, dtype=np.float32)
    c["TRI"] = np.triu(np.ones((128, 128), np.float32), 1).astype(ml_dtypes.bfloat16)
    c["TRI16"] = np.triu(np.ones((16, 16), np.float32), 1).astype(ml_dtypes.bfloat16)
    c["ONESC"] = np.ones((128, 128), np.float32).astype(ml_dtypes.bfloat16)
    c["ONES1F"] = np.ones((1, 128), np.float32)
    c["ONESCF"] = np.ones((128, 1), np.float32)
    c["IO256"] = np.broadcast_to(np.arange(256, dtype=np.float32)[None, :], (128, 256)).copy()
    q = np.arange(128)[:, None]
    k = np.arange(16)[None, :]
    c["SG"] = (128 * k + q).astype(np.float16)
    s_ = k // 4
    par_ = (k % 4) // 2
    tl_ = k % 2
    c["TG"] = (512 * s_ + 256 * tl_ + 2 * q + par_).astype(np.float16)
    return c


_NC_CACHE = {}


def kernel(L0_attn_x, L0_ffn_Wr, L0_ffn_br, L0_ffn_W1, L0_ffn_b1, L0_ffn_W2, L0_ffn_b2):
    x = np.ascontiguousarray(np.asarray(L0_attn_x, dtype=np.float32))
    Wr = np.ascontiguousarray(np.asarray(L0_ffn_Wr, dtype=np.float32).reshape(1, D))
    br = np.ascontiguousarray(np.asarray(L0_ffn_br, dtype=np.float32).reshape(1, 1))
    W1 = np.ascontiguousarray(np.asarray(L0_ffn_W1, dtype=np.float32))
    b1 = np.ascontiguousarray(np.asarray(L0_ffn_b1, dtype=np.float32).reshape(64, 128).T)
    W2 = np.ascontiguousarray(np.asarray(L0_ffn_W2, dtype=np.float32))
    b2 = np.ascontiguousarray(np.asarray(L0_ffn_b2, dtype=np.float32).reshape(16, 128).T)
    consts = _consts()

    if "nc" not in _NC_CACHE:
        _NC_CACHE["nc"] = build_nc()
    nc = _NC_CACHE["nc"]

    in_maps = []
    for c in range(8):
        m = {"x": np.ascontiguousarray(x[2 * c:2 * c + 2]),
             "Wr": Wr, "br": br, "W1": W1, "b1": b1, "W2": W2, "b2": b2}
        m.update(consts)
        in_maps.append(m)
    res = run_bass_kernel_spmd(nc, in_maps, core_ids=list(range(8)))
    out = np.concatenate([res.results[i]["out"] for i in range(8)], axis=0)
    return out.astype(np.float32)


# version 08
# speedup vs baseline: 1.1168x; 1.0653x over previous
import os as _os
import numpy as np
import ml_dtypes
import concourse.bass as bass
import concourse.mybir as mybir
from concourse.bass_utils import run_bass_kernel_spmd
from concourse.bass_core_rust import IndirectOffsetOnAxis

F32 = mybir.dt.float32
BF16 = mybir.dt.bfloat16
FP16 = mybir.dt.float16
I32 = mybir.dt.int32
AF = mybir.ActivationFunctionType
ALU = mybir.AluOpType
AX = mybir.AxisListType

NB = 2
NSEG = 4
D = 2048
S = 2048
DFF = 8192
NSLOT = 256
SCALE = float(1.0 / np.sqrt(128.0))
EPS = 1e-5
NBIS = 33
WOFF = 16.0


class Sem:
    def __init__(self, es, nc, name):
        self.h = es.enter_context(nc.semaphore(name))
        self.n = 0

    def inc(self, k=1):
        self.n += k
        return self.n


def build_nc():
    from contextlib import ExitStack
    nc = bass.Bass("TRN2", target_bir_lowering=False, detect_race_conditions=False)
    es = ExitStack()

    x_h = nc.dram_tensor("x", [NB, S, D], F32, kind="ExternalInput").ap()
    wr_h = nc.dram_tensor("Wr", [1, D], F32, kind="ExternalInput").ap()
    br_h = nc.dram_tensor("br", [1, 1], F32, kind="ExternalInput").ap()
    w1_h = nc.dram_tensor("W1", [D, DFF], F32, kind="ExternalInput").ap()
    b1_h = nc.dram_tensor("b1", [128, 64], F32, kind="ExternalInput").ap()
    w2_h = nc.dram_tensor("W2", [DFF, D], F32, kind="ExternalInput").ap()
    b2_h = nc.dram_tensor("b2", [128, 16], F32, kind="ExternalInput").ap()
    idt_h = nc.dram_tensor("IDT", [128, 128], BF16, kind="ExternalInput").ap()
    idtf_h = nc.dram_tensor("IDTF", [128, 128], F32, kind="ExternalInput").ap()
    tri_h = nc.dram_tensor("TRI", [128, 128], BF16, kind="ExternalInput").ap()
    tri16_h = nc.dram_tensor("TRI16", [16, 16], BF16, kind="ExternalInput").ap()
    onesc_h = nc.dram_tensor("ONESC", [128, 128], BF16, kind="ExternalInput").ap()
    ones1f_h = nc.dram_tensor("ONES1F", [1, 128], F32, kind="ExternalInput").ap()
    onescf_h = nc.dram_tensor("ONESCF", [128, 1], F32, kind="ExternalInput").ap()
    io256_h = nc.dram_tensor("IO256", [128, 256], F32, kind="ExternalInput").ap()
    sg_h = nc.dram_tensor("SG", [128, 16], FP16, kind="ExternalInput").ap()
    tg_h = nc.dram_tensor("TG", [128, 16], FP16, kind="ExternalInput").ap()
    out_h = nc.dram_tensor("out", [NB, S, D], F32, kind="ExternalOutput").ap()
    dbg_h = nc.dram_tensor("dbg", [128, 6176], F32, kind="ExternalOutput").ap()
    x1m_h = nc.dram_tensor("x1m", [NB, S, D], BF16, kind="ExternalOutput" if _os.environ.get("KDBG") else "Internal").ap()

    def xview(h, b, s, par, tl):
        return h[b].rearrange("(s tl p two) d -> s two tl p d", s=4, tl=2, p=128, two=2)[s, par, tl]

    sb = lambda name, shape, dt: es.enter_context(nc.sbuf_tensor(name, shape, dt))
    IDT = sb("c_idt", [128, 128], BF16)
    IDTF = sb("c_idtf", [128, 128], F32)
    TRI = sb("c_tri", [128, 128], BF16)
    TRI16 = sb("c_tri16", [16, 16], BF16)
    ONESC = sb("c_onesc", [128, 128], BF16)
    ONES1F = sb("c_ones1f", [1, 128], F32)
    ONESCF = sb("c_onescf", [128, 1], F32)
    IO256 = sb("c_io256", [128, 256], F32)
    SG = sb("c_sg", [128, 16], FP16)
    TG = sb("c_tg", [128, 16], FP16)
    WrT = sb("c_wrt", [1, D], BF16)
    WrB = sb("c_wrb", [128, D], BF16)
    brS = sb("c_brs", [1, 1], F32)
    brB = sb("c_brb", [128, 1], F32)
    b1T = sb("c_b1t", [128, 64], F32)
    b2T = sb("c_b2t", [128, 16], F32)

    lnstats = sb("lnstats", [128, 4 * 24], F32)
    lnmv = sb("lnmv", [128, 4 * 8], F32)
    lo_b = sb("lo", [128, 2], F32)
    hi_b = sb("hi", [128, 2], F32)
    mid_b = sb("mid", [128, 2], F32)
    tmp16 = sb("tmp16", [128, 16], BF16)
    prt_b = sb("prt", [128, 2], BF16)
    cge_b = sb("cge", [128, 2], F32)
    dlt_b = sb("dlt", [128, 2], F32)
    dlt2_b = sb("dlt2", [128, 2], F32)
    L2g = sb("L2g", [128, 16], F32)
    vspc = sb("vspc", [128, 2], F32)
    mask_b = sb("mask", [128, 2 * 16], F32)
    maskbf = sb("maskbf", [128, 16], BF16)
    ct16 = sb("ct16", [16, 1], BF16)
    cb16 = sb("cb16", [1, 16], BF16)
    pos_b = sb("pos", [128, 16], F32)
    lbr_b = sb("lbr", [128, 16], F32)
    wexp = sb("wexp", [128, 16], F32)
    wm32 = sb("wm32", [128, 16], F32)
    wprt = sb("wprt", [128, 1], F32)
    bndc = sb("bndc", [128, 2], F32)
    bnds = sb("bnds", [1, 2], F32)
    bndB = sb("bndB", [128, 2], F32)
    VALS = sb("VALS", [128, 16 * 3], FP16)
    zt_s = sb("zt", [1, 2], F32)
    rzB = sb("rzB", [128, 1], F32)
    slotv = sb("slotv", [128, 4 * 4], F32)
    sigI = sb("sigI", [128, 4], I32)
    tI = sb("tI", [128, 4], I32)

    pbank = [es.enter_context(nc.psum_tensor(f"pb{i}", [128, 512], F32)) for i in range(6)]
    pbank_bf = [es.enter_context(nc.psum_tensor(f"pbb{i}", [128, 512], BF16)) for i in range(2)]

    sT = Sem(es, nc, "sT")
    sV = Sem(es, nc, "sV")
    sS = Sem(es, nc, "sS")
    sC = Sem(es, nc, "sC")
    sWrt = Sem(es, nc, "sWrt")
    sX = [Sem(es, nc, f"sX{i}") for i in range(2)]
    sXB = [Sem(es, nc, f"sXB{i}") for i in range(2)]
    sE2 = [Sem(es, nc, f"sE2{i}") for i in range(2)]
    sM4 = [Sem(es, nc, f"sM4{i}") for i in range(4)]
    sW1 = [Sem(es, nc, f"sW1{i}") for i in range(2)]
    sW2 = [Sem(es, nc, f"sW2{i}") for i in range(2)]
    sGG = Sem(es, nc, "sGG")

    OPS = {"T": [], "V": [], "S": [], "G": [], "Y": []}
    T = lambda fn: OPS["T"].append(fn)
    V = lambda fn: OPS["V"].append(fn)
    SC = lambda fn: OPS["S"].append(fn)
    G = lambda fn: OPS["G"].append(fn)
    Y = lambda fn: OPS["Y"].append(fn)

    def setup_loads(e):
        for dst, src in ((IDT, idt_h), (IDTF, idtf_h), (TRI, tri_h), (TRI16, tri16_h), (ONESC, onesc_h),
                         (ONES1F, ones1f_h), (ONESCF, onescf_h), (IO256, io256_h),
                         (SG, sg_h), (TG, tg_h), (brS, br_h)):
            e.dma_start(out=dst[:, :], in_=src[:, :]).then_inc(sC.h, 16)
        e.dma_start(out=b1T[:, :], in_=b1_h[:, :]).then_inc(sC.h, 16)
        e.dma_start(out=b2T[:, :], in_=b2_h[:, :]).then_inc(sC.h, 16)
    Y(setup_loads)
    LD_CONST = sC.inc(16 * 13)

    def setup_wrt(e):
        e.dma_start(out=WrT[:, :], in_=wr_h[:, :]).then_inc(sWrt.h, 16)
    G(setup_wrt)
    LD_WRT = sWrt.inc(16)

    def setup_bcast_t(e):
        e.wait_ge(sC.h, LD_CONST)
        e.wait_ge(sWrt.h, LD_WRT)
        for i in range(4):
            e.matmul(pbank[i][:, :], ONESC[0:1, :], WrT[:, 512 * i:512 * (i + 1)], start=True, stop=True)
        e.matmul(pbank[4][:, 0:1], ONES1F[:, :], brS[:, :], start=True, stop=True).then_inc(sT.h, 1)
    T(setup_bcast_t)
    T_BCAST = sT.inc()

    def setup_bcast_v(e):
        e.memset(vspc[:, :], 0.0)
        e.wait_ge(sT.h, T_BCAST)
        for i in range(4):
            e.tensor_copy(WrB[:, 512 * i:512 * (i + 1)], pbank[i][:, :])
        e.tensor_copy(brB[:, :], pbank[4][:, 0:1]).then_inc(sV.h, 1)
    V(setup_bcast_v)
    V_BCAST = sV.inc()

    esA = ExitStack()
    xz = esA.enter_context(nc.sbuf_tensor("xz", [128, 2 * 4 * D], F32))
    xbuf = esA.enter_context(nc.sbuf_tensor("xbw", [128, 2 * 4 * D], BF16))
    x1st = esA.enter_context(nc.sbuf_tensor("x1stage", [128, 4 * D], BF16))
    sttscr = esA.enter_context(nc.sbuf_tensor("sttscr", [128, D], BF16))
    Lgrid = esA.enter_context(nc.sbuf_tensor("Lgrid", [128, 2 * 16], F32))
    outear = esA.enter_context(nc.sbuf_tensor("outear", [128, 2 * D], F32))
    PT = esA.enter_context(nc.sbuf_tensor("PT", [128, 16 * 256], FP16))
    gT = esA.enter_context(nc.sbuf_tensor("gT", [128, 2 * 256], BF16))
    Ebuf = esA.enter_context(nc.sbuf_tensor("Ebuf", [128, 2 * 512], BF16))
    Dbuf = esA.enter_context(nc.sbuf_tensor("Dbuf", [128, 2 * 2], F32))
    Rbuf = esA.enter_context(nc.sbuf_tensor("Rbuf", [128, 2 * 2], F32))

    def xz_t(sbuf_i, par, tl):
        off = (sbuf_i * 4 + par * 2 + tl) * D
        return xz[:, off:off + D]

    def xb_t(sbuf_i, par, tl):
        off = (sbuf_i * 4 + par * 2 + tl) * D
        return xbuf[:, off:off + D]

    pair_list = [(b, s) for b in range(NB) for s in range(NSEG)]
    xload_done = {}
    xcast_done = {}
    seg_consumed = {}

    uT_tp, uT_sc, uT_pv = {}, {}, {}
    uT_sc1 = {}
    uS_E = {}
    uV_gT, uV_R, uV_z = {}, {}, {}
    ls_apply, lv_logit, lm_mirror, ly_early = {}, {}, {}, {}
    lv_stats, ls_inv, lv_bias = {}, {}, {}

    pend_vz = [None]
    pend_tpv = [None]
    attn_units = []
    for b, s in pair_list:
        for par in range(2):
            for hh in range(8):
                attn_units.append((b, s, par, hh))
    last_unit_of_pair = {}
    for ui, u in enumerate(attn_units):
        last_unit_of_pair[(u[0], u[1], u[2])] = ui

    ln_units = []
    for gi, (b, s) in enumerate(pair_list):
        for par in range(2):
            for tl in range(2):
                ln_units.append((gi, b, s, par, tl))
    last_li_of_b = {b: max(i for i, u in enumerate(ln_units) if u[1] == b) for b in range(NB)}

    def emit_load(gi):
        b, s = pair_list[gi]
        sbuf_i = gi % 2

        def loadseg(e, b=b, s=s, sbuf_i=sbuf_i, gi=gi):
            if gi == 2 and _os.environ.get("KDBG"):
                e.wait_ge(sC.h, DBGC[0])
            if gi >= 2:
                tS, tT = seg_consumed[pair_list[gi - 2]]
                e.wait_ge(sS.h, tS)
                e.wait_ge(sT.h, tT)
            for par in range(2):
                for tl in range(2):
                    e.dma_start(out=xz_t(sbuf_i, par, tl), in_=xview(x_h, b, s, par, tl)).then_inc(sX[gi % 2].h, 16)
        Y(loadseg)
        xload_done[(b, s)] = (gi % 2, sX[gi % 2].inc(16 * 4))

    def emit_cast(gi):
        b, s = pair_list[gi]
        sbuf_i = gi % 2

        def castseg(e, b=b, s=s, sbuf_i=sbuf_i, gi=gi):
            if gi == 2 and _os.environ.get("KDBG"):
                e.wait_ge(sC.h, DBGC[0])
            xi, xv = xload_done[(b, s)]
            e.wait_ge(sX[xi].h, xv)
            if gi >= 2:
                tS, tT = seg_consumed[pair_list[gi - 2]]
                e.wait_ge(sS.h, tS)
                e.wait_ge(sT.h, tT)
            for par in range(2):
                for tl in range(2):
                    e.dma_start(out=xb_t(sbuf_i, par, tl), in_=xz_t(sbuf_i, par, tl)).then_inc(sXB[gi % 2].h, 16)
        G(castseg)
        xcast_done[(b, s)] = (gi % 2, sXB[gi % 2].inc(16 * 4))

    def emit_attn_unit(ui):
        b, s, par, hh = attn_units[ui]
        gi = pair_list.index((b, s))
        sbuf_i = gi % 2
        slot = ui % 2
        col = (2 * hh + par) * 128

        def t_tp(e, b=b, s=s, par=par, ui=ui, sbuf_i=sbuf_i, col=col):
            if ui == 0:
                e.wait_ge(sV.h, V_BCAST)
            ci, cv = xcast_done[(b, s)]
            e.wait_ge(sXB[ci].h, cv)
            if _os.environ.get("KSERIAL") and ui >= 1:
                e.wait_ge(sV.h, uV_z[ui - 1])
            if _os.environ.get("KSERIAL") and ui % 16 == 0 and ui >= 16:
                li_prev = max(i for i, lu in enumerate(ln_units) if lu[0] == (ui // 16) - 1)
                e.wait_ge(sV.h, lv_logit[li_prev])
            if ui >= 1:
                e.wait_ge(sV.h, uV_gT[ui - 1])
            e.transpose(pbank_bf[0][:, 0:128], xb_t(sbuf_i, par, 0)[:, col:col + 128], IDT[:, :])
            e.transpose(pbank_bf[1][:, 0:128], xb_t(sbuf_i, par, 1)[:, col:col + 128], IDT[:, :]).then_inc(sT.h, 1)
        T(t_tp)
        uT_tp[ui] = sT.inc()

        if pend_tpv[0] is not None:
            fnp, pu = pend_tpv[0]
            T(fnp)
            uT_pv[pu] = sT.inc()
            pend_tpv[0] = None

        def v_gT(e, ui=ui, slot=slot):
            e.wait_ge(sT.h, uT_tp[ui])
            if ui >= 2:
                e.wait_ge(sT.h, uT_sc[ui - 2])
            go = slot * 256
            e.tensor_copy(gT[:, go:go + 128], pbank_bf[0][:, 0:128])
            e.tensor_copy(gT[:, go + 128:go + 256], pbank_bf[1][:, 0:128]).then_inc(sV.h, 1)
        V(v_gT)
        uV_gT[ui] = sV.inc()

        if pend_vz[0] is not None:
            fn, pu = pend_vz[0]
            V(fn)
            uV_z[pu] = sV.inc()
            pend_vz[0] = None

        def t_sc(e, ui=ui, slot=slot):
            e.wait_ge(sV.h, uV_gT[ui])
            if ui >= 1:
                e.wait_ge(sS.h, uS_E[ui - 1] - 1)
            go = slot * 256
            e.matmul(pbank[2][:, 0:256], gT[:, go:go + 128], gT[:, go:go + 256], start=True, stop=True).then_inc(sT.h, 1)
            if ui >= 1:
                e.wait_ge(sS.h, uS_E[ui - 1])
            e.matmul(pbank[3][:, 0:256], gT[:, go + 128:go + 256], gT[:, go:go + 256], start=True, stop=True).then_inc(sT.h, 1)
        T(t_sc)
        uT_sc1[ui] = sT.inc()
        uT_sc[ui] = sT.inc()

        def s_E(e, ui=ui, slot=slot):
            e.wait_ge(sT.h, uT_sc1[ui])
            if ui >= 2:
                e.wait_ge(sV.h, uV_R[ui - 2])
            eo = slot * 512
            do = slot * 2
            e.activation(out=Ebuf[:, eo:eo + 256], in_=pbank[2][:, 0:256], func=AF.Exp,
                         scale=SCALE, accum_out=Dbuf[:, do:do + 1]).then_inc(sS.h, 1)
            e.wait_ge(sT.h, uT_sc[ui])
            e.activation(out=Ebuf[:, eo + 256:eo + 512], in_=pbank[3][:, 0:256], func=AF.Exp,
                         scale=SCALE, accum_out=Dbuf[:, do + 1:do + 2]).then_inc(sS.h, 1)
        SC(s_E)
        sS.inc()
        uS_E[ui] = sS.inc()

        def v_R(e, ui=ui, slot=slot):
            e.wait_ge(sS.h, uS_E[ui])
            do = slot * 2
            e.reciprocal(Rbuf[:, do:do + 2], Dbuf[:, do:do + 2]).then_inc(sV.h, 1)
        V(v_R)
        uV_R[ui] = sV.inc()

        def t_pv(e, ui=ui, slot=slot, sbuf_i=sbuf_i, par=par, col=col):
            e.wait_ge(sS.h, uS_E[ui])
            if ui >= 1:
                e.wait_ge(sV.h, uV_z[ui - 1])
            eo = slot * 512
            last = None
            for j in range(2):
                for k in range(2):
                    last = e.matmul(pbank[4 + j][:, 0:128],
                                    Ebuf[:, eo + k * 256 + j * 128:eo + k * 256 + (j + 1) * 128],
                                    xb_t(sbuf_i, par, k)[:, col:col + 128],
                                    start=(k == 0), stop=(k == 1))
            last.then_inc(sT.h, 1)
        pend_tpv[0] = (t_pv, ui)

        def v_z(e, ui=ui, slot=slot, sbuf_i=sbuf_i, par=par, col=col, flushed=False):
            e.wait_ge(sT.h, uT_pv[ui])
            e.wait_ge(sV.h, uV_R[ui])
            if flushed:
                e.drain()
            do = slot * 2
            last = None
            for j in range(2):
                zt = xz_t(sbuf_i, par, j)
                last = e.scalar_tensor_tensor(zt[:, col:col + 128], pbank[4 + j][:, 0:128],
                                              Rbuf[:, do + j:do + j + 1], zt[:, col:col + 128],
                                              ALU.mult, ALU.add)
            last.then_inc(sV.h, 1)
        pend_vz[0] = (v_z, ui)

    def emit_ln_unit(li):
        gi, b, s, par, tl = ln_units[li]
        sbuf_i = gi % 2
        k = 4 * s + 2 * par + tl
        ring = k % 4
        ering = li % 2

        def v_stats(e, li=li, b=b, s=s, par=par, tl=tl, ring=ring, sbuf_i=sbuf_i):
            e.wait_ge(sV.h, uV_z[last_unit_of_pair[(b, s, par)]])
            if li >= 4:
                e.wait_ge(sS.h, ls_apply[li - 4])
            zt = xz_t(sbuf_i, par, tl)
            for i in range(4):
                e.bn_stats(lnstats[:, ring * 24 + 6 * i:ring * 24 + 6 * (i + 1)], zt[:, 512 * i:512 * (i + 1)])
            o = ring * 8
            e.drain()
            e.bn_aggr(lnmv[:, o:o + 2], lnstats[:, ring * 24:ring * 24 + 24])
            e.drain()
            e.tensor_scalar(lnmv[:, o + 7:o + 8], lnmv[:, o + 1:o + 2], EPS, None, ALU.add)
            e.drain()
            e.reciprocal(lnmv[:, o + 7:o + 8], lnmv[:, o + 7:o + 8])
            e.drain()
            e.tensor_tensor(lnmv[:, o + 4:o + 5], lnmv[:, o + 1:o + 2], lnmv[:, o + 7:o + 8], ALU.mult)
            e.tensor_copy(vspc[:, 0:1], vspc[:, 1:2])
            e.tensor_scalar(lnmv[:, o + 4:o + 5], lnmv[:, o + 4:o + 5], EPS / 4.0, None, ALU.add).then_inc(sV.h, 1)
        V(v_stats)
        lv_stats[li] = sV.inc()

        def s_inv2(e, li=li, ring=ring):
            e.wait_ge(sV.h, lv_stats[li])
            o = ring * 8
            e.activation(out=lnmv[:, o + 2:o + 3], in_=lnmv[:, o + 7:o + 8], func=AF.Ln)
            e.activation(out=lnmv[:, o + 4:o + 5], in_=lnmv[:, o + 4:o + 5], func=AF.Ln)
            e.activation(out=lnmv[:, o + 2:o + 3], in_=lnmv[:, o + 2:o + 3], func=AF.Exp, scale=0.5)
            e.activation(out=lnmv[:, o + 4:o + 5], in_=lnmv[:, o + 4:o + 5], func=AF.Exp, scale=-0.5).then_inc(sS.h, 1)
        SC(s_inv2)
        ls_inv[li] = sS.inc()

        def v_small(e, li=li, ring=ring):
            e.wait_ge(sS.h, ls_inv[li])
            o = ring * 8
            e.scalar_tensor_tensor(lnmv[:, o + 3:o + 4], lnmv[:, o + 0:o + 1], -1.0,
                                   lnmv[:, o + 2:o + 3], ALU.mult, ALU.mult)
            e.tensor_tensor(lnmv[:, o + 5:o + 6], lnmv[:, o + 2:o + 3], lnmv[:, o + 4:o + 5], ALU.mult)
            e.tensor_tensor(lnmv[:, o + 6:o + 7], lnmv[:, o + 3:o + 4], lnmv[:, o + 4:o + 5], ALU.mult).then_inc(sV.h, 1)
        V(v_small)
        lv_sc2 = sV.inc()

        def s_apply(e, li=li, ring=ring, ering=ering, b=b, s=s, par=par, tl=tl, sbuf_i=sbuf_i, k=k, lv_sc2=lv_sc2):
            e.wait_ge(sV.h, lv_sc2)
            o = ring * 8
            zt = xz_t(sbuf_i, par, tl)
            if li == 4 and _os.environ.get("KDBG"):
                e.wait_ge(sC.h, DBGC[0])
            if li >= 4:
                e.wait_ge(sM4[li % 4].h, lm_mirror[li - 4])
                e.wait_ge(sV.h, lv_logit[li - 4])
            if li >= 2:
                e.wait_ge(sE2[li % 2].h, ly_early[li - 2])
            e.activation(out=x1st[:, ring * D:(ring + 1) * D], in_=zt[:, :], func=AF.Identity,
                         bias=lnmv[:, o + 3:o + 4], scale=lnmv[:, o + 2:o + 3])
            e.activation(out=outear[:, ering * D:(ering + 1) * D], in_=zt[:, :], func=AF.Identity,
                         bias=lnmv[:, o + 6:o + 7], scale=lnmv[:, o + 5:o + 6]).then_inc(sS.h, 1)
            e.wait_ge(sS.h, ls_apply[li])
            e.dma_start(out=x1m_h[b][128 * k:128 * (k + 1), :],
                        in_=x1st[:, ring * D:(ring + 1) * D]).then_inc(sM4[li % 4].h, 16)
        ls_apply[li] = sS.inc()
        SC(s_apply)
        lm_mirror[li] = sM4[li % 4].inc(16)

        def v_logit(e, li=li, ring=ring, b=b, k=k):
            e.wait_ge(sS.h, ls_apply[li])
            e.scalar_tensor_tensor(sttscr[:, :], x1st[:, ring * D:(ring + 1) * D], 1.0, WrB[:, :],
                                   ALU.mult, ALU.mult,
                                   accum_out=Lgrid[:, b * 16 + k:b * 16 + k + 1]).then_inc(sV.h, 1)
        V(v_logit)
        lv_logit[li] = sV.inc()

        def y_early(e, li=li, ering=ering, b=b, s=s, par=par, tl=tl):
            e.wait_ge(sS.h, ls_apply[li])
            e.dma_start(out=xview(out_h, b, s, par, tl), in_=outear[:, ering * D:(ering + 1) * D]).then_inc(sE2[li % 2].h, 16)
        Y(y_early)
        ly_early[li] = sE2[li % 2].inc(16)

        if par == 1 and tl == 1:
            seg_consumed[(b, s)] = (ls_apply[li], uT_pv[last_unit_of_pair[(b, s, 1)]])

    router_state = {}

    def build_router_ops(b, prev_ext_t):
        st = {}
        router_state[b] = st
        logits_ready = lv_logit[last_li_of_b[b]]

        def v_bnd1(e, b=b, logits_ready=logits_ready, prev_ext_t=prev_ext_t):
            e.wait_ge(sV.h, logits_ready)
            if prev_ext_t is not None:
                e.wait_ge(sT.h, prev_ext_t)
            e.reduce_max(bndc[:, 0:1], Lgrid[:, b * 16:(b + 1) * 16], axis=AX.X)
            e.tensor_scalar(lbr_b[:, :], Lgrid[:, b * 16:(b + 1) * 16], -1.0, None, ALU.mult)
            e.reduce_max(bndc[:, 1:2], lbr_b[:, :], axis=AX.X).then_inc(sV.h, 1)
        st["vbnd1"] = sV.inc()
        yield ("V", v_bnd1)

        def t_bnd(e, b=b, st=st):
            e.wait_ge(sV.h, st["vbnd1"])
            e.transpose(pbank[0][0:1, 0:128], bndc[:, 0:1], IDTF[:, :])
            e.transpose(pbank[1][0:1, 0:128], bndc[:, 1:2], IDTF[:, :]).then_inc(sT.h, 1)
        st["tbnd"] = sT.inc()
        yield ("T", t_bnd)

        def v_bnd2(e, b=b, st=st):
            e.wait_ge(sT.h, st["tbnd"])
            e.reduce_max(bnds[:, 0:1], pbank[0][0:1, 0:128], axis=AX.X)
            e.reduce_max(bnds[:, 1:2], pbank[1][0:1, 0:128], axis=AX.X).then_inc(sV.h, 1)
        st["vbnd2"] = sV.inc()
        yield ("V", v_bnd2)

        def t_bnd2(e, b=b, st=st):
            e.wait_ge(sV.h, st["vbnd2"])
            e.matmul(pbank[0][:, 0:2], ONES1F[:, :], bnds[:, :], start=True, stop=True).then_inc(sT.h, 1)
        st["tbnd2"] = sT.inc()
        yield ("T", t_bnd2)

        def v_binit(e, b=b, st=st):
            e.wait_ge(sT.h, st["tbnd2"])
            e.tensor_copy(bndB[:, :], pbank[0][:, 0:2])
            e.drain()
            e.tensor_scalar(hi_b[:, b:b + 1], bndB[:, 0:1], 2.0, None, ALU.mult)
            e.tensor_scalar(lo_b[:, b:b + 1], bndB[:, 1:2], -2.0, -2.0, ALU.mult, ALU.add)
            e.tensor_scalar(L2g[:, :], Lgrid[:, b * 16:(b + 1) * 16], 2.0, None, ALU.mult)
            e.tensor_tensor(mid_b[:, b:b + 1], lo_b[:, b:b + 1], hi_b[:, b:b + 1], ALU.add)
            e.drain()
            with nc.allow_low_precision("bisection counts <=16 are exact in bf16"):
                e.tensor_scalar(tmp16[:, :], L2g[:, :], mid_b[:, b:b + 1], None, ALU.is_gt,
                                ALU.add, accum_out=prt_b[:, b:b + 1]).then_inc(sV.h, 1)
        st[("v", 0)] = sV.inc()
        yield ("V", v_binit)

        for it in range(NBIS):
            def t_it(e, b=b, it=it, st=st):
                e.wait_ge(sV.h, st[("v", it)])
                e.matmul(pbank[it % 2][:, 0:1], ONESC[:, :], prt_b[:, b:b + 1], start=True, stop=True).then_inc(sT.h, 1)
            st[("t", it)] = sT.inc()
            yield ("T", t_it)

            def v_it(e, b=b, it=it, st=st):
                e.wait_ge(sT.h, st[("t", it)])
                ps = pbank[it % 2]
                e.tensor_scalar(cge_b[:, b:b + 1], ps[:, 0:1], float(NSLOT) - 0.5, None, ALU.is_ge)
                e.tensor_tensor(dlt_b[:, b:b + 1], mid_b[:, b:b + 1], lo_b[:, b:b + 1], ALU.subtract)
                e.tensor_tensor(dlt2_b[:, b:b + 1], hi_b[:, b:b + 1], mid_b[:, b:b + 1], ALU.subtract)
                e.scalar_tensor_tensor(lo_b[:, b:b + 1], dlt_b[:, b:b + 1], cge_b[:, b:b + 1],
                                       lo_b[:, b:b + 1], ALU.mult, ALU.add)
                e.scalar_tensor_tensor(hi_b[:, b:b + 1], dlt2_b[:, b:b + 1], cge_b[:, b:b + 1],
                                       mid_b[:, b:b + 1], ALU.mult, ALU.add)
                last = None
                if it < NBIS - 1:
                    e.drain()
                    e.tensor_tensor(mid_b[:, b:b + 1], lo_b[:, b:b + 1], hi_b[:, b:b + 1], ALU.add)
                    e.drain()
                    with nc.allow_low_precision("bisection counts <=16 are exact in bf16"):
                        last = e.tensor_scalar(tmp16[:, :], L2g[:, :], mid_b[:, b:b + 1], None, ALU.is_gt,
                                               ALU.add, accum_out=prt_b[:, b:b + 1])
                else:
                    e.drain()
                    last = e.tensor_scalar(mask_b[:, b * 16:(b + 1) * 16], L2g[:, :],
                                           lo_b[:, b:b + 1], None, ALU.is_gt)
                last.then_inc(sV.h, 1)
            st[("v", it + 1)] = sV.inc()
            yield ("V", v_it)

        def v_fin(e, b=b, st=st):
            e.wait_ge(sV.h, st[("v", NBIS)])
            e.tensor_scalar(lbr_b[:, :], Lgrid[:, b * 16:(b + 1) * 16], bndB[:, 0:1], None,
                            ALU.subtract)
            e.tensor_copy(maskbf[:, :], mask_b[:, b * 16:(b + 1) * 16]).then_inc(sV.h, 1)
        st["vfin"] = sV.inc()
        yield ("V", v_fin)

        def s_wexp(e, b=b, st=st):
            e.wait_ge(sV.h, st["vfin"])
            e.activation(out=wexp[:, :], in_=lbr_b[:, :], func=AF.Exp).then_inc(sS.h, 1)
        st["swexp"] = sS.inc()
        yield ("S", s_wexp)

        def t_ct(e, b=b, st=st):
            e.wait_ge(sV.h, st["vfin"])
            e.matmul(pbank[0][0:16, 0:1], maskbf[:, :], ONESC[:, 0:1], start=True, stop=True).then_inc(sT.h, 1)
        st["tct"] = sT.inc()
        yield ("T", t_ct)

        def v_ct(e, b=b, st=st):
            e.wait_ge(sT.h, st["tct"])
            e.wait_ge(sS.h, st["swexp"])
            e.tensor_tensor(wm32[:, :], wexp[:, :], mask_b[:, b * 16:(b + 1) * 16], ALU.mult)
            e.tensor_copy(ct16[:, :], pbank[0][0:16, 0:1])
            e.reduce_sum(wprt[:, :], wm32[:, :], axis=AX.X).then_inc(sV.h, 1)
        st["vct"] = sV.inc()
        yield ("V", v_ct)

        def t_cb(e, b=b, st=st):
            e.wait_ge(sV.h, st["vct"])
            e.matmul(pbank[1][0:1, 0:16], ct16[:, :], TRI16[:, :], start=True, stop=True)
            e.matmul(pbank[0][0:1, 0:1], wprt[:, :], ONESCF[:, :], start=True, stop=True).then_inc(sT.h, 1)
        st["tcb"] = sT.inc()
        yield ("T", t_cb)

        def v_cb(e, b=b, st=st):
            e.wait_ge(sT.h, st["tcb"])
            e.tensor_copy(zt_s[:, b:b + 1], pbank[0][0:1, 0:1])
            e.tensor_copy(cb16[:, :], pbank[1][0:1, 0:16])
            e.reciprocal(zt_s[:, b:b + 1], zt_s[:, b:b + 1]).then_inc(sV.h, 1)
        st["vcb"] = sV.inc()
        yield ("V", v_cb)

        def t_pos(e, b=b, st=st):
            e.wait_ge(sV.h, st["vcb"])
            e.matmul(pbank[0][:, 0:16], ONESC[0:1, :], cb16[:, :], start=True, stop=True)
            e.matmul(pbank[1][:, 0:16], TRI[:, :], maskbf[:, :], start=True, stop=True).then_inc(sT.h, 1)
        st["tpos"] = sT.inc()
        yield ("T", t_pos)

        def v_pos(e, b=b, st=st):
            e.wait_ge(sT.h, st["tpos"])
            e.tensor_copy(pos_b[:, :], pbank[0][:, 0:16])
            e.tensor_copy(vspc[:, 0:1], vspc[:, 1:2])
            e.tensor_tensor(pos_b[:, :], pos_b[:, :], pbank[1][:, 0:16], ALU.add)
            e.drain()
            for kk in range(16):
                e.tensor_scalar(PT[:, 256 * kk:256 * (kk + 1)], IO256[:, :], pos_b[:, kk:kk + 1],
                                mask_b[:, b * 16 + kk:b * 16 + kk + 1], ALU.is_equal, ALU.mult)
            e.tensor_copy(VALS[:, 0:48:3], SG[:, :])
            e.tensor_copy(VALS[:, 1:48:3], TG[:, :])
            last = None
            for kk in range(16):
                last = e.tensor_scalar(VALS[:, 3 * kk + 2:3 * kk + 3], wm32[:, kk:kk + 1], 1.0, None, ALU.mult)
            last.then_inc(sV.h, 1)
        st["vpos"] = sV.inc()
        yield ("V", v_pos)

        def t_rz(e, b=b, st=st):
            e.wait_ge(sV.h, st["vpos"])
            e.matmul(pbank[0][:, 0:1], ONES1F[:, :], zt_s[:, b:b + 1], start=True, stop=True).then_inc(sT.h, 1)
        st["trz"] = sT.inc()
        yield ("T", t_rz)

        def v_rz(e, b=b, st=st):
            e.wait_ge(sT.h, st["trz"])
            e.tensor_copy(rzB[:, :], pbank[0][:, 0:1]).then_inc(sV.h, 1)
        st["vrz"] = sV.inc()
        yield ("V", v_rz)

        for half in range(2):
            def t_ext(e, b=b, half=half, st=st):
                e.wait_ge(sV.h, st["vrz"] if half == 0 else st[("vext", 0)])
                last = None
                for kk in range(16):
                    last = e.matmul(pbank[half][:, 0:3],
                                    PT[:, 256 * kk + 128 * half:256 * kk + 128 * (half + 1)],
                                    VALS[:, 3 * kk:3 * (kk + 1)], start=(kk == 0), stop=(kk == 15))
                last.then_inc(sT.h, 1)
            st[("text", half)] = sT.inc()
            yield ("T", t_ext)

            def v_ext(e, b=b, half=half, st=st):
                e.wait_ge(sT.h, st[("text", half)])
                stg = b * 2 + half
                o = stg * 4
                e.tensor_copy(slotv[:, o:o + 3], pbank[half][:, 0:3])
                e.tensor_copy(vspc[:, 0:1], vspc[:, 1:2])
                e.tensor_tensor(slotv[:, o + 3:o + 4], slotv[:, o + 2:o + 3], rzB[:, :], ALU.mult)
                e.tensor_copy(sigI[:, stg:stg + 1], slotv[:, o:o + 1])
                e.tensor_copy(tI[:, stg:stg + 1], slotv[:, o + 1:o + 2]).then_inc(sV.h, 1)
            st[("vext", half)] = sV.inc()
            yield ("V", v_ext)

        router_state[b] = st

    emit_load(0)
    emit_load(1)
    emit_cast(0)
    emit_cast(1)

    n_units_per_pair = 16
    r0gen = [None]

    def splice(n):
        if r0gen[0] is None:
            return
        for _ in range(n):
            try:
                t_, f_ = next(r0gen[0])
            except StopIteration:
                r0gen[0] = None
                return
            OPS[t_].append(f_)

    DBGC = [0]
    for gi, (b, s) in enumerate(pair_list):
        for u in range(n_units_per_pair):
            emit_attn_unit(gi * n_units_per_pair + u)
            if b == 1:
                splice(2)
        if pend_tpv[0] is not None:
            fnp, pu = pend_tpv[0]
            T(fnp)
            uT_pv[pu] = sT.inc()
            pend_tpv[0] = None
        if pend_vz[0] is not None:
            fn, pu = pend_vz[0]
            import functools
            V(functools.partial(fn, flushed=True))
            uV_z[pu] = sV.inc()
            pend_vz[0] = None
        for li, lu in enumerate(ln_units):
            if lu[0] == gi:
                emit_ln_unit(li)
        if gi == 0 and _os.environ.get("KDBG"):
            def g_dbg(e, li_p0=3):
                e.wait_ge(sS.h, ls_apply[li_p0])
                e.dma_start(out=dbg_h[:, 0:2048], in_=xz[:, 0:2048]).then_inc(sC.h, 16)
                e.dma_start(out=dbg_h[:, 2048:4096], in_=xbuf[:, 0:2048]).then_inc(sC.h, 16)
                e.dma_start(out=dbg_h[:, 4096:6144], in_=x1st[:, 0:2048]).then_inc(sC.h, 16)
                e.dma_start(out=dbg_h[0:128, 6144:6176], in_=lnmv[:, :]).then_inc(sC.h, 16)
            G(g_dbg)
            DBGC[0] = sC.inc(16 * 4)
        if gi + 2 < len(pair_list):
            emit_load(gi + 2)
            emit_cast(gi + 2)
        if b == 0 and s == NSEG - 1:
            r0gen[0] = build_router_ops(0, None)

    splice(9999)

    for t_, f_ in build_router_ops(1, router_state[0][("text", 1)]):
        OPS[t_].append(f_)


    esA.close()
    esB = ExitStack()
    g1T = esB.enter_context(nc.sbuf_tensor("g1T", [128, 64 * 512], BF16))
    w1cb = esB.enter_context(nc.sbuf_tensor("w1cb", [128, 2 * 4096], BF16))
    h2sb = esB.enter_context(nc.sbuf_tensor("h2sb", [128, 2 * 512], BF16))
    w2cb = esB.enter_context(nc.sbuf_tensor("w2cb", [128, 2 * 8192], BF16))
    ybuf = esB.enter_context(nc.sbuf_tensor("ybuf", [128, 4 * D], F32))
    fxb = esB.enter_context(nc.sbuf_tensor("fxb", [128, 4 * D], BF16))
    fxT = esB.enter_context(nc.sbuf_tensor("fxT", [128, 16 * 512], BF16))
    outro = esB.enter_context(nc.sbuf_tensor("outro", [128, 2 * D], F32))

    gath_done = {}
    for b in range(NB):
        for half in range(2):
            stg = b * 2 + half

            def g_gather(e, b=b, half=half, stg=stg):
                if stg == 0:
                    tS, tT = seg_consumed[(1, NSEG - 1)]
                    e.wait_ge(sS.h, tS)
                    e.wait_ge(sT.h, tT)
                    e.wait_ge(sV.h, router_state[1][("vext", 1)])
                    for mm in range(4):
                        e.wait_ge(sM4[mm].h, sM4[mm].n)
                else:
                    e.wait_ge(sGG.h, gath_done[stg - 1])
                e.wait_ge(sV.h, router_state[b][("vext", half)])
                e.indirect_dma_start(fxb[:, stg * D:(stg + 1) * D], None,
                                     x1m_h.rearrange("b s d -> (b s) d"),
                                     IndirectOffsetOnAxis(sigI[:, stg:stg + 1], 0),
                                     element_offset=b * S * D).then_inc(sGG.h, 16)
            G(g_gather)
            gath_done[stg] = sGG.inc(16)

    fxT_done = {}
    tp_evac = {}
    for stg in range(4):
        for t in range(16):
            idx = stg * 16 + t

            def t_fxT(e, stg=stg, t=t, idx=idx):
                e.wait_ge(sGG.h, gath_done[stg])
                if idx == 0:
                    e.wait_ge(sV.h, uV_gT[len(attn_units) - 1])
                if idx >= 2:
                    e.wait_ge(sV.h, tp_evac[idx - 2])
                e.transpose(pbank_bf[idx % 2][:, 0:128], fxb[:, stg * D + 128 * t:stg * D + 128 * (t + 1)],
                            IDT[:, :]).then_inc(sT.h, 1)
            T(t_fxT)
            fxT_done[idx] = sT.inc()

            def v_fxT(e, stg=stg, t=t, idx=idx):
                e.wait_ge(sT.h, fxT_done[idx])
                e.tensor_copy(fxT[:, 512 * t + 128 * stg:512 * t + 128 * (stg + 1)],
                              pbank_bf[idx % 2][:, 0:128]).then_inc(sV.h, 1)
            V(v_fxT)
            tp_evac[idx] = sV.inc()
    FXT_ALL = sV.n

    w1_done = {}
    h1_done = {}
    g1_done = {}
    for fb in range(32):
        def g_w1(e, fb=fb):
            if fb >= 2:
                e.wait_ge(sT.h, h1_done[2 * (fb - 2) + 1])
            e.dma_start(out=w1cb[:, (fb % 2) * 4096:(fb % 2 + 1) * 4096],
                        in_=w1_h[:, 256 * fb:256 * (fb + 1)].rearrange("(t p) f -> p t f", p=128)).then_inc(sW1[fb % 2].h, 16)
        G(g_w1)
        w1_done[fb] = sW1[fb % 2].inc(16)

    for f in range(64):
        fb = f // 2
        fo = f % 2

        def t_h1(e, f=f, fb=fb, fo=fo):
            if fo == 0:
                e.wait_ge(sW1[fb % 2].h, w1_done[fb])
            if f == 0:
                e.wait_ge(sV.h, FXT_ALL)
            if f >= 4:
                e.wait_ge(sS.h, g1_done[f - 4])
            bk = f % 4
            base = (fb % 2) * 4096 + fo * 128
            last = None
            for t in range(16):
                last = e.matmul(pbank[bk][:, 0:512],
                                w1cb[:, base + 256 * t:base + 256 * t + 128],
                                fxT[:, 512 * t:512 * (t + 1)], start=(t == 0), stop=(t == 15))
            last.then_inc(sT.h, 1)
        T(t_h1)
        h1_done[f] = sT.inc()

        def s_g1(e, f=f):
            e.wait_ge(sT.h, h1_done[f])
            bk = f % 4
            e.activation(out=g1T[:, 512 * f:512 * (f + 1)], in_=pbank[bk][:, 0:512], func=AF.Gelu,
                         bias=b1T[:, f:f + 1]).then_inc(sS.h, 1)
        SC(s_g1)
        g1_done[f] = sS.inc()

    w2_done = {}
    h2_done = {}
    h2s_done = {}
    h2t_done = {}
    yev_done = {}
    for c in range(16):
        def g_w2(e, c=c):
            if c == 0:
                tS, tT = seg_consumed[(1, NSEG - 1)]
                e.wait_ge(sS.h, tS)
                e.wait_ge(sT.h, tT)
            if c >= 2:
                e.wait_ge(sT.h, h2_done[c - 2])
            e.dma_start(out=w2cb[:, (c % 2) * 8192:(c % 2 + 1) * 8192],
                        in_=w2_h[:, 128 * c:128 * (c + 1)].rearrange("(t p) f -> p t f", p=128)).then_inc(sW2[c % 2].h, 16)
        G(g_w2)
        w2_done[c] = sW2[c % 2].inc(16)

    for c in range(16):
        def t_h2(e, c=c):
            e.wait_ge(sW2[c % 2].h, w2_done[c])
            if c == 0:
                e.wait_ge(sS.h, g1_done[63])
            if c >= 2:
                e.wait_ge(sS.h, h2s_done[c - 2])
            bk = 4 + c % 2
            last = None
            for f in range(64):
                last = e.matmul(pbank[bk][:, 0:512],
                                w2cb[:, (c % 2) * 8192 + 128 * f:(c % 2) * 8192 + 128 * (f + 1)],
                                g1T[:, 512 * f:512 * (f + 1)], start=(f == 0), stop=(f == 63))
            last.then_inc(sT.h, 1)
        T(t_h2)
        h2_done[c] = sT.inc()

        def s_h2(e, c=c):
            e.wait_ge(sT.h, h2_done[c])
            if c >= 2:
                e.wait_ge(sT.h, h2t_done[(c - 2, 3)])
            bk = 4 + c % 2
            e.activation(out=h2sb[:, (c % 2) * 512:(c % 2 + 1) * 512], in_=pbank[bk][:, 0:512],
                         func=AF.Identity, bias=b2T[:, c:c + 1]).then_inc(sS.h, 1)
        SC(s_h2)
        h2s_done[c] = sS.inc()

        for stg in range(4):
            idx = c * 4 + stg

            def t_h2t(e, c=c, stg=stg, idx=idx):
                e.wait_ge(sS.h, h2s_done[c])
                if idx >= 2:
                    e.wait_ge(sV.h, yev_done[idx - 2])
                e.transpose(pbank_bf[idx % 2][:, 0:128],
                            h2sb[:, (c % 2) * 512 + 128 * stg:(c % 2) * 512 + 128 * (stg + 1)],
                            IDT[:, :]).then_inc(sT.h, 1)
            T(t_h2t)
            h2t_done[(c, stg)] = sT.inc()

            def v_yev(e, c=c, stg=stg, idx=idx):
                e.wait_ge(sT.h, h2t_done[(c, stg)])
                e.tensor_scalar(ybuf[:, stg * D + 128 * c:stg * D + 128 * (c + 1)],
                                pbank_bf[idx % 2][:, 0:128],
                                slotv[:, stg * 4 + 3:stg * 4 + 4], None, ALU.mult).then_inc(sV.h, 1)
            V(v_yev)
            yev_done[idx] = sV.inc()

    fin_scat = {}
    fin_conv = {}
    for stg in range(4):
        b = stg // 2

        def s_conv(e, stg=stg):
            e.wait_ge(sGG.h, gath_done[stg])
            if stg >= 2:
                e.wait_ge(sGG.h, fin_scat[stg - 2])
            e.activation(out=outro[:, (stg % 2) * D:(stg % 2 + 1) * D],
                         in_=fxb[:, stg * D:(stg + 1) * D], func=AF.Identity,
                         scale=2.0).then_inc(sS.h, 1)
        SC(s_conv)
        fin_conv[stg] = sS.inc()

        def v_fin2(e, stg=stg):
            e.wait_ge(sV.h, yev_done[15 * 4 + stg])
            e.wait_ge(sS.h, fin_conv[stg])
            yt = ybuf[:, stg * D:(stg + 1) * D]
            e.tensor_tensor(yt[:, :], yt[:, :], outro[:, (stg % 2) * D:(stg % 2 + 1) * D], ALU.add)
            o = stg * 8
            e.drain()
            for i in range(4):
                e.bn_stats(lnstats[:, stg * 24 + 6 * i:stg * 24 + 6 * (i + 1)], yt[:, 512 * i:512 * (i + 1)])
            e.drain()
            e.bn_aggr(lnmv[:, o:o + 2], lnstats[:, stg * 24:stg * 24 + 24])
            e.drain()
            e.tensor_scalar(lnmv[:, o + 1:o + 2], lnmv[:, o + 1:o + 2], EPS, None, ALU.add)
            e.drain()
            e.reciprocal(lnmv[:, o + 1:o + 2], lnmv[:, o + 1:o + 2]).then_inc(sV.h, 1)
        V(v_fin2)
        fin_stats = sV.inc()

        def s_fin2(e, stg=stg, fin_stats=fin_stats):
            e.wait_ge(sV.h, fin_stats)
            o = stg * 8
            e.activation(out=lnmv[:, o + 1:o + 2], in_=lnmv[:, o + 1:o + 2], func=AF.Ln)
            e.activation(out=lnmv[:, o + 1:o + 2], in_=lnmv[:, o + 1:o + 2], func=AF.Exp, scale=0.5).then_inc(sS.h, 1)
        SC(s_fin2)
        fin_inv = sS.inc()

        def v_fin3(e, stg=stg, fin_inv=fin_inv):
            e.wait_ge(sS.h, fin_inv)
            o = stg * 8
            e.scalar_tensor_tensor(lnmv[:, o + 3:o + 4], lnmv[:, o + 0:o + 1], -1.0,
                                   lnmv[:, o + 1:o + 2], ALU.mult, ALU.mult).then_inc(sV.h, 1)
        V(v_fin3)
        fin_b = sV.inc()

        def s_fin3(e, stg=stg, fin_b=fin_b):
            e.wait_ge(sV.h, fin_b)
            o = stg * 8
            e.activation(out=outro[:, (stg % 2) * D:(stg % 2 + 1) * D], in_=ybuf[:, stg * D:(stg + 1) * D],
                         func=AF.Identity, bias=lnmv[:, o + 3:o + 4],
                         scale=lnmv[:, o + 1:o + 2]).then_inc(sS.h, 1)
        SC(s_fin3)
        fin_app = sS.inc()

        def g_scat(e, stg=stg, b=b, fin_app=fin_app, prev_gg=sGG.n):
            e.wait_ge(sS.h, fin_app)
            e.wait_ge(sGG.h, prev_gg)
            if stg == 0:
                for ee in range(2):
                    e.wait_ge(sE2[ee].h, sE2[ee].n)
            e.indirect_dma_start(out_h.rearrange("b s d -> (b s) d"),
                                 IndirectOffsetOnAxis(tI[:, stg:stg + 1], 0),
                                 outro[:, (stg % 2) * D:(stg % 2 + 1) * D], None,
                                 element_offset=b * S * D).then_inc(sGG.h, 16)
        G(g_scat)
        fin_scat[stg] = sGG.inc(16)

    FINAL_GG = sGG.n
    FINAL_E = [s.n for s in sE2]

    def g_end(e):
        e.wait_ge(sGG.h, FINAL_GG)
        for i in range(2):
            e.wait_ge(sE2[i].h, FINAL_E[i])
    G(g_end)

    def y_end(e):
        e.wait_ge(sGG.h, FINAL_GG)
        for i in range(2):
            e.wait_ge(sE2[i].h, FINAL_E[i])
    Y(y_end)

    import os
    _tr = float(os.environ.get("KTRUNC", "1.0"))
    if _tr < 1.0:
        for _k in OPS:
            OPS[_k] = OPS[_k][:max(1, int(len(OPS[_k]) * _tr))]
    with nc.Block() as block:
        @block.tensor
        def _(eng):
            for fn in OPS["T"]:
                fn(eng)

        @block.vector
        def _(eng):
            for fn in OPS["V"]:
                fn(eng)

        @block.scalar
        def _(eng):
            for fn in OPS["S"]:
                fn(eng)

        @block.gpsimd
        def _(eng):
            for fn in OPS["G"]:
                fn(eng)

        @block.sync
        def _(eng):
            for fn in OPS["Y"]:
                fn(eng)

    esB.close()
    es.close()
    return nc


def _consts():
    c = {}
    c["IDT"] = np.eye(128, dtype=np.float32).astype(ml_dtypes.bfloat16)
    c["IDTF"] = np.eye(128, dtype=np.float32)
    c["TRI"] = np.triu(np.ones((128, 128), np.float32), 1).astype(ml_dtypes.bfloat16)
    c["TRI16"] = np.triu(np.ones((16, 16), np.float32), 1).astype(ml_dtypes.bfloat16)
    c["ONESC"] = np.ones((128, 128), np.float32).astype(ml_dtypes.bfloat16)
    c["ONES1F"] = np.ones((1, 128), np.float32)
    c["ONESCF"] = np.ones((128, 1), np.float32)
    c["IO256"] = np.broadcast_to(np.arange(256, dtype=np.float32)[None, :], (128, 256)).copy()
    q = np.arange(128)[:, None]
    k = np.arange(16)[None, :]
    c["SG"] = (128 * k + q).astype(np.float16)
    s_ = k // 4
    par_ = (k % 4) // 2
    tl_ = k % 2
    c["TG"] = (512 * s_ + 256 * tl_ + 2 * q + par_).astype(np.float16)
    return c


_NC_CACHE = {}


def kernel(L0_attn_x, L0_ffn_Wr, L0_ffn_br, L0_ffn_W1, L0_ffn_b1, L0_ffn_W2, L0_ffn_b2):
    x = np.ascontiguousarray(np.asarray(L0_attn_x, dtype=np.float32))
    Wr = np.ascontiguousarray(np.asarray(L0_ffn_Wr, dtype=np.float32).reshape(1, D))
    br = np.ascontiguousarray(np.asarray(L0_ffn_br, dtype=np.float32).reshape(1, 1))
    W1 = np.ascontiguousarray(np.asarray(L0_ffn_W1, dtype=np.float32))
    b1 = np.ascontiguousarray(np.asarray(L0_ffn_b1, dtype=np.float32).reshape(64, 128).T)
    W2 = np.ascontiguousarray(np.asarray(L0_ffn_W2, dtype=np.float32))
    b2 = np.ascontiguousarray(np.asarray(L0_ffn_b2, dtype=np.float32).reshape(16, 128).T)
    consts = _consts()

    if "nc" not in _NC_CACHE:
        _NC_CACHE["nc"] = build_nc()
    nc = _NC_CACHE["nc"]

    in_maps = []
    for c in range(8):
        m = {"x": np.ascontiguousarray(x[2 * c:2 * c + 2]),
             "Wr": Wr, "br": br, "W1": W1, "b1": b1, "W2": W2, "b2": b2}
        m.update(consts)
        in_maps.append(m)
    res = run_bass_kernel_spmd(nc, in_maps, core_ids=list(range(8)))
    out = np.concatenate([res.results[i]["out"] for i in range(8)], axis=0)
    return out.astype(np.float32)


# version 09
# speedup vs baseline: 1.1244x; 1.0068x over previous
import os as _os
import numpy as np
import ml_dtypes
import concourse.bass as bass
import concourse.mybir as mybir
from concourse.bass_utils import run_bass_kernel_spmd
from concourse.bass_core_rust import IndirectOffsetOnAxis

F32 = mybir.dt.float32
BF16 = mybir.dt.bfloat16
FP16 = mybir.dt.float16
I32 = mybir.dt.int32
AF = mybir.ActivationFunctionType
ALU = mybir.AluOpType
AX = mybir.AxisListType

NB = 2
NSEG = 4
D = 2048
S = 2048
DFF = 8192
NSLOT = 256
SCALE = float(1.0 / np.sqrt(128.0))
EPS = 1e-5
NBIS = 33
WOFF = 16.0


class Sem:
    def __init__(self, es, nc, name):
        self.h = es.enter_context(nc.semaphore(name))
        self.n = 0

    def inc(self, k=1):
        self.n += k
        return self.n


def build_nc():
    from contextlib import ExitStack
    nc = bass.Bass("TRN2", target_bir_lowering=False, detect_race_conditions=False)
    es = ExitStack()

    x_h = nc.dram_tensor("x", [NB, S, D], F32, kind="ExternalInput").ap()
    wr_h = nc.dram_tensor("Wr", [1, D], F32, kind="ExternalInput").ap()
    br_h = nc.dram_tensor("br", [1, 1], F32, kind="ExternalInput").ap()
    w1_h = nc.dram_tensor("W1", [D, DFF], F32, kind="ExternalInput").ap()
    b1_h = nc.dram_tensor("b1", [128, 64], F32, kind="ExternalInput").ap()
    w2_h = nc.dram_tensor("W2", [DFF, D], F32, kind="ExternalInput").ap()
    b2_h = nc.dram_tensor("b2", [128, 16], F32, kind="ExternalInput").ap()
    idt_h = nc.dram_tensor("IDT", [128, 128], BF16, kind="ExternalInput").ap()
    idtf_h = nc.dram_tensor("IDTF", [128, 128], F32, kind="ExternalInput").ap()
    tri_h = nc.dram_tensor("TRI", [128, 128], BF16, kind="ExternalInput").ap()
    tri16_h = nc.dram_tensor("TRI16", [16, 16], BF16, kind="ExternalInput").ap()
    onesc_h = nc.dram_tensor("ONESC", [128, 128], BF16, kind="ExternalInput").ap()
    ones1f_h = nc.dram_tensor("ONES1F", [1, 128], F32, kind="ExternalInput").ap()
    onescf_h = nc.dram_tensor("ONESCF", [128, 1], F32, kind="ExternalInput").ap()
    io256_h = nc.dram_tensor("IO256", [128, 256], F32, kind="ExternalInput").ap()
    lnb_h = nc.dram_tensor("LNB", [128, 2], F32, kind="ExternalInput").ap()
    sg_h = nc.dram_tensor("SG", [128, 16], FP16, kind="ExternalInput").ap()
    tg_h = nc.dram_tensor("TG", [128, 16], FP16, kind="ExternalInput").ap()
    out_h = nc.dram_tensor("out", [NB, S, D], F32, kind="ExternalOutput").ap()
    dbg_h = nc.dram_tensor("dbg", [128, 6176], F32, kind="ExternalOutput").ap()
    x1m_h = nc.dram_tensor("x1m", [NB, S, D], BF16, kind="ExternalOutput" if _os.environ.get("KDBG") else "Internal").ap()

    def xview(h, b, s, par, tl):
        return h[b].rearrange("(s tl p two) d -> s two tl p d", s=4, tl=2, p=128, two=2)[s, par, tl]

    sb = lambda name, shape, dt: es.enter_context(nc.sbuf_tensor(name, shape, dt))
    IDT = sb("c_idt", [128, 128], BF16)
    IDTF = sb("c_idtf", [128, 128], F32)
    TRI = sb("c_tri", [128, 128], BF16)
    TRI16 = sb("c_tri16", [16, 16], BF16)
    ONESC = sb("c_onesc", [128, 128], BF16)
    ONES1F = sb("c_ones1f", [1, 128], F32)
    ONESCF = sb("c_onescf", [128, 1], F32)
    IO256 = sb("c_io256", [128, 256], F32)
    LNB = sb("c_lnb", [128, 2], F32)
    SG = sb("c_sg", [128, 16], FP16)
    TG = sb("c_tg", [128, 16], FP16)
    brS = sb("c_brs", [1, 1], F32)
    brB = sb("c_brb", [128, 1], F32)
    b1T = sb("c_b1t", [128, 64], F32)
    b2T = sb("c_b2t", [128, 16], F32)

    lnstats = sb("lnstats", [128, 4 * 24], F32)
    lnmv = sb("lnmv", [128, 4 * 8], F32)
    lo_b = sb("lo", [128, 2], F32)
    hi_b = sb("hi", [128, 2], F32)
    mid_b = sb("mid", [128, 2], F32)
    tmp16 = sb("tmp16", [128, 16], BF16)
    prt_b = sb("prt", [128, 2], BF16)
    cge_b = sb("cge", [128, 2], F32)
    dlt_b = sb("dlt", [128, 2], F32)
    dlt2_b = sb("dlt2", [128, 2], F32)
    L2g = sb("L2g", [128, 16], F32)
    vspc = sb("vspc", [128, 2], F32)
    mask_b = sb("mask", [128, 2 * 16], F32)
    maskbf = sb("maskbf", [128, 16], BF16)
    ct16 = sb("ct16", [16, 1], BF16)
    cb16 = sb("cb16", [1, 16], BF16)
    pos_b = sb("pos", [128, 16], F32)
    lbr_b = sb("lbr", [128, 16], F32)
    wexp = sb("wexp", [128, 16], F32)
    wm32 = sb("wm32", [128, 16], F32)
    wprt = sb("wprt", [128, 1], F32)
    bndc = sb("bndc", [128, 2], F32)
    bnds = sb("bnds", [1, 2], F32)
    bndB = sb("bndB", [128, 2], F32)
    VALS = sb("VALS", [128, 16 * 3], FP16)
    zt_s = sb("zt", [1, 2], F32)
    rzB = sb("rzB", [128, 1], F32)
    slotv = sb("slotv", [128, 4 * 4], F32)
    sigI = sb("sigI", [128, 4], I32)
    tI = sb("tI", [128, 4], I32)

    pbank = [es.enter_context(nc.psum_tensor(f"pb{i}", [128, 512], F32)) for i in range(6)]
    pbank_bf = [es.enter_context(nc.psum_tensor(f"pbb{i}", [128, 512], BF16)) for i in range(2)]

    sT = Sem(es, nc, "sT")
    sV = Sem(es, nc, "sV")
    sS = Sem(es, nc, "sS")
    sC = Sem(es, nc, "sC")
    sWrt = Sem(es, nc, "sWrt")
    sX = [Sem(es, nc, f"sX{i}") for i in range(2)]
    sXB = [Sem(es, nc, f"sXB{i}") for i in range(2)]
    sE2 = [Sem(es, nc, f"sE2{i}") for i in range(2)]
    sM4 = [Sem(es, nc, f"sM4{i}") for i in range(4)]
    sW1 = [Sem(es, nc, f"sW1{i}") for i in range(3)]
    sW2 = [Sem(es, nc, f"sW2{i}") for i in range(2)]
    sGG = Sem(es, nc, "sGG")

    OPS = {"T": [], "V": [], "S": [], "G": [], "Y": []}
    T = lambda fn: OPS["T"].append(fn)
    V = lambda fn: OPS["V"].append(fn)
    SC = lambda fn: OPS["S"].append(fn)
    G = lambda fn: OPS["G"].append(fn)
    Y = lambda fn: OPS["Y"].append(fn)

    def setup_loads(e):
        for dst, src in ((IDT, idt_h), (IDTF, idtf_h), (TRI, tri_h), (TRI16, tri16_h), (ONESC, onesc_h),
                         (ONES1F, ones1f_h), (ONESCF, onescf_h), (IO256, io256_h),
                         (LNB, lnb_h), (SG, sg_h), (TG, tg_h), (brS, br_h)):
            e.dma_start(out=dst[:, :], in_=src[:, :]).then_inc(sC.h, 16)
        e.dma_start(out=b1T[:, :], in_=b1_h[:, :]).then_inc(sC.h, 16)
        e.dma_start(out=b2T[:, :], in_=b2_h[:, :]).then_inc(sC.h, 16)
    Y(setup_loads)
    LD_CONST = sC.inc(16 * 14)

    def setup_wrt(e):
        e.dma_start(out=WrT[:, :], in_=wr_h[:, :]).then_inc(sWrt.h, 16)
    G(setup_wrt)
    LD_WRT = sWrt.inc(16)

    def setup_bcast_t(e):
        e.wait_ge(sC.h, LD_CONST)
        e.wait_ge(sWrt.h, LD_WRT)
        for i in range(4):
            e.matmul(pbank[i][:, :], ONESC[0:1, :], WrT[:, 512 * i:512 * (i + 1)], start=True, stop=True)
        e.matmul(pbank[4][:, 0:1], ONES1F[:, :], brS[:, :], start=True, stop=True).then_inc(sT.h, 1)
    T(setup_bcast_t)
    T_BCAST = sT.inc()

    def setup_bcast_v(e):
        e.memset(vspc[:, :], 0.0)
        e.wait_ge(sT.h, T_BCAST)
        for i in range(4):
            e.tensor_copy(WrB[:, 512 * i:512 * (i + 1)], pbank[i][:, :])
        e.tensor_copy(brB[:, :], pbank[4][:, 0:1]).then_inc(sV.h, 1)
    V(setup_bcast_v)
    V_BCAST = sV.inc()

    esA = ExitStack()
    WrT = esA.enter_context(nc.sbuf_tensor("c_wrt", [1, D], BF16))
    WrB = esA.enter_context(nc.sbuf_tensor("c_wrb", [128, D], BF16))
    xz = esA.enter_context(nc.sbuf_tensor("xz", [128, 2 * 4 * D], F32))
    xbuf = esA.enter_context(nc.sbuf_tensor("xbw", [128, 2 * 4 * D], BF16))
    x1st = esA.enter_context(nc.sbuf_tensor("x1stage", [128, 4 * D], BF16))
    sttscr = esA.enter_context(nc.sbuf_tensor("sttscr", [128, D], BF16))
    Lgrid = esA.enter_context(nc.sbuf_tensor("Lgrid", [128, 2 * 16], F32))
    outear = esA.enter_context(nc.sbuf_tensor("outear", [128, 2 * D], F32))
    PT = esA.enter_context(nc.sbuf_tensor("PT", [128, 16 * 256], FP16))
    gT = esA.enter_context(nc.sbuf_tensor("gT", [128, 2 * 256], BF16))
    Ebuf = esA.enter_context(nc.sbuf_tensor("Ebuf", [128, 2 * 512], BF16))
    Dbuf = esA.enter_context(nc.sbuf_tensor("Dbuf", [128, 2 * 2], F32))
    Rbuf = esA.enter_context(nc.sbuf_tensor("Rbuf", [128, 2 * 2], F32))

    def xz_t(sbuf_i, par, tl):
        off = (sbuf_i * 4 + par * 2 + tl) * D
        return xz[:, off:off + D]

    def xb_t(sbuf_i, par, tl):
        off = (sbuf_i * 4 + par * 2 + tl) * D
        return xbuf[:, off:off + D]

    pair_list = [(b, s) for b in range(NB) for s in range(NSEG)]
    xload_done = {}
    xcast_done = {}
    seg_consumed = {}

    uT_tp, uT_sc, uT_pv = {}, {}, {}
    uT_sc1 = {}
    uS_E = {}
    uV_gT, uV_R, uV_z = {}, {}, {}
    ls_apply, lv_logit, lm_mirror, ly_early = {}, {}, {}, {}
    lv_stats, ls_inv, lv_bias = {}, {}, {}

    pend_vz = [None]
    pend_tpv = [None]
    attn_units = []
    for b, s in pair_list:
        for par in range(2):
            for hh in range(8):
                attn_units.append((b, s, par, hh))
    last_unit_of_pair = {}
    for ui, u in enumerate(attn_units):
        last_unit_of_pair[(u[0], u[1], u[2])] = ui

    ln_units = []
    for gi, (b, s) in enumerate(pair_list):
        for par in range(2):
            for tl in range(2):
                ln_units.append((gi, b, s, par, tl))
    last_li_of_b = {b: max(i for i, u in enumerate(ln_units) if u[1] == b) for b in range(NB)}

    def emit_load(gi):
        b, s = pair_list[gi]
        sbuf_i = gi % 2

        def loadseg(e, b=b, s=s, sbuf_i=sbuf_i, gi=gi):
            if gi == 2 and _os.environ.get("KDBG"):
                e.wait_ge(sC.h, DBGC[0])
            if gi >= 2:
                tS, tT = seg_consumed[pair_list[gi - 2]]
                e.wait_ge(sS.h, tS)
                e.wait_ge(sT.h, tT)
            for par in range(2):
                for tl in range(2):
                    e.dma_start(out=xz_t(sbuf_i, par, tl), in_=xview(x_h, b, s, par, tl)).then_inc(sX[gi % 2].h, 16)
        Y(loadseg)
        xload_done[(b, s)] = (gi % 2, sX[gi % 2].inc(16 * 4))

    def emit_cast(gi):
        b, s = pair_list[gi]
        sbuf_i = gi % 2

        def castseg(e, b=b, s=s, sbuf_i=sbuf_i, gi=gi):
            if gi == 2 and _os.environ.get("KDBG"):
                e.wait_ge(sC.h, DBGC[0])
            xi, xv = xload_done[(b, s)]
            e.wait_ge(sX[xi].h, xv)
            if gi >= 2:
                tS, tT = seg_consumed[pair_list[gi - 2]]
                e.wait_ge(sS.h, tS)
                e.wait_ge(sT.h, tT)
            for par in range(2):
                for tl in range(2):
                    e.dma_start(out=xb_t(sbuf_i, par, tl), in_=xz_t(sbuf_i, par, tl)).then_inc(sXB[gi % 2].h, 16)
        G(castseg)
        xcast_done[(b, s)] = (gi % 2, sXB[gi % 2].inc(16 * 4))

    def emit_attn_unit(ui):
        b, s, par, hh = attn_units[ui]
        gi = pair_list.index((b, s))
        sbuf_i = gi % 2
        slot = ui % 2
        col = (2 * hh + par) * 128

        def t_tp(e, b=b, s=s, par=par, ui=ui, sbuf_i=sbuf_i, col=col):
            if ui == 0:
                e.wait_ge(sV.h, V_BCAST)
            ci, cv = xcast_done[(b, s)]
            e.wait_ge(sXB[ci].h, cv)
            if _os.environ.get("KSERIAL") and ui >= 1:
                e.wait_ge(sV.h, uV_z[ui - 1])
            if _os.environ.get("KSERIAL") and ui % 16 == 0 and ui >= 16:
                li_prev = max(i for i, lu in enumerate(ln_units) if lu[0] == (ui // 16) - 1)
                e.wait_ge(sV.h, lv_logit[li_prev])
            if ui >= 1:
                e.wait_ge(sV.h, uV_gT[ui - 1])
            e.transpose(pbank_bf[0][:, 0:128], xb_t(sbuf_i, par, 0)[:, col:col + 128], IDT[:, :])
            e.transpose(pbank_bf[1][:, 0:128], xb_t(sbuf_i, par, 1)[:, col:col + 128], IDT[:, :]).then_inc(sT.h, 1)
        T(t_tp)
        uT_tp[ui] = sT.inc()

        if pend_tpv[0] is not None:
            fnp, pu = pend_tpv[0]
            T(fnp)
            uT_pv[pu] = sT.inc()
            pend_tpv[0] = None

        def v_gT(e, ui=ui, slot=slot):
            e.wait_ge(sT.h, uT_tp[ui])
            if ui >= 2:
                e.wait_ge(sT.h, uT_sc[ui - 2])
            go = slot * 256
            e.tensor_copy(gT[:, go:go + 128], pbank_bf[0][:, 0:128])
            e.tensor_copy(gT[:, go + 128:go + 256], pbank_bf[1][:, 0:128]).then_inc(sV.h, 1)
        V(v_gT)
        uV_gT[ui] = sV.inc()

        if pend_vz[0] is not None:
            fn, pu = pend_vz[0]
            V(fn)
            uV_z[pu] = sV.inc()
            pend_vz[0] = None

        def t_sc(e, ui=ui, slot=slot):
            e.wait_ge(sV.h, uV_gT[ui])
            if ui >= 1:
                e.wait_ge(sS.h, uS_E[ui - 1] - 1)
            go = slot * 256
            e.matmul(pbank[2][:, 0:256], gT[:, go:go + 128], gT[:, go:go + 256], start=True, stop=True).then_inc(sT.h, 1)
            if ui >= 1:
                e.wait_ge(sS.h, uS_E[ui - 1])
            e.matmul(pbank[3][:, 0:256], gT[:, go + 128:go + 256], gT[:, go:go + 256], start=True, stop=True).then_inc(sT.h, 1)
        T(t_sc)
        uT_sc1[ui] = sT.inc()
        uT_sc[ui] = sT.inc()

        def s_E(e, ui=ui, slot=slot):
            e.wait_ge(sT.h, uT_sc1[ui])
            if ui >= 2:
                e.wait_ge(sV.h, uV_R[ui - 2])
            eo = slot * 512
            do = slot * 2
            e.activation(out=Ebuf[:, eo:eo + 256], in_=pbank[2][:, 0:256], func=AF.Exp,
                         scale=SCALE, accum_out=Dbuf[:, do:do + 1]).then_inc(sS.h, 1)
            e.wait_ge(sT.h, uT_sc[ui])
            e.activation(out=Ebuf[:, eo + 256:eo + 512], in_=pbank[3][:, 0:256], func=AF.Exp,
                         scale=SCALE, accum_out=Dbuf[:, do + 1:do + 2]).then_inc(sS.h, 1)
        SC(s_E)
        sS.inc()
        uS_E[ui] = sS.inc()

        def v_R(e, ui=ui, slot=slot):
            e.wait_ge(sS.h, uS_E[ui])
            do = slot * 2
            e.reciprocal(Rbuf[:, do:do + 2], Dbuf[:, do:do + 2]).then_inc(sV.h, 1)
        V(v_R)
        uV_R[ui] = sV.inc()

        def t_pv(e, ui=ui, slot=slot, sbuf_i=sbuf_i, par=par, col=col):
            e.wait_ge(sS.h, uS_E[ui])
            if ui >= 1:
                e.wait_ge(sV.h, uV_z[ui - 1])
            eo = slot * 512
            last = None
            for j in range(2):
                for k in range(2):
                    last = e.matmul(pbank[4 + j][:, 0:128],
                                    Ebuf[:, eo + k * 256 + j * 128:eo + k * 256 + (j + 1) * 128],
                                    xb_t(sbuf_i, par, k)[:, col:col + 128],
                                    start=(k == 0), stop=(k == 1))
            last.then_inc(sT.h, 1)
        pend_tpv[0] = (t_pv, ui)

        def v_z(e, ui=ui, slot=slot, sbuf_i=sbuf_i, par=par, col=col, flushed=False):
            e.wait_ge(sT.h, uT_pv[ui])
            e.wait_ge(sV.h, uV_R[ui])
            if flushed:
                e.drain()
            do = slot * 2
            last = None
            for j in range(2):
                zt = xz_t(sbuf_i, par, j)
                last = e.scalar_tensor_tensor(zt[:, col:col + 128], pbank[4 + j][:, 0:128],
                                              Rbuf[:, do + j:do + j + 1], zt[:, col:col + 128],
                                              ALU.mult, ALU.add)
            last.then_inc(sV.h, 1)
        pend_vz[0] = (v_z, ui)

    def emit_ln_unit(li):
        gi, b, s, par, tl = ln_units[li]
        sbuf_i = gi % 2
        k = 4 * s + 2 * par + tl
        ring = k % 4
        ering = li % 2

        def v_stats(e, li=li, b=b, s=s, par=par, tl=tl, ring=ring, sbuf_i=sbuf_i):
            e.wait_ge(sV.h, uV_z[last_unit_of_pair[(b, s, par)]])
            if li >= 4:
                e.wait_ge(sS.h, ls_apply[li - 4])
            zt = xz_t(sbuf_i, par, tl)
            for i in range(4):
                e.bn_stats(lnstats[:, ring * 24 + 6 * i:ring * 24 + 6 * (i + 1)], zt[:, 512 * i:512 * (i + 1)])
            o = ring * 8
            e.drain()
            e.bn_aggr(lnmv[:, o:o + 2], lnstats[:, ring * 24:ring * 24 + 24]).then_inc(sV.h, 1)
        V(v_stats)
        lv_stats[li] = sV.inc()

        def s_inv2(e, li=li, ring=ring):
            e.wait_ge(sV.h, lv_stats[li])
            o = ring * 8
            e.activation(out=lnmv[:, o + 7:o + 8], in_=lnmv[:, o + 1:o + 2], func=AF.Ln, bias=LNB[:, 0:1])
            e.activation(out=lnmv[:, o + 2:o + 3], in_=lnmv[:, o + 7:o + 8], func=AF.Exp, scale=-0.5)
            e.activation(out=lnmv[:, o + 7:o + 8], in_=lnmv[:, o + 7:o + 8], func=AF.Exp, scale=-1.0)
            e.activation(out=lnmv[:, o + 4:o + 5], in_=lnmv[:, o + 7:o + 8], func=AF.Ln,
                         scale=-EPS, bias=LNB[:, 1:2])
            e.activation(out=lnmv[:, o + 4:o + 5], in_=lnmv[:, o + 4:o + 5], func=AF.Exp, scale=-0.5).then_inc(sS.h, 1)
        SC(s_inv2)
        ls_inv[li] = sS.inc()

        def v_small(e, li=li, ring=ring):
            e.wait_ge(sS.h, ls_inv[li])
            o = ring * 8
            e.scalar_tensor_tensor(lnmv[:, o + 3:o + 4], lnmv[:, o + 0:o + 1], -1.0,
                                   lnmv[:, o + 2:o + 3], ALU.mult, ALU.mult)
            e.tensor_tensor(lnmv[:, o + 5:o + 6], lnmv[:, o + 2:o + 3], lnmv[:, o + 4:o + 5], ALU.mult)
            e.tensor_tensor(lnmv[:, o + 6:o + 7], lnmv[:, o + 3:o + 4], lnmv[:, o + 4:o + 5], ALU.mult).then_inc(sV.h, 1)
        V(v_small)
        lv_sc2 = sV.inc()

        def s_apply(e, li=li, ring=ring, ering=ering, b=b, s=s, par=par, tl=tl, sbuf_i=sbuf_i, k=k, lv_sc2=lv_sc2):
            e.wait_ge(sV.h, lv_sc2)
            o = ring * 8
            zt = xz_t(sbuf_i, par, tl)
            if li == 4 and _os.environ.get("KDBG"):
                e.wait_ge(sC.h, DBGC[0])
            if li >= 4:
                e.wait_ge(sM4[li % 4].h, lm_mirror[li - 4])
                e.wait_ge(sV.h, lv_logit[li - 4])
            if li >= 2:
                e.wait_ge(sE2[li % 2].h, ly_early[li - 2])
            e.activation(out=x1st[:, ring * D:(ring + 1) * D], in_=zt[:, :], func=AF.Identity,
                         bias=lnmv[:, o + 3:o + 4], scale=lnmv[:, o + 2:o + 3])
            e.activation(out=outear[:, ering * D:(ering + 1) * D], in_=zt[:, :], func=AF.Identity,
                         bias=lnmv[:, o + 6:o + 7], scale=lnmv[:, o + 5:o + 6]).then_inc(sS.h, 1)
            e.wait_ge(sS.h, ls_apply[li])
            e.dma_start(out=x1m_h[b][128 * k:128 * (k + 1), :],
                        in_=x1st[:, ring * D:(ring + 1) * D]).then_inc(sM4[li % 4].h, 16)
        ls_apply[li] = sS.inc()
        SC(s_apply)
        lm_mirror[li] = sM4[li % 4].inc(16)

        def v_logit(e, li=li, ring=ring, b=b, k=k):
            e.wait_ge(sS.h, ls_apply[li])
            e.scalar_tensor_tensor(sttscr[:, :], x1st[:, ring * D:(ring + 1) * D], 1.0, WrB[:, :],
                                   ALU.mult, ALU.mult,
                                   accum_out=Lgrid[:, b * 16 + k:b * 16 + k + 1]).then_inc(sV.h, 1)
        V(v_logit)
        lv_logit[li] = sV.inc()

        def y_early(e, li=li, ering=ering, b=b, s=s, par=par, tl=tl):
            e.wait_ge(sS.h, ls_apply[li])
            e.dma_start(out=xview(out_h, b, s, par, tl), in_=outear[:, ering * D:(ering + 1) * D]).then_inc(sE2[li % 2].h, 16)
        Y(y_early)
        ly_early[li] = sE2[li % 2].inc(16)

        if par == 1 and tl == 1:
            seg_consumed[(b, s)] = (ls_apply[li], uT_pv[last_unit_of_pair[(b, s, 1)]])

    router_state = {}

    def build_router_ops(b, prev_ext_t):
        st = {}
        router_state[b] = st
        logits_ready = lv_logit[last_li_of_b[b]]

        def v_bnd1(e, b=b, logits_ready=logits_ready, prev_ext_t=prev_ext_t):
            e.wait_ge(sV.h, logits_ready)
            if prev_ext_t is not None:
                e.wait_ge(sT.h, prev_ext_t)
            e.reduce_max(bndc[:, 0:1], Lgrid[:, b * 16:(b + 1) * 16], axis=AX.X)
            e.tensor_scalar(lbr_b[:, :], Lgrid[:, b * 16:(b + 1) * 16], -1.0, None, ALU.mult)
            e.reduce_max(bndc[:, 1:2], lbr_b[:, :], axis=AX.X).then_inc(sV.h, 1)
        st["vbnd1"] = sV.inc()
        yield ("V", v_bnd1)

        def t_bnd(e, b=b, st=st):
            e.wait_ge(sV.h, st["vbnd1"])
            e.transpose(pbank[0][0:1, 0:128], bndc[:, 0:1], IDTF[:, :])
            e.transpose(pbank[1][0:1, 0:128], bndc[:, 1:2], IDTF[:, :]).then_inc(sT.h, 1)
        st["tbnd"] = sT.inc()
        yield ("T", t_bnd)

        def v_bnd2(e, b=b, st=st):
            e.wait_ge(sT.h, st["tbnd"])
            e.reduce_max(bnds[:, 0:1], pbank[0][0:1, 0:128], axis=AX.X)
            e.reduce_max(bnds[:, 1:2], pbank[1][0:1, 0:128], axis=AX.X).then_inc(sV.h, 1)
        st["vbnd2"] = sV.inc()
        yield ("V", v_bnd2)

        def t_bnd2(e, b=b, st=st):
            e.wait_ge(sV.h, st["vbnd2"])
            e.matmul(pbank[0][:, 0:2], ONES1F[:, :], bnds[:, :], start=True, stop=True).then_inc(sT.h, 1)
        st["tbnd2"] = sT.inc()
        yield ("T", t_bnd2)

        def v_binit(e, b=b, st=st):
            e.wait_ge(sT.h, st["tbnd2"])
            e.tensor_copy(bndB[:, :], pbank[0][:, 0:2])
            e.drain()
            e.tensor_scalar(hi_b[:, b:b + 1], bndB[:, 0:1], 2.0, None, ALU.mult)
            e.tensor_scalar(lo_b[:, b:b + 1], bndB[:, 1:2], -2.0, -2.0, ALU.mult, ALU.add)
            e.tensor_scalar(L2g[:, :], Lgrid[:, b * 16:(b + 1) * 16], 2.0, None, ALU.mult)
            e.tensor_tensor(mid_b[:, b:b + 1], lo_b[:, b:b + 1], hi_b[:, b:b + 1], ALU.add)
            e.drain()
            with nc.allow_low_precision("bisection counts <=16 are exact in bf16"):
                e.tensor_scalar(tmp16[:, :], L2g[:, :], mid_b[:, b:b + 1], None, ALU.is_gt,
                                ALU.add, accum_out=prt_b[:, b:b + 1]).then_inc(sV.h, 1)
        st[("v", 0)] = sV.inc()
        yield ("V", v_binit)

        for it in range(NBIS):
            def t_it(e, b=b, it=it, st=st):
                e.wait_ge(sV.h, st[("v", it)])
                e.matmul(pbank[it % 2][:, 0:1], ONESC[:, :], prt_b[:, b:b + 1], start=True, stop=True).then_inc(sT.h, 1)
            st[("t", it)] = sT.inc()
            yield ("T", t_it)

            def v_it(e, b=b, it=it, st=st):
                e.wait_ge(sT.h, st[("t", it)])
                ps = pbank[it % 2]
                e.tensor_scalar(cge_b[:, b:b + 1], ps[:, 0:1], float(NSLOT) - 0.5, None, ALU.is_ge)
                e.tensor_tensor(dlt_b[:, b:b + 1], mid_b[:, b:b + 1], lo_b[:, b:b + 1], ALU.subtract)
                e.tensor_tensor(dlt2_b[:, b:b + 1], hi_b[:, b:b + 1], mid_b[:, b:b + 1], ALU.subtract)
                e.scalar_tensor_tensor(lo_b[:, b:b + 1], dlt_b[:, b:b + 1], cge_b[:, b:b + 1],
                                       lo_b[:, b:b + 1], ALU.mult, ALU.add)
                e.scalar_tensor_tensor(hi_b[:, b:b + 1], dlt2_b[:, b:b + 1], cge_b[:, b:b + 1],
                                       mid_b[:, b:b + 1], ALU.mult, ALU.add)
                last = None
                if it < NBIS - 1:
                    e.drain()
                    e.tensor_tensor(mid_b[:, b:b + 1], lo_b[:, b:b + 1], hi_b[:, b:b + 1], ALU.add)
                    e.drain()
                    with nc.allow_low_precision("bisection counts <=16 are exact in bf16"):
                        last = e.tensor_scalar(tmp16[:, :], L2g[:, :], mid_b[:, b:b + 1], None, ALU.is_gt,
                                               ALU.add, accum_out=prt_b[:, b:b + 1])
                else:
                    e.drain()
                    last = e.tensor_scalar(mask_b[:, b * 16:(b + 1) * 16], L2g[:, :],
                                           lo_b[:, b:b + 1], None, ALU.is_gt)
                last.then_inc(sV.h, 1)
            st[("v", it + 1)] = sV.inc()
            yield ("V", v_it)

        def v_fin(e, b=b, st=st):
            e.wait_ge(sV.h, st[("v", NBIS)])
            e.tensor_scalar(lbr_b[:, :], Lgrid[:, b * 16:(b + 1) * 16], bndB[:, 0:1], None,
                            ALU.subtract)
            e.tensor_copy(maskbf[:, :], mask_b[:, b * 16:(b + 1) * 16]).then_inc(sV.h, 1)
        st["vfin"] = sV.inc()
        yield ("V", v_fin)

        def s_wexp(e, b=b, st=st):
            e.wait_ge(sV.h, st["vfin"])
            e.activation(out=wexp[:, :], in_=lbr_b[:, :], func=AF.Exp).then_inc(sS.h, 1)
        st["swexp"] = sS.inc()
        yield ("S", s_wexp)

        def t_ct(e, b=b, st=st):
            e.wait_ge(sV.h, st["vfin"])
            e.matmul(pbank[0][0:16, 0:1], maskbf[:, :], ONESC[:, 0:1], start=True, stop=True).then_inc(sT.h, 1)
        st["tct"] = sT.inc()
        yield ("T", t_ct)

        def v_ct(e, b=b, st=st):
            e.wait_ge(sT.h, st["tct"])
            e.wait_ge(sS.h, st["swexp"])
            e.tensor_tensor(wm32[:, :], wexp[:, :], mask_b[:, b * 16:(b + 1) * 16], ALU.mult)
            e.tensor_copy(ct16[:, :], pbank[0][0:16, 0:1])
            e.reduce_sum(wprt[:, :], wm32[:, :], axis=AX.X).then_inc(sV.h, 1)
        st["vct"] = sV.inc()
        yield ("V", v_ct)

        def t_cb(e, b=b, st=st):
            e.wait_ge(sV.h, st["vct"])
            e.matmul(pbank[1][0:1, 0:16], ct16[:, :], TRI16[:, :], start=True, stop=True)
            e.matmul(pbank[0][0:1, 0:1], wprt[:, :], ONESCF[:, :], start=True, stop=True).then_inc(sT.h, 1)
        st["tcb"] = sT.inc()
        yield ("T", t_cb)

        def v_cb(e, b=b, st=st):
            e.wait_ge(sT.h, st["tcb"])
            e.tensor_copy(zt_s[:, b:b + 1], pbank[0][0:1, 0:1])
            e.tensor_copy(cb16[:, :], pbank[1][0:1, 0:16])
            e.reciprocal(zt_s[:, b:b + 1], zt_s[:, b:b + 1]).then_inc(sV.h, 1)
        st["vcb"] = sV.inc()
        yield ("V", v_cb)

        def t_pos(e, b=b, st=st):
            e.wait_ge(sV.h, st["vcb"])
            e.matmul(pbank[0][:, 0:16], ONESC[0:1, :], cb16[:, :], start=True, stop=True)
            e.matmul(pbank[1][:, 0:16], TRI[:, :], maskbf[:, :], start=True, stop=True).then_inc(sT.h, 1)
        st["tpos"] = sT.inc()
        yield ("T", t_pos)

        def v_pos(e, b=b, st=st):
            e.wait_ge(sT.h, st["tpos"])
            e.tensor_copy(pos_b[:, :], pbank[0][:, 0:16])
            e.tensor_copy(vspc[:, 0:1], vspc[:, 1:2])
            e.tensor_tensor(pos_b[:, :], pos_b[:, :], pbank[1][:, 0:16], ALU.add)
            e.drain()
            for kk in range(16):
                e.tensor_scalar(PT[:, 256 * kk:256 * (kk + 1)], IO256[:, :], pos_b[:, kk:kk + 1],
                                mask_b[:, b * 16 + kk:b * 16 + kk + 1], ALU.is_equal, ALU.mult)
            e.tensor_copy(VALS[:, 0:48:3], SG[:, :])
            e.tensor_copy(VALS[:, 1:48:3], TG[:, :])
            last = None
            for kk in range(16):
                last = e.tensor_scalar(VALS[:, 3 * kk + 2:3 * kk + 3], wm32[:, kk:kk + 1], 1.0, None, ALU.mult)
            last.then_inc(sV.h, 1)
        st["vpos"] = sV.inc()
        yield ("V", v_pos)

        def t_rz(e, b=b, st=st):
            e.wait_ge(sV.h, st["vpos"])
            e.matmul(pbank[0][:, 0:1], ONES1F[:, :], zt_s[:, b:b + 1], start=True, stop=True).then_inc(sT.h, 1)
        st["trz"] = sT.inc()
        yield ("T", t_rz)

        def v_rz(e, b=b, st=st):
            e.wait_ge(sT.h, st["trz"])
            e.tensor_copy(rzB[:, :], pbank[0][:, 0:1]).then_inc(sV.h, 1)
        st["vrz"] = sV.inc()
        yield ("V", v_rz)

        for half in range(2):
            def t_ext(e, b=b, half=half, st=st):
                e.wait_ge(sV.h, st["vrz"] if half == 0 else st[("vext", 0)])
                last = None
                for kk in range(16):
                    last = e.matmul(pbank[half][:, 0:3],
                                    PT[:, 256 * kk + 128 * half:256 * kk + 128 * (half + 1)],
                                    VALS[:, 3 * kk:3 * (kk + 1)], start=(kk == 0), stop=(kk == 15))
                last.then_inc(sT.h, 1)
            st[("text", half)] = sT.inc()
            yield ("T", t_ext)

            def v_ext(e, b=b, half=half, st=st):
                e.wait_ge(sT.h, st[("text", half)])
                stg = b * 2 + half
                o = stg * 4
                e.tensor_copy(slotv[:, o:o + 3], pbank[half][:, 0:3])
                e.tensor_copy(vspc[:, 0:1], vspc[:, 1:2])
                e.tensor_tensor(slotv[:, o + 3:o + 4], slotv[:, o + 2:o + 3], rzB[:, :], ALU.mult)
                e.tensor_copy(sigI[:, stg:stg + 1], slotv[:, o:o + 1])
                e.tensor_copy(tI[:, stg:stg + 1], slotv[:, o + 1:o + 2]).then_inc(sV.h, 1)
            st[("vext", half)] = sV.inc()
            yield ("V", v_ext)

        router_state[b] = st

    emit_load(0)
    emit_load(1)
    emit_cast(0)
    emit_cast(1)

    n_units_per_pair = 16
    r0gen = [None]

    def splice(n):
        if r0gen[0] is None:
            return
        for _ in range(n):
            try:
                t_, f_ = next(r0gen[0])
            except StopIteration:
                r0gen[0] = None
                return
            OPS[t_].append(f_)

    DBGC = [0]
    for gi, (b, s) in enumerate(pair_list):
        for u in range(n_units_per_pair):
            emit_attn_unit(gi * n_units_per_pair + u)
            if b == 1:
                splice(2)
        if pend_tpv[0] is not None:
            fnp, pu = pend_tpv[0]
            T(fnp)
            uT_pv[pu] = sT.inc()
            pend_tpv[0] = None
        if pend_vz[0] is not None:
            fn, pu = pend_vz[0]
            import functools
            V(functools.partial(fn, flushed=True))
            uV_z[pu] = sV.inc()
            pend_vz[0] = None
        for li, lu in enumerate(ln_units):
            if lu[0] == gi:
                emit_ln_unit(li)
        if gi == 0 and _os.environ.get("KDBG"):
            def g_dbg(e, li_p0=3):
                e.wait_ge(sS.h, ls_apply[li_p0])
                e.dma_start(out=dbg_h[:, 0:2048], in_=xz[:, 0:2048]).then_inc(sC.h, 16)
                e.dma_start(out=dbg_h[:, 2048:4096], in_=xbuf[:, 0:2048]).then_inc(sC.h, 16)
                e.dma_start(out=dbg_h[:, 4096:6144], in_=x1st[:, 0:2048]).then_inc(sC.h, 16)
                e.dma_start(out=dbg_h[0:128, 6144:6176], in_=lnmv[:, :]).then_inc(sC.h, 16)
            G(g_dbg)
            DBGC[0] = sC.inc(16 * 4)
        if gi + 2 < len(pair_list):
            emit_load(gi + 2)
            emit_cast(gi + 2)
        if b == 0 and s == NSEG - 1:
            r0gen[0] = build_router_ops(0, None)

    splice(9999)

    for t_, f_ in build_router_ops(1, router_state[0][("text", 1)]):
        OPS[t_].append(f_)


    esA.close()
    esB = ExitStack()
    g1T = esB.enter_context(nc.sbuf_tensor("g1T", [128, 64 * 512], BF16))
    w1cb = esB.enter_context(nc.sbuf_tensor("w1cb", [128, 3 * 4096], BF16))
    h2sb = esB.enter_context(nc.sbuf_tensor("h2sb", [128, 2 * 512], BF16))
    w2cb = esB.enter_context(nc.sbuf_tensor("w2cb", [128, 2 * 8192], BF16))
    ybuf = esB.enter_context(nc.sbuf_tensor("ybuf", [128, 4 * D], F32))
    fxb = esB.enter_context(nc.sbuf_tensor("fxb", [128, 4 * D], BF16))
    fxT = esB.enter_context(nc.sbuf_tensor("fxT", [128, 16 * 512], BF16))
    outro = esB.enter_context(nc.sbuf_tensor("outro", [128, 2 * D], F32))

    gath_done = {}
    for b in range(NB):
        for half in range(2):
            stg = b * 2 + half

            def g_gather(e, b=b, half=half, stg=stg):
                if stg == 0:
                    tS, tT = seg_consumed[(1, NSEG - 1)]
                    e.wait_ge(sS.h, tS)
                    e.wait_ge(sT.h, tT)
                    e.wait_ge(sV.h, router_state[1][("vext", 1)])
                    for mm in range(4):
                        e.wait_ge(sM4[mm].h, sM4[mm].n)
                else:
                    e.wait_ge(sGG.h, gath_done[stg - 1])
                e.wait_ge(sV.h, router_state[b][("vext", half)])
                e.indirect_dma_start(fxb[:, stg * D:(stg + 1) * D], None,
                                     x1m_h.rearrange("b s d -> (b s) d"),
                                     IndirectOffsetOnAxis(sigI[:, stg:stg + 1], 0),
                                     element_offset=b * S * D).then_inc(sGG.h, 16)
            G(g_gather)
            gath_done[stg] = sGG.inc(16)

    fxT_done = {}
    tp_evac = {}
    for stg in range(4):
        for t in range(16):
            idx = stg * 16 + t

            def t_fxT(e, stg=stg, t=t, idx=idx):
                e.wait_ge(sGG.h, gath_done[stg])
                if idx == 0:
                    e.wait_ge(sV.h, uV_gT[len(attn_units) - 1])
                if idx >= 2:
                    e.wait_ge(sV.h, tp_evac[idx - 2])
                e.transpose(pbank_bf[idx % 2][:, 0:128], fxb[:, stg * D + 128 * t:stg * D + 128 * (t + 1)],
                            IDT[:, :]).then_inc(sT.h, 1)
            T(t_fxT)
            fxT_done[idx] = sT.inc()

            def v_fxT(e, stg=stg, t=t, idx=idx):
                e.wait_ge(sT.h, fxT_done[idx])
                e.tensor_copy(fxT[:, 512 * t + 128 * stg:512 * t + 128 * (stg + 1)],
                              pbank_bf[idx % 2][:, 0:128]).then_inc(sV.h, 1)
            V(v_fxT)
            tp_evac[idx] = sV.inc()
    FXT_ALL = sV.n

    w1_done = {}
    h1_done = {}
    g1_done = {}
    for fb in range(32):
        def g_w1(e, fb=fb):
            if fb >= 3:
                e.wait_ge(sT.h, h1_done[2 * (fb - 3) + 1])
            e.dma_start(out=w1cb[:, (fb % 3) * 4096:(fb % 3 + 1) * 4096],
                        in_=w1_h[:, 256 * fb:256 * (fb + 1)].rearrange("(t p) f -> p t f", p=128)).then_inc(sW1[fb % 3].h, 16)
        G(g_w1)
        w1_done[fb] = sW1[fb % 3].inc(16)

    for f in range(64):
        fb = f // 2
        fo = f % 2

        def t_h1(e, f=f, fb=fb, fo=fo):
            if fo == 0:
                e.wait_ge(sW1[fb % 3].h, w1_done[fb])
            if f == 0:
                e.wait_ge(sV.h, FXT_ALL)
            if f >= 4:
                e.wait_ge(sS.h, g1_done[f - 4])
            bk = f % 4
            base = (fb % 3) * 4096 + fo * 128
            last = None
            for t in range(16):
                last = e.matmul(pbank[bk][:, 0:512],
                                w1cb[:, base + 256 * t:base + 256 * t + 128],
                                fxT[:, 512 * t:512 * (t + 1)], start=(t == 0), stop=(t == 15))
            last.then_inc(sT.h, 1)
        T(t_h1)
        h1_done[f] = sT.inc()

        def s_g1(e, f=f):
            e.wait_ge(sT.h, h1_done[f])
            bk = f % 4
            e.activation(out=g1T[:, 512 * f:512 * (f + 1)], in_=pbank[bk][:, 0:512], func=AF.Gelu,
                         bias=b1T[:, f:f + 1]).then_inc(sS.h, 1)
        SC(s_g1)
        g1_done[f] = sS.inc()

    w2_done = {}
    h2_done = {}
    h2s_done = {}
    h2t_done = {}
    yev_done = {}
    for c in range(16):
        def g_w2(e, c=c):
            if c == 0:
                tS, tT = seg_consumed[(1, NSEG - 1)]
                e.wait_ge(sS.h, tS)
                e.wait_ge(sT.h, tT)
            if c >= 2:
                e.wait_ge(sT.h, h2_done[c - 2])
            e.dma_start(out=w2cb[:, (c % 2) * 8192:(c % 2 + 1) * 8192],
                        in_=w2_h[:, 128 * c:128 * (c + 1)].rearrange("(t p) f -> p t f", p=128)).then_inc(sW2[c % 2].h, 16)
        G(g_w2)
        w2_done[c] = sW2[c % 2].inc(16)

    for c in range(16):
        def t_h2(e, c=c):
            e.wait_ge(sW2[c % 2].h, w2_done[c])
            if c == 0:
                e.wait_ge(sS.h, g1_done[63])
            if c >= 2:
                e.wait_ge(sS.h, h2s_done[c - 2])
            bk = 4 + c % 2
            last = None
            for f in range(64):
                last = e.matmul(pbank[bk][:, 0:512],
                                w2cb[:, (c % 2) * 8192 + 128 * f:(c % 2) * 8192 + 128 * (f + 1)],
                                g1T[:, 512 * f:512 * (f + 1)], start=(f == 0), stop=(f == 63))
            last.then_inc(sT.h, 1)
        T(t_h2)
        h2_done[c] = sT.inc()

        def s_h2(e, c=c):
            e.wait_ge(sT.h, h2_done[c])
            if c >= 2:
                e.wait_ge(sT.h, h2t_done[(c - 2, 3)])
            bk = 4 + c % 2
            e.activation(out=h2sb[:, (c % 2) * 512:(c % 2 + 1) * 512], in_=pbank[bk][:, 0:512],
                         func=AF.Identity, bias=b2T[:, c:c + 1]).then_inc(sS.h, 1)
        SC(s_h2)
        h2s_done[c] = sS.inc()

        for stg in range(4):
            idx = c * 4 + stg

            def t_h2t(e, c=c, stg=stg, idx=idx):
                e.wait_ge(sS.h, h2s_done[c])
                if idx >= 2:
                    e.wait_ge(sV.h, yev_done[idx - 2])
                e.transpose(pbank_bf[idx % 2][:, 0:128],
                            h2sb[:, (c % 2) * 512 + 128 * stg:(c % 2) * 512 + 128 * (stg + 1)],
                            IDT[:, :]).then_inc(sT.h, 1)
            T(t_h2t)
            h2t_done[(c, stg)] = sT.inc()

            def v_yev(e, c=c, stg=stg, idx=idx):
                e.wait_ge(sT.h, h2t_done[(c, stg)])
                e.tensor_scalar(ybuf[:, stg * D + 128 * c:stg * D + 128 * (c + 1)],
                                pbank_bf[idx % 2][:, 0:128],
                                slotv[:, stg * 4 + 3:stg * 4 + 4], None, ALU.mult).then_inc(sV.h, 1)
            V(v_yev)
            yev_done[idx] = sV.inc()

    fin_scat = {}
    fin_conv = {}
    for stg in range(4):
        b = stg // 2

        def s_conv(e, stg=stg):
            e.wait_ge(sGG.h, gath_done[stg])
            if stg >= 2:
                e.wait_ge(sGG.h, fin_scat[stg - 2])
            e.activation(out=outro[:, (stg % 2) * D:(stg % 2 + 1) * D],
                         in_=fxb[:, stg * D:(stg + 1) * D], func=AF.Identity,
                         scale=2.0).then_inc(sS.h, 1)
        SC(s_conv)
        fin_conv[stg] = sS.inc()

        def v_fin2(e, stg=stg):
            e.wait_ge(sV.h, yev_done[15 * 4 + stg])
            e.wait_ge(sS.h, fin_conv[stg])
            yt = ybuf[:, stg * D:(stg + 1) * D]
            e.tensor_tensor(yt[:, :], yt[:, :], outro[:, (stg % 2) * D:(stg % 2 + 1) * D], ALU.add)
            o = stg * 8
            e.drain()
            for i in range(4):
                e.bn_stats(lnstats[:, stg * 24 + 6 * i:stg * 24 + 6 * (i + 1)], yt[:, 512 * i:512 * (i + 1)])
            e.drain()
            e.bn_aggr(lnmv[:, o:o + 2], lnstats[:, stg * 24:stg * 24 + 24])
            e.drain()
            e.tensor_scalar(lnmv[:, o + 1:o + 2], lnmv[:, o + 1:o + 2], EPS, None, ALU.add)
            e.drain()
            e.reciprocal(lnmv[:, o + 1:o + 2], lnmv[:, o + 1:o + 2]).then_inc(sV.h, 1)
        V(v_fin2)
        fin_stats = sV.inc()

        def s_fin2(e, stg=stg, fin_stats=fin_stats):
            e.wait_ge(sV.h, fin_stats)
            o = stg * 8
            e.activation(out=lnmv[:, o + 1:o + 2], in_=lnmv[:, o + 1:o + 2], func=AF.Ln)
            e.activation(out=lnmv[:, o + 1:o + 2], in_=lnmv[:, o + 1:o + 2], func=AF.Exp, scale=0.5).then_inc(sS.h, 1)
        SC(s_fin2)
        fin_inv = sS.inc()

        def v_fin3(e, stg=stg, fin_inv=fin_inv):
            e.wait_ge(sS.h, fin_inv)
            o = stg * 8
            e.scalar_tensor_tensor(lnmv[:, o + 3:o + 4], lnmv[:, o + 0:o + 1], -1.0,
                                   lnmv[:, o + 1:o + 2], ALU.mult, ALU.mult).then_inc(sV.h, 1)
        V(v_fin3)
        fin_b = sV.inc()

        def s_fin3(e, stg=stg, fin_b=fin_b):
            e.wait_ge(sV.h, fin_b)
            o = stg * 8
            e.activation(out=outro[:, (stg % 2) * D:(stg % 2 + 1) * D], in_=ybuf[:, stg * D:(stg + 1) * D],
                         func=AF.Identity, bias=lnmv[:, o + 3:o + 4],
                         scale=lnmv[:, o + 1:o + 2]).then_inc(sS.h, 1)
        SC(s_fin3)
        fin_app = sS.inc()

        def g_scat(e, stg=stg, b=b, fin_app=fin_app, prev_gg=sGG.n):
            e.wait_ge(sS.h, fin_app)
            e.wait_ge(sGG.h, prev_gg)
            if stg == 0:
                for ee in range(2):
                    e.wait_ge(sE2[ee].h, sE2[ee].n)
            e.indirect_dma_start(out_h.rearrange("b s d -> (b s) d"),
                                 IndirectOffsetOnAxis(tI[:, stg:stg + 1], 0),
                                 outro[:, (stg % 2) * D:(stg % 2 + 1) * D], None,
                                 element_offset=b * S * D).then_inc(sGG.h, 16)
        G(g_scat)
        fin_scat[stg] = sGG.inc(16)

    FINAL_GG = sGG.n
    FINAL_E = [s.n for s in sE2]

    def g_end(e):
        e.wait_ge(sGG.h, FINAL_GG)
        for i in range(2):
            e.wait_ge(sE2[i].h, FINAL_E[i])
    G(g_end)

    def y_end(e):
        e.wait_ge(sGG.h, FINAL_GG)
        for i in range(2):
            e.wait_ge(sE2[i].h, FINAL_E[i])
    Y(y_end)

    import os
    _tr = float(os.environ.get("KTRUNC", "1.0"))
    if _tr < 1.0:
        for _k in OPS:
            OPS[_k] = OPS[_k][:max(1, int(len(OPS[_k]) * _tr))]
    with nc.Block() as block:
        @block.tensor
        def _(eng):
            for fn in OPS["T"]:
                fn(eng)

        @block.vector
        def _(eng):
            for fn in OPS["V"]:
                fn(eng)

        @block.scalar
        def _(eng):
            for fn in OPS["S"]:
                fn(eng)

        @block.gpsimd
        def _(eng):
            for fn in OPS["G"]:
                fn(eng)

        @block.sync
        def _(eng):
            for fn in OPS["Y"]:
                fn(eng)

    esB.close()
    es.close()
    return nc


def _consts():
    c = {}
    c["IDT"] = np.eye(128, dtype=np.float32).astype(ml_dtypes.bfloat16)
    c["IDTF"] = np.eye(128, dtype=np.float32)
    c["TRI"] = np.triu(np.ones((128, 128), np.float32), 1).astype(ml_dtypes.bfloat16)
    c["TRI16"] = np.triu(np.ones((16, 16), np.float32), 1).astype(ml_dtypes.bfloat16)
    c["ONESC"] = np.ones((128, 128), np.float32).astype(ml_dtypes.bfloat16)
    c["ONES1F"] = np.ones((1, 128), np.float32)
    c["ONESCF"] = np.ones((128, 1), np.float32)
    c["IO256"] = np.broadcast_to(np.arange(256, dtype=np.float32)[None, :], (128, 256)).copy()
    c["LNB"] = np.broadcast_to(np.array([[1e-5, 1.0 + 1e-5 / 4.0]], np.float32), (128, 2)).copy()
    q = np.arange(128)[:, None]
    k = np.arange(16)[None, :]
    c["SG"] = (128 * k + q).astype(np.float16)
    s_ = k // 4
    par_ = (k % 4) // 2
    tl_ = k % 2
    c["TG"] = (512 * s_ + 256 * tl_ + 2 * q + par_).astype(np.float16)
    return c


_NC_CACHE = {}


def kernel(L0_attn_x, L0_ffn_Wr, L0_ffn_br, L0_ffn_W1, L0_ffn_b1, L0_ffn_W2, L0_ffn_b2):
    x = np.ascontiguousarray(np.asarray(L0_attn_x, dtype=np.float32))
    Wr = np.ascontiguousarray(np.asarray(L0_ffn_Wr, dtype=np.float32).reshape(1, D))
    br = np.ascontiguousarray(np.asarray(L0_ffn_br, dtype=np.float32).reshape(1, 1))
    W1 = np.ascontiguousarray(np.asarray(L0_ffn_W1, dtype=np.float32))
    b1 = np.ascontiguousarray(np.asarray(L0_ffn_b1, dtype=np.float32).reshape(64, 128).T)
    W2 = np.ascontiguousarray(np.asarray(L0_ffn_W2, dtype=np.float32))
    b2 = np.ascontiguousarray(np.asarray(L0_ffn_b2, dtype=np.float32).reshape(16, 128).T)
    consts = _consts()

    if "nc" not in _NC_CACHE:
        _NC_CACHE["nc"] = build_nc()
    nc = _NC_CACHE["nc"]

    in_maps = []
    for c in range(8):
        m = {"x": np.ascontiguousarray(x[2 * c:2 * c + 2]),
             "Wr": Wr, "br": br, "W1": W1, "b1": b1, "W2": W2, "b2": b2}
        m.update(consts)
        in_maps.append(m)
    res = run_bass_kernel_spmd(nc, in_maps, core_ids=list(range(8)))
    out = np.concatenate([res.results[i]["out"] for i in range(8)], axis=0)
    return out.astype(np.float32)


# version 10
# speedup vs baseline: 1.1600x; 1.0316x over previous
import os as _os
import numpy as np
import ml_dtypes
import concourse.bass as bass
import concourse.mybir as mybir
from concourse.bass_utils import run_bass_kernel_spmd
from concourse.bass_core_rust import IndirectOffsetOnAxis

F32 = mybir.dt.float32
BF16 = mybir.dt.bfloat16
FP16 = mybir.dt.float16
I32 = mybir.dt.int32
AF = mybir.ActivationFunctionType
ALU = mybir.AluOpType
AX = mybir.AxisListType

NB = 2
NSEG = 4
D = 2048
S = 2048
DFF = 8192
NSLOT = 256
SCALE = float(1.0 / np.sqrt(128.0))
EPS = 1e-5
NBIS = 33
WOFF = 16.0


class Sem:
    def __init__(self, es, nc, name):
        self.h = es.enter_context(nc.semaphore(name))
        self.n = 0

    def inc(self, k=1):
        self.n += k
        return self.n


def build_nc():
    from contextlib import ExitStack
    nc = bass.Bass("TRN2", target_bir_lowering=False, detect_race_conditions=False)
    es = ExitStack()

    x_h = nc.dram_tensor("x", [NB, S, D], F32, kind="ExternalInput").ap()
    wr_h = nc.dram_tensor("Wr", [1, D], F32, kind="ExternalInput").ap()
    br_h = nc.dram_tensor("br", [1, 1], F32, kind="ExternalInput").ap()
    w1_h = nc.dram_tensor("W1", [D, DFF], F32, kind="ExternalInput").ap()
    b1_h = nc.dram_tensor("b1", [128, 64], F32, kind="ExternalInput").ap()
    w2_h = nc.dram_tensor("W2", [DFF, D], F32, kind="ExternalInput").ap()
    b2_h = nc.dram_tensor("b2", [128, 16], F32, kind="ExternalInput").ap()
    idt_h = nc.dram_tensor("IDT", [128, 128], BF16, kind="ExternalInput").ap()
    idtf_h = nc.dram_tensor("IDTF", [128, 128], F32, kind="ExternalInput").ap()
    tri_h = nc.dram_tensor("TRI", [128, 128], BF16, kind="ExternalInput").ap()
    tri16_h = nc.dram_tensor("TRI16", [16, 16], BF16, kind="ExternalInput").ap()
    onesc_h = nc.dram_tensor("ONESC", [128, 128], BF16, kind="ExternalInput").ap()
    ones1f_h = nc.dram_tensor("ONES1F", [1, 128], F32, kind="ExternalInput").ap()
    onescf_h = nc.dram_tensor("ONESCF", [128, 1], F32, kind="ExternalInput").ap()
    io256_h = nc.dram_tensor("IO256", [128, 256], F32, kind="ExternalInput").ap()
    lnb_h = nc.dram_tensor("LNB", [128, 2], F32, kind="ExternalInput").ap()
    sg_h = nc.dram_tensor("SG", [128, 16], FP16, kind="ExternalInput").ap()
    tg_h = nc.dram_tensor("TG", [128, 16], FP16, kind="ExternalInput").ap()
    out_h = nc.dram_tensor("out", [NB, S, D], F32, kind="ExternalOutput").ap()
    dbg_h = nc.dram_tensor("dbg", [128, 6176], F32, kind="ExternalOutput").ap()
    x1m_h = nc.dram_tensor("x1m", [NB, S, D], BF16, kind="ExternalOutput" if _os.environ.get("KDBG") else "Internal").ap()

    def xview(h, b, s, par, tl):
        return h[b].rearrange("(s tl p two) d -> s two tl p d", s=4, tl=2, p=128, two=2)[s, par, tl]

    sb = lambda name, shape, dt: es.enter_context(nc.sbuf_tensor(name, shape, dt))
    IDT = sb("c_idt", [128, 128], BF16)
    IDTF = sb("c_idtf", [128, 128], F32)
    TRI = sb("c_tri", [128, 128], BF16)
    TRI16 = sb("c_tri16", [16, 16], BF16)
    ONESC = sb("c_onesc", [128, 128], BF16)
    ONES1F = sb("c_ones1f", [1, 128], F32)
    ONESCF = sb("c_onescf", [128, 1], F32)
    IO256 = sb("c_io256", [128, 256], F32)
    LNB = sb("c_lnb", [128, 2], F32)
    SG = sb("c_sg", [128, 16], FP16)
    TG = sb("c_tg", [128, 16], FP16)
    brS = sb("c_brs", [1, 1], F32)
    brB = sb("c_brb", [128, 1], F32)
    b1T = sb("c_b1t", [128, 64], F32)
    b2T = sb("c_b2t", [128, 16], F32)

    lnstats = sb("lnstats", [128, 4 * 24], F32)
    lnmv = sb("lnmv", [128, 4 * 8], F32)
    lo_b = sb("lo", [128, 2], F32)
    hi_b = sb("hi", [128, 2], F32)
    mid_b = sb("mid", [128, 2], F32)
    tmp16 = sb("tmp16", [128, 16], BF16)
    prt_b = sb("prt", [128, 2], BF16)
    cge_b = sb("cge", [128, 2], F32)
    dlt_b = sb("dlt", [128, 2], F32)
    dlt2_b = sb("dlt2", [128, 2], F32)
    L2g = sb("L2g", [128, 16], F32)
    vspc = sb("vspc", [128, 2], F32)
    mask_b = sb("mask", [128, 2 * 16], F32)
    maskbf = sb("maskbf", [128, 16], BF16)
    ct16 = sb("ct16", [16, 1], BF16)
    cb16 = sb("cb16", [1, 16], BF16)
    pos_b = sb("pos", [128, 16], F32)
    lbr_b = sb("lbr", [128, 16], F32)
    wexp = sb("wexp", [128, 16], F32)
    wm32 = sb("wm32", [128, 16], F32)
    wprt = sb("wprt", [128, 1], F32)
    bndc = sb("bndc", [128, 2], F32)
    bnds = sb("bnds", [1, 2], F32)
    bndB = sb("bndB", [128, 2], F32)
    VALS = sb("VALS", [128, 16 * 3], FP16)
    zt_s = sb("zt", [1, 2], F32)
    rzB = sb("rzB", [128, 1], F32)
    slotv = sb("slotv", [128, 4 * 4], F32)
    sigI = sb("sigI", [128, 4], I32)
    tI = sb("tI", [128, 4], I32)

    pbank = [es.enter_context(nc.psum_tensor(f"pb{i}", [128, 512], F32)) for i in range(6)]
    pbank_bf = [es.enter_context(nc.psum_tensor(f"pbb{i}", [128, 512], BF16)) for i in range(2)]

    sT = Sem(es, nc, "sT")
    sV = Sem(es, nc, "sV")
    sS = Sem(es, nc, "sS")
    sC = Sem(es, nc, "sC")
    sWrt = Sem(es, nc, "sWrt")
    sX = [Sem(es, nc, f"sX{i}") for i in range(2)]
    sXB = [Sem(es, nc, f"sXB{i}") for i in range(2)]
    sE2 = [Sem(es, nc, f"sE2{i}") for i in range(2)]
    sM4 = [Sem(es, nc, f"sM4{i}") for i in range(4)]
    sW1 = [Sem(es, nc, f"sW1{i}") for i in range(3)]
    sW2 = [Sem(es, nc, f"sW2{i}") for i in range(2)]
    sGG = Sem(es, nc, "sGG")

    OPS = {"T": [], "V": [], "S": [], "G": [], "Y": []}
    T = lambda fn: OPS["T"].append(fn)
    V = lambda fn: OPS["V"].append(fn)
    SC = lambda fn: OPS["S"].append(fn)
    G = lambda fn: OPS["G"].append(fn)
    Y = lambda fn: OPS["Y"].append(fn)

    def setup_loads(e):
        for dst, src in ((IDT, idt_h), (IDTF, idtf_h), (TRI, tri_h), (TRI16, tri16_h), (ONESC, onesc_h),
                         (ONES1F, ones1f_h), (ONESCF, onescf_h), (IO256, io256_h),
                         (LNB, lnb_h), (SG, sg_h), (TG, tg_h), (brS, br_h)):
            e.dma_start(out=dst[:, :], in_=src[:, :]).then_inc(sC.h, 16)
        e.dma_start(out=b1T[:, :], in_=b1_h[:, :]).then_inc(sC.h, 16)
        e.dma_start(out=b2T[:, :], in_=b2_h[:, :]).then_inc(sC.h, 16)
    Y(setup_loads)
    LD_CONST = sC.inc(16 * 14)

    def setup_wrt(e):
        e.dma_start(out=WrT[:, :], in_=wr_h[:, :]).then_inc(sWrt.h, 16)
    G(setup_wrt)
    LD_WRT = sWrt.inc(16)

    def setup_bcast_t(e):
        e.wait_ge(sC.h, LD_CONST)
        e.wait_ge(sWrt.h, LD_WRT)
        for i in range(4):
            e.matmul(pbank[i][:, :], ONESC[0:1, :], WrT[:, 512 * i:512 * (i + 1)], start=True, stop=True)
        e.matmul(pbank[4][:, 0:1], ONES1F[:, :], brS[:, :], start=True, stop=True).then_inc(sT.h, 1)
    T(setup_bcast_t)
    T_BCAST = sT.inc()

    def setup_bcast_v(e):
        e.memset(vspc[:, :], 0.0)
        e.wait_ge(sT.h, T_BCAST)
        for i in range(4):
            e.tensor_copy(WrB[:, 512 * i:512 * (i + 1)], pbank[i][:, :])
        e.tensor_copy(brB[:, :], pbank[4][:, 0:1]).then_inc(sV.h, 1)
    V(setup_bcast_v)
    V_BCAST = sV.inc()

    esA = ExitStack()
    WrT = esA.enter_context(nc.sbuf_tensor("c_wrt", [1, D], BF16))
    WrB = esA.enter_context(nc.sbuf_tensor("c_wrb", [128, D], BF16))
    xz = esA.enter_context(nc.sbuf_tensor("xz", [128, 2 * 4 * D], F32))
    xbuf = esA.enter_context(nc.sbuf_tensor("xbw", [128, 2 * 4 * D], BF16))
    x1st = esA.enter_context(nc.sbuf_tensor("x1stage", [128, 4 * D], BF16))
    sttscr = esA.enter_context(nc.sbuf_tensor("sttscr", [128, D], BF16))
    Lgrid = esA.enter_context(nc.sbuf_tensor("Lgrid", [128, 2 * 16], F32))
    outear = esA.enter_context(nc.sbuf_tensor("outear", [128, 2 * D], F32))
    PT = esA.enter_context(nc.sbuf_tensor("PT", [128, 16 * 256], FP16))
    gT = esA.enter_context(nc.sbuf_tensor("gT", [128, 2 * 256], BF16))
    Ebuf = esA.enter_context(nc.sbuf_tensor("Ebuf", [128, 2 * 512], BF16))
    Dbuf = esA.enter_context(nc.sbuf_tensor("Dbuf", [128, 2 * 2], F32))
    Rbuf = esA.enter_context(nc.sbuf_tensor("Rbuf", [128, 2 * 2], F32))

    def xz_t(sbuf_i, par, tl):
        off = (sbuf_i * 4 + par * 2 + tl) * D
        return xz[:, off:off + D]

    def xb_t(sbuf_i, par, tl):
        off = (sbuf_i * 4 + par * 2 + tl) * D
        return xbuf[:, off:off + D]

    pair_list = [(b, s) for b in range(NB) for s in range(NSEG)]
    xload_done = {}
    xcast_done = {}
    seg_consumed = {}

    uT_tp, uT_sc, uT_pv = {}, {}, {}
    uT_sc1 = {}
    uS_E = {}
    uV_gT, uV_R, uV_z = {}, {}, {}
    ls_apply, lv_logit, lm_mirror, ly_early = {}, {}, {}, {}
    lv_stats, ls_inv, lv_bias = {}, {}, {}

    pend_vz = [None]
    pend_tpv = [None]
    pend_logit = []
    attn_units = []
    for b, s in pair_list:
        for par in range(2):
            for hh in range(8):
                attn_units.append((b, s, par, hh))
    last_unit_of_pair = {}
    for ui, u in enumerate(attn_units):
        last_unit_of_pair[(u[0], u[1], u[2])] = ui

    ln_units = []
    for gi, (b, s) in enumerate(pair_list):
        for par in range(2):
            for tl in range(2):
                ln_units.append((gi, b, s, par, tl))
    last_li_of_b = {b: max(i for i, u in enumerate(ln_units) if u[1] == b) for b in range(NB)}

    def emit_load(gi):
        b, s = pair_list[gi]
        sbuf_i = gi % 2

        def loadseg(e, b=b, s=s, sbuf_i=sbuf_i, gi=gi):
            if gi == 2 and _os.environ.get("KDBG"):
                e.wait_ge(sC.h, DBGC[0])
            if gi >= 2:
                tS, tT = seg_consumed[pair_list[gi - 2]]
                e.wait_ge(sS.h, tS)
                e.wait_ge(sT.h, tT)
            for par in range(2):
                for tl in range(2):
                    e.dma_start(out=xz_t(sbuf_i, par, tl), in_=xview(x_h, b, s, par, tl)).then_inc(sX[gi % 2].h, 16)
        Y(loadseg)
        xload_done[(b, s)] = (gi % 2, sX[gi % 2].inc(16 * 4))

    def emit_cast(gi):
        b, s = pair_list[gi]
        sbuf_i = gi % 2

        def castseg(e, b=b, s=s, sbuf_i=sbuf_i, gi=gi):
            if gi == 2 and _os.environ.get("KDBG"):
                e.wait_ge(sC.h, DBGC[0])
            xi, xv = xload_done[(b, s)]
            e.wait_ge(sX[xi].h, xv)
            if gi >= 2:
                tS, tT = seg_consumed[pair_list[gi - 2]]
                e.wait_ge(sS.h, tS)
                e.wait_ge(sT.h, tT)
            for par in range(2):
                for tl in range(2):
                    e.dma_start(out=xb_t(sbuf_i, par, tl), in_=xz_t(sbuf_i, par, tl)).then_inc(sXB[gi % 2].h, 16)
        G(castseg)
        xcast_done[(b, s)] = (gi % 2, sXB[gi % 2].inc(16 * 4))

    def emit_attn_unit(ui):
        b, s, par, hh = attn_units[ui]
        gi = pair_list.index((b, s))
        sbuf_i = gi % 2
        slot = ui % 2
        col = (2 * hh + par) * 128

        def t_tp(e, b=b, s=s, par=par, ui=ui, sbuf_i=sbuf_i, col=col):
            if ui == 0:
                e.wait_ge(sV.h, V_BCAST)
            ci, cv = xcast_done[(b, s)]
            e.wait_ge(sXB[ci].h, cv)
            if _os.environ.get("KSERIAL") and ui >= 1:
                e.wait_ge(sV.h, uV_z[ui - 1])
            if _os.environ.get("KSERIAL") and ui % 16 == 0 and ui >= 16:
                li_prev = max(i for i, lu in enumerate(ln_units) if lu[0] == (ui // 16) - 1)
                e.wait_ge(sV.h, lv_logit[li_prev])
            if ui >= 1:
                e.wait_ge(sV.h, uV_gT[ui - 1])
            e.transpose(pbank_bf[0][:, 0:128], xb_t(sbuf_i, par, 0)[:, col:col + 128], IDT[:, :])
            e.transpose(pbank_bf[1][:, 0:128], xb_t(sbuf_i, par, 1)[:, col:col + 128], IDT[:, :]).then_inc(sT.h, 1)
        T(t_tp)
        uT_tp[ui] = sT.inc()

        if pend_tpv[0] is not None:
            fnp, pu = pend_tpv[0]
            T(fnp)
            uT_pv[pu] = sT.inc()
            pend_tpv[0] = None

        def v_gT(e, ui=ui, slot=slot):
            e.wait_ge(sT.h, uT_tp[ui])
            if ui >= 2:
                e.wait_ge(sT.h, uT_sc[ui - 2])
            go = slot * 256
            e.tensor_copy(gT[:, go:go + 128], pbank_bf[0][:, 0:128])
            e.tensor_copy(gT[:, go + 128:go + 256], pbank_bf[1][:, 0:128]).then_inc(sV.h, 1)
        V(v_gT)
        uV_gT[ui] = sV.inc()

        if pend_vz[0] is not None:
            fn, pu = pend_vz[0]
            V(fn)
            uV_z[pu] = sV.inc()
            pend_vz[0] = None

        def t_sc(e, ui=ui, slot=slot):
            e.wait_ge(sV.h, uV_gT[ui])
            if ui >= 1:
                e.wait_ge(sS.h, uS_E[ui - 1] - 1)
            go = slot * 256
            e.matmul(pbank[2][:, 0:256], gT[:, go:go + 128], gT[:, go:go + 256], start=True, stop=True).then_inc(sT.h, 1)
            if ui >= 1:
                e.wait_ge(sS.h, uS_E[ui - 1])
            e.matmul(pbank[3][:, 0:256], gT[:, go + 128:go + 256], gT[:, go:go + 256], start=True, stop=True).then_inc(sT.h, 1)
        T(t_sc)
        uT_sc1[ui] = sT.inc()
        uT_sc[ui] = sT.inc()

        def s_E(e, ui=ui, slot=slot):
            e.wait_ge(sT.h, uT_sc1[ui])
            if ui >= 2:
                e.wait_ge(sV.h, uV_R[ui - 2])
            eo = slot * 512
            do = slot * 2
            e.activation(out=Ebuf[:, eo:eo + 256], in_=pbank[2][:, 0:256], func=AF.Exp,
                         scale=SCALE, accum_out=Dbuf[:, do:do + 1]).then_inc(sS.h, 1)
            e.wait_ge(sT.h, uT_sc[ui])
            e.activation(out=Ebuf[:, eo + 256:eo + 512], in_=pbank[3][:, 0:256], func=AF.Exp,
                         scale=SCALE, accum_out=Dbuf[:, do + 1:do + 2]).then_inc(sS.h, 1)
        SC(s_E)
        sS.inc()
        uS_E[ui] = sS.inc()

        def v_R(e, ui=ui, slot=slot):
            e.wait_ge(sS.h, uS_E[ui])
            do = slot * 2
            e.reciprocal(Rbuf[:, do:do + 2], Dbuf[:, do:do + 2]).then_inc(sV.h, 1)
        V(v_R)
        uV_R[ui] = sV.inc()

        def t_pv(e, ui=ui, slot=slot, sbuf_i=sbuf_i, par=par, col=col):
            e.wait_ge(sS.h, uS_E[ui])
            if ui >= 1:
                e.wait_ge(sV.h, uV_z[ui - 1])
            eo = slot * 512
            last = None
            for j in range(2):
                for k in range(2):
                    last = e.matmul(pbank[4 + j][:, 0:128],
                                    Ebuf[:, eo + k * 256 + j * 128:eo + k * 256 + (j + 1) * 128],
                                    xb_t(sbuf_i, par, k)[:, col:col + 128],
                                    start=(k == 0), stop=(k == 1))
            last.then_inc(sT.h, 1)
        pend_tpv[0] = (t_pv, ui)

        def v_z(e, ui=ui, slot=slot, sbuf_i=sbuf_i, par=par, col=col, flushed=False):
            e.wait_ge(sT.h, uT_pv[ui])
            e.wait_ge(sV.h, uV_R[ui])
            if flushed:
                e.drain()
            do = slot * 2
            last = None
            for j in range(2):
                zt = xz_t(sbuf_i, par, j)
                last = e.scalar_tensor_tensor(zt[:, col:col + 128], pbank[4 + j][:, 0:128],
                                              Rbuf[:, do + j:do + j + 1], zt[:, col:col + 128],
                                              ALU.mult, ALU.add)
            last.then_inc(sV.h, 1)
        pend_vz[0] = (v_z, ui)

    def emit_ln_unit(li):
        gi, b, s, par, tl = ln_units[li]
        sbuf_i = gi % 2
        k = 4 * s + 2 * par + tl
        ring = k % 4
        ering = li % 2

        def v_stats(e, li=li, b=b, s=s, par=par, tl=tl, ring=ring, sbuf_i=sbuf_i):
            e.wait_ge(sV.h, uV_z[last_unit_of_pair[(b, s, par)]])
            if li >= 4:
                e.wait_ge(sS.h, ls_apply[li - 4])
            zt = xz_t(sbuf_i, par, tl)
            for i in range(4):
                e.bn_stats(lnstats[:, ring * 24 + 6 * i:ring * 24 + 6 * (i + 1)], zt[:, 512 * i:512 * (i + 1)])
            o = ring * 8
            e.drain()
            e.bn_aggr(lnmv[:, o:o + 2], lnstats[:, ring * 24:ring * 24 + 24]).then_inc(sV.h, 1)
        V(v_stats)
        lv_stats[li] = sV.inc()

        def s_inv2(e, li=li, ring=ring):
            e.wait_ge(sV.h, lv_stats[li])
            o = ring * 8
            e.activation(out=lnmv[:, o + 7:o + 8], in_=lnmv[:, o + 1:o + 2], func=AF.Ln, bias=LNB[:, 0:1])
            e.activation(out=lnmv[:, o + 2:o + 3], in_=lnmv[:, o + 7:o + 8], func=AF.Exp, scale=-0.5)
            e.activation(out=lnmv[:, o + 7:o + 8], in_=lnmv[:, o + 7:o + 8], func=AF.Exp, scale=-1.0)
            e.activation(out=lnmv[:, o + 4:o + 5], in_=lnmv[:, o + 7:o + 8], func=AF.Ln,
                         scale=-EPS, bias=LNB[:, 1:2])
            e.activation(out=lnmv[:, o + 4:o + 5], in_=lnmv[:, o + 4:o + 5], func=AF.Exp, scale=-0.5).then_inc(sS.h, 1)
        SC(s_inv2)
        ls_inv[li] = sS.inc()

        def v_small(e, li=li, ring=ring):
            e.wait_ge(sS.h, ls_inv[li])
            o = ring * 8
            e.scalar_tensor_tensor(lnmv[:, o + 3:o + 4], lnmv[:, o + 0:o + 1], -1.0,
                                   lnmv[:, o + 2:o + 3], ALU.mult, ALU.mult)
            e.tensor_tensor(lnmv[:, o + 5:o + 6], lnmv[:, o + 2:o + 3], lnmv[:, o + 4:o + 5], ALU.mult)
            e.tensor_tensor(lnmv[:, o + 6:o + 7], lnmv[:, o + 3:o + 4], lnmv[:, o + 4:o + 5], ALU.mult).then_inc(sV.h, 1)
        V(v_small)
        lv_sc2 = sV.inc()

        def s_apply(e, li=li, ring=ring, ering=ering, b=b, s=s, par=par, tl=tl, sbuf_i=sbuf_i, k=k, lv_sc2=lv_sc2):
            e.wait_ge(sV.h, lv_sc2)
            o = ring * 8
            zt = xz_t(sbuf_i, par, tl)
            if li == 4 and _os.environ.get("KDBG"):
                e.wait_ge(sC.h, DBGC[0])
            if li >= 4:
                e.wait_ge(sM4[li % 4].h, lm_mirror[li - 4])
                e.wait_ge(sV.h, lv_logit[li - 4])
            if li >= 2:
                e.wait_ge(sE2[li % 2].h, ly_early[li - 2])
            e.activation(out=x1st[:, ring * D:(ring + 1) * D], in_=zt[:, :], func=AF.Identity,
                         bias=lnmv[:, o + 3:o + 4], scale=lnmv[:, o + 2:o + 3])
            e.activation(out=outear[:, ering * D:(ering + 1) * D], in_=zt[:, :], func=AF.Identity,
                         bias=lnmv[:, o + 6:o + 7], scale=lnmv[:, o + 5:o + 6]).then_inc(sS.h, 1)
            e.wait_ge(sS.h, ls_apply[li])
            e.dma_start(out=x1m_h[b][128 * k:128 * (k + 1), :],
                        in_=x1st[:, ring * D:(ring + 1) * D]).then_inc(sM4[li % 4].h, 16)
        ls_apply[li] = sS.inc()
        SC(s_apply)
        lm_mirror[li] = sM4[li % 4].inc(16)

        def v_logit(e, li=li, ring=ring, b=b, k=k):
            e.wait_ge(sS.h, ls_apply[li])
            e.scalar_tensor_tensor(sttscr[:, :], x1st[:, ring * D:(ring + 1) * D], 1.0, WrB[:, :],
                                   ALU.mult, ALU.mult,
                                   accum_out=Lgrid[:, b * 16 + k:b * 16 + k + 1]).then_inc(sV.h, 1)
        pend_logit.append((li, v_logit))

        def y_early(e, li=li, ering=ering, b=b, s=s, par=par, tl=tl):
            e.wait_ge(sS.h, ls_apply[li])
            e.dma_start(out=xview(out_h, b, s, par, tl), in_=outear[:, ering * D:(ering + 1) * D]).then_inc(sE2[li % 2].h, 16)
        Y(y_early)
        ly_early[li] = sE2[li % 2].inc(16)

        if par == 1 and tl == 1:
            seg_consumed[(b, s)] = (ls_apply[li], uT_pv[last_unit_of_pair[(b, s, 1)]])

    router_state = {}

    def build_router_ops(b, prev_ext_t):
        st = {}
        router_state[b] = st
        def v_bnd1(e, b=b, prev_ext_t=prev_ext_t):
            e.wait_ge(sV.h, lv_logit[last_li_of_b[b]])
            if prev_ext_t is not None:
                e.wait_ge(sT.h, prev_ext_t)
            e.reduce_max(bndc[:, 0:1], Lgrid[:, b * 16:(b + 1) * 16], axis=AX.X)
            e.tensor_scalar(lbr_b[:, :], Lgrid[:, b * 16:(b + 1) * 16], -1.0, None, ALU.mult)
            e.reduce_max(bndc[:, 1:2], lbr_b[:, :], axis=AX.X).then_inc(sV.h, 1)
        st["vbnd1"] = sV.inc()
        yield ("V", v_bnd1)

        def t_bnd(e, b=b, st=st):
            e.wait_ge(sV.h, st["vbnd1"])
            e.transpose(pbank[0][0:1, 0:128], bndc[:, 0:1], IDTF[:, :])
            e.transpose(pbank[1][0:1, 0:128], bndc[:, 1:2], IDTF[:, :]).then_inc(sT.h, 1)
        st["tbnd"] = sT.inc()
        yield ("T", t_bnd)

        def v_bnd2(e, b=b, st=st):
            e.wait_ge(sT.h, st["tbnd"])
            e.reduce_max(bnds[:, 0:1], pbank[0][0:1, 0:128], axis=AX.X)
            e.reduce_max(bnds[:, 1:2], pbank[1][0:1, 0:128], axis=AX.X).then_inc(sV.h, 1)
        st["vbnd2"] = sV.inc()
        yield ("V", v_bnd2)

        def t_bnd2(e, b=b, st=st):
            e.wait_ge(sV.h, st["vbnd2"])
            e.matmul(pbank[0][:, 0:2], ONES1F[:, :], bnds[:, :], start=True, stop=True).then_inc(sT.h, 1)
        st["tbnd2"] = sT.inc()
        yield ("T", t_bnd2)

        def v_binit(e, b=b, st=st):
            e.wait_ge(sT.h, st["tbnd2"])
            e.tensor_copy(bndB[:, :], pbank[0][:, 0:2])
            e.drain()
            e.tensor_scalar(hi_b[:, b:b + 1], bndB[:, 0:1], 2.0, None, ALU.mult)
            e.tensor_scalar(lo_b[:, b:b + 1], bndB[:, 1:2], -2.0, -2.0, ALU.mult, ALU.add)
            e.tensor_scalar(L2g[:, :], Lgrid[:, b * 16:(b + 1) * 16], 2.0, None, ALU.mult)
            e.tensor_tensor(mid_b[:, b:b + 1], lo_b[:, b:b + 1], hi_b[:, b:b + 1], ALU.add)
            e.drain()
            with nc.allow_low_precision("bisection counts <=16 are exact in bf16"):
                e.tensor_scalar(tmp16[:, :], L2g[:, :], mid_b[:, b:b + 1], None, ALU.is_gt,
                                ALU.add, accum_out=prt_b[:, b:b + 1]).then_inc(sV.h, 1)
        st[("v", 0)] = sV.inc()
        yield ("V", v_binit)

        for it in range(NBIS):
            def t_it(e, b=b, it=it, st=st):
                e.wait_ge(sV.h, st[("v", it)])
                e.matmul(pbank[it % 2][:, 0:1], ONESC[:, :], prt_b[:, b:b + 1], start=True, stop=True).then_inc(sT.h, 1)
            st[("t", it)] = sT.inc()
            yield ("T", t_it)

            def v_it(e, b=b, it=it, st=st):
                e.wait_ge(sT.h, st[("t", it)])
                ps = pbank[it % 2]
                e.tensor_scalar(cge_b[:, b:b + 1], ps[:, 0:1], float(NSLOT) - 0.5, None, ALU.is_ge)
                e.tensor_tensor(dlt_b[:, b:b + 1], mid_b[:, b:b + 1], lo_b[:, b:b + 1], ALU.subtract)
                e.tensor_tensor(dlt2_b[:, b:b + 1], hi_b[:, b:b + 1], mid_b[:, b:b + 1], ALU.subtract)
                e.scalar_tensor_tensor(lo_b[:, b:b + 1], dlt_b[:, b:b + 1], cge_b[:, b:b + 1],
                                       lo_b[:, b:b + 1], ALU.mult, ALU.add)
                e.scalar_tensor_tensor(hi_b[:, b:b + 1], dlt2_b[:, b:b + 1], cge_b[:, b:b + 1],
                                       mid_b[:, b:b + 1], ALU.mult, ALU.add)
                last = None
                if it < NBIS - 1:
                    e.drain()
                    e.tensor_tensor(mid_b[:, b:b + 1], lo_b[:, b:b + 1], hi_b[:, b:b + 1], ALU.add)
                    e.drain()
                    with nc.allow_low_precision("bisection counts <=16 are exact in bf16"):
                        last = e.tensor_scalar(tmp16[:, :], L2g[:, :], mid_b[:, b:b + 1], None, ALU.is_gt,
                                               ALU.add, accum_out=prt_b[:, b:b + 1])
                else:
                    e.drain()
                    last = e.tensor_scalar(mask_b[:, b * 16:(b + 1) * 16], L2g[:, :],
                                           lo_b[:, b:b + 1], None, ALU.is_gt)
                last.then_inc(sV.h, 1)
            st[("v", it + 1)] = sV.inc()
            yield ("V", v_it)

        def v_fin(e, b=b, st=st):
            e.wait_ge(sV.h, st[("v", NBIS)])
            e.tensor_scalar(lbr_b[:, :], Lgrid[:, b * 16:(b + 1) * 16], bndB[:, 0:1], None,
                            ALU.subtract)
            e.tensor_copy(maskbf[:, :], mask_b[:, b * 16:(b + 1) * 16]).then_inc(sV.h, 1)
        st["vfin"] = sV.inc()
        yield ("V", v_fin)

        def s_wexp(e, b=b, st=st):
            e.wait_ge(sV.h, st["vfin"])
            e.activation(out=wexp[:, :], in_=lbr_b[:, :], func=AF.Exp).then_inc(sS.h, 1)
        st["swexp"] = sS.inc()
        yield ("S", s_wexp)

        def t_ct(e, b=b, st=st):
            e.wait_ge(sV.h, st["vfin"])
            e.matmul(pbank[0][0:16, 0:1], maskbf[:, :], ONESC[:, 0:1], start=True, stop=True).then_inc(sT.h, 1)
        st["tct"] = sT.inc()
        yield ("T", t_ct)

        def v_ct(e, b=b, st=st):
            e.wait_ge(sT.h, st["tct"])
            e.wait_ge(sS.h, st["swexp"])
            e.tensor_tensor(wm32[:, :], wexp[:, :], mask_b[:, b * 16:(b + 1) * 16], ALU.mult)
            e.tensor_copy(ct16[:, :], pbank[0][0:16, 0:1])
            e.reduce_sum(wprt[:, :], wm32[:, :], axis=AX.X).then_inc(sV.h, 1)
        st["vct"] = sV.inc()
        yield ("V", v_ct)

        def t_cb(e, b=b, st=st):
            e.wait_ge(sV.h, st["vct"])
            e.matmul(pbank[1][0:1, 0:16], ct16[:, :], TRI16[:, :], start=True, stop=True)
            e.matmul(pbank[0][0:1, 0:1], wprt[:, :], ONESCF[:, :], start=True, stop=True).then_inc(sT.h, 1)
        st["tcb"] = sT.inc()
        yield ("T", t_cb)

        def v_cb(e, b=b, st=st):
            e.wait_ge(sT.h, st["tcb"])
            e.tensor_copy(zt_s[:, b:b + 1], pbank[0][0:1, 0:1])
            e.tensor_copy(cb16[:, :], pbank[1][0:1, 0:16])
            e.reciprocal(zt_s[:, b:b + 1], zt_s[:, b:b + 1]).then_inc(sV.h, 1)
        st["vcb"] = sV.inc()
        yield ("V", v_cb)

        def t_pos(e, b=b, st=st):
            e.wait_ge(sV.h, st["vcb"])
            e.matmul(pbank[0][:, 0:16], ONESC[0:1, :], cb16[:, :], start=True, stop=True)
            e.matmul(pbank[1][:, 0:16], TRI[:, :], maskbf[:, :], start=True, stop=True).then_inc(sT.h, 1)
        st["tpos"] = sT.inc()
        yield ("T", t_pos)

        def v_pos(e, b=b, st=st):
            e.wait_ge(sT.h, st["tpos"])
            e.tensor_copy(pos_b[:, :], pbank[0][:, 0:16])
            e.tensor_copy(vspc[:, 0:1], vspc[:, 1:2])
            e.tensor_tensor(pos_b[:, :], pos_b[:, :], pbank[1][:, 0:16], ALU.add)
            e.drain()
            for kk in range(16):
                e.tensor_scalar(PT[:, 256 * kk:256 * (kk + 1)], IO256[:, :], pos_b[:, kk:kk + 1],
                                mask_b[:, b * 16 + kk:b * 16 + kk + 1], ALU.is_equal, ALU.mult)
            e.tensor_copy(VALS[:, 0:48:3], SG[:, :])
            e.tensor_copy(VALS[:, 1:48:3], TG[:, :])
            last = None
            for kk in range(16):
                last = e.tensor_scalar(VALS[:, 3 * kk + 2:3 * kk + 3], wm32[:, kk:kk + 1], 1.0, None, ALU.mult)
            last.then_inc(sV.h, 1)
        st["vpos"] = sV.inc()
        yield ("V", v_pos)

        def t_rz(e, b=b, st=st):
            e.wait_ge(sV.h, st["vpos"])
            e.matmul(pbank[0][:, 0:1], ONES1F[:, :], zt_s[:, b:b + 1], start=True, stop=True).then_inc(sT.h, 1)
        st["trz"] = sT.inc()
        yield ("T", t_rz)

        def v_rz(e, b=b, st=st):
            e.wait_ge(sT.h, st["trz"])
            e.tensor_copy(rzB[:, :], pbank[0][:, 0:1]).then_inc(sV.h, 1)
        st["vrz"] = sV.inc()
        yield ("V", v_rz)

        for half in range(2):
            def t_ext(e, b=b, half=half, st=st):
                e.wait_ge(sV.h, st["vrz"] if half == 0 else st[("vext", 0)])
                last = None
                for kk in range(16):
                    last = e.matmul(pbank[half][:, 0:3],
                                    PT[:, 256 * kk + 128 * half:256 * kk + 128 * (half + 1)],
                                    VALS[:, 3 * kk:3 * (kk + 1)], start=(kk == 0), stop=(kk == 15))
                last.then_inc(sT.h, 1)
            st[("text", half)] = sT.inc()
            yield ("T", t_ext)

            def v_ext(e, b=b, half=half, st=st):
                e.wait_ge(sT.h, st[("text", half)])
                stg = b * 2 + half
                o = stg * 4
                e.tensor_copy(slotv[:, o:o + 3], pbank[half][:, 0:3])
                e.tensor_copy(vspc[:, 0:1], vspc[:, 1:2])
                e.tensor_tensor(slotv[:, o + 3:o + 4], slotv[:, o + 2:o + 3], rzB[:, :], ALU.mult)
                e.tensor_copy(sigI[:, stg:stg + 1], slotv[:, o:o + 1])
                e.tensor_copy(tI[:, stg:stg + 1], slotv[:, o + 1:o + 2]).then_inc(sV.h, 1)
            st[("vext", half)] = sV.inc()
            yield ("V", v_ext)

        router_state[b] = st

    emit_load(0)
    emit_load(1)
    emit_cast(0)
    emit_cast(1)

    n_units_per_pair = 16
    r0gen = [None]

    def splice(n):
        if r0gen[0] is None:
            return
        for _ in range(n):
            try:
                t_, f_ = next(r0gen[0])
            except StopIteration:
                r0gen[0] = None
                return
            OPS[t_].append(f_)

    DBGC = [0]
    for gi, (b, s) in enumerate(pair_list):
        logs = pend_logit[:]
        del pend_logit[:]
        for u in range(n_units_per_pair):
            emit_attn_unit(gi * n_units_per_pair + u)
            if logs and u < 2:
                for _ in range(2):
                    li_, fn_ = logs.pop(0)
                    V(fn_)
                    lv_logit[li_] = sV.inc()
            if b == 1 and not logs:
                splice(2)
        if pend_tpv[0] is not None:
            fnp, pu = pend_tpv[0]
            T(fnp)
            uT_pv[pu] = sT.inc()
            pend_tpv[0] = None
        if pend_vz[0] is not None:
            fn, pu = pend_vz[0]
            import functools
            V(functools.partial(fn, flushed=True))
            uV_z[pu] = sV.inc()
            pend_vz[0] = None
        for li, lu in enumerate(ln_units):
            if lu[0] == gi:
                emit_ln_unit(li)
        if gi == 0 and _os.environ.get("KDBG"):
            def g_dbg(e, li_p0=3):
                e.wait_ge(sS.h, ls_apply[li_p0])
                e.dma_start(out=dbg_h[:, 0:2048], in_=xz[:, 0:2048]).then_inc(sC.h, 16)
                e.dma_start(out=dbg_h[:, 2048:4096], in_=xbuf[:, 0:2048]).then_inc(sC.h, 16)
                e.dma_start(out=dbg_h[:, 4096:6144], in_=x1st[:, 0:2048]).then_inc(sC.h, 16)
                e.dma_start(out=dbg_h[0:128, 6144:6176], in_=lnmv[:, :]).then_inc(sC.h, 16)
            G(g_dbg)
            DBGC[0] = sC.inc(16 * 4)
        if gi + 2 < len(pair_list):
            emit_load(gi + 2)
            emit_cast(gi + 2)
        if b == 0 and s == NSEG - 1:
            r0gen[0] = build_router_ops(0, None)

    for li_, fn_ in pend_logit:
        V(fn_)
        lv_logit[li_] = sV.inc()
    del pend_logit[:]

    splice(9999)

    for t_, f_ in build_router_ops(1, router_state[0][("text", 1)]):
        OPS[t_].append(f_)


    esA.close()
    esB = ExitStack()
    g1T = esB.enter_context(nc.sbuf_tensor("g1T", [128, 64 * 512], BF16))
    w1cb = esB.enter_context(nc.sbuf_tensor("w1cb", [128, 3 * 4096], BF16))
    h2sb = esB.enter_context(nc.sbuf_tensor("h2sb", [128, 2 * 512], BF16))
    w2cb = esB.enter_context(nc.sbuf_tensor("w2cb", [128, 2 * 8192], BF16))
    ybuf = esB.enter_context(nc.sbuf_tensor("ybuf", [128, 4 * D], F32))
    fxb = esB.enter_context(nc.sbuf_tensor("fxb", [128, 4 * D], BF16))
    fxT = esB.enter_context(nc.sbuf_tensor("fxT", [128, 16 * 512], BF16))
    outro = esB.enter_context(nc.sbuf_tensor("outro", [128, 2 * D], F32))

    gath_done = {}
    for b in range(NB):
        for half in range(2):
            stg = b * 2 + half

            def g_gather(e, b=b, half=half, stg=stg):
                if stg == 0:
                    tS, tT = seg_consumed[(1, NSEG - 1)]
                    e.wait_ge(sS.h, tS)
                    e.wait_ge(sT.h, tT)
                    e.wait_ge(sV.h, router_state[1][("vext", 1)])
                    for mm in range(4):
                        e.wait_ge(sM4[mm].h, sM4[mm].n)
                else:
                    e.wait_ge(sGG.h, gath_done[stg - 1])
                e.wait_ge(sV.h, router_state[b][("vext", half)])
                e.indirect_dma_start(fxb[:, stg * D:(stg + 1) * D], None,
                                     x1m_h.rearrange("b s d -> (b s) d"),
                                     IndirectOffsetOnAxis(sigI[:, stg:stg + 1], 0),
                                     element_offset=b * S * D).then_inc(sGG.h, 16)
            G(g_gather)
            gath_done[stg] = sGG.inc(16)

    fxT_done = {}
    tp_evac = {}
    for stg in range(4):
        for t in range(16):
            idx = stg * 16 + t

            def t_fxT(e, stg=stg, t=t, idx=idx):
                e.wait_ge(sGG.h, gath_done[stg])
                if idx == 0:
                    e.wait_ge(sV.h, uV_gT[len(attn_units) - 1])
                if idx >= 2:
                    e.wait_ge(sV.h, tp_evac[idx - 2])
                e.transpose(pbank_bf[idx % 2][:, 0:128], fxb[:, stg * D + 128 * t:stg * D + 128 * (t + 1)],
                            IDT[:, :]).then_inc(sT.h, 1)
            T(t_fxT)
            fxT_done[idx] = sT.inc()

            def v_fxT(e, stg=stg, t=t, idx=idx):
                e.wait_ge(sT.h, fxT_done[idx])
                e.tensor_copy(fxT[:, 512 * t + 128 * stg:512 * t + 128 * (stg + 1)],
                              pbank_bf[idx % 2][:, 0:128]).then_inc(sV.h, 1)
            V(v_fxT)
            tp_evac[idx] = sV.inc()
    FXT_ALL = sV.n

    w1_done = {}
    h1_done = {}
    g1_done = {}
    for fb in range(32):
        def g_w1(e, fb=fb):
            if fb >= 3:
                e.wait_ge(sT.h, h1_done[2 * (fb - 3) + 1])
            e.dma_start(out=w1cb[:, (fb % 3) * 4096:(fb % 3 + 1) * 4096],
                        in_=w1_h[:, 256 * fb:256 * (fb + 1)].rearrange("(t p) f -> p t f", p=128)).then_inc(sW1[fb % 3].h, 16)
        G(g_w1)
        w1_done[fb] = sW1[fb % 3].inc(16)

    for f in range(64):
        fb = f // 2
        fo = f % 2

        def t_h1(e, f=f, fb=fb, fo=fo):
            if fo == 0:
                e.wait_ge(sW1[fb % 3].h, w1_done[fb])
            if f == 0:
                e.wait_ge(sV.h, FXT_ALL)
            if f >= 4:
                e.wait_ge(sS.h, g1_done[f - 4])
            bk = f % 4
            base = (fb % 3) * 4096 + fo * 128
            last = None
            for t in range(16):
                last = e.matmul(pbank[bk][:, 0:512],
                                w1cb[:, base + 256 * t:base + 256 * t + 128],
                                fxT[:, 512 * t:512 * (t + 1)], start=(t == 0), stop=(t == 15))
            last.then_inc(sT.h, 1)
        T(t_h1)
        h1_done[f] = sT.inc()

        def s_g1(e, f=f):
            e.wait_ge(sT.h, h1_done[f])
            bk = f % 4
            e.activation(out=g1T[:, 512 * f:512 * (f + 1)], in_=pbank[bk][:, 0:512], func=AF.Gelu,
                         bias=b1T[:, f:f + 1]).then_inc(sS.h, 1)
        SC(s_g1)
        g1_done[f] = sS.inc()

    w2_done = {}
    h2_done = {}
    h2s_done = {}
    h2t_done = {}
    yev_done = {}
    for c in range(16):
        def g_w2(e, c=c):
            if c == 0:
                tS, tT = seg_consumed[(1, NSEG - 1)]
                e.wait_ge(sS.h, tS)
                e.wait_ge(sT.h, tT)
            if c >= 2:
                e.wait_ge(sT.h, h2_done[c - 2])
            e.dma_start(out=w2cb[:, (c % 2) * 8192:(c % 2 + 1) * 8192],
                        in_=w2_h[:, 128 * c:128 * (c + 1)].rearrange("(t p) f -> p t f", p=128)).then_inc(sW2[c % 2].h, 16)
        G(g_w2)
        w2_done[c] = sW2[c % 2].inc(16)

    for c in range(16):
        def t_h2(e, c=c):
            e.wait_ge(sW2[c % 2].h, w2_done[c])
            if c == 0:
                e.wait_ge(sS.h, g1_done[63])
            if c >= 2:
                e.wait_ge(sS.h, h2s_done[c - 2])
            bk = 4 + c % 2
            last = None
            for f in range(64):
                last = e.matmul(pbank[bk][:, 0:512],
                                w2cb[:, (c % 2) * 8192 + 128 * f:(c % 2) * 8192 + 128 * (f + 1)],
                                g1T[:, 512 * f:512 * (f + 1)], start=(f == 0), stop=(f == 63))
            last.then_inc(sT.h, 1)
        T(t_h2)
        h2_done[c] = sT.inc()

        def s_h2(e, c=c):
            e.wait_ge(sT.h, h2_done[c])
            if c >= 2:
                e.wait_ge(sT.h, h2t_done[(c - 2, 3)])
            bk = 4 + c % 2
            e.activation(out=h2sb[:, (c % 2) * 512:(c % 2 + 1) * 512], in_=pbank[bk][:, 0:512],
                         func=AF.Identity, bias=b2T[:, c:c + 1]).then_inc(sS.h, 1)
        SC(s_h2)
        h2s_done[c] = sS.inc()

        for stg in range(4):
            idx = c * 4 + stg

            def t_h2t(e, c=c, stg=stg, idx=idx):
                e.wait_ge(sS.h, h2s_done[c])
                if idx >= 2:
                    e.wait_ge(sV.h, yev_done[idx - 2])
                e.transpose(pbank_bf[idx % 2][:, 0:128],
                            h2sb[:, (c % 2) * 512 + 128 * stg:(c % 2) * 512 + 128 * (stg + 1)],
                            IDT[:, :]).then_inc(sT.h, 1)
            T(t_h2t)
            h2t_done[(c, stg)] = sT.inc()

            def v_yev(e, c=c, stg=stg, idx=idx):
                e.wait_ge(sT.h, h2t_done[(c, stg)])
                e.tensor_scalar(ybuf[:, stg * D + 128 * c:stg * D + 128 * (c + 1)],
                                pbank_bf[idx % 2][:, 0:128],
                                slotv[:, stg * 4 + 3:stg * 4 + 4], None, ALU.mult).then_inc(sV.h, 1)
            V(v_yev)
            yev_done[idx] = sV.inc()

    fin_scat = {}
    fin_conv = {}
    for stg in range(4):
        b = stg // 2

        def s_conv(e, stg=stg):
            e.wait_ge(sGG.h, gath_done[stg])
            if stg >= 2:
                e.wait_ge(sGG.h, fin_scat[stg - 2])
            e.activation(out=outro[:, (stg % 2) * D:(stg % 2 + 1) * D],
                         in_=fxb[:, stg * D:(stg + 1) * D], func=AF.Identity,
                         scale=2.0).then_inc(sS.h, 1)
        SC(s_conv)
        fin_conv[stg] = sS.inc()

        def v_fin2(e, stg=stg):
            e.wait_ge(sV.h, yev_done[15 * 4 + stg])
            e.wait_ge(sS.h, fin_conv[stg])
            yt = ybuf[:, stg * D:(stg + 1) * D]
            e.tensor_tensor(yt[:, :], yt[:, :], outro[:, (stg % 2) * D:(stg % 2 + 1) * D], ALU.add)
            o = stg * 8
            e.drain()
            for i in range(4):
                e.bn_stats(lnstats[:, stg * 24 + 6 * i:stg * 24 + 6 * (i + 1)], yt[:, 512 * i:512 * (i + 1)])
            e.drain()
            e.bn_aggr(lnmv[:, o:o + 2], lnstats[:, stg * 24:stg * 24 + 24])
            e.drain()
            e.tensor_scalar(lnmv[:, o + 1:o + 2], lnmv[:, o + 1:o + 2], EPS, None, ALU.add)
            e.drain()
            e.reciprocal(lnmv[:, o + 1:o + 2], lnmv[:, o + 1:o + 2]).then_inc(sV.h, 1)
        V(v_fin2)
        fin_stats = sV.inc()

        def s_fin2(e, stg=stg, fin_stats=fin_stats):
            e.wait_ge(sV.h, fin_stats)
            o = stg * 8
            e.activation(out=lnmv[:, o + 1:o + 2], in_=lnmv[:, o + 1:o + 2], func=AF.Ln)
            e.activation(out=lnmv[:, o + 1:o + 2], in_=lnmv[:, o + 1:o + 2], func=AF.Exp, scale=0.5).then_inc(sS.h, 1)
        SC(s_fin2)
        fin_inv = sS.inc()

        def v_fin3(e, stg=stg, fin_inv=fin_inv):
            e.wait_ge(sS.h, fin_inv)
            o = stg * 8
            e.scalar_tensor_tensor(lnmv[:, o + 3:o + 4], lnmv[:, o + 0:o + 1], -1.0,
                                   lnmv[:, o + 1:o + 2], ALU.mult, ALU.mult).then_inc(sV.h, 1)
        V(v_fin3)
        fin_b = sV.inc()

        def s_fin3(e, stg=stg, fin_b=fin_b):
            e.wait_ge(sV.h, fin_b)
            o = stg * 8
            e.activation(out=outro[:, (stg % 2) * D:(stg % 2 + 1) * D], in_=ybuf[:, stg * D:(stg + 1) * D],
                         func=AF.Identity, bias=lnmv[:, o + 3:o + 4],
                         scale=lnmv[:, o + 1:o + 2]).then_inc(sS.h, 1)
        SC(s_fin3)
        fin_app = sS.inc()

        def g_scat(e, stg=stg, b=b, fin_app=fin_app, prev_gg=sGG.n):
            e.wait_ge(sS.h, fin_app)
            e.wait_ge(sGG.h, prev_gg)
            if stg == 0:
                for ee in range(2):
                    e.wait_ge(sE2[ee].h, sE2[ee].n)
            e.indirect_dma_start(out_h.rearrange("b s d -> (b s) d"),
                                 IndirectOffsetOnAxis(tI[:, stg:stg + 1], 0),
                                 outro[:, (stg % 2) * D:(stg % 2 + 1) * D], None,
                                 element_offset=b * S * D).then_inc(sGG.h, 16)
        G(g_scat)
        fin_scat[stg] = sGG.inc(16)

    FINAL_GG = sGG.n
    FINAL_E = [s.n for s in sE2]

    def g_end(e):
        e.wait_ge(sGG.h, FINAL_GG)
        for i in range(2):
            e.wait_ge(sE2[i].h, FINAL_E[i])
    G(g_end)

    def y_end(e):
        e.wait_ge(sGG.h, FINAL_GG)
        for i in range(2):
            e.wait_ge(sE2[i].h, FINAL_E[i])
    Y(y_end)

    import os
    _tr = float(os.environ.get("KTRUNC", "1.0"))
    if _tr < 1.0:
        for _k in OPS:
            OPS[_k] = OPS[_k][:max(1, int(len(OPS[_k]) * _tr))]
    with nc.Block() as block:
        @block.tensor
        def _(eng):
            for fn in OPS["T"]:
                fn(eng)

        @block.vector
        def _(eng):
            for fn in OPS["V"]:
                fn(eng)

        @block.scalar
        def _(eng):
            for fn in OPS["S"]:
                fn(eng)

        @block.gpsimd
        def _(eng):
            for fn in OPS["G"]:
                fn(eng)

        @block.sync
        def _(eng):
            for fn in OPS["Y"]:
                fn(eng)

    esB.close()
    es.close()
    return nc


def _consts():
    c = {}
    c["IDT"] = np.eye(128, dtype=np.float32).astype(ml_dtypes.bfloat16)
    c["IDTF"] = np.eye(128, dtype=np.float32)
    c["TRI"] = np.triu(np.ones((128, 128), np.float32), 1).astype(ml_dtypes.bfloat16)
    c["TRI16"] = np.triu(np.ones((16, 16), np.float32), 1).astype(ml_dtypes.bfloat16)
    c["ONESC"] = np.ones((128, 128), np.float32).astype(ml_dtypes.bfloat16)
    c["ONES1F"] = np.ones((1, 128), np.float32)
    c["ONESCF"] = np.ones((128, 1), np.float32)
    c["IO256"] = np.broadcast_to(np.arange(256, dtype=np.float32)[None, :], (128, 256)).copy()
    c["LNB"] = np.broadcast_to(np.array([[1e-5, 1.0 + 1e-5 / 4.0]], np.float32), (128, 2)).copy()
    q = np.arange(128)[:, None]
    k = np.arange(16)[None, :]
    c["SG"] = (128 * k + q).astype(np.float16)
    s_ = k // 4
    par_ = (k % 4) // 2
    tl_ = k % 2
    c["TG"] = (512 * s_ + 256 * tl_ + 2 * q + par_).astype(np.float16)
    return c


_NC_CACHE = {}


def kernel(L0_attn_x, L0_ffn_Wr, L0_ffn_br, L0_ffn_W1, L0_ffn_b1, L0_ffn_W2, L0_ffn_b2):
    x = np.ascontiguousarray(np.asarray(L0_attn_x, dtype=np.float32))
    Wr = np.ascontiguousarray(np.asarray(L0_ffn_Wr, dtype=np.float32).reshape(1, D))
    br = np.ascontiguousarray(np.asarray(L0_ffn_br, dtype=np.float32).reshape(1, 1))
    W1 = np.ascontiguousarray(np.asarray(L0_ffn_W1, dtype=np.float32))
    b1 = np.ascontiguousarray(np.asarray(L0_ffn_b1, dtype=np.float32).reshape(64, 128).T)
    W2 = np.ascontiguousarray(np.asarray(L0_ffn_W2, dtype=np.float32))
    b2 = np.ascontiguousarray(np.asarray(L0_ffn_b2, dtype=np.float32).reshape(16, 128).T)
    consts = _consts()

    if "nc" not in _NC_CACHE:
        _NC_CACHE["nc"] = build_nc()
    nc = _NC_CACHE["nc"]

    in_maps = []
    for c in range(8):
        m = {"x": np.ascontiguousarray(x[2 * c:2 * c + 2]),
             "Wr": Wr, "br": br, "W1": W1, "b1": b1, "W2": W2, "b2": b2}
        m.update(consts)
        in_maps.append(m)
    res = run_bass_kernel_spmd(nc, in_maps, core_ids=list(range(8)))
    out = np.concatenate([res.results[i]["out"] for i in range(8)], axis=0)
    return out.astype(np.float32)


# version 11
# speedup vs baseline: 1.3064x; 1.1262x over previous
import os as _os
import numpy as np
import ml_dtypes
import concourse.bass as bass
import concourse.mybir as mybir
from concourse.bass_utils import run_bass_kernel_spmd
from concourse.bass_core_rust import IndirectOffsetOnAxis

F32 = mybir.dt.float32
BF16 = mybir.dt.bfloat16
FP16 = mybir.dt.float16
I32 = mybir.dt.int32
AF = mybir.ActivationFunctionType
ALU = mybir.AluOpType
AX = mybir.AxisListType

NB = 2
NSEG = 4
D = 2048
S = 2048
DFF = 8192
NSLOT = 256
SCALE = float(1.0 / np.sqrt(128.0))
EPS = 1e-5
NBIS = 33
WOFF = 16.0


class Sem:
    def __init__(self, es, nc, name):
        self.h = es.enter_context(nc.semaphore(name))
        self.n = 0

    def inc(self, k=1):
        self.n += k
        return self.n


def build_nc():
    from contextlib import ExitStack
    nc = bass.Bass("TRN2", target_bir_lowering=False, detect_race_conditions=False)
    es = ExitStack()

    x_h = nc.dram_tensor("x", [NB, S, D], F32, kind="ExternalInput").ap()
    wr_h = nc.dram_tensor("Wr", [1, D], F32, kind="ExternalInput").ap()
    br_h = nc.dram_tensor("br", [1, 1], F32, kind="ExternalInput").ap()
    w1_h = nc.dram_tensor("W1", [D, DFF], F32, kind="ExternalInput").ap()
    b1_h = nc.dram_tensor("b1", [128, 64], F32, kind="ExternalInput").ap()
    w2_h = nc.dram_tensor("W2", [DFF, D], F32, kind="ExternalInput").ap()
    b2_h = nc.dram_tensor("b2", [128, 16], F32, kind="ExternalInput").ap()
    idt_h = nc.dram_tensor("IDT", [128, 128], BF16, kind="ExternalInput").ap()
    idtf_h = nc.dram_tensor("IDTF", [128, 128], F32, kind="ExternalInput").ap()
    tri_h = nc.dram_tensor("TRI", [128, 128], BF16, kind="ExternalInput").ap()
    tri16_h = nc.dram_tensor("TRI16", [16, 16], BF16, kind="ExternalInput").ap()
    onesc_h = nc.dram_tensor("ONESC", [128, 128], BF16, kind="ExternalInput").ap()
    ones1f_h = nc.dram_tensor("ONES1F", [1, 128], F32, kind="ExternalInput").ap()
    onescf_h = nc.dram_tensor("ONESCF", [128, 1], F32, kind="ExternalInput").ap()
    io256_h = nc.dram_tensor("IO256", [128, 256], F32, kind="ExternalInput").ap()
    lnb_h = nc.dram_tensor("LNB", [128, 2], F32, kind="ExternalInput").ap()
    sg_h = nc.dram_tensor("SG", [128, 16], FP16, kind="ExternalInput").ap()
    tg_h = nc.dram_tensor("TG", [128, 16], FP16, kind="ExternalInput").ap()
    out_h = nc.dram_tensor("out", [NB, S, D], F32, kind="ExternalOutput").ap()
    dbg_h = nc.dram_tensor("dbg", [128, 6176], F32, kind="ExternalOutput").ap()
    x1m_h = nc.dram_tensor("x1m", [NB, S, D], BF16, kind="ExternalOutput" if _os.environ.get("KDBG") else "Internal").ap()

    def xview(h, b, s, par, tl):
        return h[b].rearrange("(s tl p two) d -> s two tl p d", s=4, tl=2, p=128, two=2)[s, par, tl]

    sb = lambda name, shape, dt: es.enter_context(nc.sbuf_tensor(name, shape, dt))
    IDT = sb("c_idt", [128, 128], BF16)
    IDTF = sb("c_idtf", [128, 128], F32)
    TRI = sb("c_tri", [128, 128], BF16)
    TRI16 = sb("c_tri16", [16, 16], BF16)
    ONESC = sb("c_onesc", [128, 128], BF16)
    ONES1F = sb("c_ones1f", [1, 128], F32)
    ONESCF = sb("c_onescf", [128, 1], F32)
    IO256 = sb("c_io256", [128, 256], F32)
    LNB = sb("c_lnb", [128, 2], F32)
    SG = sb("c_sg", [128, 16], FP16)
    TG = sb("c_tg", [128, 16], FP16)
    brS = sb("c_brs", [1, 1], F32)
    brB = sb("c_brb", [128, 1], F32)
    b1T = sb("c_b1t", [128, 64], F32)
    b2T = sb("c_b2t", [128, 16], F32)

    lnstats = sb("lnstats", [128, 4 * 24], F32)
    lnmv = sb("lnmv", [128, 4 * 8], F32)
    lo_b = sb("lo", [128, 2], F32)
    hi_b = sb("hi", [128, 2], F32)
    mid_b = sb("mid", [128, 2], F32)
    tmp16 = sb("tmp16", [128, 16], BF16)
    prt_b = sb("prt", [128, 2], BF16)
    cge_b = sb("cge", [128, 2], F32)
    dlt_b = sb("dlt", [128, 2], F32)
    dlt2_b = sb("dlt2", [128, 2], F32)
    L2g = sb("L2g", [128, 16], F32)
    vspc = sb("vspc", [128, 2], F32)
    mask_b = sb("mask", [128, 2 * 16], F32)
    maskbf = sb("maskbf", [128, 16], BF16)
    ct16 = sb("ct16", [16, 1], BF16)
    cb16 = sb("cb16", [1, 16], BF16)
    pos_b = sb("pos", [128, 16], F32)
    lbr_b = sb("lbr", [128, 16], F32)
    wexp = sb("wexp", [128, 16], F32)
    wm32 = sb("wm32", [128, 16], F32)
    wprt = sb("wprt", [128, 1], F32)
    bndc = sb("bndc", [128, 2], F32)
    bnds = sb("bnds", [1, 2], F32)
    bndB = sb("bndB", [128, 2], F32)
    VALS = sb("VALS", [128, 16 * 3], FP16)
    zt_s = sb("zt", [1, 2], F32)
    rzB = sb("rzB", [128, 1], F32)
    slotv = sb("slotv", [128, 4 * 4], F32)
    sigI = sb("sigI", [128, 4], I32)
    tI = sb("tI", [128, 4], I32)

    pbank = [es.enter_context(nc.psum_tensor(f"pb{i}", [128, 512], F32)) for i in range(6)]
    pbank_bf = [es.enter_context(nc.psum_tensor(f"pbb{i}", [128, 512], BF16)) for i in range(2)]

    sT = Sem(es, nc, "sT")
    sV = Sem(es, nc, "sV")
    sS = Sem(es, nc, "sS")
    sC = Sem(es, nc, "sC")
    sWrt = Sem(es, nc, "sWrt")
    sX = [Sem(es, nc, f"sX{i}") for i in range(2)]
    sXB = [Sem(es, nc, f"sXB{i}") for i in range(2)]
    sE2 = [Sem(es, nc, f"sE2{i}") for i in range(2)]
    sM4 = [Sem(es, nc, f"sM4{i}") for i in range(4)]
    sW1 = [Sem(es, nc, f"sW1{i}") for i in range(3)]
    sW2 = [Sem(es, nc, f"sW2{i}") for i in range(2)]
    sWF = Sem(es, nc, "sWF")
    sGG = Sem(es, nc, "sGG")

    OPS = {"T": [], "V": [], "S": [], "G": [], "Y": []}
    T = lambda fn: OPS["T"].append(fn)
    V = lambda fn: OPS["V"].append(fn)
    SC = lambda fn: OPS["S"].append(fn)
    G = lambda fn: OPS["G"].append(fn)
    Y = lambda fn: OPS["Y"].append(fn)

    def setup_loads(e):
        for dst, src in ((IDT, idt_h), (IDTF, idtf_h), (TRI, tri_h), (TRI16, tri16_h), (ONESC, onesc_h),
                         (ONES1F, ones1f_h), (ONESCF, onescf_h), (IO256, io256_h),
                         (LNB, lnb_h), (SG, sg_h), (TG, tg_h), (brS, br_h)):
            e.dma_start(out=dst[:, :], in_=src[:, :]).then_inc(sC.h, 16)
        e.dma_start(out=b1T[:, :], in_=b1_h[:, :]).then_inc(sC.h, 16)
        e.dma_start(out=b2T[:, :], in_=b2_h[:, :]).then_inc(sC.h, 16)
    Y(setup_loads)
    LD_CONST = sC.inc(16 * 14)

    def setup_wrt(e):
        e.dma_start(out=WrT[:, :], in_=wr_h[:, :]).then_inc(sWrt.h, 16)
    G(setup_wrt)
    LD_WRT = sWrt.inc(16)

    def setup_bcast_t(e):
        e.wait_ge(sC.h, LD_CONST)
        e.wait_ge(sWrt.h, LD_WRT)
        for i in range(4):
            e.matmul(pbank[i][:, :], ONESC[0:1, :], WrT[:, 512 * i:512 * (i + 1)], start=True, stop=True)
        e.matmul(pbank[4][:, 0:1], ONES1F[:, :], brS[:, :], start=True, stop=True).then_inc(sT.h, 1)
    T(setup_bcast_t)
    T_BCAST = sT.inc()

    def setup_bcast_v(e):
        e.memset(vspc[:, :], 0.0)
        e.wait_ge(sT.h, T_BCAST)
        for i in range(4):
            e.tensor_copy(WrB[:, 512 * i:512 * (i + 1)], pbank[i][:, :])
        e.tensor_copy(brB[:, :], pbank[4][:, 0:1]).then_inc(sV.h, 1)
    V(setup_bcast_v)
    V_BCAST = sV.inc()

    esA = ExitStack()
    WrT = esA.enter_context(nc.sbuf_tensor("c_wrt", [1, D], BF16))
    WrB = esA.enter_context(nc.sbuf_tensor("c_wrb", [128, D], BF16))
    xz = esA.enter_context(nc.sbuf_tensor("xz", [128, 2 * 4 * D], F32))
    xbuf = esA.enter_context(nc.sbuf_tensor("xbw", [128, 2 * 4 * D], BF16))
    x1st = esA.enter_context(nc.sbuf_tensor("x1stage", [128, 4 * D], BF16))
    sttscr = esA.enter_context(nc.sbuf_tensor("sttscr", [128, D], BF16))
    Lgrid = esA.enter_context(nc.sbuf_tensor("Lgrid", [128, 2 * 16], F32))
    outear = esA.enter_context(nc.sbuf_tensor("outear", [128, 2 * D], F32))
    PT = esA.enter_context(nc.sbuf_tensor("PT", [128, 16 * 256], FP16))
    gT = esA.enter_context(nc.sbuf_tensor("gT", [128, 2 * 256], BF16))
    Ebuf = esA.enter_context(nc.sbuf_tensor("Ebuf", [128, 2 * 512], BF16))
    Dbuf = esA.enter_context(nc.sbuf_tensor("Dbuf", [128, 2 * 2], F32))
    Rbuf = esA.enter_context(nc.sbuf_tensor("Rbuf", [128, 2 * 2], F32))

    def xz_t(sbuf_i, par, tl):
        off = (sbuf_i * 4 + par * 2 + tl) * D
        return xz[:, off:off + D]

    def xb_t(sbuf_i, par, tl):
        off = (sbuf_i * 4 + par * 2 + tl) * D
        return xbuf[:, off:off + D]

    pair_list = [(b, s) for b in range(NB) for s in range(NSEG)]
    xload_done = {}
    xcast_done = {}
    seg_consumed = {}

    uT_tp, uT_sc, uT_pv = {}, {}, {}
    uT_sc1 = {}
    uS_E = {}
    uV_gT, uV_R, uV_z = {}, {}, {}
    ls_apply, lv_logit, lm_mirror, ly_early = {}, {}, {}, {}
    lv_stats, ls_inv, lv_bias = {}, {}, {}

    pend_vz = [None]
    pend_tpv = [None]
    pend_logit = []
    attn_units = []
    for b, s in pair_list:
        for par in range(2):
            for hh in range(8):
                attn_units.append((b, s, par, hh))
    last_unit_of_pair = {}
    for ui, u in enumerate(attn_units):
        last_unit_of_pair[(u[0], u[1], u[2])] = ui

    ln_units = []
    for gi, (b, s) in enumerate(pair_list):
        for par in range(2):
            for tl in range(2):
                ln_units.append((gi, b, s, par, tl))
    last_li_of_b = {b: max(i for i, u in enumerate(ln_units) if u[1] == b) for b in range(NB)}

    def emit_load(gi):
        b, s = pair_list[gi]
        sbuf_i = gi % 2

        def loadseg(e, b=b, s=s, sbuf_i=sbuf_i, gi=gi):
            if gi == 2 and _os.environ.get("KDBG"):
                e.wait_ge(sC.h, DBGC[0])
            if gi >= 2:
                tS, tT = seg_consumed[pair_list[gi - 2]]
                e.wait_ge(sS.h, tS)
                e.wait_ge(sT.h, tT)
            for par in range(2):
                for tl in range(2):
                    e.dma_start(out=xz_t(sbuf_i, par, tl), in_=xview(x_h, b, s, par, tl)).then_inc(sX[gi % 2].h, 16)
        Y(loadseg)
        xload_done[(b, s)] = (gi % 2, sX[gi % 2].inc(16 * 4))

    def emit_cast(gi):
        b, s = pair_list[gi]
        sbuf_i = gi % 2

        def castseg(e, b=b, s=s, sbuf_i=sbuf_i, gi=gi):
            if gi == 2 and _os.environ.get("KDBG"):
                e.wait_ge(sC.h, DBGC[0])
            xi, xv = xload_done[(b, s)]
            e.wait_ge(sX[xi].h, xv)
            if gi >= 2:
                tS, tT = seg_consumed[pair_list[gi - 2]]
                e.wait_ge(sS.h, tS)
                e.wait_ge(sT.h, tT)
            for par in range(2):
                for tl in range(2):
                    e.dma_start(out=xb_t(sbuf_i, par, tl), in_=xz_t(sbuf_i, par, tl)).then_inc(sXB[gi % 2].h, 16)
        G(castseg)
        xcast_done[(b, s)] = (gi % 2, sXB[gi % 2].inc(16 * 4))

    def emit_attn_unit(ui):
        b, s, par, hh = attn_units[ui]
        gi = pair_list.index((b, s))
        sbuf_i = gi % 2
        slot = ui % 2
        col = (2 * hh + par) * 128

        def t_tp(e, b=b, s=s, par=par, ui=ui, sbuf_i=sbuf_i, col=col):
            if ui == 0:
                e.wait_ge(sV.h, V_BCAST)
            ci, cv = xcast_done[(b, s)]
            e.wait_ge(sXB[ci].h, cv)
            if _os.environ.get("KSERIAL") and ui >= 1:
                e.wait_ge(sV.h, uV_z[ui - 1])
            if _os.environ.get("KSERIAL") and ui % 16 == 0 and ui >= 16:
                li_prev = max(i for i, lu in enumerate(ln_units) if lu[0] == (ui // 16) - 1)
                e.wait_ge(sV.h, lv_logit[li_prev])
            if ui >= 1:
                e.wait_ge(sV.h, uV_gT[ui - 1])
            e.transpose(pbank_bf[0][:, 0:128], xb_t(sbuf_i, par, 0)[:, col:col + 128], IDT[:, :])
            e.transpose(pbank_bf[1][:, 0:128], xb_t(sbuf_i, par, 1)[:, col:col + 128], IDT[:, :]).then_inc(sT.h, 1)
        T(t_tp)
        uT_tp[ui] = sT.inc()

        if pend_tpv[0] is not None:
            fnp, pu = pend_tpv[0]
            T(fnp)
            uT_pv[pu] = sT.inc()
            pend_tpv[0] = None

        def v_gT(e, ui=ui, slot=slot):
            e.wait_ge(sT.h, uT_tp[ui])
            if ui >= 2:
                e.wait_ge(sT.h, uT_sc[ui - 2])
            go = slot * 256
            e.tensor_copy(gT[:, go:go + 128], pbank_bf[0][:, 0:128])
            e.tensor_copy(gT[:, go + 128:go + 256], pbank_bf[1][:, 0:128]).then_inc(sV.h, 1)
        V(v_gT)
        uV_gT[ui] = sV.inc()

        if pend_vz[0] is not None:
            fn, pu = pend_vz[0]
            V(fn)
            uV_z[pu] = sV.inc()
            pend_vz[0] = None

        def t_sc(e, ui=ui, slot=slot):
            e.wait_ge(sV.h, uV_gT[ui])
            if ui >= 1:
                e.wait_ge(sS.h, uS_E[ui - 1] - 1)
            go = slot * 256
            e.matmul(pbank[2][:, 0:256], gT[:, go:go + 128], gT[:, go:go + 256], start=True, stop=True).then_inc(sT.h, 1)
            if ui >= 1:
                e.wait_ge(sS.h, uS_E[ui - 1])
            e.matmul(pbank[3][:, 0:256], gT[:, go + 128:go + 256], gT[:, go:go + 256], start=True, stop=True).then_inc(sT.h, 1)
        T(t_sc)
        uT_sc1[ui] = sT.inc()
        uT_sc[ui] = sT.inc()

        def s_E(e, ui=ui, slot=slot):
            e.wait_ge(sT.h, uT_sc1[ui])
            if ui >= 2:
                e.wait_ge(sV.h, uV_R[ui - 2])
            eo = slot * 512
            do = slot * 2
            e.activation(out=Ebuf[:, eo:eo + 256], in_=pbank[2][:, 0:256], func=AF.Exp,
                         scale=SCALE, accum_out=Dbuf[:, do:do + 1]).then_inc(sS.h, 1)
            e.wait_ge(sT.h, uT_sc[ui])
            e.activation(out=Ebuf[:, eo + 256:eo + 512], in_=pbank[3][:, 0:256], func=AF.Exp,
                         scale=SCALE, accum_out=Dbuf[:, do + 1:do + 2]).then_inc(sS.h, 1)
        SC(s_E)
        sS.inc()
        uS_E[ui] = sS.inc()

        def v_R(e, ui=ui, slot=slot):
            e.wait_ge(sS.h, uS_E[ui])
            do = slot * 2
            e.reciprocal(Rbuf[:, do:do + 2], Dbuf[:, do:do + 2]).then_inc(sV.h, 1)
        V(v_R)
        uV_R[ui] = sV.inc()

        def t_pv(e, ui=ui, slot=slot, sbuf_i=sbuf_i, par=par, col=col):
            e.wait_ge(sS.h, uS_E[ui])
            if ui >= 1:
                e.wait_ge(sV.h, uV_z[ui - 1])
            eo = slot * 512
            last = None
            for j in range(2):
                for k in range(2):
                    last = e.matmul(pbank[4 + j][:, 0:128],
                                    Ebuf[:, eo + k * 256 + j * 128:eo + k * 256 + (j + 1) * 128],
                                    xb_t(sbuf_i, par, k)[:, col:col + 128],
                                    start=(k == 0), stop=(k == 1))
            last.then_inc(sT.h, 1)
        pend_tpv[0] = (t_pv, ui)

        def v_z(e, ui=ui, slot=slot, sbuf_i=sbuf_i, par=par, col=col, flushed=False):
            e.wait_ge(sT.h, uT_pv[ui])
            e.wait_ge(sV.h, uV_R[ui])
            if flushed:
                e.drain()
            do = slot * 2
            last = None
            for j in range(2):
                zt = xz_t(sbuf_i, par, j)
                last = e.scalar_tensor_tensor(zt[:, col:col + 128], pbank[4 + j][:, 0:128],
                                              Rbuf[:, do + j:do + j + 1], zt[:, col:col + 128],
                                              ALU.mult, ALU.add)
            last.then_inc(sV.h, 1)
        pend_vz[0] = (v_z, ui)

    def emit_ln_unit(li):
        gi, b, s, par, tl = ln_units[li]
        sbuf_i = gi % 2
        k = 4 * s + 2 * par + tl
        ring = k % 4
        ering = li % 2

        def v_stats(e, li=li, b=b, s=s, par=par, tl=tl, ring=ring, sbuf_i=sbuf_i):
            e.wait_ge(sV.h, uV_z[last_unit_of_pair[(b, s, par)]])
            if li >= 4:
                e.wait_ge(sS.h, ls_apply[li - 4])
            zt = xz_t(sbuf_i, par, tl)
            for i in range(4):
                e.bn_stats(lnstats[:, ring * 24 + 6 * i:ring * 24 + 6 * (i + 1)], zt[:, 512 * i:512 * (i + 1)])
            o = ring * 8
            e.drain()
            e.bn_aggr(lnmv[:, o:o + 2], lnstats[:, ring * 24:ring * 24 + 24]).then_inc(sV.h, 1)
        V(v_stats)
        lv_stats[li] = sV.inc()

        def s_inv2(e, li=li, ring=ring):
            e.wait_ge(sV.h, lv_stats[li])
            o = ring * 8
            e.activation(out=lnmv[:, o + 7:o + 8], in_=lnmv[:, o + 1:o + 2], func=AF.Ln, bias=LNB[:, 0:1])
            e.activation(out=lnmv[:, o + 2:o + 3], in_=lnmv[:, o + 7:o + 8], func=AF.Exp, scale=-0.5)
            e.activation(out=lnmv[:, o + 7:o + 8], in_=lnmv[:, o + 7:o + 8], func=AF.Exp, scale=-1.0)
            e.activation(out=lnmv[:, o + 4:o + 5], in_=lnmv[:, o + 7:o + 8], func=AF.Ln,
                         scale=-EPS, bias=LNB[:, 1:2])
            e.activation(out=lnmv[:, o + 4:o + 5], in_=lnmv[:, o + 4:o + 5], func=AF.Exp, scale=-0.5).then_inc(sS.h, 1)
        SC(s_inv2)
        ls_inv[li] = sS.inc()

        def v_small(e, li=li, ring=ring):
            e.wait_ge(sS.h, ls_inv[li])
            o = ring * 8
            e.scalar_tensor_tensor(lnmv[:, o + 3:o + 4], lnmv[:, o + 0:o + 1], -1.0,
                                   lnmv[:, o + 2:o + 3], ALU.mult, ALU.mult)
            e.tensor_tensor(lnmv[:, o + 5:o + 6], lnmv[:, o + 2:o + 3], lnmv[:, o + 4:o + 5], ALU.mult)
            e.tensor_tensor(lnmv[:, o + 6:o + 7], lnmv[:, o + 3:o + 4], lnmv[:, o + 4:o + 5], ALU.mult).then_inc(sV.h, 1)
        V(v_small)
        lv_sc2 = sV.inc()

        def s_apply(e, li=li, ring=ring, ering=ering, b=b, s=s, par=par, tl=tl, sbuf_i=sbuf_i, k=k, lv_sc2=lv_sc2):
            e.wait_ge(sV.h, lv_sc2)
            o = ring * 8
            zt = xz_t(sbuf_i, par, tl)
            if li == 4 and _os.environ.get("KDBG"):
                e.wait_ge(sC.h, DBGC[0])
            if li >= 4:
                e.wait_ge(sM4[li % 4].h, lm_mirror[li - 4])
                e.wait_ge(sV.h, lv_logit[li - 4])
            if li >= 2:
                e.wait_ge(sE2[li % 2].h, ly_early[li - 2])
            e.activation(out=x1st[:, ring * D:(ring + 1) * D], in_=zt[:, :], func=AF.Identity,
                         bias=lnmv[:, o + 3:o + 4], scale=lnmv[:, o + 2:o + 3])
            e.activation(out=outear[:, ering * D:(ering + 1) * D], in_=zt[:, :], func=AF.Identity,
                         bias=lnmv[:, o + 6:o + 7], scale=lnmv[:, o + 5:o + 6]).then_inc(sS.h, 1)
            e.wait_ge(sS.h, ls_apply[li])
            e.dma_start(out=x1m_h[b][128 * k:128 * (k + 1), :],
                        in_=x1st[:, ring * D:(ring + 1) * D]).then_inc(sM4[li % 4].h, 16)
        ls_apply[li] = sS.inc()
        SC(s_apply)
        lm_mirror[li] = sM4[li % 4].inc(16)

        def v_logit(e, li=li, ring=ring, b=b, k=k):
            e.wait_ge(sS.h, ls_apply[li])
            e.scalar_tensor_tensor(sttscr[:, :], x1st[:, ring * D:(ring + 1) * D], 1.0, WrB[:, :],
                                   ALU.mult, ALU.mult,
                                   accum_out=Lgrid[:, b * 16 + k:b * 16 + k + 1]).then_inc(sV.h, 1)
        pend_logit.append((li, v_logit))

        def y_early(e, li=li, ering=ering, b=b, s=s, par=par, tl=tl):
            e.wait_ge(sS.h, ls_apply[li])
            e.dma_start(out=xview(out_h, b, s, par, tl), in_=outear[:, ering * D:(ering + 1) * D]).then_inc(sE2[li % 2].h, 16)
        Y(y_early)
        ly_early[li] = sE2[li % 2].inc(16)

        if par == 1 and tl == 1:
            seg_consumed[(b, s)] = (ls_apply[li], uT_pv[last_unit_of_pair[(b, s, 1)]])

    router_state = {}

    def build_router_ops(b, prev_ext_t):
        st = {}
        router_state[b] = st
        def v_bnd1(e, b=b, prev_ext_t=prev_ext_t):
            e.wait_ge(sV.h, lv_logit[last_li_of_b[b]])
            if prev_ext_t is not None:
                e.wait_ge(sT.h, prev_ext_t)
            e.reduce_max(bndc[:, 0:1], Lgrid[:, b * 16:(b + 1) * 16], axis=AX.X)
            e.tensor_scalar(lbr_b[:, :], Lgrid[:, b * 16:(b + 1) * 16], -1.0, None, ALU.mult)
            e.reduce_max(bndc[:, 1:2], lbr_b[:, :], axis=AX.X).then_inc(sV.h, 1)
        st["vbnd1"] = sV.inc()
        yield ("V", v_bnd1)

        def t_bnd(e, b=b, st=st):
            e.wait_ge(sV.h, st["vbnd1"])
            e.transpose(pbank[0][0:1, 0:128], bndc[:, 0:1], IDTF[:, :])
            e.transpose(pbank[1][0:1, 0:128], bndc[:, 1:2], IDTF[:, :]).then_inc(sT.h, 1)
        st["tbnd"] = sT.inc()
        yield ("T", t_bnd)

        def v_bnd2(e, b=b, st=st):
            e.wait_ge(sT.h, st["tbnd"])
            e.reduce_max(bnds[:, 0:1], pbank[0][0:1, 0:128], axis=AX.X)
            e.reduce_max(bnds[:, 1:2], pbank[1][0:1, 0:128], axis=AX.X).then_inc(sV.h, 1)
        st["vbnd2"] = sV.inc()
        yield ("V", v_bnd2)

        def t_bnd2(e, b=b, st=st):
            e.wait_ge(sV.h, st["vbnd2"])
            e.matmul(pbank[0][:, 0:2], ONES1F[:, :], bnds[:, :], start=True, stop=True).then_inc(sT.h, 1)
        st["tbnd2"] = sT.inc()
        yield ("T", t_bnd2)

        def v_binit(e, b=b, st=st):
            e.wait_ge(sT.h, st["tbnd2"])
            e.tensor_copy(bndB[:, :], pbank[0][:, 0:2])
            e.drain()
            e.tensor_scalar(hi_b[:, b:b + 1], bndB[:, 0:1], 2.0, None, ALU.mult)
            e.tensor_scalar(lo_b[:, b:b + 1], bndB[:, 1:2], -2.0, -2.0, ALU.mult, ALU.add)
            e.tensor_scalar(L2g[:, :], Lgrid[:, b * 16:(b + 1) * 16], 2.0, None, ALU.mult)
            e.tensor_tensor(mid_b[:, b:b + 1], lo_b[:, b:b + 1], hi_b[:, b:b + 1], ALU.add)
            e.drain()
            with nc.allow_low_precision("bisection counts <=16 are exact in bf16"):
                e.tensor_scalar(tmp16[:, :], L2g[:, :], mid_b[:, b:b + 1], None, ALU.is_gt,
                                ALU.add, accum_out=prt_b[:, b:b + 1]).then_inc(sV.h, 1)
        st[("v", 0)] = sV.inc()
        yield ("V", v_binit)

        for it in range(NBIS):
            def t_it(e, b=b, it=it, st=st):
                e.wait_ge(sV.h, st[("v", it)])
                e.matmul(pbank[it % 2][:, 0:1], ONESC[:, :], prt_b[:, b:b + 1], start=True, stop=True).then_inc(sT.h, 1)
            st[("t", it)] = sT.inc()
            yield ("T", t_it)

            def v_it(e, b=b, it=it, st=st):
                e.wait_ge(sT.h, st[("t", it)])
                ps = pbank[it % 2]
                e.tensor_scalar(cge_b[:, b:b + 1], ps[:, 0:1], float(NSLOT) - 0.5, None, ALU.is_ge)
                e.tensor_tensor(dlt_b[:, b:b + 1], mid_b[:, b:b + 1], lo_b[:, b:b + 1], ALU.subtract)
                e.tensor_tensor(dlt2_b[:, b:b + 1], hi_b[:, b:b + 1], mid_b[:, b:b + 1], ALU.subtract)
                e.scalar_tensor_tensor(lo_b[:, b:b + 1], dlt_b[:, b:b + 1], cge_b[:, b:b + 1],
                                       lo_b[:, b:b + 1], ALU.mult, ALU.add)
                e.scalar_tensor_tensor(hi_b[:, b:b + 1], dlt2_b[:, b:b + 1], cge_b[:, b:b + 1],
                                       mid_b[:, b:b + 1], ALU.mult, ALU.add)
                last = None
                if it < NBIS - 1:
                    e.drain()
                    e.tensor_tensor(mid_b[:, b:b + 1], lo_b[:, b:b + 1], hi_b[:, b:b + 1], ALU.add)
                    e.drain()
                    with nc.allow_low_precision("bisection counts <=16 are exact in bf16"):
                        last = e.tensor_scalar(tmp16[:, :], L2g[:, :], mid_b[:, b:b + 1], None, ALU.is_gt,
                                               ALU.add, accum_out=prt_b[:, b:b + 1])
                else:
                    e.drain()
                    last = e.tensor_scalar(mask_b[:, b * 16:(b + 1) * 16], L2g[:, :],
                                           lo_b[:, b:b + 1], None, ALU.is_gt)
                last.then_inc(sV.h, 1)
            st[("v", it + 1)] = sV.inc()
            yield ("V", v_it)

        def v_fin(e, b=b, st=st):
            e.wait_ge(sV.h, st[("v", NBIS)])
            e.tensor_scalar(lbr_b[:, :], Lgrid[:, b * 16:(b + 1) * 16], bndB[:, 0:1], None,
                            ALU.subtract)
            e.tensor_copy(maskbf[:, :], mask_b[:, b * 16:(b + 1) * 16]).then_inc(sV.h, 1)
        st["vfin"] = sV.inc()
        yield ("V", v_fin)

        def s_wexp(e, b=b, st=st):
            e.wait_ge(sV.h, st["vfin"])
            e.activation(out=wexp[:, :], in_=lbr_b[:, :], func=AF.Exp).then_inc(sS.h, 1)
        st["swexp"] = sS.inc()
        yield ("S", s_wexp)

        def t_ct(e, b=b, st=st):
            e.wait_ge(sV.h, st["vfin"])
            e.matmul(pbank[0][0:16, 0:1], maskbf[:, :], ONESC[:, 0:1], start=True, stop=True).then_inc(sT.h, 1)
        st["tct"] = sT.inc()
        yield ("T", t_ct)

        def v_ct(e, b=b, st=st):
            e.wait_ge(sT.h, st["tct"])
            e.wait_ge(sS.h, st["swexp"])
            e.tensor_tensor(wm32[:, :], wexp[:, :], mask_b[:, b * 16:(b + 1) * 16], ALU.mult)
            e.tensor_copy(ct16[:, :], pbank[0][0:16, 0:1])
            e.reduce_sum(wprt[:, :], wm32[:, :], axis=AX.X).then_inc(sV.h, 1)
        st["vct"] = sV.inc()
        yield ("V", v_ct)

        def t_cb(e, b=b, st=st):
            e.wait_ge(sV.h, st["vct"])
            e.matmul(pbank[1][0:1, 0:16], ct16[:, :], TRI16[:, :], start=True, stop=True)
            e.matmul(pbank[0][0:1, 0:1], wprt[:, :], ONESCF[:, :], start=True, stop=True).then_inc(sT.h, 1)
        st["tcb"] = sT.inc()
        yield ("T", t_cb)

        def v_cb(e, b=b, st=st):
            e.wait_ge(sT.h, st["tcb"])
            e.tensor_copy(zt_s[:, b:b + 1], pbank[0][0:1, 0:1])
            e.tensor_copy(cb16[:, :], pbank[1][0:1, 0:16])
            e.reciprocal(zt_s[:, b:b + 1], zt_s[:, b:b + 1]).then_inc(sV.h, 1)
        st["vcb"] = sV.inc()
        yield ("V", v_cb)

        def t_pos(e, b=b, st=st):
            e.wait_ge(sV.h, st["vcb"])
            e.matmul(pbank[0][:, 0:16], ONESC[0:1, :], cb16[:, :], start=True, stop=True)
            e.matmul(pbank[1][:, 0:16], TRI[:, :], maskbf[:, :], start=True, stop=True).then_inc(sT.h, 1)
        st["tpos"] = sT.inc()
        yield ("T", t_pos)

        def v_pos(e, b=b, st=st):
            e.wait_ge(sT.h, st["tpos"])
            e.tensor_copy(pos_b[:, :], pbank[0][:, 0:16])
            e.tensor_copy(vspc[:, 0:1], vspc[:, 1:2])
            e.tensor_tensor(pos_b[:, :], pos_b[:, :], pbank[1][:, 0:16], ALU.add)
            e.drain()
            for kk in range(16):
                e.tensor_scalar(PT[:, 256 * kk:256 * (kk + 1)], IO256[:, :], pos_b[:, kk:kk + 1],
                                mask_b[:, b * 16 + kk:b * 16 + kk + 1], ALU.is_equal, ALU.mult)
            e.tensor_copy(VALS[:, 0:48:3], SG[:, :])
            e.tensor_copy(VALS[:, 1:48:3], TG[:, :])
            last = None
            for kk in range(16):
                last = e.tensor_scalar(VALS[:, 3 * kk + 2:3 * kk + 3], wm32[:, kk:kk + 1], 1.0, None, ALU.mult)
            last.then_inc(sV.h, 1)
        st["vpos"] = sV.inc()
        yield ("V", v_pos)

        def t_rz(e, b=b, st=st):
            e.wait_ge(sV.h, st["vpos"])
            e.matmul(pbank[0][:, 0:1], ONES1F[:, :], zt_s[:, b:b + 1], start=True, stop=True).then_inc(sT.h, 1)
        st["trz"] = sT.inc()
        yield ("T", t_rz)

        def v_rz(e, b=b, st=st):
            e.wait_ge(sT.h, st["trz"])
            e.tensor_copy(rzB[:, :], pbank[0][:, 0:1]).then_inc(sV.h, 1)
        st["vrz"] = sV.inc()
        yield ("V", v_rz)

        for half in range(2):
            def t_ext(e, b=b, half=half, st=st):
                e.wait_ge(sV.h, st["vrz"] if half == 0 else st[("vext", 0)])
                last = None
                for kk in range(16):
                    last = e.matmul(pbank[half][:, 0:3],
                                    PT[:, 256 * kk + 128 * half:256 * kk + 128 * (half + 1)],
                                    VALS[:, 3 * kk:3 * (kk + 1)], start=(kk == 0), stop=(kk == 15))
                last.then_inc(sT.h, 1)
            st[("text", half)] = sT.inc()
            yield ("T", t_ext)

            def v_ext(e, b=b, half=half, st=st):
                e.wait_ge(sT.h, st[("text", half)])
                stg = b * 2 + half
                o = stg * 4
                e.tensor_copy(slotv[:, o:o + 3], pbank[half][:, 0:3])
                e.tensor_copy(vspc[:, 0:1], vspc[:, 1:2])
                e.tensor_tensor(slotv[:, o + 3:o + 4], slotv[:, o + 2:o + 3], rzB[:, :], ALU.mult)
                e.tensor_copy(sigI[:, stg:stg + 1], slotv[:, o:o + 1])
                e.tensor_copy(tI[:, stg:stg + 1], slotv[:, o + 1:o + 2]).then_inc(sV.h, 1)
            st[("vext", half)] = sV.inc()
            yield ("V", v_ext)

        router_state[b] = st

    emit_load(0)
    emit_load(1)
    emit_cast(0)
    emit_cast(1)

    n_units_per_pair = 16
    r0gen = [None]

    def splice(n):
        if r0gen[0] is None:
            return
        for _ in range(n):
            try:
                t_, f_ = next(r0gen[0])
            except StopIteration:
                r0gen[0] = None
                return
            OPS[t_].append(f_)

    DBGC = [0]
    for gi, (b, s) in enumerate(pair_list):
        logs = pend_logit[:]
        del pend_logit[:]
        for u in range(n_units_per_pair):
            emit_attn_unit(gi * n_units_per_pair + u)
            if logs and u < 2:
                for _ in range(2):
                    li_, fn_ = logs.pop(0)
                    V(fn_)
                    lv_logit[li_] = sV.inc()
            if b == 1 and not logs:
                splice(2)
        if pend_tpv[0] is not None:
            fnp, pu = pend_tpv[0]
            T(fnp)
            uT_pv[pu] = sT.inc()
            pend_tpv[0] = None
        if pend_vz[0] is not None:
            fn, pu = pend_vz[0]
            import functools
            V(functools.partial(fn, flushed=True))
            uV_z[pu] = sV.inc()
            pend_vz[0] = None
        for li, lu in enumerate(ln_units):
            if lu[0] == gi:
                emit_ln_unit(li)
        if gi == 0 and _os.environ.get("KDBG"):
            def g_dbg(e, li_p0=3):
                e.wait_ge(sS.h, ls_apply[li_p0])
                e.dma_start(out=dbg_h[:, 0:2048], in_=xz[:, 0:2048]).then_inc(sC.h, 16)
                e.dma_start(out=dbg_h[:, 2048:4096], in_=xbuf[:, 0:2048]).then_inc(sC.h, 16)
                e.dma_start(out=dbg_h[:, 4096:6144], in_=x1st[:, 0:2048]).then_inc(sC.h, 16)
                e.dma_start(out=dbg_h[0:128, 6144:6176], in_=lnmv[:, :]).then_inc(sC.h, 16)
            G(g_dbg)
            DBGC[0] = sC.inc(16 * 4)
        if gi + 2 < len(pair_list):
            emit_load(gi + 2)
            emit_cast(gi + 2)
        if b == 0 and s == NSEG - 1:
            r0gen[0] = build_router_ops(0, None)

    for li_, fn_ in pend_logit:
        V(fn_)
        lv_logit[li_] = sV.inc()
    del pend_logit[:]

    splice(9999)

    for t_, f_ in build_router_ops(1, router_state[0][("text", 1)]):
        OPS[t_].append(f_)


    esA.close()
    esB = ExitStack()
    g1T = esB.enter_context(nc.sbuf_tensor("g1T", [128, 64 * 512], BF16))
    w1cb = esB.enter_context(nc.sbuf_tensor("w1cb", [128, 3 * 4096], BF16))
    h2sb = esB.enter_context(nc.sbuf_tensor("h2sb", [128, 2 * 512], BF16))
    w2cb = esB.enter_context(nc.sbuf_tensor("w2cb", [128, 2 * 8192], BF16))
    ybuf = esB.enter_context(nc.sbuf_tensor("ybuf", [128, 4 * D], F32))
    fxb = esB.enter_context(nc.sbuf_tensor("fxb", [128, 4 * D], BF16))
    fxT = esB.enter_context(nc.sbuf_tensor("fxT", [128, 16 * 512], BF16))
    outro = esB.enter_context(nc.sbuf_tensor("outro", [128, 2 * D], F32))

    gath_done = {}
    for b in range(NB):
        for half in range(2):
            stg = b * 2 + half

            def g_gather(e, b=b, half=half, stg=stg):
                if stg == 0:
                    tS, tT = seg_consumed[(1, NSEG - 1)]
                    e.wait_ge(sS.h, tS)
                    e.wait_ge(sT.h, tT)
                    e.wait_ge(sV.h, router_state[1][("vext", 1)])
                    for mm in range(4):
                        e.wait_ge(sM4[mm].h, sM4[mm].n)
                else:
                    e.wait_ge(sGG.h, gath_done[stg - 1])
                e.wait_ge(sV.h, router_state[b][("vext", half)])
                e.indirect_dma_start(fxb[:, stg * D:(stg + 1) * D], None,
                                     x1m_h.rearrange("b s d -> (b s) d"),
                                     IndirectOffsetOnAxis(sigI[:, stg:stg + 1], 0),
                                     element_offset=b * S * D).then_inc(sGG.h, 16)
            G(g_gather)
            gath_done[stg] = sGG.inc(16)

    fxT_done = {}
    tp_evac = {}
    for stg in range(4):
        for t in range(16):
            idx = stg * 16 + t

            def t_fxT(e, stg=stg, t=t, idx=idx):
                e.wait_ge(sGG.h, gath_done[stg])
                if idx == 0:
                    e.wait_ge(sV.h, uV_gT[len(attn_units) - 1])
                if idx >= 2:
                    e.wait_ge(sV.h, tp_evac[idx - 2])
                e.transpose(pbank_bf[idx % 2][:, 0:128], fxb[:, stg * D + 128 * t:stg * D + 128 * (t + 1)],
                            IDT[:, :]).then_inc(sT.h, 1)
            T(t_fxT)
            fxT_done[idx] = sT.inc()

            def v_fxT(e, stg=stg, t=t, idx=idx):
                e.wait_ge(sT.h, fxT_done[idx])
                e.tensor_copy(fxT[:, 512 * t + 128 * stg:512 * t + 128 * (stg + 1)],
                              pbank_bf[idx % 2][:, 0:128]).then_inc(sV.h, 1)
            V(v_fxT)
            tp_evac[idx] = sV.inc()
    FXT_ALL = sV.n

    w1_done = {}
    h1_done = {}
    g1_done = {}
    for fb in range(32):
        def g_w1(e, fb=fb):
            if fb >= 3:
                e.wait_ge(sT.h, h1_done[2 * (fb - 3) + 1])
            e.dma_start(out=w1cb[:, (fb % 3) * 4096:(fb % 3 + 1) * 4096],
                        in_=w1_h[:, 256 * fb:256 * (fb + 1)].rearrange("(t p) f -> p t f", p=128)).then_inc(sW1[fb % 3].h, 16)
        G(g_w1)
        w1_done[fb] = sW1[fb % 3].inc(16)

    for f in range(64):
        fb = f // 2
        fo = f % 2

        def t_h1(e, f=f, fb=fb, fo=fo):
            if fo == 0:
                e.wait_ge(sW1[fb % 3].h, w1_done[fb])
            if f == 0:
                e.wait_ge(sV.h, FXT_ALL)
            if f >= 4:
                e.wait_ge(sS.h, g1_done[f - 4])
            bk = f % 4
            base = (fb % 3) * 4096 + fo * 128
            last = None
            for t in range(16):
                last = e.matmul(pbank[bk][:, 0:512],
                                w1cb[:, base + 256 * t:base + 256 * t + 128],
                                fxT[:, 512 * t:512 * (t + 1)], start=(t == 0), stop=(t == 15))
            last.then_inc(sT.h, 1)
        T(t_h1)
        h1_done[f] = sT.inc()

        def s_g1(e, f=f):
            e.wait_ge(sT.h, h1_done[f])
            bk = f % 4
            e.activation(out=g1T[:, 512 * f:512 * (f + 1)], in_=pbank[bk][:, 0:512], func=AF.Gelu,
                         bias=b1T[:, f:f + 1]).then_inc(sS.h, 1)
        SC(s_g1)
        g1_done[f] = sS.inc()

    w2_done = {}
    h2_done = {}
    h2s_done = {}
    h2t_done = {}
    yev_done = {}
    wf_done = {}
    vc = {}
    for c in range(16):
        def g_w2(e, c=c):
            if c == 0:
                tS, tT = seg_consumed[(1, NSEG - 1)]
                e.wait_ge(sS.h, tS)
                e.wait_ge(sT.h, tT)
            if c >= 2:
                e.wait_ge(sT.h, h2_done[c - 2])
            src = w2_h[:, 128 * c:128 * (c + 1)].rearrange("(t p) f -> p t f", p=128)
            if c < 2:
                e.dma_start(out=w2cb[:, (c % 2) * 8192:(c % 2 + 1) * 8192],
                            in_=src).then_inc(sW2[c % 2].h, 16)
            else:
                e.dma_start(out=w2cb[:, (c % 2) * 8192:(c % 2) * 8192 + 4096],
                            in_=src[:, 0:32, :]).then_inc(sW2[c % 2].h, 16)
        G(g_w2)
        w2_done[c] = sW2[c % 2].inc(16)

    for c in range(2, 16):
        def g_wf(e, c=c):
            if c == 2:
                e.wait_ge(sT.h, h1_done[63])
            else:
                e.wait_ge(sV.h, vc[c - 1])
            e.dma_start(out=fxT[:, :].bitcast(F32),
                        in_=w2_h[:, 128 * c:128 * (c + 1)].rearrange("(t p) f -> p t f", p=128)[:, 32:64, :]).then_inc(sWF.h, 16)
        Y(g_wf)
        wf_done[c] = sWF.inc(16)

    for c in range(16):
        def t_h2(e, c=c):
            e.wait_ge(sW2[c % 2].h, w2_done[c])
            if c >= 2:
                e.wait_ge(sV.h, vc[c])
            if c == 0:
                e.wait_ge(sS.h, g1_done[63])
            if c >= 2:
                e.wait_ge(sS.h, h2s_done[c - 2])
            bk = 4 + c % 2
            last = None
            for f in range(64):
                last = e.matmul(pbank[bk][:, 0:512],
                                w2cb[:, (c % 2) * 8192 + 128 * f:(c % 2) * 8192 + 128 * (f + 1)],
                                g1T[:, 512 * f:512 * (f + 1)], start=(f == 0), stop=(f == 63))
            last.then_inc(sT.h, 1)
        T(t_h2)
        h2_done[c] = sT.inc()

        def s_h2(e, c=c):
            e.wait_ge(sT.h, h2_done[c])
            if c >= 2:
                e.wait_ge(sT.h, h2t_done[(c - 2, 3)])
            bk = 4 + c % 2
            e.activation(out=h2sb[:, (c % 2) * 512:(c % 2 + 1) * 512], in_=pbank[bk][:, 0:512],
                         func=AF.Identity, bias=b2T[:, c:c + 1]).then_inc(sS.h, 1)
        SC(s_h2)
        h2s_done[c] = sS.inc()

        for stg in range(4):
            idx = c * 4 + stg

            def t_h2t(e, c=c, stg=stg, idx=idx):
                e.wait_ge(sS.h, h2s_done[c])
                if idx >= 2:
                    e.wait_ge(sV.h, yev_done[idx - 2])
                e.transpose(pbank_bf[idx % 2][:, 0:128],
                            h2sb[:, (c % 2) * 512 + 128 * stg:(c % 2) * 512 + 128 * (stg + 1)],
                            IDT[:, :]).then_inc(sT.h, 1)
            T(t_h2t)
            h2t_done[(c, stg)] = sT.inc()

            def v_yev(e, c=c, stg=stg, idx=idx):
                e.wait_ge(sT.h, h2t_done[(c, stg)])
                e.tensor_scalar(ybuf[:, stg * D + 128 * c:stg * D + 128 * (c + 1)],
                                pbank_bf[idx % 2][:, 0:128],
                                slotv[:, stg * 4 + 3:stg * 4 + 4], None, ALU.mult).then_inc(sV.h, 1)
            V(v_yev)
            yev_done[idx] = sV.inc()

        if c + 2 < 16:
            def v_wcast(e, c=c + 2):
                e.wait_ge(sWF.h, wf_done[c])
                e.wait_ge(sT.h, h2_done[c - 2])
                e.tensor_copy(w2cb[:, (c % 2) * 8192 + 4096:(c % 2 + 1) * 8192],
                              fxT[:, :].bitcast(F32)).then_inc(sV.h, 1)
            V(v_wcast)
            vc[c + 2] = sV.inc()

    fin_scat = {}
    fin_conv = {}
    for stg in range(4):
        b = stg // 2

        def s_conv(e, stg=stg):
            e.wait_ge(sGG.h, gath_done[stg])
            if stg >= 2:
                e.wait_ge(sGG.h, fin_scat[stg - 2])
            e.activation(out=outro[:, (stg % 2) * D:(stg % 2 + 1) * D],
                         in_=fxb[:, stg * D:(stg + 1) * D], func=AF.Identity,
                         scale=2.0).then_inc(sS.h, 1)
        SC(s_conv)
        fin_conv[stg] = sS.inc()

        def v_fin2(e, stg=stg):
            e.wait_ge(sV.h, yev_done[15 * 4 + stg])
            e.wait_ge(sS.h, fin_conv[stg])
            yt = ybuf[:, stg * D:(stg + 1) * D]
            e.tensor_tensor(yt[:, :], yt[:, :], outro[:, (stg % 2) * D:(stg % 2 + 1) * D], ALU.add)
            o = stg * 8
            e.drain()
            for i in range(4):
                e.bn_stats(lnstats[:, stg * 24 + 6 * i:stg * 24 + 6 * (i + 1)], yt[:, 512 * i:512 * (i + 1)])
            e.drain()
            e.bn_aggr(lnmv[:, o:o + 2], lnstats[:, stg * 24:stg * 24 + 24])
            e.drain()
            e.tensor_scalar(lnmv[:, o + 1:o + 2], lnmv[:, o + 1:o + 2], EPS, None, ALU.add)
            e.drain()
            e.reciprocal(lnmv[:, o + 1:o + 2], lnmv[:, o + 1:o + 2]).then_inc(sV.h, 1)
        V(v_fin2)
        fin_stats = sV.inc()

        def s_fin2(e, stg=stg, fin_stats=fin_stats):
            e.wait_ge(sV.h, fin_stats)
            o = stg * 8
            e.activation(out=lnmv[:, o + 1:o + 2], in_=lnmv[:, o + 1:o + 2], func=AF.Ln)
            e.activation(out=lnmv[:, o + 1:o + 2], in_=lnmv[:, o + 1:o + 2], func=AF.Exp, scale=0.5).then_inc(sS.h, 1)
        SC(s_fin2)
        fin_inv = sS.inc()

        def v_fin3(e, stg=stg, fin_inv=fin_inv):
            e.wait_ge(sS.h, fin_inv)
            o = stg * 8
            e.scalar_tensor_tensor(lnmv[:, o + 3:o + 4], lnmv[:, o + 0:o + 1], -1.0,
                                   lnmv[:, o + 1:o + 2], ALU.mult, ALU.mult).then_inc(sV.h, 1)
        V(v_fin3)
        fin_b = sV.inc()

        def s_fin3(e, stg=stg, fin_b=fin_b):
            e.wait_ge(sV.h, fin_b)
            o = stg * 8
            e.activation(out=outro[:, (stg % 2) * D:(stg % 2 + 1) * D], in_=ybuf[:, stg * D:(stg + 1) * D],
                         func=AF.Identity, bias=lnmv[:, o + 3:o + 4],
                         scale=lnmv[:, o + 1:o + 2]).then_inc(sS.h, 1)
        SC(s_fin3)
        fin_app = sS.inc()

        def g_scat(e, stg=stg, b=b, fin_app=fin_app, prev_gg=sGG.n):
            e.wait_ge(sS.h, fin_app)
            e.wait_ge(sGG.h, prev_gg)
            if stg == 0:
                for ee in range(2):
                    e.wait_ge(sE2[ee].h, sE2[ee].n)
            e.indirect_dma_start(out_h.rearrange("b s d -> (b s) d"),
                                 IndirectOffsetOnAxis(tI[:, stg:stg + 1], 0),
                                 outro[:, (stg % 2) * D:(stg % 2 + 1) * D], None,
                                 element_offset=b * S * D).then_inc(sGG.h, 16)
        G(g_scat)
        fin_scat[stg] = sGG.inc(16)

    FINAL_GG = sGG.n
    FINAL_E = [s.n for s in sE2]

    def g_end(e):
        e.wait_ge(sGG.h, FINAL_GG)
        for i in range(2):
            e.wait_ge(sE2[i].h, FINAL_E[i])
    G(g_end)

    def y_end(e):
        e.wait_ge(sGG.h, FINAL_GG)
        for i in range(2):
            e.wait_ge(sE2[i].h, FINAL_E[i])
    Y(y_end)

    import os
    _tr = float(os.environ.get("KTRUNC", "1.0"))
    if _tr < 1.0:
        for _k in OPS:
            OPS[_k] = OPS[_k][:max(1, int(len(OPS[_k]) * _tr))]
    with nc.Block() as block:
        @block.tensor
        def _(eng):
            for fn in OPS["T"]:
                fn(eng)

        @block.vector
        def _(eng):
            for fn in OPS["V"]:
                fn(eng)

        @block.scalar
        def _(eng):
            for fn in OPS["S"]:
                fn(eng)

        @block.gpsimd
        def _(eng):
            for fn in OPS["G"]:
                fn(eng)

        @block.sync
        def _(eng):
            for fn in OPS["Y"]:
                fn(eng)

    esB.close()
    es.close()
    return nc


def _consts():
    c = {}
    c["IDT"] = np.eye(128, dtype=np.float32).astype(ml_dtypes.bfloat16)
    c["IDTF"] = np.eye(128, dtype=np.float32)
    c["TRI"] = np.triu(np.ones((128, 128), np.float32), 1).astype(ml_dtypes.bfloat16)
    c["TRI16"] = np.triu(np.ones((16, 16), np.float32), 1).astype(ml_dtypes.bfloat16)
    c["ONESC"] = np.ones((128, 128), np.float32).astype(ml_dtypes.bfloat16)
    c["ONES1F"] = np.ones((1, 128), np.float32)
    c["ONESCF"] = np.ones((128, 1), np.float32)
    c["IO256"] = np.broadcast_to(np.arange(256, dtype=np.float32)[None, :], (128, 256)).copy()
    c["LNB"] = np.broadcast_to(np.array([[1e-5, 1.0 + 1e-5 / 4.0]], np.float32), (128, 2)).copy()
    q = np.arange(128)[:, None]
    k = np.arange(16)[None, :]
    c["SG"] = (128 * k + q).astype(np.float16)
    s_ = k // 4
    par_ = (k % 4) // 2
    tl_ = k % 2
    c["TG"] = (512 * s_ + 256 * tl_ + 2 * q + par_).astype(np.float16)
    return c


_NC_CACHE = {}


def kernel(L0_attn_x, L0_ffn_Wr, L0_ffn_br, L0_ffn_W1, L0_ffn_b1, L0_ffn_W2, L0_ffn_b2):
    x = np.ascontiguousarray(np.asarray(L0_attn_x, dtype=np.float32))
    Wr = np.ascontiguousarray(np.asarray(L0_ffn_Wr, dtype=np.float32).reshape(1, D))
    br = np.ascontiguousarray(np.asarray(L0_ffn_br, dtype=np.float32).reshape(1, 1))
    W1 = np.ascontiguousarray(np.asarray(L0_ffn_W1, dtype=np.float32))
    b1 = np.ascontiguousarray(np.asarray(L0_ffn_b1, dtype=np.float32).reshape(64, 128).T)
    W2 = np.ascontiguousarray(np.asarray(L0_ffn_W2, dtype=np.float32))
    b2 = np.ascontiguousarray(np.asarray(L0_ffn_b2, dtype=np.float32).reshape(16, 128).T)
    consts = _consts()

    if "nc" not in _NC_CACHE:
        _NC_CACHE["nc"] = build_nc()
    nc = _NC_CACHE["nc"]

    in_maps = []
    for c in range(8):
        m = {"x": np.ascontiguousarray(x[2 * c:2 * c + 2]),
             "Wr": Wr, "br": br, "W1": W1, "b1": b1, "W2": W2, "b2": b2}
        m.update(consts)
        in_maps.append(m)
    res = run_bass_kernel_spmd(nc, in_maps, core_ids=list(range(8)))
    out = np.concatenate([res.results[i]["out"] for i in range(8)], axis=0)
    return out.astype(np.float32)


# version 12
# speedup vs baseline: 1.3261x; 1.0151x over previous
import os as _os
import numpy as np
import ml_dtypes
import concourse.bass as bass
import concourse.mybir as mybir
from concourse.bass_utils import run_bass_kernel_spmd
from concourse.bass_core_rust import IndirectOffsetOnAxis

F32 = mybir.dt.float32
BF16 = mybir.dt.bfloat16
FP16 = mybir.dt.float16
I32 = mybir.dt.int32
AF = mybir.ActivationFunctionType
ALU = mybir.AluOpType
AX = mybir.AxisListType

NB = 2
NSEG = 4
D = 2048
S = 2048
DFF = 8192
NSLOT = 256
SCALE = float(1.0 / np.sqrt(128.0))
EPS = 1e-5
NBIS = 33
WOFF = 16.0


class Sem:
    def __init__(self, es, nc, name):
        self.h = es.enter_context(nc.semaphore(name))
        self.n = 0

    def inc(self, k=1):
        self.n += k
        return self.n


def build_nc():
    from contextlib import ExitStack
    nc = bass.Bass("TRN2", target_bir_lowering=False, detect_race_conditions=False)
    es = ExitStack()

    x_h = nc.dram_tensor("x", [NB, S, D], F32, kind="ExternalInput").ap()
    wr_h = nc.dram_tensor("Wr", [1, D], F32, kind="ExternalInput").ap()
    br_h = nc.dram_tensor("br", [1, 1], F32, kind="ExternalInput").ap()
    w1_h = nc.dram_tensor("W1", [D, DFF], F32, kind="ExternalInput").ap()
    b1_h = nc.dram_tensor("b1", [128, 64], F32, kind="ExternalInput").ap()
    w2_h = nc.dram_tensor("W2", [DFF, D], F32, kind="ExternalInput").ap()
    b2_h = nc.dram_tensor("b2", [128, 16], F32, kind="ExternalInput").ap()
    idt_h = nc.dram_tensor("IDT", [128, 128], BF16, kind="ExternalInput").ap()
    idtf_h = nc.dram_tensor("IDTF", [128, 128], F32, kind="ExternalInput").ap()
    tri_h = nc.dram_tensor("TRI", [128, 128], BF16, kind="ExternalInput").ap()
    tri16_h = nc.dram_tensor("TRI16", [16, 16], BF16, kind="ExternalInput").ap()
    onesc_h = nc.dram_tensor("ONESC", [128, 128], BF16, kind="ExternalInput").ap()
    ones1f_h = nc.dram_tensor("ONES1F", [1, 128], F32, kind="ExternalInput").ap()
    onescf_h = nc.dram_tensor("ONESCF", [128, 1], F32, kind="ExternalInput").ap()
    io256_h = nc.dram_tensor("IO256", [128, 256], F32, kind="ExternalInput").ap()
    lnb_h = nc.dram_tensor("LNB", [128, 2], F32, kind="ExternalInput").ap()
    sg_h = nc.dram_tensor("SG", [128, 16], FP16, kind="ExternalInput").ap()
    tg_h = nc.dram_tensor("TG", [128, 16], FP16, kind="ExternalInput").ap()
    out_h = nc.dram_tensor("out", [NB, S, D], F32, kind="ExternalOutput").ap()
    dbg_h = nc.dram_tensor("dbg", [128, 6176], F32, kind="ExternalOutput").ap()
    x1m_h = nc.dram_tensor("x1m", [NB, S, D], BF16, kind="ExternalOutput" if _os.environ.get("KDBG") else "Internal").ap()

    def xview(h, b, s, par, tl):
        return h[b].rearrange("(s tl p two) d -> s two tl p d", s=4, tl=2, p=128, two=2)[s, par, tl]

    sb = lambda name, shape, dt: es.enter_context(nc.sbuf_tensor(name, shape, dt))
    IDT = sb("c_idt", [128, 128], BF16)
    IDTF = sb("c_idtf", [128, 128], F32)
    TRI = sb("c_tri", [128, 128], BF16)
    TRI16 = sb("c_tri16", [16, 16], BF16)
    ONESC = sb("c_onesc", [128, 128], BF16)
    ONES1F = sb("c_ones1f", [1, 128], F32)
    ONESCF = sb("c_onescf", [128, 1], F32)
    IO256 = sb("c_io256", [128, 256], F32)
    LNB = sb("c_lnb", [128, 2], F32)
    SG = sb("c_sg", [128, 16], FP16)
    TG = sb("c_tg", [128, 16], FP16)
    brS = sb("c_brs", [1, 1], F32)
    brB = sb("c_brb", [128, 1], F32)
    b1T = sb("c_b1t", [128, 64], F32)
    b2T = sb("c_b2t", [128, 16], F32)

    lnstats = sb("lnstats", [128, 4 * 24], F32)
    lnmv = sb("lnmv", [128, 4 * 8], F32)
    lo_b = sb("lo", [128, 2], F32)
    hi_b = sb("hi", [128, 2], F32)
    mid_b = sb("mid", [128, 2], F32)
    tmp16 = sb("tmp16", [128, 16], BF16)
    prt_b = sb("prt", [128, 2], BF16)
    cge_b = sb("cge", [128, 2], F32)
    dlt_b = sb("dlt", [128, 2], F32)
    dlt2_b = sb("dlt2", [128, 2], F32)
    L2g = sb("L2g", [128, 16], F32)
    vspc = sb("vspc", [128, 2], F32)
    mask_b = sb("mask", [128, 2 * 16], F32)
    maskbf = sb("maskbf", [128, 16], BF16)
    ct16 = sb("ct16", [16, 1], BF16)
    cb16 = sb("cb16", [1, 16], BF16)
    pos_b = sb("pos", [128, 16], F32)
    lbr_b = sb("lbr", [128, 16], F32)
    wexp = sb("wexp", [128, 16], F32)
    wm32 = sb("wm32", [128, 16], F32)
    wprt = sb("wprt", [128, 1], F32)
    bndc = sb("bndc", [128, 2], F32)
    bnds = sb("bnds", [1, 2], F32)
    bndB = sb("bndB", [128, 2], F32)
    VALS = sb("VALS", [128, 16 * 3], FP16)
    zt_s = sb("zt", [1, 2], F32)
    rzB = sb("rzB", [128, 1], F32)
    slotv = sb("slotv", [128, 4 * 4], F32)
    sigI = sb("sigI", [128, 4], I32)
    tI = sb("tI", [128, 4], I32)

    pbank = [es.enter_context(nc.psum_tensor(f"pb{i}", [128, 512], F32)) for i in range(6)]
    pbank_bf = [es.enter_context(nc.psum_tensor(f"pbb{i}", [128, 512], BF16)) for i in range(2)]

    sT = Sem(es, nc, "sT")
    sV = Sem(es, nc, "sV")
    sS = Sem(es, nc, "sS")
    sC = Sem(es, nc, "sC")
    sWrt = Sem(es, nc, "sWrt")
    sX = [Sem(es, nc, f"sX{i}") for i in range(2)]
    sXB = [Sem(es, nc, f"sXB{i}") for i in range(2)]
    sE2 = [Sem(es, nc, f"sE2{i}") for i in range(2)]
    sM4 = [Sem(es, nc, f"sM4{i}") for i in range(4)]
    sW1 = [Sem(es, nc, f"sW1{i}") for i in range(3)]
    sW2 = [Sem(es, nc, f"sW2{i}") for i in range(2)]
    sWF = Sem(es, nc, "sWF")
    sGG = Sem(es, nc, "sGG")

    OPS = {"T": [], "V": [], "S": [], "G": [], "Y": []}
    T = lambda fn: OPS["T"].append(fn)
    V = lambda fn: OPS["V"].append(fn)
    SC = lambda fn: OPS["S"].append(fn)
    G = lambda fn: OPS["G"].append(fn)
    Y = lambda fn: OPS["Y"].append(fn)

    def setup_loads(e):
        for dst, src in ((IDT, idt_h), (IDTF, idtf_h), (TRI, tri_h), (TRI16, tri16_h), (ONESC, onesc_h),
                         (ONES1F, ones1f_h), (ONESCF, onescf_h), (IO256, io256_h),
                         (LNB, lnb_h), (SG, sg_h), (TG, tg_h), (brS, br_h)):
            e.dma_start(out=dst[:, :], in_=src[:, :]).then_inc(sC.h, 16)
        e.dma_start(out=b1T[:, :], in_=b1_h[:, :]).then_inc(sC.h, 16)
        e.dma_start(out=b2T[:, :], in_=b2_h[:, :]).then_inc(sC.h, 16)
    Y(setup_loads)
    LD_CONST = sC.inc(16 * 14)

    def setup_wrt(e):
        e.dma_start(out=WrT[:, :], in_=wr_h[:, :]).then_inc(sWrt.h, 16)
    G(setup_wrt)
    LD_WRT = sWrt.inc(16)

    def setup_bcast_t(e):
        e.wait_ge(sC.h, LD_CONST)
        e.wait_ge(sWrt.h, LD_WRT)
        for i in range(4):
            e.matmul(pbank[i][:, :], ONESC[0:1, :], WrT[:, 512 * i:512 * (i + 1)], start=True, stop=True)
        e.matmul(pbank[4][:, 0:1], ONES1F[:, :], brS[:, :], start=True, stop=True).then_inc(sT.h, 1)
    T(setup_bcast_t)
    T_BCAST = sT.inc()

    def setup_bcast_v(e):
        e.memset(vspc[:, :], 0.0)
        e.wait_ge(sT.h, T_BCAST)
        for i in range(4):
            e.tensor_copy(WrB[:, 512 * i:512 * (i + 1)], pbank[i][:, :])
        e.tensor_copy(brB[:, :], pbank[4][:, 0:1]).then_inc(sV.h, 1)
    V(setup_bcast_v)
    V_BCAST = sV.inc()

    esA = ExitStack()
    WrT = esA.enter_context(nc.sbuf_tensor("c_wrt", [1, D], BF16))
    WrB = esA.enter_context(nc.sbuf_tensor("c_wrb", [128, D], BF16))
    xz = esA.enter_context(nc.sbuf_tensor("xz", [128, 2 * 4 * D], F32))
    xbuf = esA.enter_context(nc.sbuf_tensor("xbw", [128, 2 * 4 * D], BF16))
    x1st = esA.enter_context(nc.sbuf_tensor("x1stage", [128, 4 * D], BF16))
    sttscr = esA.enter_context(nc.sbuf_tensor("sttscr", [128, D], BF16))
    Lgrid = esA.enter_context(nc.sbuf_tensor("Lgrid", [128, 2 * 16], F32))
    outear = esA.enter_context(nc.sbuf_tensor("outear", [128, 2 * D], F32))
    PT = esA.enter_context(nc.sbuf_tensor("PT", [128, 16 * 256], FP16))
    gT = esA.enter_context(nc.sbuf_tensor("gT", [128, 2 * 256], BF16))
    Ebuf = esA.enter_context(nc.sbuf_tensor("Ebuf", [128, 2 * 512], BF16))
    Dbuf = esA.enter_context(nc.sbuf_tensor("Dbuf", [128, 2 * 2], F32))
    Rbuf = esA.enter_context(nc.sbuf_tensor("Rbuf", [128, 2 * 2], F32))

    def xz_t(sbuf_i, par, tl):
        off = (sbuf_i * 4 + par * 2 + tl) * D
        return xz[:, off:off + D]

    def xb_t(sbuf_i, par, tl):
        off = (sbuf_i * 4 + par * 2 + tl) * D
        return xbuf[:, off:off + D]

    pair_list = [(b, s) for b in range(NB) for s in range(NSEG)]
    xload_done = {}
    xcast_done = {}
    seg_consumed = {}

    uT_tp, uT_sc, uT_pv = {}, {}, {}
    uT_sc1 = {}
    uS_E = {}
    uV_gT, uV_R, uV_z = {}, {}, {}
    ls_apply, lv_logit, lm_mirror, ly_early = {}, {}, {}, {}
    lv_stats, ls_inv, lv_bias = {}, {}, {}

    pend_vz = [None]
    pend_tpv = [None]
    pend_logit = []
    attn_units = []
    for b, s in pair_list:
        for par in range(2):
            for hh in range(8):
                attn_units.append((b, s, par, hh))
    last_unit_of_pair = {}
    for ui, u in enumerate(attn_units):
        last_unit_of_pair[(u[0], u[1], u[2])] = ui

    ln_units = []
    for gi, (b, s) in enumerate(pair_list):
        for par in range(2):
            for tl in range(2):
                ln_units.append((gi, b, s, par, tl))
    last_li_of_b = {b: max(i for i, u in enumerate(ln_units) if u[1] == b) for b in range(NB)}

    def emit_load(gi):
        b, s = pair_list[gi]
        sbuf_i = gi % 2

        def loadseg(e, b=b, s=s, sbuf_i=sbuf_i, gi=gi):
            if gi == 2 and _os.environ.get("KDBG"):
                e.wait_ge(sC.h, DBGC[0])
            if gi >= 2:
                tS, tT = seg_consumed[pair_list[gi - 2]]
                e.wait_ge(sS.h, tS)
                e.wait_ge(sT.h, tT)
            for par in range(2):
                for tl in range(2):
                    e.dma_start(out=xz_t(sbuf_i, par, tl), in_=xview(x_h, b, s, par, tl)).then_inc(sX[gi % 2].h, 16)
        Y(loadseg)
        xload_done[(b, s)] = (gi % 2, sX[gi % 2].inc(16 * 4))

    def emit_cast(gi):
        b, s = pair_list[gi]
        sbuf_i = gi % 2

        def castseg(e, b=b, s=s, sbuf_i=sbuf_i, gi=gi):
            if gi == 2 and _os.environ.get("KDBG"):
                e.wait_ge(sC.h, DBGC[0])
            xi, xv = xload_done[(b, s)]
            e.wait_ge(sX[xi].h, xv)
            if gi >= 2:
                tS, tT = seg_consumed[pair_list[gi - 2]]
                e.wait_ge(sS.h, tS)
                e.wait_ge(sT.h, tT)
            for par in range(2):
                for tl in range(2):
                    e.dma_start(out=xb_t(sbuf_i, par, tl), in_=xz_t(sbuf_i, par, tl)).then_inc(sXB[gi % 2].h, 16)
        G(castseg)
        xcast_done[(b, s)] = (gi % 2, sXB[gi % 2].inc(16 * 4))

    def emit_attn_unit(ui):
        b, s, par, hh = attn_units[ui]
        gi = pair_list.index((b, s))
        sbuf_i = gi % 2
        slot = ui % 2
        col = (2 * hh + par) * 128

        def t_tp(e, b=b, s=s, par=par, ui=ui, sbuf_i=sbuf_i, col=col):
            if ui == 0:
                e.wait_ge(sV.h, V_BCAST)
            ci, cv = xcast_done[(b, s)]
            e.wait_ge(sXB[ci].h, cv)
            if _os.environ.get("KSERIAL") and ui >= 1:
                e.wait_ge(sV.h, uV_z[ui - 1])
            if _os.environ.get("KSERIAL") and ui % 16 == 0 and ui >= 16:
                li_prev = max(i for i, lu in enumerate(ln_units) if lu[0] == (ui // 16) - 1)
                e.wait_ge(sV.h, lv_logit[li_prev])
            if ui >= 1:
                e.wait_ge(sV.h, uV_gT[ui - 1])
            e.transpose(pbank_bf[0][:, 0:128], xb_t(sbuf_i, par, 0)[:, col:col + 128], IDT[:, :])
            e.transpose(pbank_bf[1][:, 0:128], xb_t(sbuf_i, par, 1)[:, col:col + 128], IDT[:, :]).then_inc(sT.h, 1)
        T(t_tp)
        uT_tp[ui] = sT.inc()

        if pend_tpv[0] is not None:
            fnp, pu = pend_tpv[0]
            T(fnp)
            uT_pv[pu] = sT.inc()
            pend_tpv[0] = None

        def v_gT(e, ui=ui, slot=slot):
            e.wait_ge(sT.h, uT_tp[ui])
            if ui >= 2:
                e.wait_ge(sT.h, uT_sc[ui - 2])
            go = slot * 256
            e.tensor_copy(gT[:, go:go + 128], pbank_bf[0][:, 0:128])
            e.tensor_copy(gT[:, go + 128:go + 256], pbank_bf[1][:, 0:128]).then_inc(sV.h, 1)
        V(v_gT)
        uV_gT[ui] = sV.inc()

        if pend_vz[0] is not None:
            fn, pu = pend_vz[0]
            V(fn)
            uV_z[pu] = sV.inc()
            pend_vz[0] = None

        def t_sc(e, ui=ui, slot=slot):
            e.wait_ge(sV.h, uV_gT[ui])
            if ui >= 1:
                e.wait_ge(sS.h, uS_E[ui - 1] - 1)
            go = slot * 256
            e.matmul(pbank[2][:, 0:256], gT[:, go:go + 128], gT[:, go:go + 256], start=True, stop=True).then_inc(sT.h, 1)
            if ui >= 1:
                e.wait_ge(sS.h, uS_E[ui - 1])
            e.matmul(pbank[3][:, 0:256], gT[:, go + 128:go + 256], gT[:, go:go + 256], start=True, stop=True).then_inc(sT.h, 1)
        T(t_sc)
        uT_sc1[ui] = sT.inc()
        uT_sc[ui] = sT.inc()

        def s_E(e, ui=ui, slot=slot):
            e.wait_ge(sT.h, uT_sc1[ui])
            if ui >= 2:
                e.wait_ge(sV.h, uV_R[ui - 2])
            eo = slot * 512
            do = slot * 2
            e.activation(out=Ebuf[:, eo:eo + 256], in_=pbank[2][:, 0:256], func=AF.Exp,
                         scale=SCALE, accum_out=Dbuf[:, do:do + 1]).then_inc(sS.h, 1)
            e.wait_ge(sT.h, uT_sc[ui])
            e.activation(out=Ebuf[:, eo + 256:eo + 512], in_=pbank[3][:, 0:256], func=AF.Exp,
                         scale=SCALE, accum_out=Dbuf[:, do + 1:do + 2]).then_inc(sS.h, 1)
        SC(s_E)
        sS.inc()
        uS_E[ui] = sS.inc()

        def v_R(e, ui=ui, slot=slot):
            e.wait_ge(sS.h, uS_E[ui])
            do = slot * 2
            e.reciprocal(Rbuf[:, do:do + 2], Dbuf[:, do:do + 2]).then_inc(sV.h, 1)
        V(v_R)
        uV_R[ui] = sV.inc()

        def t_pv(e, ui=ui, slot=slot, sbuf_i=sbuf_i, par=par, col=col):
            e.wait_ge(sS.h, uS_E[ui])
            if ui >= 1:
                e.wait_ge(sV.h, uV_z[ui - 1])
            eo = slot * 512
            last = None
            for j in range(2):
                for k in range(2):
                    last = e.matmul(pbank[4 + j][:, 0:128],
                                    Ebuf[:, eo + k * 256 + j * 128:eo + k * 256 + (j + 1) * 128],
                                    xb_t(sbuf_i, par, k)[:, col:col + 128],
                                    start=(k == 0), stop=(k == 1))
            last.then_inc(sT.h, 1)
        pend_tpv[0] = (t_pv, ui)

        def v_z(e, ui=ui, slot=slot, sbuf_i=sbuf_i, par=par, col=col, flushed=False):
            e.wait_ge(sT.h, uT_pv[ui])
            e.wait_ge(sV.h, uV_R[ui])
            if flushed:
                e.drain()
            do = slot * 2
            last = None
            for j in range(2):
                zt = xz_t(sbuf_i, par, j)
                last = e.scalar_tensor_tensor(zt[:, col:col + 128], pbank[4 + j][:, 0:128],
                                              Rbuf[:, do + j:do + j + 1], zt[:, col:col + 128],
                                              ALU.mult, ALU.add)
            last.then_inc(sV.h, 1)
        pend_vz[0] = (v_z, ui)

    def emit_ln_unit(li):
        gi, b, s, par, tl = ln_units[li]
        sbuf_i = gi % 2
        k = 4 * s + 2 * par + tl
        ring = k % 4
        ering = li % 2

        def v_stats(e, li=li, b=b, s=s, par=par, tl=tl, ring=ring, sbuf_i=sbuf_i):
            e.wait_ge(sV.h, uV_z[last_unit_of_pair[(b, s, par)]])
            if li >= 4:
                e.wait_ge(sS.h, ls_apply[li - 4])
            zt = xz_t(sbuf_i, par, tl)
            for i in range(4):
                e.bn_stats(lnstats[:, ring * 24 + 6 * i:ring * 24 + 6 * (i + 1)], zt[:, 512 * i:512 * (i + 1)])
            o = ring * 8
            e.drain()
            e.bn_aggr(lnmv[:, o:o + 2], lnstats[:, ring * 24:ring * 24 + 24]).then_inc(sV.h, 1)
        V(v_stats)
        lv_stats[li] = sV.inc()

        def s_inv2(e, li=li, ring=ring):
            e.wait_ge(sV.h, lv_stats[li])
            o = ring * 8
            e.activation(out=lnmv[:, o + 7:o + 8], in_=lnmv[:, o + 1:o + 2], func=AF.Ln, bias=LNB[:, 0:1])
            e.activation(out=lnmv[:, o + 2:o + 3], in_=lnmv[:, o + 7:o + 8], func=AF.Exp, scale=-0.5)
            e.activation(out=lnmv[:, o + 7:o + 8], in_=lnmv[:, o + 7:o + 8], func=AF.Exp, scale=-1.0)
            e.activation(out=lnmv[:, o + 4:o + 5], in_=lnmv[:, o + 7:o + 8], func=AF.Ln,
                         scale=-EPS, bias=LNB[:, 1:2])
            e.activation(out=lnmv[:, o + 4:o + 5], in_=lnmv[:, o + 4:o + 5], func=AF.Exp, scale=-0.5).then_inc(sS.h, 1)
        SC(s_inv2)
        ls_inv[li] = sS.inc()

        def v_small(e, li=li, ring=ring):
            e.wait_ge(sS.h, ls_inv[li])
            o = ring * 8
            e.scalar_tensor_tensor(lnmv[:, o + 3:o + 4], lnmv[:, o + 0:o + 1], -1.0,
                                   lnmv[:, o + 2:o + 3], ALU.mult, ALU.mult)
            e.tensor_tensor(lnmv[:, o + 5:o + 6], lnmv[:, o + 2:o + 3], lnmv[:, o + 4:o + 5], ALU.mult)
            e.tensor_tensor(lnmv[:, o + 6:o + 7], lnmv[:, o + 3:o + 4], lnmv[:, o + 4:o + 5], ALU.mult).then_inc(sV.h, 1)
        V(v_small)
        lv_sc2 = sV.inc()

        def s_apply(e, li=li, ring=ring, ering=ering, b=b, s=s, par=par, tl=tl, sbuf_i=sbuf_i, k=k, lv_sc2=lv_sc2):
            e.wait_ge(sV.h, lv_sc2)
            o = ring * 8
            zt = xz_t(sbuf_i, par, tl)
            if li == 4 and _os.environ.get("KDBG"):
                e.wait_ge(sC.h, DBGC[0])
            if li >= 4:
                e.wait_ge(sM4[li % 4].h, lm_mirror[li - 4])
                e.wait_ge(sV.h, lv_logit[li - 4])
            if li >= 2:
                e.wait_ge(sE2[li % 2].h, ly_early[li - 2])
            e.activation(out=x1st[:, ring * D:(ring + 1) * D], in_=zt[:, :], func=AF.Identity,
                         bias=lnmv[:, o + 3:o + 4], scale=lnmv[:, o + 2:o + 3])
            e.activation(out=outear[:, ering * D:(ering + 1) * D], in_=zt[:, :], func=AF.Identity,
                         bias=lnmv[:, o + 6:o + 7], scale=lnmv[:, o + 5:o + 6]).then_inc(sS.h, 1)
            e.wait_ge(sS.h, ls_apply[li])
            e.dma_start(out=x1m_h[b][128 * k:128 * (k + 1), :],
                        in_=x1st[:, ring * D:(ring + 1) * D]).then_inc(sM4[li % 4].h, 16)
        ls_apply[li] = sS.inc()
        SC(s_apply)
        lm_mirror[li] = sM4[li % 4].inc(16)

        def v_logit(e, li=li, ring=ring, b=b, k=k):
            e.wait_ge(sS.h, ls_apply[li])
            e.scalar_tensor_tensor(sttscr[:, :], x1st[:, ring * D:(ring + 1) * D], 1.0, WrB[:, :],
                                   ALU.mult, ALU.mult,
                                   accum_out=Lgrid[:, b * 16 + k:b * 16 + k + 1]).then_inc(sV.h, 1)
        pend_logit.append((li, v_logit))

        def y_early(e, li=li, ering=ering, b=b, s=s, par=par, tl=tl):
            e.wait_ge(sS.h, ls_apply[li])
            e.dma_start(out=xview(out_h, b, s, par, tl), in_=outear[:, ering * D:(ering + 1) * D]).then_inc(sE2[li % 2].h, 16)
        Y(y_early)
        ly_early[li] = sE2[li % 2].inc(16)

        if par == 1 and tl == 1:
            seg_consumed[(b, s)] = (ls_apply[li], uT_pv[last_unit_of_pair[(b, s, 1)]])

    router_state = {}

    def build_router_ops(b, prev_ext_t):
        st = {}
        router_state[b] = st
        def v_bnd1(e, b=b, prev_ext_t=prev_ext_t):
            e.wait_ge(sV.h, lv_logit[last_li_of_b[b]])
            if prev_ext_t is not None:
                e.wait_ge(sT.h, prev_ext_t)
            e.reduce_max(bndc[:, 0:1], Lgrid[:, b * 16:(b + 1) * 16], axis=AX.X)
            e.tensor_scalar(lbr_b[:, :], Lgrid[:, b * 16:(b + 1) * 16], -1.0, None, ALU.mult)
            e.reduce_max(bndc[:, 1:2], lbr_b[:, :], axis=AX.X).then_inc(sV.h, 1)
        st["vbnd1"] = sV.inc()
        yield ("V", v_bnd1)

        def t_bnd(e, b=b, st=st):
            e.wait_ge(sV.h, st["vbnd1"])
            e.transpose(pbank[0][0:1, 0:128], bndc[:, 0:1], IDTF[:, :])
            e.transpose(pbank[1][0:1, 0:128], bndc[:, 1:2], IDTF[:, :]).then_inc(sT.h, 1)
        st["tbnd"] = sT.inc()
        yield ("T", t_bnd)

        def v_bnd2(e, b=b, st=st):
            e.wait_ge(sT.h, st["tbnd"])
            e.reduce_max(bnds[:, 0:1], pbank[0][0:1, 0:128], axis=AX.X)
            e.reduce_max(bnds[:, 1:2], pbank[1][0:1, 0:128], axis=AX.X).then_inc(sV.h, 1)
        st["vbnd2"] = sV.inc()
        yield ("V", v_bnd2)

        def t_bnd2(e, b=b, st=st):
            e.wait_ge(sV.h, st["vbnd2"])
            e.matmul(pbank[0][:, 0:2], ONES1F[:, :], bnds[:, :], start=True, stop=True).then_inc(sT.h, 1)
        st["tbnd2"] = sT.inc()
        yield ("T", t_bnd2)

        def v_binit(e, b=b, st=st):
            e.wait_ge(sT.h, st["tbnd2"])
            e.tensor_copy(bndB[:, :], pbank[0][:, 0:2])
            e.drain()
            e.tensor_scalar(hi_b[:, b:b + 1], bndB[:, 0:1], 2.0, None, ALU.mult)
            e.tensor_scalar(lo_b[:, b:b + 1], bndB[:, 1:2], -2.0, -2.0, ALU.mult, ALU.add)
            e.tensor_scalar(L2g[:, :], Lgrid[:, b * 16:(b + 1) * 16], 2.0, None, ALU.mult)
            e.tensor_tensor(mid_b[:, b:b + 1], lo_b[:, b:b + 1], hi_b[:, b:b + 1], ALU.add)
            e.drain()
            with nc.allow_low_precision("bisection counts <=16 are exact in bf16"):
                e.tensor_scalar(tmp16[:, :], L2g[:, :], mid_b[:, b:b + 1], None, ALU.is_gt,
                                ALU.add, accum_out=prt_b[:, b:b + 1]).then_inc(sV.h, 1)
        st[("v", 0)] = sV.inc()
        yield ("V", v_binit)

        for it in range(NBIS):
            def t_it(e, b=b, it=it, st=st):
                e.wait_ge(sV.h, st[("v", it)])
                e.matmul(pbank[it % 2][:, 0:1], ONESC[:, :], prt_b[:, b:b + 1], start=True, stop=True).then_inc(sT.h, 1)
            st[("t", it)] = sT.inc()
            yield ("T", t_it)

            def v_it(e, b=b, it=it, st=st):
                e.wait_ge(sT.h, st[("t", it)])
                ps = pbank[it % 2]
                e.tensor_scalar(cge_b[:, b:b + 1], ps[:, 0:1], float(NSLOT) - 0.5, None, ALU.is_ge)
                e.tensor_tensor(dlt_b[:, b:b + 1], mid_b[:, b:b + 1], lo_b[:, b:b + 1], ALU.subtract)
                e.tensor_tensor(dlt2_b[:, b:b + 1], hi_b[:, b:b + 1], mid_b[:, b:b + 1], ALU.subtract)
                e.scalar_tensor_tensor(lo_b[:, b:b + 1], dlt_b[:, b:b + 1], cge_b[:, b:b + 1],
                                       lo_b[:, b:b + 1], ALU.mult, ALU.add)
                e.scalar_tensor_tensor(hi_b[:, b:b + 1], dlt2_b[:, b:b + 1], cge_b[:, b:b + 1],
                                       mid_b[:, b:b + 1], ALU.mult, ALU.add)
                last = None
                if it < NBIS - 1:
                    e.drain()
                    e.tensor_tensor(mid_b[:, b:b + 1], lo_b[:, b:b + 1], hi_b[:, b:b + 1], ALU.add)
                    e.drain()
                    with nc.allow_low_precision("bisection counts <=16 are exact in bf16"):
                        last = e.tensor_scalar(tmp16[:, :], L2g[:, :], mid_b[:, b:b + 1], None, ALU.is_gt,
                                               ALU.add, accum_out=prt_b[:, b:b + 1])
                else:
                    e.drain()
                    last = e.tensor_scalar(mask_b[:, b * 16:(b + 1) * 16], L2g[:, :],
                                           lo_b[:, b:b + 1], None, ALU.is_gt)
                last.then_inc(sV.h, 1)
            st[("v", it + 1)] = sV.inc()
            yield ("V", v_it)

        def v_fin(e, b=b, st=st):
            e.wait_ge(sV.h, st[("v", NBIS)])
            e.tensor_scalar(lbr_b[:, :], Lgrid[:, b * 16:(b + 1) * 16], bndB[:, 0:1], None,
                            ALU.subtract)
            e.tensor_copy(maskbf[:, :], mask_b[:, b * 16:(b + 1) * 16]).then_inc(sV.h, 1)
        st["vfin"] = sV.inc()
        yield ("V", v_fin)

        def s_wexp(e, b=b, st=st):
            e.wait_ge(sV.h, st["vfin"])
            e.activation(out=wexp[:, :], in_=lbr_b[:, :], func=AF.Exp).then_inc(sS.h, 1)
        st["swexp"] = sS.inc()
        yield ("S", s_wexp)

        def t_ct(e, b=b, st=st):
            e.wait_ge(sV.h, st["vfin"])
            e.matmul(pbank[0][0:16, 0:1], maskbf[:, :], ONESC[:, 0:1], start=True, stop=True).then_inc(sT.h, 1)
        st["tct"] = sT.inc()
        yield ("T", t_ct)

        def v_ct(e, b=b, st=st):
            e.wait_ge(sT.h, st["tct"])
            e.wait_ge(sS.h, st["swexp"])
            e.tensor_tensor(wm32[:, :], wexp[:, :], mask_b[:, b * 16:(b + 1) * 16], ALU.mult)
            e.tensor_copy(ct16[:, :], pbank[0][0:16, 0:1])
            e.reduce_sum(wprt[:, :], wm32[:, :], axis=AX.X).then_inc(sV.h, 1)
        st["vct"] = sV.inc()
        yield ("V", v_ct)

        def t_cb(e, b=b, st=st):
            e.wait_ge(sV.h, st["vct"])
            e.matmul(pbank[1][0:1, 0:16], ct16[:, :], TRI16[:, :], start=True, stop=True)
            e.matmul(pbank[0][0:1, 0:1], wprt[:, :], ONESCF[:, :], start=True, stop=True).then_inc(sT.h, 1)
        st["tcb"] = sT.inc()
        yield ("T", t_cb)

        def v_cb(e, b=b, st=st):
            e.wait_ge(sT.h, st["tcb"])
            e.tensor_copy(zt_s[:, b:b + 1], pbank[0][0:1, 0:1])
            e.tensor_copy(cb16[:, :], pbank[1][0:1, 0:16])
            e.reciprocal(zt_s[:, b:b + 1], zt_s[:, b:b + 1]).then_inc(sV.h, 1)
        st["vcb"] = sV.inc()
        yield ("V", v_cb)

        def t_pos(e, b=b, st=st):
            e.wait_ge(sV.h, st["vcb"])
            e.matmul(pbank[0][:, 0:16], ONESC[0:1, :], cb16[:, :], start=True, stop=True)
            e.matmul(pbank[1][:, 0:16], TRI[:, :], maskbf[:, :], start=True, stop=True).then_inc(sT.h, 1)
        st["tpos"] = sT.inc()
        yield ("T", t_pos)

        def v_pos(e, b=b, st=st):
            e.wait_ge(sT.h, st["tpos"])
            e.tensor_copy(pos_b[:, :], pbank[0][:, 0:16])
            e.tensor_copy(vspc[:, 0:1], vspc[:, 1:2])
            e.tensor_tensor(pos_b[:, :], pos_b[:, :], pbank[1][:, 0:16], ALU.add)
            e.drain()
            for kk in range(16):
                e.tensor_scalar(PT[:, 256 * kk:256 * (kk + 1)], IO256[:, :], pos_b[:, kk:kk + 1],
                                mask_b[:, b * 16 + kk:b * 16 + kk + 1], ALU.is_equal, ALU.mult)
            e.tensor_copy(VALS[:, 0:48:3], SG[:, :])
            e.tensor_copy(VALS[:, 1:48:3], TG[:, :])
            last = None
            for kk in range(16):
                last = e.tensor_scalar(VALS[:, 3 * kk + 2:3 * kk + 3], wm32[:, kk:kk + 1], 1.0, None, ALU.mult)
            last.then_inc(sV.h, 1)
        st["vpos"] = sV.inc()
        yield ("V", v_pos)

        def t_rz(e, b=b, st=st):
            e.wait_ge(sV.h, st["vpos"])
            e.matmul(pbank[0][:, 0:1], ONES1F[:, :], zt_s[:, b:b + 1], start=True, stop=True).then_inc(sT.h, 1)
        st["trz"] = sT.inc()
        yield ("T", t_rz)

        def v_rz(e, b=b, st=st):
            e.wait_ge(sT.h, st["trz"])
            e.tensor_copy(rzB[:, :], pbank[0][:, 0:1]).then_inc(sV.h, 1)
        st["vrz"] = sV.inc()
        yield ("V", v_rz)

        for half in range(2):
            def t_ext(e, b=b, half=half, st=st):
                e.wait_ge(sV.h, st["vrz"] if half == 0 else st[("vext", 0)])
                last = None
                for kk in range(16):
                    last = e.matmul(pbank[half][:, 0:3],
                                    PT[:, 256 * kk + 128 * half:256 * kk + 128 * (half + 1)],
                                    VALS[:, 3 * kk:3 * (kk + 1)], start=(kk == 0), stop=(kk == 15))
                last.then_inc(sT.h, 1)
            st[("text", half)] = sT.inc()
            yield ("T", t_ext)

            def v_ext(e, b=b, half=half, st=st):
                e.wait_ge(sT.h, st[("text", half)])
                stg = b * 2 + half
                o = stg * 4
                e.tensor_copy(slotv[:, o:o + 3], pbank[half][:, 0:3])
                e.tensor_copy(vspc[:, 0:1], vspc[:, 1:2])
                e.tensor_tensor(slotv[:, o + 3:o + 4], slotv[:, o + 2:o + 3], rzB[:, :], ALU.mult)
                e.tensor_copy(sigI[:, stg:stg + 1], slotv[:, o:o + 1])
                e.tensor_copy(tI[:, stg:stg + 1], slotv[:, o + 1:o + 2]).then_inc(sV.h, 1)
            st[("vext", half)] = sV.inc()
            yield ("V", v_ext)

        router_state[b] = st

    emit_load(0)
    emit_load(1)
    emit_cast(0)
    emit_cast(1)

    n_units_per_pair = 16
    r0gen = [None]

    def splice(n):
        if r0gen[0] is None:
            return
        for _ in range(n):
            try:
                t_, f_ = next(r0gen[0])
            except StopIteration:
                r0gen[0] = None
                return
            OPS[t_].append(f_)

    DBGC = [0]
    for gi, (b, s) in enumerate(pair_list):
        logs = pend_logit[:]
        del pend_logit[:]
        for u in range(n_units_per_pair):
            emit_attn_unit(gi * n_units_per_pair + u)
            if logs and u < 2:
                for _ in range(2):
                    li_, fn_ = logs.pop(0)
                    V(fn_)
                    lv_logit[li_] = sV.inc()
            if b == 1 and not logs:
                splice(2)
        if pend_tpv[0] is not None:
            fnp, pu = pend_tpv[0]
            T(fnp)
            uT_pv[pu] = sT.inc()
            pend_tpv[0] = None
        if pend_vz[0] is not None:
            fn, pu = pend_vz[0]
            import functools
            V(functools.partial(fn, flushed=True))
            uV_z[pu] = sV.inc()
            pend_vz[0] = None
        for li, lu in enumerate(ln_units):
            if lu[0] == gi:
                emit_ln_unit(li)
        if gi == 0 and _os.environ.get("KDBG"):
            def g_dbg(e, li_p0=3):
                e.wait_ge(sS.h, ls_apply[li_p0])
                e.dma_start(out=dbg_h[:, 0:2048], in_=xz[:, 0:2048]).then_inc(sC.h, 16)
                e.dma_start(out=dbg_h[:, 2048:4096], in_=xbuf[:, 0:2048]).then_inc(sC.h, 16)
                e.dma_start(out=dbg_h[:, 4096:6144], in_=x1st[:, 0:2048]).then_inc(sC.h, 16)
                e.dma_start(out=dbg_h[0:128, 6144:6176], in_=lnmv[:, :]).then_inc(sC.h, 16)
            G(g_dbg)
            DBGC[0] = sC.inc(16 * 4)
        if gi + 2 < len(pair_list):
            emit_load(gi + 2)
            emit_cast(gi + 2)
        if b == 0 and s == NSEG - 1:
            r0gen[0] = build_router_ops(0, None)

    for li_, fn_ in pend_logit:
        V(fn_)
        lv_logit[li_] = sV.inc()
    del pend_logit[:]

    splice(9999)

    for t_, f_ in build_router_ops(1, router_state[0][("text", 1)]):
        OPS[t_].append(f_)


    esA.close()
    esB = ExitStack()
    g1T = esB.enter_context(nc.sbuf_tensor("g1T", [128, 64 * 512], BF16))
    w1cb = esB.enter_context(nc.sbuf_tensor("w1cb", [128, 3 * 4096], BF16))
    h2sb = esB.enter_context(nc.sbuf_tensor("h2sb", [128, 2 * 512], BF16))
    w2cb = esB.enter_context(nc.sbuf_tensor("w2cb", [128, 2 * 8192], BF16))
    ybuf = esB.enter_context(nc.sbuf_tensor("ybuf", [128, 4 * D], F32))
    fxb = esB.enter_context(nc.sbuf_tensor("fxb", [128, 4 * D], BF16))
    fxT = esB.enter_context(nc.sbuf_tensor("fxT", [128, 16 * 512], BF16))
    outro = esB.enter_context(nc.sbuf_tensor("outro", [128, 2 * D], F32))

    gath_done = {}
    for b in range(NB):
        for half in range(2):
            stg = b * 2 + half

            def g_gather(e, b=b, half=half, stg=stg):
                if stg == 0:
                    tS, tT = seg_consumed[(1, NSEG - 1)]
                    e.wait_ge(sS.h, tS)
                    e.wait_ge(sT.h, tT)
                    e.wait_ge(sV.h, router_state[1][("vext", 1)])
                    for mm in range(4):
                        e.wait_ge(sM4[mm].h, sM4[mm].n)
                else:
                    e.wait_ge(sGG.h, gath_done[stg - 1])
                e.wait_ge(sV.h, router_state[b][("vext", half)])
                e.indirect_dma_start(fxb[:, stg * D:(stg + 1) * D], None,
                                     x1m_h.rearrange("b s d -> (b s) d"),
                                     IndirectOffsetOnAxis(sigI[:, stg:stg + 1], 0),
                                     element_offset=b * S * D).then_inc(sGG.h, 16)
            G(g_gather)
            gath_done[stg] = sGG.inc(16)

    fxT_done = {}
    tp_evac = {}
    for stg in range(4):
        for t in range(16):
            idx = stg * 16 + t

            def t_fxT(e, stg=stg, t=t, idx=idx):
                e.wait_ge(sGG.h, gath_done[stg])
                if idx == 0:
                    e.wait_ge(sV.h, uV_gT[len(attn_units) - 1])
                if idx >= 2:
                    e.wait_ge(sV.h, tp_evac[idx - 2])
                e.transpose(pbank_bf[idx % 2][:, 0:128], fxb[:, stg * D + 128 * t:stg * D + 128 * (t + 1)],
                            IDT[:, :]).then_inc(sT.h, 1)
            T(t_fxT)
            fxT_done[idx] = sT.inc()

            def v_fxT(e, stg=stg, t=t, idx=idx):
                e.wait_ge(sT.h, fxT_done[idx])
                e.tensor_copy(fxT[:, 512 * t + 128 * stg:512 * t + 128 * (stg + 1)],
                              pbank_bf[idx % 2][:, 0:128]).then_inc(sV.h, 1)
            V(v_fxT)
            tp_evac[idx] = sV.inc()
    FXT_ALL = sV.n

    w1_done = {}
    h1_done = {}
    g1_done = {}
    for fb in range(32):
        def g_w1(e, fb=fb):
            if fb >= 3:
                e.wait_ge(sT.h, h1_done[2 * (fb - 3) + 1])
            e.dma_start(out=w1cb[:, (fb % 3) * 4096:(fb % 3 + 1) * 4096],
                        in_=w1_h[:, 256 * fb:256 * (fb + 1)].rearrange("(t p) f -> p t f", p=128)).then_inc(sW1[fb % 3].h, 16)
        G(g_w1)
        w1_done[fb] = sW1[fb % 3].inc(16)

    for f in range(64):
        fb = f // 2
        fo = f % 2

        def t_h1(e, f=f, fb=fb, fo=fo):
            if fo == 0:
                e.wait_ge(sW1[fb % 3].h, w1_done[fb])
            if f == 0:
                e.wait_ge(sV.h, FXT_ALL)
            if f >= 4:
                e.wait_ge(sS.h, g1_done[f - 4])
            bk = f % 4
            base = (fb % 3) * 4096 + fo * 128
            last = None
            for t in range(16):
                last = e.matmul(pbank[bk][:, 0:512],
                                w1cb[:, base + 256 * t:base + 256 * t + 128],
                                fxT[:, 512 * t:512 * (t + 1)], start=(t == 0), stop=(t == 15))
            last.then_inc(sT.h, 1)
        T(t_h1)
        h1_done[f] = sT.inc()

        def s_g1(e, f=f):
            e.wait_ge(sT.h, h1_done[f])
            bk = f % 4
            e.activation(out=g1T[:, 512 * f:512 * (f + 1)], in_=pbank[bk][:, 0:512], func=AF.Gelu,
                         bias=b1T[:, f:f + 1]).then_inc(sS.h, 1)
        SC(s_g1)
        g1_done[f] = sS.inc()

    w2_done = {}
    h2_done = {}
    h2s_done = {}
    h2t_done = {}
    yev_done = {}
    wf_done = {}
    vc = {}
    for c in range(16):
        def g_w2(e, c=c):
            if c == 0:
                tS, tT = seg_consumed[(1, NSEG - 1)]
                e.wait_ge(sS.h, tS)
                e.wait_ge(sT.h, tT)
            if c >= 2:
                e.wait_ge(sT.h, h2_done[c - 2])
            src = w2_h[:, 128 * c:128 * (c + 1)].rearrange("(t p) f -> p t f", p=128)
            if c < 2:
                e.dma_start(out=w2cb[:, (c % 2) * 8192:(c % 2 + 1) * 8192],
                            in_=src).then_inc(sW2[c % 2].h, 16)
            else:
                e.dma_start(out=w2cb[:, (c % 2) * 8192:(c % 2) * 8192 + 4096],
                            in_=src[:, 0:32, :]).then_inc(sW2[c % 2].h, 16)
        G(g_w2)
        w2_done[c] = sW2[c % 2].inc(16)

    for c in range(2, 16):
        def g_wf(e, c=c):
            if c == 2:
                e.wait_ge(sT.h, h1_done[63])
            else:
                e.wait_ge(sS.h, vc[c - 1])
            e.dma_start(out=fxT[:, :].bitcast(F32),
                        in_=w2_h[:, 128 * c:128 * (c + 1)].rearrange("(t p) f -> p t f", p=128)[:, 32:64, :]).then_inc(sWF.h, 16)
        Y(g_wf)
        wf_done[c] = sWF.inc(16)

    for c in range(16):
        def t_h2(e, c=c):
            e.wait_ge(sW2[c % 2].h, w2_done[c])
            if c >= 2:
                e.wait_ge(sS.h, vc[c])
            if c == 0:
                e.wait_ge(sS.h, g1_done[63])
            if c >= 2:
                e.wait_ge(sS.h, h2s_done[c - 2])
            bk = 4 + c % 2
            last = None
            for f in range(64):
                last = e.matmul(pbank[bk][:, 0:512],
                                w2cb[:, (c % 2) * 8192 + 128 * f:(c % 2) * 8192 + 128 * (f + 1)],
                                g1T[:, 512 * f:512 * (f + 1)], start=(f == 0), stop=(f == 63))
            last.then_inc(sT.h, 1)
        T(t_h2)
        h2_done[c] = sT.inc()

        def s_h2(e, c=c):
            e.wait_ge(sT.h, h2_done[c])
            if c >= 2:
                e.wait_ge(sT.h, h2t_done[(c - 2, 3)])
            bk = 4 + c % 2
            e.activation(out=h2sb[:, (c % 2) * 512:(c % 2 + 1) * 512], in_=pbank[bk][:, 0:512],
                         func=AF.Identity, bias=b2T[:, c:c + 1]).then_inc(sS.h, 1)
        SC(s_h2)
        h2s_done[c] = sS.inc()

        if c + 2 < 16:
            def s_wcast(e, c=c + 2):
                e.wait_ge(sWF.h, wf_done[c])
                e.wait_ge(sT.h, h2_done[c - 2])
                e.activation(out=w2cb[:, (c % 2) * 8192 + 4096:(c % 2 + 1) * 8192],
                             in_=fxT[:, :].bitcast(F32), func=AF.Identity).then_inc(sS.h, 1)
            SC(s_wcast)
            vc[c + 2] = sS.inc()

        for stg in range(4):
            idx = c * 4 + stg

            def t_h2t(e, c=c, stg=stg, idx=idx):
                e.wait_ge(sS.h, h2s_done[c])
                if idx >= 2:
                    e.wait_ge(sV.h, yev_done[idx - 2])
                e.transpose(pbank_bf[idx % 2][:, 0:128],
                            h2sb[:, (c % 2) * 512 + 128 * stg:(c % 2) * 512 + 128 * (stg + 1)],
                            IDT[:, :]).then_inc(sT.h, 1)
            T(t_h2t)
            h2t_done[(c, stg)] = sT.inc()

            def v_yev(e, c=c, stg=stg, idx=idx):
                e.wait_ge(sT.h, h2t_done[(c, stg)])
                e.tensor_scalar(ybuf[:, stg * D + 128 * c:stg * D + 128 * (c + 1)],
                                pbank_bf[idx % 2][:, 0:128],
                                slotv[:, stg * 4 + 3:stg * 4 + 4], None, ALU.mult).then_inc(sV.h, 1)
            V(v_yev)
            yev_done[idx] = sV.inc()

    fin_scat = {}
    fin_conv = {}
    for stg in range(4):
        b = stg // 2

        def s_conv(e, stg=stg):
            e.wait_ge(sGG.h, gath_done[stg])
            if stg >= 2:
                e.wait_ge(sGG.h, fin_scat[stg - 2])
            e.activation(out=outro[:, (stg % 2) * D:(stg % 2 + 1) * D],
                         in_=fxb[:, stg * D:(stg + 1) * D], func=AF.Identity,
                         scale=2.0).then_inc(sS.h, 1)
        SC(s_conv)
        fin_conv[stg] = sS.inc()

        def v_fin2(e, stg=stg):
            e.wait_ge(sV.h, yev_done[15 * 4 + stg])
            e.wait_ge(sS.h, fin_conv[stg])
            yt = ybuf[:, stg * D:(stg + 1) * D]
            e.tensor_tensor(yt[:, :], yt[:, :], outro[:, (stg % 2) * D:(stg % 2 + 1) * D], ALU.add)
            o = stg * 8
            e.drain()
            for i in range(4):
                e.bn_stats(lnstats[:, stg * 24 + 6 * i:stg * 24 + 6 * (i + 1)], yt[:, 512 * i:512 * (i + 1)])
            e.drain()
            e.bn_aggr(lnmv[:, o:o + 2], lnstats[:, stg * 24:stg * 24 + 24])
            e.drain()
            e.tensor_scalar(lnmv[:, o + 1:o + 2], lnmv[:, o + 1:o + 2], EPS, None, ALU.add)
            e.drain()
            e.reciprocal(lnmv[:, o + 1:o + 2], lnmv[:, o + 1:o + 2]).then_inc(sV.h, 1)
        V(v_fin2)
        fin_stats = sV.inc()

        def s_fin2(e, stg=stg, fin_stats=fin_stats):
            e.wait_ge(sV.h, fin_stats)
            o = stg * 8
            e.activation(out=lnmv[:, o + 1:o + 2], in_=lnmv[:, o + 1:o + 2], func=AF.Ln)
            e.activation(out=lnmv[:, o + 1:o + 2], in_=lnmv[:, o + 1:o + 2], func=AF.Exp, scale=0.5).then_inc(sS.h, 1)
        SC(s_fin2)
        fin_inv = sS.inc()

        def v_fin3(e, stg=stg, fin_inv=fin_inv):
            e.wait_ge(sS.h, fin_inv)
            o = stg * 8
            e.scalar_tensor_tensor(lnmv[:, o + 3:o + 4], lnmv[:, o + 0:o + 1], -1.0,
                                   lnmv[:, o + 1:o + 2], ALU.mult, ALU.mult).then_inc(sV.h, 1)
        V(v_fin3)
        fin_b = sV.inc()

        def s_fin3(e, stg=stg, fin_b=fin_b):
            e.wait_ge(sV.h, fin_b)
            o = stg * 8
            e.activation(out=outro[:, (stg % 2) * D:(stg % 2 + 1) * D], in_=ybuf[:, stg * D:(stg + 1) * D],
                         func=AF.Identity, bias=lnmv[:, o + 3:o + 4],
                         scale=lnmv[:, o + 1:o + 2]).then_inc(sS.h, 1)
        SC(s_fin3)
        fin_app = sS.inc()

        def g_scat(e, stg=stg, b=b, fin_app=fin_app, prev_gg=sGG.n):
            e.wait_ge(sS.h, fin_app)
            e.wait_ge(sGG.h, prev_gg)
            if stg == 0:
                for ee in range(2):
                    e.wait_ge(sE2[ee].h, sE2[ee].n)
            e.indirect_dma_start(out_h.rearrange("b s d -> (b s) d"),
                                 IndirectOffsetOnAxis(tI[:, stg:stg + 1], 0),
                                 outro[:, (stg % 2) * D:(stg % 2 + 1) * D], None,
                                 element_offset=b * S * D).then_inc(sGG.h, 16)
        G(g_scat)
        fin_scat[stg] = sGG.inc(16)

    FINAL_GG = sGG.n
    FINAL_E = [s.n for s in sE2]

    def g_end(e):
        e.wait_ge(sGG.h, FINAL_GG)
        for i in range(2):
            e.wait_ge(sE2[i].h, FINAL_E[i])
    G(g_end)

    def y_end(e):
        e.wait_ge(sGG.h, FINAL_GG)
        for i in range(2):
            e.wait_ge(sE2[i].h, FINAL_E[i])
    Y(y_end)

    import os
    _tr = float(os.environ.get("KTRUNC", "1.0"))
    if _tr < 1.0:
        for _k in OPS:
            OPS[_k] = OPS[_k][:max(1, int(len(OPS[_k]) * _tr))]
    with nc.Block() as block:
        @block.tensor
        def _(eng):
            for fn in OPS["T"]:
                fn(eng)

        @block.vector
        def _(eng):
            for fn in OPS["V"]:
                fn(eng)

        @block.scalar
        def _(eng):
            for fn in OPS["S"]:
                fn(eng)

        @block.gpsimd
        def _(eng):
            for fn in OPS["G"]:
                fn(eng)

        @block.sync
        def _(eng):
            for fn in OPS["Y"]:
                fn(eng)

    esB.close()
    es.close()
    return nc


def _consts():
    c = {}
    c["IDT"] = np.eye(128, dtype=np.float32).astype(ml_dtypes.bfloat16)
    c["IDTF"] = np.eye(128, dtype=np.float32)
    c["TRI"] = np.triu(np.ones((128, 128), np.float32), 1).astype(ml_dtypes.bfloat16)
    c["TRI16"] = np.triu(np.ones((16, 16), np.float32), 1).astype(ml_dtypes.bfloat16)
    c["ONESC"] = np.ones((128, 128), np.float32).astype(ml_dtypes.bfloat16)
    c["ONES1F"] = np.ones((1, 128), np.float32)
    c["ONESCF"] = np.ones((128, 1), np.float32)
    c["IO256"] = np.broadcast_to(np.arange(256, dtype=np.float32)[None, :], (128, 256)).copy()
    c["LNB"] = np.broadcast_to(np.array([[1e-5, 1.0 + 1e-5 / 4.0]], np.float32), (128, 2)).copy()
    q = np.arange(128)[:, None]
    k = np.arange(16)[None, :]
    c["SG"] = (128 * k + q).astype(np.float16)
    s_ = k // 4
    par_ = (k % 4) // 2
    tl_ = k % 2
    c["TG"] = (512 * s_ + 256 * tl_ + 2 * q + par_).astype(np.float16)
    return c


_NC_CACHE = {}


def kernel(L0_attn_x, L0_ffn_Wr, L0_ffn_br, L0_ffn_W1, L0_ffn_b1, L0_ffn_W2, L0_ffn_b2):
    x = np.ascontiguousarray(np.asarray(L0_attn_x, dtype=np.float32))
    Wr = np.ascontiguousarray(np.asarray(L0_ffn_Wr, dtype=np.float32).reshape(1, D))
    br = np.ascontiguousarray(np.asarray(L0_ffn_br, dtype=np.float32).reshape(1, 1))
    W1 = np.ascontiguousarray(np.asarray(L0_ffn_W1, dtype=np.float32))
    b1 = np.ascontiguousarray(np.asarray(L0_ffn_b1, dtype=np.float32).reshape(64, 128).T)
    W2 = np.ascontiguousarray(np.asarray(L0_ffn_W2, dtype=np.float32))
    b2 = np.ascontiguousarray(np.asarray(L0_ffn_b2, dtype=np.float32).reshape(16, 128).T)
    consts = _consts()

    if "nc" not in _NC_CACHE:
        _NC_CACHE["nc"] = build_nc()
    nc = _NC_CACHE["nc"]

    in_maps = []
    for c in range(8):
        m = {"x": np.ascontiguousarray(x[2 * c:2 * c + 2]),
             "Wr": Wr, "br": br, "W1": W1, "b1": b1, "W2": W2, "b2": b2}
        m.update(consts)
        in_maps.append(m)
    res = run_bass_kernel_spmd(nc, in_maps, core_ids=list(range(8)))
    out = np.concatenate([res.results[i]["out"] for i in range(8)], axis=0)
    return out.astype(np.float32)


# version 13 (final)
# speedup vs baseline: 1.3269x; 1.0006x over previous
import os as _os
import numpy as np
import ml_dtypes
import concourse.bass as bass
import concourse.mybir as mybir
from concourse.bass_utils import run_bass_kernel_spmd
from concourse.bass_core_rust import IndirectOffsetOnAxis

F32 = mybir.dt.float32
BF16 = mybir.dt.bfloat16
FP16 = mybir.dt.float16
I32 = mybir.dt.int32
AF = mybir.ActivationFunctionType
ALU = mybir.AluOpType
AX = mybir.AxisListType

NB = 2
NSEG = 4
D = 2048
S = 2048
DFF = 8192
NSLOT = 256
SCALE = float(1.0 / np.sqrt(128.0))
EPS = 1e-5
NBIS = 33
WOFF = 16.0


class Sem:
    def __init__(self, es, nc, name):
        self.h = es.enter_context(nc.semaphore(name))
        self.n = 0

    def inc(self, k=1):
        self.n += k
        return self.n


def build_nc():
    from contextlib import ExitStack
    nc = bass.Bass("TRN2", target_bir_lowering=False, detect_race_conditions=False)
    es = ExitStack()

    x_h = nc.dram_tensor("x", [NB, S, D], F32, kind="ExternalInput").ap()
    wr_h = nc.dram_tensor("Wr", [1, D], F32, kind="ExternalInput").ap()
    br_h = nc.dram_tensor("br", [1, 1], F32, kind="ExternalInput").ap()
    w1_h = nc.dram_tensor("W1", [D, DFF], F32, kind="ExternalInput").ap()
    b1_h = nc.dram_tensor("b1", [128, 64], F32, kind="ExternalInput").ap()
    w2_h = nc.dram_tensor("W2", [DFF, D], F32, kind="ExternalInput").ap()
    b2_h = nc.dram_tensor("b2", [128, 16], F32, kind="ExternalInput").ap()
    idt_h = nc.dram_tensor("IDT", [128, 128], BF16, kind="ExternalInput").ap()
    idtf_h = nc.dram_tensor("IDTF", [128, 128], F32, kind="ExternalInput").ap()
    tri_h = nc.dram_tensor("TRI", [128, 128], BF16, kind="ExternalInput").ap()
    tri16_h = nc.dram_tensor("TRI16", [16, 16], BF16, kind="ExternalInput").ap()
    onesc_h = nc.dram_tensor("ONESC", [128, 128], BF16, kind="ExternalInput").ap()
    ones1f_h = nc.dram_tensor("ONES1F", [1, 128], F32, kind="ExternalInput").ap()
    onescf_h = nc.dram_tensor("ONESCF", [128, 1], F32, kind="ExternalInput").ap()
    io256_h = nc.dram_tensor("IO256", [128, 256], F32, kind="ExternalInput").ap()
    lnb_h = nc.dram_tensor("LNB", [128, 2], F32, kind="ExternalInput").ap()
    sg_h = nc.dram_tensor("SG", [128, 16], FP16, kind="ExternalInput").ap()
    tg_h = nc.dram_tensor("TG", [128, 16], FP16, kind="ExternalInput").ap()
    out_h = nc.dram_tensor("out", [NB, S, D], F32, kind="ExternalOutput").ap()
    dbg_h = nc.dram_tensor("dbg", [128, 6176], F32, kind="ExternalOutput").ap()
    x1m_h = nc.dram_tensor("x1m", [NB, S, D], BF16, kind="ExternalOutput" if _os.environ.get("KDBG") else "Internal").ap()

    def xview(h, b, s, par, tl):
        return h[b].rearrange("(s tl p two) d -> s two tl p d", s=4, tl=2, p=128, two=2)[s, par, tl]

    sb = lambda name, shape, dt: es.enter_context(nc.sbuf_tensor(name, shape, dt))
    IDT = sb("c_idt", [128, 128], BF16)
    IDTF = sb("c_idtf", [128, 128], F32)
    TRI = sb("c_tri", [128, 128], BF16)
    TRI16 = sb("c_tri16", [16, 16], BF16)
    ONESC = sb("c_onesc", [128, 128], BF16)
    ONES1F = sb("c_ones1f", [1, 128], F32)
    ONESCF = sb("c_onescf", [128, 1], F32)
    IO256 = sb("c_io256", [128, 256], F32)
    LNB = sb("c_lnb", [128, 2], F32)
    SG = sb("c_sg", [128, 16], FP16)
    TG = sb("c_tg", [128, 16], FP16)
    brS = sb("c_brs", [1, 1], F32)
    brB = sb("c_brb", [128, 1], F32)
    b1T = sb("c_b1t", [128, 64], F32)
    b2T = sb("c_b2t", [128, 16], F32)

    lnstats = sb("lnstats", [128, 4 * 24], F32)
    lnmv = sb("lnmv", [128, 4 * 8], F32)
    lo_b = sb("lo", [128, 2], F32)
    hi_b = sb("hi", [128, 2], F32)
    mid_b = sb("mid", [128, 2], F32)
    tmp16 = sb("tmp16", [128, 16], BF16)
    prt_b = sb("prt", [128, 2], BF16)
    cge_b = sb("cge", [128, 2], F32)
    dlt_b = sb("dlt", [128, 2], F32)
    dlt2_b = sb("dlt2", [128, 2], F32)
    L2g = sb("L2g", [128, 16], F32)
    vspc = sb("vspc", [128, 2], F32)
    mask_b = sb("mask", [128, 2 * 16], F32)
    maskbf = sb("maskbf", [128, 16], BF16)
    ct16 = sb("ct16", [16, 1], BF16)
    cb16 = sb("cb16", [1, 16], BF16)
    pos_b = sb("pos", [128, 16], F32)
    lbr_b = sb("lbr", [128, 16], F32)
    wexp = sb("wexp", [128, 16], F32)
    wm32 = sb("wm32", [128, 16], F32)
    wprt = sb("wprt", [128, 1], F32)
    bndc = sb("bndc", [128, 2], F32)
    bnds = sb("bnds", [1, 2], F32)
    bndB = sb("bndB", [128, 2], F32)
    VALS = sb("VALS", [128, 16 * 3], FP16)
    zt_s = sb("zt", [1, 2], F32)
    rzB = sb("rzB", [128, 1], F32)
    slotv = sb("slotv", [128, 4 * 4], F32)
    sigI = sb("sigI", [128, 4], I32)
    tI = sb("tI", [128, 4], I32)

    pbank = [es.enter_context(nc.psum_tensor(f"pb{i}", [128, 512], F32)) for i in range(6)]
    pbank_bf = [es.enter_context(nc.psum_tensor(f"pbb{i}", [128, 512], BF16)) for i in range(2)]

    sT = Sem(es, nc, "sT")
    sV = Sem(es, nc, "sV")
    sS = Sem(es, nc, "sS")
    sC = Sem(es, nc, "sC")
    sWrt = Sem(es, nc, "sWrt")
    sX = [Sem(es, nc, f"sX{i}") for i in range(2)]
    sXB = [Sem(es, nc, f"sXB{i}") for i in range(2)]
    sE2 = [Sem(es, nc, f"sE2{i}") for i in range(2)]
    sM4 = [Sem(es, nc, f"sM4{i}") for i in range(4)]
    sW1 = [Sem(es, nc, f"sW1{i}") for i in range(3)]
    sW2 = [Sem(es, nc, f"sW2{i}") for i in range(2)]
    sWF = Sem(es, nc, "sWF")
    sGG = Sem(es, nc, "sGG")

    OPS = {"T": [], "V": [], "S": [], "G": [], "Y": []}
    T = lambda fn: OPS["T"].append(fn)
    V = lambda fn: OPS["V"].append(fn)
    SC = lambda fn: OPS["S"].append(fn)
    G = lambda fn: OPS["G"].append(fn)
    Y = lambda fn: OPS["Y"].append(fn)

    def setup_loads(e):
        for dst, src in ((IDT, idt_h), (IDTF, idtf_h), (TRI, tri_h), (TRI16, tri16_h), (ONESC, onesc_h),
                         (ONES1F, ones1f_h), (ONESCF, onescf_h), (IO256, io256_h),
                         (LNB, lnb_h), (SG, sg_h), (TG, tg_h), (brS, br_h)):
            e.dma_start(out=dst[:, :], in_=src[:, :]).then_inc(sC.h, 16)
        e.dma_start(out=b1T[:, :], in_=b1_h[:, :]).then_inc(sC.h, 16)
        e.dma_start(out=b2T[:, :], in_=b2_h[:, :]).then_inc(sC.h, 16)
    Y(setup_loads)
    LD_CONST = sC.inc(16 * 14)

    def setup_wrt(e):
        e.dma_start(out=WrT[:, :], in_=wr_h[:, :]).then_inc(sWrt.h, 16)
    G(setup_wrt)
    LD_WRT = sWrt.inc(16)

    def setup_bcast_t(e):
        e.wait_ge(sC.h, LD_CONST)
        e.wait_ge(sWrt.h, LD_WRT)
        for i in range(4):
            e.matmul(pbank[i][:, :], ONESC[0:1, :], WrT[:, 512 * i:512 * (i + 1)], start=True, stop=True)
        e.matmul(pbank[4][:, 0:1], ONES1F[:, :], brS[:, :], start=True, stop=True).then_inc(sT.h, 1)
    T(setup_bcast_t)
    T_BCAST = sT.inc()

    def setup_bcast_v(e):
        e.memset(vspc[:, :], 0.0)
        e.wait_ge(sT.h, T_BCAST)
        for i in range(4):
            e.tensor_copy(WrB[:, 512 * i:512 * (i + 1)], pbank[i][:, :])
        e.tensor_copy(brB[:, :], pbank[4][:, 0:1]).then_inc(sV.h, 1)
    V(setup_bcast_v)
    V_BCAST = sV.inc()

    esA = ExitStack()
    WrT = esA.enter_context(nc.sbuf_tensor("c_wrt", [1, D], BF16))
    WrB = esA.enter_context(nc.sbuf_tensor("c_wrb", [128, D], BF16))
    xz = esA.enter_context(nc.sbuf_tensor("xz", [128, 2 * 4 * D], F32))
    xbuf = esA.enter_context(nc.sbuf_tensor("xbw", [128, 2 * 4 * D], BF16))
    x1st = esA.enter_context(nc.sbuf_tensor("x1stage", [128, 4 * D], BF16))
    sttscr = esA.enter_context(nc.sbuf_tensor("sttscr", [128, D], BF16))
    Lgrid = esA.enter_context(nc.sbuf_tensor("Lgrid", [128, 2 * 16], F32))
    outear = esA.enter_context(nc.sbuf_tensor("outear", [128, 2 * D], F32))
    PT = esA.enter_context(nc.sbuf_tensor("PT", [128, 16 * 256], FP16))
    gT = esA.enter_context(nc.sbuf_tensor("gT", [128, 2 * 256], BF16))
    Ebuf = esA.enter_context(nc.sbuf_tensor("Ebuf", [128, 2 * 512], BF16))
    Dbuf = esA.enter_context(nc.sbuf_tensor("Dbuf", [128, 2 * 2], F32))
    Rbuf = esA.enter_context(nc.sbuf_tensor("Rbuf", [128, 2 * 2], F32))

    def xz_t(sbuf_i, par, tl):
        off = (sbuf_i * 4 + par * 2 + tl) * D
        return xz[:, off:off + D]

    def xb_t(sbuf_i, par, tl):
        off = (sbuf_i * 4 + par * 2 + tl) * D
        return xbuf[:, off:off + D]

    pair_list = [(b, s) for b in range(NB) for s in range(NSEG)]
    xload_done = {}
    xcast_done = {}
    seg_consumed = {}

    uT_tp, uT_sc, uT_pv = {}, {}, {}
    uT_sc1 = {}
    uS_E = {}
    uV_gT, uV_R, uV_z = {}, {}, {}
    ls_apply, lv_logit, lm_mirror, ly_early = {}, {}, {}, {}
    lv_stats, ls_inv, lv_bias = {}, {}, {}

    pend_vz = [None]
    pend_tpv = [None]
    pend_logit = []
    attn_units = []
    for b, s in pair_list:
        for par in range(2):
            for hh in range(8):
                attn_units.append((b, s, par, hh))
    last_unit_of_pair = {}
    for ui, u in enumerate(attn_units):
        last_unit_of_pair[(u[0], u[1], u[2])] = ui

    ln_units = []
    for gi, (b, s) in enumerate(pair_list):
        for par in range(2):
            for tl in range(2):
                ln_units.append((gi, b, s, par, tl))
    last_li_of_b = {b: max(i for i, u in enumerate(ln_units) if u[1] == b) for b in range(NB)}

    def emit_load(gi):
        b, s = pair_list[gi]
        sbuf_i = gi % 2

        def loadseg(e, b=b, s=s, sbuf_i=sbuf_i, gi=gi):
            if gi == 2 and _os.environ.get("KDBG"):
                e.wait_ge(sC.h, DBGC[0])
            if gi >= 2:
                tS, tT = seg_consumed[pair_list[gi - 2]]
                e.wait_ge(sS.h, tS)
                e.wait_ge(sT.h, tT)
            for par in range(2):
                for tl in range(2):
                    e.dma_start(out=xz_t(sbuf_i, par, tl), in_=xview(x_h, b, s, par, tl)).then_inc(sX[gi % 2].h, 16)
        Y(loadseg)
        xload_done[(b, s)] = (gi % 2, sX[gi % 2].inc(16 * 4))

    def emit_cast(gi):
        b, s = pair_list[gi]
        sbuf_i = gi % 2

        def castseg(e, b=b, s=s, sbuf_i=sbuf_i, gi=gi):
            if gi == 2 and _os.environ.get("KDBG"):
                e.wait_ge(sC.h, DBGC[0])
            xi, xv = xload_done[(b, s)]
            e.wait_ge(sX[xi].h, xv)
            if gi >= 2:
                tS, tT = seg_consumed[pair_list[gi - 2]]
                e.wait_ge(sS.h, tS)
                e.wait_ge(sT.h, tT)
            for par in range(2):
                for tl in range(2):
                    e.dma_start(out=xb_t(sbuf_i, par, tl), in_=xz_t(sbuf_i, par, tl)).then_inc(sXB[gi % 2].h, 16)
        G(castseg)
        xcast_done[(b, s)] = (gi % 2, sXB[gi % 2].inc(16 * 4))

    def emit_attn_unit(ui):
        b, s, par, hh = attn_units[ui]
        gi = pair_list.index((b, s))
        sbuf_i = gi % 2
        slot = ui % 2
        col = (2 * hh + par) * 128

        def t_tp(e, b=b, s=s, par=par, ui=ui, sbuf_i=sbuf_i, col=col):
            if ui == 0:
                e.wait_ge(sV.h, V_BCAST)
            ci, cv = xcast_done[(b, s)]
            e.wait_ge(sXB[ci].h, cv)
            if _os.environ.get("KSERIAL") and ui >= 1:
                e.wait_ge(sV.h, uV_z[ui - 1])
            if _os.environ.get("KSERIAL") and ui % 16 == 0 and ui >= 16:
                li_prev = max(i for i, lu in enumerate(ln_units) if lu[0] == (ui // 16) - 1)
                e.wait_ge(sV.h, lv_logit[li_prev])
            if ui >= 1:
                e.wait_ge(sV.h, uV_gT[ui - 1])
            e.transpose(pbank_bf[0][:, 0:128], xb_t(sbuf_i, par, 0)[:, col:col + 128], IDT[:, :])
            e.transpose(pbank_bf[1][:, 0:128], xb_t(sbuf_i, par, 1)[:, col:col + 128], IDT[:, :]).then_inc(sT.h, 1)
        T(t_tp)
        uT_tp[ui] = sT.inc()

        if pend_tpv[0] is not None:
            fnp, pu = pend_tpv[0]
            T(fnp)
            uT_pv[pu] = sT.inc()
            pend_tpv[0] = None

        def v_gT(e, ui=ui, slot=slot):
            e.wait_ge(sT.h, uT_tp[ui])
            if ui >= 2:
                e.wait_ge(sT.h, uT_sc[ui - 2])
            go = slot * 256
            e.tensor_copy(gT[:, go:go + 128], pbank_bf[0][:, 0:128])
            e.tensor_copy(gT[:, go + 128:go + 256], pbank_bf[1][:, 0:128]).then_inc(sV.h, 1)
        V(v_gT)
        uV_gT[ui] = sV.inc()

        if pend_vz[0] is not None:
            fn, pu = pend_vz[0]
            V(fn)
            uV_z[pu] = sV.inc()
            pend_vz[0] = None

        def t_sc(e, ui=ui, slot=slot):
            e.wait_ge(sV.h, uV_gT[ui])
            if ui >= 1:
                e.wait_ge(sS.h, uS_E[ui - 1] - 1)
            go = slot * 256
            e.matmul(pbank[2][:, 0:256], gT[:, go:go + 128], gT[:, go:go + 256], start=True, stop=True).then_inc(sT.h, 1)
            if ui >= 1:
                e.wait_ge(sS.h, uS_E[ui - 1])
            e.matmul(pbank[3][:, 0:256], gT[:, go + 128:go + 256], gT[:, go:go + 256], start=True, stop=True).then_inc(sT.h, 1)
        T(t_sc)
        uT_sc1[ui] = sT.inc()
        uT_sc[ui] = sT.inc()

        def s_E(e, ui=ui, slot=slot):
            e.wait_ge(sT.h, uT_sc1[ui])
            if ui >= 2:
                e.wait_ge(sV.h, uV_R[ui - 2])
            eo = slot * 512
            do = slot * 2
            e.activation(out=Ebuf[:, eo:eo + 256], in_=pbank[2][:, 0:256], func=AF.Exp,
                         scale=SCALE, accum_out=Dbuf[:, do:do + 1]).then_inc(sS.h, 1)
            e.wait_ge(sT.h, uT_sc[ui])
            e.activation(out=Ebuf[:, eo + 256:eo + 512], in_=pbank[3][:, 0:256], func=AF.Exp,
                         scale=SCALE, accum_out=Dbuf[:, do + 1:do + 2]).then_inc(sS.h, 1)
        SC(s_E)
        sS.inc()
        uS_E[ui] = sS.inc()

        def v_R(e, ui=ui, slot=slot):
            e.wait_ge(sS.h, uS_E[ui])
            do = slot * 2
            e.reciprocal(Rbuf[:, do:do + 2], Dbuf[:, do:do + 2]).then_inc(sV.h, 1)
        V(v_R)
        uV_R[ui] = sV.inc()

        def t_pv(e, ui=ui, slot=slot, sbuf_i=sbuf_i, par=par, col=col):
            e.wait_ge(sS.h, uS_E[ui])
            if ui >= 1:
                e.wait_ge(sV.h, uV_z[ui - 1])
            eo = slot * 512
            last = None
            for j in range(2):
                for k in range(2):
                    last = e.matmul(pbank[4 + j][:, 0:128],
                                    Ebuf[:, eo + k * 256 + j * 128:eo + k * 256 + (j + 1) * 128],
                                    xb_t(sbuf_i, par, k)[:, col:col + 128],
                                    start=(k == 0), stop=(k == 1))
            last.then_inc(sT.h, 1)
        pend_tpv[0] = (t_pv, ui)

        def v_z(e, ui=ui, slot=slot, sbuf_i=sbuf_i, par=par, col=col, flushed=False):
            e.wait_ge(sT.h, uT_pv[ui])
            e.wait_ge(sV.h, uV_R[ui])
            if flushed:
                e.drain()
            do = slot * 2
            last = None
            for j in range(2):
                zt = xz_t(sbuf_i, par, j)
                last = e.scalar_tensor_tensor(zt[:, col:col + 128], pbank[4 + j][:, 0:128],
                                              Rbuf[:, do + j:do + j + 1], zt[:, col:col + 128],
                                              ALU.mult, ALU.add)
            last.then_inc(sV.h, 1)
        pend_vz[0] = (v_z, ui)

    def emit_ln_unit(li):
        gi, b, s, par, tl = ln_units[li]
        sbuf_i = gi % 2
        k = 4 * s + 2 * par + tl
        ring = k % 4
        ering = li % 2

        def v_stats(e, li=li, b=b, s=s, par=par, tl=tl, ring=ring, sbuf_i=sbuf_i):
            e.wait_ge(sV.h, uV_z[last_unit_of_pair[(b, s, par)]])
            if li >= 4:
                e.wait_ge(sS.h, ls_apply[li - 4])
            zt = xz_t(sbuf_i, par, tl)
            for i in range(4):
                e.bn_stats(lnstats[:, ring * 24 + 6 * i:ring * 24 + 6 * (i + 1)], zt[:, 512 * i:512 * (i + 1)])
            o = ring * 8
            e.drain()
            e.bn_aggr(lnmv[:, o:o + 2], lnstats[:, ring * 24:ring * 24 + 24]).then_inc(sV.h, 1)
        V(v_stats)
        lv_stats[li] = sV.inc()

        def s_inv2(e, li=li, ring=ring):
            e.wait_ge(sV.h, lv_stats[li])
            o = ring * 8
            e.activation(out=lnmv[:, o + 7:o + 8], in_=lnmv[:, o + 1:o + 2], func=AF.Ln, bias=LNB[:, 0:1])
            e.activation(out=lnmv[:, o + 2:o + 3], in_=lnmv[:, o + 7:o + 8], func=AF.Exp, scale=-0.5)
            e.activation(out=lnmv[:, o + 7:o + 8], in_=lnmv[:, o + 7:o + 8], func=AF.Exp, scale=-1.0)
            e.activation(out=lnmv[:, o + 4:o + 5], in_=lnmv[:, o + 7:o + 8], func=AF.Ln,
                         scale=-EPS, bias=LNB[:, 1:2])
            e.activation(out=lnmv[:, o + 4:o + 5], in_=lnmv[:, o + 4:o + 5], func=AF.Exp, scale=-0.5).then_inc(sS.h, 1)
        SC(s_inv2)
        ls_inv[li] = sS.inc()

        def v_small(e, li=li, ring=ring):
            e.wait_ge(sS.h, ls_inv[li])
            o = ring * 8
            e.scalar_tensor_tensor(lnmv[:, o + 3:o + 4], lnmv[:, o + 0:o + 1], -1.0,
                                   lnmv[:, o + 2:o + 3], ALU.mult, ALU.mult)
            e.tensor_tensor(lnmv[:, o + 5:o + 6], lnmv[:, o + 2:o + 3], lnmv[:, o + 4:o + 5], ALU.mult)
            e.tensor_tensor(lnmv[:, o + 6:o + 7], lnmv[:, o + 3:o + 4], lnmv[:, o + 4:o + 5], ALU.mult).then_inc(sV.h, 1)
        V(v_small)
        lv_sc2 = sV.inc()

        def s_apply(e, li=li, ring=ring, ering=ering, b=b, s=s, par=par, tl=tl, sbuf_i=sbuf_i, k=k, lv_sc2=lv_sc2):
            e.wait_ge(sV.h, lv_sc2)
            o = ring * 8
            zt = xz_t(sbuf_i, par, tl)
            if li == 4 and _os.environ.get("KDBG"):
                e.wait_ge(sC.h, DBGC[0])
            if li >= 4:
                e.wait_ge(sM4[li % 4].h, lm_mirror[li - 4])
                e.wait_ge(sV.h, lv_logit[li - 4])
            if li >= 2:
                e.wait_ge(sE2[li % 2].h, ly_early[li - 2])
            e.activation(out=x1st[:, ring * D:(ring + 1) * D], in_=zt[:, :], func=AF.Identity,
                         bias=lnmv[:, o + 3:o + 4], scale=lnmv[:, o + 2:o + 3])
            e.activation(out=outear[:, ering * D:(ering + 1) * D], in_=zt[:, :], func=AF.Identity,
                         bias=lnmv[:, o + 6:o + 7], scale=lnmv[:, o + 5:o + 6]).then_inc(sS.h, 1)
            e.wait_ge(sS.h, ls_apply[li])
            e.dma_start(out=x1m_h[b][128 * k:128 * (k + 1), :],
                        in_=x1st[:, ring * D:(ring + 1) * D]).then_inc(sM4[li % 4].h, 16)
        ls_apply[li] = sS.inc()
        SC(s_apply)
        lm_mirror[li] = sM4[li % 4].inc(16)

        def v_logit(e, li=li, ring=ring, b=b, k=k):
            e.wait_ge(sS.h, ls_apply[li])
            e.scalar_tensor_tensor(sttscr[:, :], x1st[:, ring * D:(ring + 1) * D], 1.0, WrB[:, :],
                                   ALU.mult, ALU.mult,
                                   accum_out=Lgrid[:, b * 16 + k:b * 16 + k + 1]).then_inc(sV.h, 1)
        pend_logit.append((li, v_logit))

        def y_early(e, li=li, ering=ering, b=b, s=s, par=par, tl=tl):
            e.wait_ge(sS.h, ls_apply[li])
            e.dma_start(out=xview(out_h, b, s, par, tl), in_=outear[:, ering * D:(ering + 1) * D]).then_inc(sE2[li % 2].h, 16)
        Y(y_early)
        ly_early[li] = sE2[li % 2].inc(16)

        if par == 1 and tl == 1:
            seg_consumed[(b, s)] = (ls_apply[li], uT_pv[last_unit_of_pair[(b, s, 1)]])

    router_state = {}

    def build_router_ops(b, prev_ext_t):
        st = {}
        router_state[b] = st
        def v_bnd1(e, b=b, prev_ext_t=prev_ext_t):
            e.wait_ge(sV.h, lv_logit[last_li_of_b[b]])
            if prev_ext_t is not None:
                e.wait_ge(sT.h, prev_ext_t)
            e.reduce_max(bndc[:, 0:1], Lgrid[:, b * 16:(b + 1) * 16], axis=AX.X)
            e.tensor_scalar(lbr_b[:, :], Lgrid[:, b * 16:(b + 1) * 16], -1.0, None, ALU.mult)
            e.reduce_max(bndc[:, 1:2], lbr_b[:, :], axis=AX.X).then_inc(sV.h, 1)
        st["vbnd1"] = sV.inc()
        yield ("V", v_bnd1)

        def t_bnd(e, b=b, st=st):
            e.wait_ge(sV.h, st["vbnd1"])
            e.transpose(pbank[0][0:1, 0:128], bndc[:, 0:1], IDTF[:, :])
            e.transpose(pbank[1][0:1, 0:128], bndc[:, 1:2], IDTF[:, :]).then_inc(sT.h, 1)
        st["tbnd"] = sT.inc()
        yield ("T", t_bnd)

        def v_bnd2(e, b=b, st=st):
            e.wait_ge(sT.h, st["tbnd"])
            e.reduce_max(bnds[:, 0:1], pbank[0][0:1, 0:128], axis=AX.X)
            e.reduce_max(bnds[:, 1:2], pbank[1][0:1, 0:128], axis=AX.X).then_inc(sV.h, 1)
        st["vbnd2"] = sV.inc()
        yield ("V", v_bnd2)

        def t_bnd2(e, b=b, st=st):
            e.wait_ge(sV.h, st["vbnd2"])
            e.matmul(pbank[0][:, 0:2], ONES1F[:, :], bnds[:, :], start=True, stop=True).then_inc(sT.h, 1)
        st["tbnd2"] = sT.inc()
        yield ("T", t_bnd2)

        def v_binit(e, b=b, st=st):
            e.wait_ge(sT.h, st["tbnd2"])
            e.tensor_copy(bndB[:, :], pbank[0][:, 0:2])
            e.drain()
            e.tensor_scalar(hi_b[:, b:b + 1], bndB[:, 0:1], 2.0, None, ALU.mult)
            e.tensor_scalar(lo_b[:, b:b + 1], bndB[:, 1:2], -2.0, -2.0, ALU.mult, ALU.add)
            e.tensor_scalar(L2g[:, :], Lgrid[:, b * 16:(b + 1) * 16], 2.0, None, ALU.mult)
            e.tensor_tensor(mid_b[:, b:b + 1], lo_b[:, b:b + 1], hi_b[:, b:b + 1], ALU.add)
            e.drain()
            with nc.allow_low_precision("bisection counts <=16 are exact in bf16"):
                e.tensor_scalar(tmp16[:, :], L2g[:, :], mid_b[:, b:b + 1], None, ALU.is_gt,
                                ALU.add, accum_out=prt_b[:, b:b + 1]).then_inc(sV.h, 1)
        st[("v", 0)] = sV.inc()
        yield ("V", v_binit)

        for it in range(NBIS):
            def t_it(e, b=b, it=it, st=st):
                e.wait_ge(sV.h, st[("v", it)])
                e.matmul(pbank[it % 2][:, 0:1], ONESC[:, :], prt_b[:, b:b + 1], start=True, stop=True).then_inc(sT.h, 1)
            st[("t", it)] = sT.inc()
            yield ("T", t_it)

            def v_it(e, b=b, it=it, st=st):
                e.wait_ge(sT.h, st[("t", it)])
                ps = pbank[it % 2]
                e.tensor_scalar(cge_b[:, b:b + 1], ps[:, 0:1], float(NSLOT) - 0.5, None, ALU.is_ge)
                e.tensor_tensor(dlt_b[:, b:b + 1], mid_b[:, b:b + 1], lo_b[:, b:b + 1], ALU.subtract)
                e.tensor_tensor(dlt2_b[:, b:b + 1], hi_b[:, b:b + 1], mid_b[:, b:b + 1], ALU.subtract)
                e.scalar_tensor_tensor(lo_b[:, b:b + 1], dlt_b[:, b:b + 1], cge_b[:, b:b + 1],
                                       lo_b[:, b:b + 1], ALU.mult, ALU.add)
                e.scalar_tensor_tensor(hi_b[:, b:b + 1], dlt2_b[:, b:b + 1], cge_b[:, b:b + 1],
                                       mid_b[:, b:b + 1], ALU.mult, ALU.add)
                last = None
                if it < NBIS - 1:
                    e.drain()
                    e.tensor_tensor(mid_b[:, b:b + 1], lo_b[:, b:b + 1], hi_b[:, b:b + 1], ALU.add)
                    e.drain()
                    with nc.allow_low_precision("bisection counts <=16 are exact in bf16"):
                        last = e.tensor_scalar(tmp16[:, :], L2g[:, :], mid_b[:, b:b + 1], None, ALU.is_gt,
                                               ALU.add, accum_out=prt_b[:, b:b + 1])
                else:
                    e.drain()
                    last = e.tensor_scalar(mask_b[:, b * 16:(b + 1) * 16], L2g[:, :],
                                           lo_b[:, b:b + 1], None, ALU.is_gt)
                last.then_inc(sV.h, 1)
            st[("v", it + 1)] = sV.inc()
            yield ("V", v_it)

        def v_fin(e, b=b, st=st):
            e.wait_ge(sV.h, st[("v", NBIS)])
            e.tensor_scalar(lbr_b[:, :], Lgrid[:, b * 16:(b + 1) * 16], bndB[:, 0:1], None,
                            ALU.subtract)
            e.tensor_copy(maskbf[:, :], mask_b[:, b * 16:(b + 1) * 16]).then_inc(sV.h, 1)
        st["vfin"] = sV.inc()
        yield ("V", v_fin)

        def s_wexp(e, b=b, st=st):
            e.wait_ge(sV.h, st["vfin"])
            e.activation(out=wexp[:, :], in_=lbr_b[:, :], func=AF.Exp).then_inc(sS.h, 1)
        st["swexp"] = sS.inc()
        yield ("S", s_wexp)

        def t_ct(e, b=b, st=st):
            e.wait_ge(sV.h, st["vfin"])
            e.matmul(pbank[0][0:16, 0:1], maskbf[:, :], ONESC[:, 0:1], start=True, stop=True).then_inc(sT.h, 1)
        st["tct"] = sT.inc()
        yield ("T", t_ct)

        def v_ct(e, b=b, st=st):
            e.wait_ge(sT.h, st["tct"])
            e.wait_ge(sS.h, st["swexp"])
            e.tensor_tensor(wm32[:, :], wexp[:, :], mask_b[:, b * 16:(b + 1) * 16], ALU.mult)
            e.tensor_copy(ct16[:, :], pbank[0][0:16, 0:1])
            e.reduce_sum(wprt[:, :], wm32[:, :], axis=AX.X).then_inc(sV.h, 1)
        st["vct"] = sV.inc()
        yield ("V", v_ct)

        def t_cb(e, b=b, st=st):
            e.wait_ge(sV.h, st["vct"])
            e.matmul(pbank[1][0:1, 0:16], ct16[:, :], TRI16[:, :], start=True, stop=True)
            e.matmul(pbank[0][0:1, 0:1], wprt[:, :], ONESCF[:, :], start=True, stop=True).then_inc(sT.h, 1)
        st["tcb"] = sT.inc()
        yield ("T", t_cb)

        def v_cb(e, b=b, st=st):
            e.wait_ge(sT.h, st["tcb"])
            e.tensor_copy(zt_s[:, b:b + 1], pbank[0][0:1, 0:1])
            e.tensor_copy(cb16[:, :], pbank[1][0:1, 0:16])
            e.reciprocal(zt_s[:, b:b + 1], zt_s[:, b:b + 1]).then_inc(sV.h, 1)
        st["vcb"] = sV.inc()
        yield ("V", v_cb)

        def t_pos(e, b=b, st=st):
            e.wait_ge(sV.h, st["vcb"])
            e.matmul(pbank[0][:, 0:16], ONESC[0:1, :], cb16[:, :], start=True, stop=True)
            e.matmul(pbank[1][:, 0:16], TRI[:, :], maskbf[:, :], start=True, stop=True).then_inc(sT.h, 1)
        st["tpos"] = sT.inc()
        yield ("T", t_pos)

        def v_pos(e, b=b, st=st):
            e.wait_ge(sT.h, st["tpos"])
            e.tensor_copy(pos_b[:, :], pbank[0][:, 0:16])
            e.tensor_copy(vspc[:, 0:1], vspc[:, 1:2])
            e.tensor_tensor(pos_b[:, :], pos_b[:, :], pbank[1][:, 0:16], ALU.add)
            e.drain()
            for kk in range(16):
                e.tensor_scalar(PT[:, 256 * kk:256 * (kk + 1)], IO256[:, :], pos_b[:, kk:kk + 1],
                                mask_b[:, b * 16 + kk:b * 16 + kk + 1], ALU.is_equal, ALU.mult)
            e.tensor_copy(VALS[:, 0:48:3], SG[:, :])
            e.tensor_copy(VALS[:, 1:48:3], TG[:, :])
            last = None
            for kk in range(16):
                last = e.tensor_scalar(VALS[:, 3 * kk + 2:3 * kk + 3], wm32[:, kk:kk + 1], 1.0, None, ALU.mult)
            last.then_inc(sV.h, 1)
        st["vpos"] = sV.inc()
        yield ("V", v_pos)

        def t_rz(e, b=b, st=st):
            e.wait_ge(sV.h, st["vpos"])
            e.matmul(pbank[0][:, 0:1], ONES1F[:, :], zt_s[:, b:b + 1], start=True, stop=True).then_inc(sT.h, 1)
        st["trz"] = sT.inc()
        yield ("T", t_rz)

        def v_rz(e, b=b, st=st):
            e.wait_ge(sT.h, st["trz"])
            e.tensor_copy(rzB[:, :], pbank[0][:, 0:1]).then_inc(sV.h, 1)
        st["vrz"] = sV.inc()
        yield ("V", v_rz)

        for half in range(2):
            def t_ext(e, b=b, half=half, st=st):
                e.wait_ge(sV.h, st["vrz"] if half == 0 else st[("vext", 0)])
                last = None
                for kk in range(16):
                    last = e.matmul(pbank[half][:, 0:3],
                                    PT[:, 256 * kk + 128 * half:256 * kk + 128 * (half + 1)],
                                    VALS[:, 3 * kk:3 * (kk + 1)], start=(kk == 0), stop=(kk == 15))
                last.then_inc(sT.h, 1)
            st[("text", half)] = sT.inc()
            yield ("T", t_ext)

            def v_ext(e, b=b, half=half, st=st):
                e.wait_ge(sT.h, st[("text", half)])
                stg = b * 2 + half
                o = stg * 4
                e.tensor_copy(slotv[:, o:o + 3], pbank[half][:, 0:3])
                e.tensor_copy(vspc[:, 0:1], vspc[:, 1:2])
                e.tensor_tensor(slotv[:, o + 3:o + 4], slotv[:, o + 2:o + 3], rzB[:, :], ALU.mult)
                e.tensor_copy(sigI[:, stg:stg + 1], slotv[:, o:o + 1])
                e.tensor_copy(tI[:, stg:stg + 1], slotv[:, o + 1:o + 2]).then_inc(sV.h, 1)
            st[("vext", half)] = sV.inc()
            yield ("V", v_ext)

        router_state[b] = st

    emit_load(0)
    emit_load(1)
    emit_cast(0)
    emit_cast(1)

    n_units_per_pair = 16
    r0gen = [None]

    def splice(n):
        if r0gen[0] is None:
            return
        for _ in range(n):
            try:
                t_, f_ = next(r0gen[0])
            except StopIteration:
                r0gen[0] = None
                return
            OPS[t_].append(f_)

    DBGC = [0]
    for gi, (b, s) in enumerate(pair_list):
        logs = pend_logit[:]
        del pend_logit[:]
        for u in range(n_units_per_pair):
            emit_attn_unit(gi * n_units_per_pair + u)
            if logs and u < 2:
                for _ in range(2):
                    li_, fn_ = logs.pop(0)
                    V(fn_)
                    lv_logit[li_] = sV.inc()
            if b == 1 and not logs:
                splice(2)
        if pend_tpv[0] is not None:
            fnp, pu = pend_tpv[0]
            T(fnp)
            uT_pv[pu] = sT.inc()
            pend_tpv[0] = None
        if pend_vz[0] is not None:
            fn, pu = pend_vz[0]
            import functools
            V(functools.partial(fn, flushed=True))
            uV_z[pu] = sV.inc()
            pend_vz[0] = None
        for li, lu in enumerate(ln_units):
            if lu[0] == gi:
                emit_ln_unit(li)
        if gi == 0 and _os.environ.get("KDBG"):
            def g_dbg(e, li_p0=3):
                e.wait_ge(sS.h, ls_apply[li_p0])
                e.dma_start(out=dbg_h[:, 0:2048], in_=xz[:, 0:2048]).then_inc(sC.h, 16)
                e.dma_start(out=dbg_h[:, 2048:4096], in_=xbuf[:, 0:2048]).then_inc(sC.h, 16)
                e.dma_start(out=dbg_h[:, 4096:6144], in_=x1st[:, 0:2048]).then_inc(sC.h, 16)
                e.dma_start(out=dbg_h[0:128, 6144:6176], in_=lnmv[:, :]).then_inc(sC.h, 16)
            G(g_dbg)
            DBGC[0] = sC.inc(16 * 4)
        if gi + 2 < len(pair_list):
            emit_load(gi + 2)
            emit_cast(gi + 2)
        if b == 0 and s == NSEG - 1:
            r0gen[0] = build_router_ops(0, None)

    for li_, fn_ in pend_logit:
        V(fn_)
        lv_logit[li_] = sV.inc()
    del pend_logit[:]

    splice(9999)

    for t_, f_ in build_router_ops(1, router_state[0][("text", 1)]):
        OPS[t_].append(f_)


    esA.close()
    esB = ExitStack()
    g1T = esB.enter_context(nc.sbuf_tensor("g1T", [128, 64 * 512], BF16))
    w1cb = esB.enter_context(nc.sbuf_tensor("w1cb", [128, 3 * 4096], BF16))
    h2sb = esB.enter_context(nc.sbuf_tensor("h2sb", [128, 2 * 512], BF16))
    w2cb = esB.enter_context(nc.sbuf_tensor("w2cb", [128, 2 * 8192], BF16))
    ybuf = esB.enter_context(nc.sbuf_tensor("ybuf", [128, 4 * D], F32))
    fxb = esB.enter_context(nc.sbuf_tensor("fxb", [128, 4 * D], BF16))
    fxT = esB.enter_context(nc.sbuf_tensor("fxT", [128, 16 * 512], BF16))
    outro = esB.enter_context(nc.sbuf_tensor("outro", [128, 2 * D], F32))

    gath_done = {}
    for b in range(NB):
        for half in range(2):
            stg = b * 2 + half

            def g_gather(e, b=b, half=half, stg=stg):
                if stg == 0:
                    tS, tT = seg_consumed[(1, NSEG - 1)]
                    e.wait_ge(sS.h, tS)
                    e.wait_ge(sT.h, tT)
                    e.wait_ge(sV.h, router_state[1][("vext", 1)])
                    for mm in range(4):
                        e.wait_ge(sM4[mm].h, sM4[mm].n)
                else:
                    e.wait_ge(sGG.h, gath_done[stg - 1])
                e.wait_ge(sV.h, router_state[b][("vext", half)])
                e.indirect_dma_start(fxb[:, stg * D:(stg + 1) * D], None,
                                     x1m_h.rearrange("b s d -> (b s) d"),
                                     IndirectOffsetOnAxis(sigI[:, stg:stg + 1], 0),
                                     element_offset=b * S * D).then_inc(sGG.h, 16)
            G(g_gather)
            gath_done[stg] = sGG.inc(16)

    fxT_done = {}
    tp_evac = {}
    for stg in range(4):
        for t in range(16):
            idx = stg * 16 + t

            def t_fxT(e, stg=stg, t=t, idx=idx):
                e.wait_ge(sGG.h, gath_done[stg])
                if idx == 0:
                    e.wait_ge(sV.h, uV_gT[len(attn_units) - 1])
                if idx >= 2:
                    e.wait_ge(sV.h, tp_evac[idx - 2])
                e.transpose(pbank_bf[idx % 2][:, 0:128], fxb[:, stg * D + 128 * t:stg * D + 128 * (t + 1)],
                            IDT[:, :]).then_inc(sT.h, 1)
            T(t_fxT)
            fxT_done[idx] = sT.inc()

            def v_fxT(e, stg=stg, t=t, idx=idx):
                e.wait_ge(sT.h, fxT_done[idx])
                e.tensor_copy(fxT[:, 512 * t + 128 * stg:512 * t + 128 * (stg + 1)],
                              pbank_bf[idx % 2][:, 0:128]).then_inc(sV.h, 1)
            V(v_fxT)
            tp_evac[idx] = sV.inc()
    FXT_ALL = sV.n

    w1_done = {}
    h1_done = {}
    g1_done = {}
    for fb in range(32):
        def g_w1(e, fb=fb):
            if fb >= 3:
                e.wait_ge(sT.h, h1_done[2 * (fb - 3) + 1])
            e.dma_start(out=w1cb[:, (fb % 3) * 4096:(fb % 3 + 1) * 4096],
                        in_=w1_h[:, 256 * fb:256 * (fb + 1)].rearrange("(t p) f -> p t f", p=128)).then_inc(sW1[fb % 3].h, 16)
        G(g_w1)
        w1_done[fb] = sW1[fb % 3].inc(16)

    for f in range(64):
        fb = f // 2
        fo = f % 2

        def t_h1(e, f=f, fb=fb, fo=fo):
            if fo == 0:
                e.wait_ge(sW1[fb % 3].h, w1_done[fb])
            if f == 0:
                e.wait_ge(sV.h, FXT_ALL)
            if f >= 4:
                e.wait_ge(sS.h, g1_done[f - 4])
            bk = f % 4
            base = (fb % 3) * 4096 + fo * 128
            last = None
            for t in range(16):
                last = e.matmul(pbank[bk][:, 0:512],
                                w1cb[:, base + 256 * t:base + 256 * t + 128],
                                fxT[:, 512 * t:512 * (t + 1)], start=(t == 0), stop=(t == 15))
            last.then_inc(sT.h, 1)
        T(t_h1)
        h1_done[f] = sT.inc()

        def s_g1(e, f=f):
            e.wait_ge(sT.h, h1_done[f])
            bk = f % 4
            e.activation(out=g1T[:, 512 * f:512 * (f + 1)], in_=pbank[bk][:, 0:512], func=AF.Gelu,
                         bias=b1T[:, f:f + 1]).then_inc(sS.h, 1)
        SC(s_g1)
        g1_done[f] = sS.inc()

    w2_done = {}
    h2_done = {}
    h2s_done = {}
    h2t_done = {}
    yev_done = {}
    wf_done = {}
    vc = {}
    for c in range(16):
        def g_w2(e, c=c):
            if c == 0:
                tS, tT = seg_consumed[(1, NSEG - 1)]
                e.wait_ge(sS.h, tS)
                e.wait_ge(sT.h, tT)
            if c >= 2:
                e.wait_ge(sT.h, h2_done[c - 2])
            src = w2_h[:, 128 * c:128 * (c + 1)].rearrange("(t p) f -> p t f", p=128)
            if c < 2:
                e.dma_start(out=w2cb[:, (c % 2) * 8192:(c % 2 + 1) * 8192],
                            in_=src).then_inc(sW2[c % 2].h, 16)
            else:
                e.dma_start(out=w2cb[:, (c % 2) * 8192:(c % 2) * 8192 + 4096],
                            in_=src[:, 0:32, :]).then_inc(sW2[c % 2].h, 16)
        G(g_w2)
        w2_done[c] = sW2[c % 2].inc(16)

    for c in range(2, 16):
        def g_wf(e, c=c):
            if c == 2:
                e.wait_ge(sT.h, h1_done[63])
            else:
                e.wait_ge(sS.h, vc[c - 1])
            srcw = w2_h[:, 128 * c:128 * (c + 1)].rearrange("(t p) f -> p t f", p=128)
            e.dma_start(out=fxT[:, :].bitcast(F32)[:, 0:2048],
                        in_=srcw[:, 32:48, :]).then_inc(sWF.h, 16)
            e.dma_start(out=fxT[:, :].bitcast(F32)[:, 2048:4096],
                        in_=srcw[:, 48:64, :]).then_inc(sWF.h, 16)
        Y(g_wf)
        wf_done[c] = sWF.inc(32)

    for c in range(16):
        def t_h2(e, c=c):
            e.wait_ge(sW2[c % 2].h, w2_done[c])
            if c >= 2:
                e.wait_ge(sS.h, vc[c])
            if c == 0:
                e.wait_ge(sS.h, g1_done[63])
            if c >= 2:
                e.wait_ge(sS.h, h2s_done[c - 2])
            bk = 4 + c % 2
            last = None
            for f in range(64):
                last = e.matmul(pbank[bk][:, 0:512],
                                w2cb[:, (c % 2) * 8192 + 128 * f:(c % 2) * 8192 + 128 * (f + 1)],
                                g1T[:, 512 * f:512 * (f + 1)], start=(f == 0), stop=(f == 63))
            last.then_inc(sT.h, 1)
        T(t_h2)
        h2_done[c] = sT.inc()

        def s_h2(e, c=c):
            e.wait_ge(sT.h, h2_done[c])
            if c >= 2:
                e.wait_ge(sT.h, h2t_done[(c - 2, 3)])
            bk = 4 + c % 2
            e.activation(out=h2sb[:, (c % 2) * 512:(c % 2 + 1) * 512], in_=pbank[bk][:, 0:512],
                         func=AF.Identity, bias=b2T[:, c:c + 1]).then_inc(sS.h, 1)
        SC(s_h2)
        h2s_done[c] = sS.inc()

        if c + 2 < 16:
            def s_wcast(e, c=c + 2):
                e.wait_ge(sWF.h, wf_done[c])
                e.wait_ge(sT.h, h2_done[c - 2])
                e.activation(out=w2cb[:, (c % 2) * 8192 + 4096:(c % 2 + 1) * 8192],
                             in_=fxT[:, :].bitcast(F32), func=AF.Identity).then_inc(sS.h, 1)
            SC(s_wcast)
            vc[c + 2] = sS.inc()

        for stg in range(4):
            idx = c * 4 + stg

            def t_h2t(e, c=c, stg=stg, idx=idx):
                e.wait_ge(sS.h, h2s_done[c])
                if idx >= 2:
                    e.wait_ge(sV.h, yev_done[idx - 2])
                e.transpose(pbank_bf[idx % 2][:, 0:128],
                            h2sb[:, (c % 2) * 512 + 128 * stg:(c % 2) * 512 + 128 * (stg + 1)],
                            IDT[:, :]).then_inc(sT.h, 1)
            T(t_h2t)
            h2t_done[(c, stg)] = sT.inc()

            def v_yev(e, c=c, stg=stg, idx=idx):
                e.wait_ge(sT.h, h2t_done[(c, stg)])
                e.tensor_scalar(ybuf[:, stg * D + 128 * c:stg * D + 128 * (c + 1)],
                                pbank_bf[idx % 2][:, 0:128],
                                slotv[:, stg * 4 + 3:stg * 4 + 4], None, ALU.mult).then_inc(sV.h, 1)
            V(v_yev)
            yev_done[idx] = sV.inc()

    fin_scat = {}
    fin_conv = {}
    for stg in range(4):
        b = stg // 2

        def s_conv(e, stg=stg):
            e.wait_ge(sGG.h, gath_done[stg])
            if stg >= 2:
                e.wait_ge(sGG.h, fin_scat[stg - 2])
            e.activation(out=outro[:, (stg % 2) * D:(stg % 2 + 1) * D],
                         in_=fxb[:, stg * D:(stg + 1) * D], func=AF.Identity,
                         scale=2.0).then_inc(sS.h, 1)
        SC(s_conv)
        fin_conv[stg] = sS.inc()

        def v_fin2(e, stg=stg):
            e.wait_ge(sV.h, yev_done[15 * 4 + stg])
            e.wait_ge(sS.h, fin_conv[stg])
            yt = ybuf[:, stg * D:(stg + 1) * D]
            e.tensor_tensor(yt[:, :], yt[:, :], outro[:, (stg % 2) * D:(stg % 2 + 1) * D], ALU.add)
            o = stg * 8
            e.drain()
            for i in range(4):
                e.bn_stats(lnstats[:, stg * 24 + 6 * i:stg * 24 + 6 * (i + 1)], yt[:, 512 * i:512 * (i + 1)])
            e.drain()
            e.bn_aggr(lnmv[:, o:o + 2], lnstats[:, stg * 24:stg * 24 + 24])
            e.drain()
            e.tensor_scalar(lnmv[:, o + 1:o + 2], lnmv[:, o + 1:o + 2], EPS, None, ALU.add)
            e.drain()
            e.reciprocal(lnmv[:, o + 1:o + 2], lnmv[:, o + 1:o + 2]).then_inc(sV.h, 1)
        V(v_fin2)
        fin_stats = sV.inc()

        def s_fin2(e, stg=stg, fin_stats=fin_stats):
            e.wait_ge(sV.h, fin_stats)
            o = stg * 8
            e.activation(out=lnmv[:, o + 1:o + 2], in_=lnmv[:, o + 1:o + 2], func=AF.Ln)
            e.activation(out=lnmv[:, o + 1:o + 2], in_=lnmv[:, o + 1:o + 2], func=AF.Exp, scale=0.5).then_inc(sS.h, 1)
        SC(s_fin2)
        fin_inv = sS.inc()

        def v_fin3(e, stg=stg, fin_inv=fin_inv):
            e.wait_ge(sS.h, fin_inv)
            o = stg * 8
            e.scalar_tensor_tensor(lnmv[:, o + 3:o + 4], lnmv[:, o + 0:o + 1], -1.0,
                                   lnmv[:, o + 1:o + 2], ALU.mult, ALU.mult).then_inc(sV.h, 1)
        V(v_fin3)
        fin_b = sV.inc()

        def s_fin3(e, stg=stg, fin_b=fin_b):
            e.wait_ge(sV.h, fin_b)
            o = stg * 8
            e.activation(out=outro[:, (stg % 2) * D:(stg % 2 + 1) * D], in_=ybuf[:, stg * D:(stg + 1) * D],
                         func=AF.Identity, bias=lnmv[:, o + 3:o + 4],
                         scale=lnmv[:, o + 1:o + 2]).then_inc(sS.h, 1)
        SC(s_fin3)
        fin_app = sS.inc()

        def g_scat(e, stg=stg, b=b, fin_app=fin_app, prev_gg=sGG.n):
            e.wait_ge(sS.h, fin_app)
            e.wait_ge(sGG.h, prev_gg)
            if stg == 0:
                for ee in range(2):
                    e.wait_ge(sE2[ee].h, sE2[ee].n)
            e.indirect_dma_start(out_h.rearrange("b s d -> (b s) d"),
                                 IndirectOffsetOnAxis(tI[:, stg:stg + 1], 0),
                                 outro[:, (stg % 2) * D:(stg % 2 + 1) * D], None,
                                 element_offset=b * S * D).then_inc(sGG.h, 16)
        G(g_scat)
        fin_scat[stg] = sGG.inc(16)

    FINAL_GG = sGG.n
    FINAL_E = [s.n for s in sE2]

    def g_end(e):
        e.wait_ge(sGG.h, FINAL_GG)
        for i in range(2):
            e.wait_ge(sE2[i].h, FINAL_E[i])
    G(g_end)

    def y_end(e):
        e.wait_ge(sGG.h, FINAL_GG)
        for i in range(2):
            e.wait_ge(sE2[i].h, FINAL_E[i])
    Y(y_end)

    import os
    _tr = float(os.environ.get("KTRUNC", "1.0"))
    if _tr < 1.0:
        for _k in OPS:
            OPS[_k] = OPS[_k][:max(1, int(len(OPS[_k]) * _tr))]
    with nc.Block() as block:
        @block.tensor
        def _(eng):
            for fn in OPS["T"]:
                fn(eng)

        @block.vector
        def _(eng):
            for fn in OPS["V"]:
                fn(eng)

        @block.scalar
        def _(eng):
            for fn in OPS["S"]:
                fn(eng)

        @block.gpsimd
        def _(eng):
            for fn in OPS["G"]:
                fn(eng)

        @block.sync
        def _(eng):
            for fn in OPS["Y"]:
                fn(eng)

    esB.close()
    es.close()
    return nc


def _consts():
    c = {}
    c["IDT"] = np.eye(128, dtype=np.float32).astype(ml_dtypes.bfloat16)
    c["IDTF"] = np.eye(128, dtype=np.float32)
    c["TRI"] = np.triu(np.ones((128, 128), np.float32), 1).astype(ml_dtypes.bfloat16)
    c["TRI16"] = np.triu(np.ones((16, 16), np.float32), 1).astype(ml_dtypes.bfloat16)
    c["ONESC"] = np.ones((128, 128), np.float32).astype(ml_dtypes.bfloat16)
    c["ONES1F"] = np.ones((1, 128), np.float32)
    c["ONESCF"] = np.ones((128, 1), np.float32)
    c["IO256"] = np.broadcast_to(np.arange(256, dtype=np.float32)[None, :], (128, 256)).copy()
    c["LNB"] = np.broadcast_to(np.array([[1e-5, 1.0 + 1e-5 / 4.0]], np.float32), (128, 2)).copy()
    q = np.arange(128)[:, None]
    k = np.arange(16)[None, :]
    c["SG"] = (128 * k + q).astype(np.float16)
    s_ = k // 4
    par_ = (k % 4) // 2
    tl_ = k % 2
    c["TG"] = (512 * s_ + 256 * tl_ + 2 * q + par_).astype(np.float16)
    return c


_NC_CACHE = {}


def kernel(L0_attn_x, L0_ffn_Wr, L0_ffn_br, L0_ffn_W1, L0_ffn_b1, L0_ffn_W2, L0_ffn_b2):
    x = np.ascontiguousarray(np.asarray(L0_attn_x, dtype=np.float32))
    Wr = np.ascontiguousarray(np.asarray(L0_ffn_Wr, dtype=np.float32).reshape(1, D))
    br = np.ascontiguousarray(np.asarray(L0_ffn_br, dtype=np.float32).reshape(1, 1))
    W1 = np.ascontiguousarray(np.asarray(L0_ffn_W1, dtype=np.float32))
    b1 = np.ascontiguousarray(np.asarray(L0_ffn_b1, dtype=np.float32).reshape(64, 128).T)
    W2 = np.ascontiguousarray(np.asarray(L0_ffn_W2, dtype=np.float32))
    b2 = np.ascontiguousarray(np.asarray(L0_ffn_b2, dtype=np.float32).reshape(16, 128).T)
    consts = _consts()

    if "nc" not in _NC_CACHE:
        _NC_CACHE["nc"] = build_nc()
    nc = _NC_CACHE["nc"]

    in_maps = []
    for c in range(8):
        m = {"x": np.ascontiguousarray(x[2 * c:2 * c + 2]),
             "Wr": Wr, "br": br, "W1": W1, "b1": b1, "W2": W2, "b2": b2}
        m.update(consts)
        in_maps.append(m)
    res = run_bass_kernel_spmd(nc, in_maps, core_ids=list(range(8)))
    out = np.concatenate([res.results[i]["out"] for i in range(8)], axis=0)
    return out.astype(np.float32)
